# version 12 (final)
# baseline (speedup 1.0000x reference)
import contextlib
import numpy as np
import ml_dtypes
import concourse.bass as bass
import concourse.mybir as mybir
from concourse.bass_utils import run_bass_kernel_spmd

F32 = mybir.dt.float32
BF16 = mybir.dt.bfloat16
AF = mybir.ActivationFunctionType
ALU = mybir.AluOpType
AX = mybir.AxisListType

NCORE = 8
BLK = 512
NPOS = 8192
NB = 2
D = 1024
TOKS = 2048
LOCP = 1024
EPS = 1e-5
BF = ml_dtypes.bfloat16



def zz_rows(c):
    return np.arange(LOCP * c, LOCP * (c + 1))


def shard_xT(x):
    out = []
    for c in range(NCORE):
        xs = x[:, zz_rows(c), :]
        out.append(np.ascontiguousarray(xs.transpose(2, 0, 1).reshape(D, TOKS)))
    return out


def unshard_xT(shards):
    out = np.empty((NB, NPOS, D), dtype=np.float32)
    for c in range(NCORE):
        xT = np.asarray(shards[c], dtype=np.float32).reshape(D, NB, LOCP)
        out[:, zz_rows(c), :] = xT.transpose(1, 2, 0)
    return out


def colpack(v):
    return np.ascontiguousarray(np.asarray(v, dtype=np.float32).reshape(8, 128).T)


def transpose_w(w):
    return np.ascontiguousarray(np.asarray(w, dtype=np.float32).T)


_JAX_TABLES = {}


def rope_tables():
    if _JAX_TABLES:
        return _JAX_TABLES
    import jax
    try:
        cpu = jax.devices("cpu")[0]
    except Exception:
        jax.config.update("jax_platforms", "cpu")
        cpu = jax.devices("cpu")[0]
    import jax.numpy as jnp
    with jax.default_device(cpu):
        half = 64
        pos = jnp.arange(NPOS, dtype=jnp.float32)[:, None]
        invf = 10000.0 ** (jnp.arange(half, dtype=jnp.float32) / half)
        arg = pos * invf[None, :]
        l0_sin = np.asarray(jnp.sin(arg))
        l0_cos = np.asarray(jnp.cos(arg))
        invf1 = 1.0 / (10000.0 ** (jnp.arange(0, 64, 2, dtype=jnp.float32) / 64))
        t = jnp.arange(NPOS, dtype=jnp.float32)
        fr = jnp.outer(t, invf1)
        l1_sin = np.asarray(jnp.sin(fr))
        l1_cos = np.asarray(jnp.cos(fr))
    _JAX_TABLES.update(l0_sin=l0_sin, l0_cos=l0_cos, l1_sin=l1_sin, l1_cos=l1_cos)
    return _JAX_TABLES



class Em:

    COMPUTE_SEM = {"pe": "pe", "act": "act", "dve": "dve", "pool": "pool"}
    _w_ready = None

    def __init__(self, nc, stack):
        self.nc = nc
        E = stack.enter_context
        self.sem = {n: E(nc.semaphore("sem_" + n))
                    for n in ("pe", "act", "dve", "pool", "qs", "qg",
                              "cc0", "cc1", "cc2", "cc3", "cc4", "cc5")}
        self.cc_i = 0
        self.barrier_toks = []
        self.cnt = {n: 0 for n in self.sem}
        self.seen = {}
        self.q = {n: [] for n in ("pe", "act", "dve", "pool", "sync")}

    def next_cc(self):
        n = f"cc{self.cc_i}"
        self.cc_i += 1
        return n

    def barrier(self, tiny_out, tiny_in):
        self.push("pe", [], lambda t: t.nop(), inc="pe")
        self.push("act", [], lambda s: s.copy(tiny_out, tiny_in), inc="act")
        self.push("dve", [], lambda v: v.tensor_copy(tiny_out, tiny_in),
                  inc="dve")
        self.push("pool", [], lambda g: g.tensor_copy(tiny_out, tiny_in),
                  inc="pool")
        self.barrier_toks = [(s, c) for s, c in self.cnt.items() if c > 0]

    def push(self, engq, waits, build, inc=None, incn=1):
        rw = []
        for w in list(waits) + self.barrier_toks:
            if w is None:
                continue
            s, v = w
            if self.COMPUTE_SEM.get(engq) == s:
                continue
            if self.seen.get((engq, s), 0) >= v:
                continue
            self.seen[(engq, s)] = v
            rw.append((s, v))
        tok = None
        if inc is not None:
            self.cnt[inc] += incn
            tok = (inc, self.cnt[inc])
        sems = self.sem

        def thunk(eng, rw=rw, build=build, tok=tok, incn=incn):
            for s, v in rw:
                eng.wait_ge(sems[s], v)
            ins = build(eng)
            if tok is not None:
                ins.then_inc(sems[tok[0]], incn)
        self.q[engq].append(thunk)
        return tok

    def cnt_tok(self, sem):
        return (sem, self.cnt[sem])

    def fence(self, waits, tiny_out, tiny_in):
        return self.push("dve", waits,
                         lambda v: v.tensor_copy(tiny_out, tiny_in), inc="dve")

    def dma(self, engq, waits, out, in_, **kw):
        qsem = "qs" if engq == "sync" else "qg"
        return self.push(engq, waits,
                         lambda e: e.dma_start(out=out, in_=in_, **kw),
                         inc=qsem, incn=16)

    def run(self, block):
        q = self.q

        @block.tensor
        def _(t):
            for f in q["pe"]:
                f(t)

        @block.scalar
        def _(s):
            for f in q["act"]:
                f(s)

        @block.vector
        def _(v):
            for f in q["dve"]:
                f(v)

        @block.gpsimd
        def _(g):
            for f in q["pool"]:
                f(g)

        @block.sync
        def _(sy):
            for f in q["sync"]:
                f(sy)


class Bufs:
    pass



def emit_ln(em, b, T, xsrc, xn, nc0, extra_waits=(), keep_var=False):
    ncg = T // 512
    xsrc3 = xsrc.rearrange("(ic p) t -> ic p t", p=128)
    tok_st = [None] * 8
    tok_cgst = {}
    tok_ld = [None] * 8
    for ic in range(8):
        s = ic % 2
        tok_ld[ic] = em.dma("pool",
                            [tok_st[ic - 2] if ic >= 2 else None],
                            b.xchunk[s][:, 0:T], xsrc3[ic])
        last = None
        for cg in range(ncg):
            c0 = cg * 512
            sq = b.xsq[cg % 2]
            prev_sq = tok_cgst.get((ic, cg - 2)) or (
                tok_cgst.get((ic - 1, ncg - 2 + cg)) if cg < 2 else None)
            tok_sq = em.push("act", [tok_ld[ic], prev_sq],
                             lambda e, s=s, c0=c0, sq=sq: e.activation(
                                 sq, b.xchunk[s][:, c0:c0 + 512], AF.Square),
                             inc="act")

            def stats(t, ic=ic, s=s, c0=c0, sq=sq):
                t.matmul(b.ps_a[0:1, c0:c0 + 512], b.ones_col,
                         b.xchunk[s][:, c0:c0 + 512], start=(ic == 0), stop=(ic == 7))
                return t.matmul(b.ps_b[0:1, c0:c0 + 512], b.ones_col,
                                sq, start=(ic == 0), stop=(ic == 7))
            last = em.push("pe", [tok_sq, *extra_waits], stats, inc="pe")
            tok_cgst[(ic, cg)] = last
        tok_st[ic] = last
    em.push("dve", [tok_st[7]],
            lambda v: v.tensor_scalar(b.m_row[0:1, 0:T], b.ps_a[0:1, 0:T],
                                      1.0 / D, None, ALU.mult))
    em.push("dve", [],
            lambda v: v.tensor_tensor(b.rs_row[0:1, 0:T], b.m_row[0:1, 0:T],
                                      b.m_row[0:1, 0:T], ALU.mult))
    tok_v = em.push("dve", [],
                    lambda v: v.scalar_tensor_tensor(
                        b.rs_row[0:1, 0:T], b.ps_b[0:1, 0:T], 1.0 / D,
                        b.rs_row[0:1, 0:T], ALU.mult, ALU.subtract), inc="dve")
    if keep_var:
        tok_v = em.push("dve", [],
                        lambda v: v.tensor_copy(b.v_row[0:1, 0:T],
                                                b.rs_row[0:1, 0:T]), inc="dve")
    em.push("act", [tok_v],
            lambda s: s.activation(b.rs_row[0:1, 0:T], b.rs_row[0:1, 0:T],
                                   AF.Ln, bias=b.eps_col[0:1, :]))
    tok_rs = em.push("act", [],
                     lambda s: s.activation(b.rs_row[0:1, 0:T], b.rs_row[0:1, 0:T],
                                            AF.Exp, scale=-0.5), inc="act")
    def bcast(t):
        for cg in range(ncg):
            c0 = cg * 512
            t.matmul(b.ps_a[:, c0:c0 + 512], b.ones_row[0:1, :],
                     b.m_row[0:1, c0:c0 + 512], start=True, stop=True)
            ins = t.matmul(b.ps_b[:, c0:c0 + 512], b.ones_row[0:1, :],
                           b.rs_row[0:1, c0:c0 + 512], start=True, stop=True)
        return ins
    tok_bc = em.push("pe", [tok_rs], bcast, inc="pe")

    tok_xn = None
    tok_n = [None] * 8
    for ic in range(8):
        s = ic % 2
        war = tok_n[ic - 2] if ic >= 2 else tok_st[7]
        tok_l = em.dma("pool", [war],
                       b.xchunk[s][:, 0:T], xsrc3[ic])
        for cg in range(ncg):
            c0 = cg * 512
            osl = slice(nc0 + ic * T + c0, nc0 + ic * T + c0 + 512)
            em.push("dve", [tok_bc, tok_l],
                    lambda v, s=s, c0=c0: v.tensor_tensor(
                        b.ln_tmp, b.xchunk[s][:, c0:c0 + 512],
                        b.ps_a[:, c0:c0 + 512], ALU.subtract))
            tok_xn = em.push("dve", [],
                             lambda v, osl=osl, c0=c0: v.tensor_tensor(
                                 xn[:, osl], b.ln_tmp,
                                 b.ps_b[:, c0:c0 + 512], ALU.mult),
                             inc="dve")
        tok_n[ic] = tok_xn
    return tok_xn



def build_ffn(debug=False):
    nc = bass.Bass("TRN2", target_bir_lowering=False)
    xt_d = nc.dram_tensor("xt", [D, TOKS], F32, kind="ExternalInput").ap()
    wnames = ("wf", "uf", "wi", "ui", "wc", "uc", "wo", "uo")
    w_d = {n: nc.dram_tensor(n + "T", [D, D], F32, kind="ExternalInput").ap()
           for n in wnames}
    bias_d = nc.dram_tensor("biases", [128, 40], F32, kind="ExternalInput").ap()
    consts_d = nc.dram_tensor("consts", [128, 2], F32, kind="ExternalInput").ap()
    onesrow_d = nc.dram_tensor("onesrow", [1, 128], F32, kind="ExternalInput").ap()
    out_d = nc.dram_tensor("xo", [D, TOKS], F32, kind="ExternalOutput").ap()
    dbg = {}
    if debug:
        dbg["xn"] = nc.dram_tensor("dbg_xn", [8 * 128, TOKS], BF16,
                                   kind="ExternalOutput").ap()
        dbg["c32"] = nc.dram_tensor("dbg_c32", [8 * 128, LOCP], F32,
                                    kind="ExternalOutput").ap()
        dbg["cbf"] = nc.dram_tensor("dbg_cbf", [8 * 128, LOCP], BF16,
                                    kind="ExternalOutput").ap()
        dbg["c1"] = nc.dram_tensor("dbg_c1", [8 * 128, LOCP], F32,
                                   kind="ExternalOutput").ap()
        dbg["c0"] = nc.dram_tensor("dbg_c0", [8 * 128, LOCP], F32,
                                   kind="ExternalOutput").ap()
        for gi in range(3):
            dbg[f"g{gi}"] = nc.dram_tensor(f"dbg_g{gi}", [8 * 128, 512], BF16,
                                           kind="ExternalOutput").ap()

    with contextlib.ExitStack() as st:
        E = st.enter_context
        em = Em(nc, st)
        b = Bufs()
        b.c32 = E(nc.sbuf_tensor("c32", [128, 8 * LOCP], F32))
        b.c_bf = E(nc.sbuf_tensor("c_bf", [128, 8 * LOCP], BF16))
        b.xn = E(nc.sbuf_tensor("xn", [128, 8 * TOKS], BF16))
        b.gate = [E(nc.sbuf_tensor(f"gate{i}", [128, 8 * 512], BF16))
                  for i in range(3)]
        b.wbuf = [E(nc.sbuf_tensor(f"wbuf{i}", [128, 8 * D], BF16))
                  for i in range(4)]
        b.xchunk = [b.c32[:, 0:TOKS], b.c32[:, TOKS:2 * TOKS]]
        b.scratch = E(nc.sbuf_tensor("scratch", [128, 2560], F32))
        b.xsq = [b.scratch[:, 0:512], b.scratch[:, 512:1024]]
        b.ln_tmp = b.scratch[:, 1024:1536]
        b.t1 = b.scratch[:, 0:512]
        b.t2 = b.scratch[:, 512:1024]
        b.hbuf = [b.scratch[:, 1024:1280].bitcast(BF16),
                  b.scratch[:, 1280:1536].bitcast(BF16)]
        b.xres = [b.scratch[:, 1536:2048], b.scratch[:, 2048:2560]]
        b.ores = [b.scratch[:, 0:512], b.scratch[:, 512:1024]]
        b.biases = E(nc.sbuf_tensor("biases_sb", [128, 40], F32))
        b.m_row = E(nc.sbuf_tensor("m_row", [1, TOKS], F32))
        b.rs_row = E(nc.sbuf_tensor("rs_row", [1, TOKS], F32))
        b.consts = E(nc.sbuf_tensor("consts_sb", [128, 2], F32))
        b.ones_col = b.consts[:, 0:1]
        b.eps_col = b.consts[:, 1:2]
        b.ones_row = E(nc.sbuf_tensor("ones_row", [1, 128], F32))
        b.ps_a = E(nc.psum_tensor("ps_a", [128, 2048], F32))
        b.ps_b = E(nc.psum_tensor("ps_b", [128, 2048], F32))
        block = E(nc.Block())

        emit_ffn_body(em, b, xt_d, w_d, bias_d, out_d, dbg,
                      consts_d, onesrow_d)
        em.run(block)
    return nc


def emit_ffn_body(em, b, xt_d, w_d, bias_d, out_d, dbg=None,
                  consts_d=None, onesrow_d=None):
    em._tok_consts = em.dma("pool", [], b.consts[:, :], consts_d)
    tok_ones = em.dma("pool", [], b.ones_row[0:1, :], onesrow_d)
    tok_bias = em.dma("pool", [], b.biases[:, :], bias_d)

    tok_xn = emit_ln(em, b, TOKS, xt_d, b.xn, 0,
                     extra_waits=(tok_ones, em._tok_consts))

    tok_c0 = None
    for mc in range(8):
        sl = slice(mc * LOCP, (mc + 1) * LOCP)
        tok_c0 = em.push("dve", [tok_bias, tok_xn],
                         lambda v, sl=sl, mc=mc: v.tensor_scalar(
                             b.c32[:, sl], b.xn[:, 0:LOCP],
                             0.0, b.biases[:, 32 + mc:33 + mc],
                             ALU.mult, ALU.add),
                         inc="dve")
    tok_cbf = None
    for mc in range(8):
        sl = slice(mc * LOCP, (mc + 1) * LOCP)
        tok_cbf = em.push("dve", [],
                          lambda v, sl=sl: v.tensor_copy(b.c_bf[:, sl],
                                                         b.c32[:, sl]),
                          inc="dve")

    if dbg:
        em.dma("sync", [tok_cbf],
               dbg["c0"].rearrange("(ic p) t -> p ic t", p=128),
               b.c32[:, :].rearrange("p (ic t) -> p ic t", ic=8))
    order = ["wf", "uf", "wi", "ui", "wc", "uc"] * 4 + ["wo", "uo"]
    NL = len(order)
    wtok = [None] * NL
    wfree = [None] * NL
    emitted_load = [False] * NL

    def need(k):
        if k >= NL or emitted_load[k]:
            return
        emitted_load[k] = True
        wait = wfree[k - 4] if k >= 4 else None
        wtok[k] = em.dma(
            "pool", [wait, em._w_ready],
            b.wbuf[k % 4][:, :].rearrange("p (ic m) -> p ic m", ic=8),
            w_d[order[k]].rearrange("(ic p) m -> p ic m", p=128))

    for k in range(4):
        need(k)

    psq = [b.ps_a[:, i * 512:(i + 1) * 512] for i in range(4)] + \
          [b.ps_b[:, i * 512:(i + 1) * 512] for i in range(4)]
    ps_free = [tok_xn] * 8
    qctr = 0


    tok_upd_all = None
    for tstep in range(2):
        for half in range(2):
            base = (tstep * 2 + half) * 6
            gate_act_tok = {}
            for gi in range(3):
                kw, ku = base + 2 * gi, base + 2 * gi + 1
                need(kw), need(ku)
                fn = AF.Sigmoid if gi < 2 else AF.Tanh
                last_mm = None
                for mt in range(8):
                    q = qctr % 8
                    qctr += 1
                    waits = [wtok[kw], wtok[ku], tok_cbf, tok_xn, ps_free[q]]
                    def f(t, kw=kw, ku=ku, q=q, mt=mt, half=half, tstep=tstep):
                        ps = psq[q]
                        bw, bu = b.wbuf[kw % 4], b.wbuf[ku % 4]
                        xc = tstep * LOCP + half * 512
                        cc = half * 512
                        for ic in range(8):
                            t.matmul(ps, bw[:, ic * D + mt * 128:
                                            ic * D + mt * 128 + 128],
                                     b.xn[:, ic * TOKS + xc:ic * TOKS + xc + 512],
                                     start=(ic == 0), stop=False)
                        for ic in range(8):
                            ins = t.matmul(ps, bu[:, ic * D + mt * 128:
                                                  ic * D + mt * 128 + 128],
                                           b.c_bf[:, ic * LOCP + cc:
                                                  ic * LOCP + cc + 512],
                                           start=False, stop=(ic == 7))
                        return ins
                    mmtok = em.push("pe", waits, f, inc="pe")
                    last_mm = mmtok
                    atok = em.push(
                        "act", [mmtok, tok_upd_all],
                        lambda s, gi=gi, q=q, fn=fn, mt=mt:
                        s.activation(
                            b.gate[gi][:, mt * 512:(mt + 1) * 512],
                            psq[q], fn, bias=b.biases[:, gi * 8 + mt:
                                                      gi * 8 + mt + 1]),
                        inc="act")
                    ps_free[q] = atok
                    gate_act_tok[(gi, mt)] = atok
                wfree[kw] = last_mm
                wfree[ku] = last_mm
                need(kw + 4)
                need(ku + 4)
            for mt in range(8):
                gsl = slice(mt * 512, (mt + 1) * 512)
                csl = slice(mt * LOCP + half * 512, mt * LOCP + half * 512 + 512)
                em.push("dve", [gate_act_tok[(0, mt)]],
                        lambda v, gsl=gsl, csl=csl: v.tensor_tensor(
                            b.t1, b.gate[0][:, gsl], b.c32[:, csl], ALU.mult))
                em.push("dve", [gate_act_tok[(1, mt)],
                                gate_act_tok[(2, mt)]],
                        lambda v, gsl=gsl: v.tensor_tensor(
                            b.t2, b.gate[1][:, gsl], b.gate[2][:, gsl],
                            ALU.mult))
                tok_upd_all = em.push(
                    "dve", [],
                    lambda v, csl=csl: v.tensor_tensor(
                        b.c32[:, csl], b.t1, b.t2, ALU.add),
                    inc="dve")
        if dbg and tstep == 0:
            em.dma("sync", [tok_upd_all],
                   dbg["c1"].rearrange("(ic p) t -> p ic t", p=128),
                   b.c32[:, :].rearrange("p (ic t) -> p ic t", ic=8))
        for mc in range(8):
            sl = slice(mc * LOCP, (mc + 1) * LOCP)
            tok_cbf = em.push("dve", [tok_upd_all],
                              lambda v, sl=sl: v.tensor_copy(
                                  b.c_bf[:, sl], b.c32[:, sl]),
                              inc="dve")

    kwo, kuo = 24, 25
    need(kwo), need(kuo)
    store_toks = []
    stt_tok = [None, None]
    slot_i = 0
    last_mm = None
    for tstep in range(2):
        for half in range(2):
            for mt in range(8):
                slot = slot_i % 2
                slot_i += 1
                rsl = (slice(mt * 128, (mt + 1) * 128),
                       slice(tstep * LOCP + half * 512,
                             tstep * LOCP + half * 512 + 512))
                xr_tok = em.dma("sync", [stt_tok[slot]],
                                b.xres[slot], xt_d[rsl[0], rsl[1]])
                q = qctr % 8
                qctr += 1

                def f(t, q=q, mt=mt, half=half, tstep=tstep):
                    ps = psq[q]
                    bw, bu = b.wbuf[kwo % 4], b.wbuf[kuo % 4]
                    xc = tstep * LOCP + half * 512
                    cc = half * 512
                    for ic in range(8):
                        t.matmul(ps, bw[:, ic * D + mt * 128:
                                        ic * D + mt * 128 + 128],
                                 b.xn[:, ic * TOKS + xc:ic * TOKS + xc + 512],
                                 start=(ic == 0), stop=False)
                    for ic in range(8):
                        ins = t.matmul(ps, bu[:, ic * D + mt * 128:
                                              ic * D + mt * 128 + 128],
                                       b.c_bf[:, ic * LOCP + cc:
                                              ic * LOCP + cc + 512],
                                       start=False, stop=(ic == 7))
                    return ins
                mmtok = em.push("pe", [wtok[kwo], wtok[kuo], tok_cbf,
                                       ps_free[q]], f, inc="pe")
                last_mm = mmtok
                atok = em.push("act", [mmtok, stt_tok[slot]],
                               lambda s, q=q, mt=mt, slot=slot: s.activation(
                                   b.hbuf[slot], psq[q], AF.Tanh,
                                   bias=b.biases[:, 24 + mt:25 + mt]),
                               inc="act")
                ps_free[q] = atok
                prev_store = store_toks[-2] if len(store_toks) >= 2 else None
                stt_tok[slot] = em.push(
                    "dve", [atok, xr_tok, prev_store],
                    lambda v, slot=slot: v.scalar_tensor_tensor(
                        b.ores[slot], b.hbuf[slot], 0.5,
                        b.xres[slot], ALU.mult, ALU.add),
                    inc="dve")
                store_toks.append(em.dma("sync", [stt_tok[slot]],
                                         out_d[rsl[0], rsl[1]],
                                         b.ores[slot]))
    wfree[kwo] = last_mm
    wfree[kuo] = last_mm

    if dbg:
        for gi in range(3):
            em.dma("sync", [tok_upd_all],
                   dbg[f"g{gi}"].rearrange("(ic p) t -> p ic t", p=128),
                   b.gate[gi][:, :].rearrange("p (ic t) -> p ic t", ic=8))
        em.dma("sync", [tok_xn],
               dbg["xn"].rearrange("(ic p) t -> p ic t", p=128),
               b.xn[:, :].rearrange("p (ic t) -> p ic t", ic=8))
        em.dma("sync", [tok_upd_all],
               dbg["c32"].rearrange("(ic p) t -> p ic t", p=128),
               b.c32[:, :].rearrange("p (ic t) -> p ic t", ic=8))
        store_toks.append(em.dma(
            "sync", [tok_cbf],
            dbg["cbf"].rearrange("(ic p) t -> p ic t", p=128),
            b.c_bf[:, :].rearrange("p (ic t) -> p ic t", ic=8)))
    em.push("sync", [store_toks[-1]], lambda sy: sy.nop())




SQS = 0.08838834764831845


def build_attn0(debug=False):
    nc = bass.Bass("TRN2", target_bir_lowering=False)
    xt_d = nc.dram_tensor("xt", [D, TOKS], F32, kind="ExternalInput").ap()
    uvw_d = nc.dram_tensor("uvwT", [D, 4224], F32, kind="ExternalInput").ap()
    ow_d = nc.dram_tensor("owT", [2048, D], F32, kind="ExternalInput").ap()
    gb_d = nc.dram_tensor("gbcols", [128, 4], F32, kind="ExternalInput").ap()
    g_d = nc.dram_tensor("gval", [1, 1], F32, kind="ExternalInput").ap()
    w2046_d = nc.dram_tensor("w2046", [1, 1], F32, kind="ExternalInput").ap()
    corner_d = nc.dram_tensor("corner", [64 * 128, 256], F32,
                              kind="ExternalInput").ap()
    rsc_d = nc.dram_tensor("rsc", [128, 8], F32, kind="ExternalInput").ap()
    cos_d = nc.dram_tensor("cosT", [128, LOCP], F32, kind="ExternalInput").ap()
    sin_d = nc.dram_tensor("sinT", [128, LOCP], F32, kind="ExternalInput").ap()
    ident_d = nc.dram_tensor("identb", [128, 128], F32, kind="ExternalInput").ap()
    consts_d = nc.dram_tensor("consts", [128, 2], F32, kind="ExternalInput").ap()
    onesrow_d = nc.dram_tensor("onesrow", [1, 128], F32, kind="ExternalInput").ap()
    out_d = nc.dram_tensor("xo", [D, TOKS], F32, kind="ExternalOutput").ap()

    dbg = {}
    if debug:
        for nm, shape, dt in [("y", [1024, TOKS], BF16), ("xn2", [1024, TOKS], BF16),
                              ("qT", [128, TOKS], BF16), ("kTloc", [128, TOKS], BF16),
                              ("kTall", [128, 8192], BF16),
                              ("kern", [128, 16384], BF16),
                              ("Bsb", [128, 4096], BF16),
                              ("base", [128, TOKS], F32),
                              ("uvwb", [128, 1024], BF16),
                              ("wp0", [128, 2048], BF16),
                              ("qpre", [128, TOKS], F32),
                              ("BTsb", [128, 4096], BF16)]:
            dbg[nm] = nc.dram_tensor("dbg_" + nm, shape, dt,
                                     kind="ExternalOutput").ap()
    cc_in = nc.dram_tensor("cc0_in", [2176, 2048], BF16).ap()
    cc_out = nc.dram_tensor("cc0_out", [8 * 2176, 2048], BF16).ap()
    u_spill = nc.dram_tensor("u_spill", [2048, 2048], BF16).ap()

    with contextlib.ExitStack() as st:
        E = st.enter_context
        em = Em(nc, st)
        b = Bufs()
        b.bigbuf = E(nc.sbuf_tensor("bigbuf", [128, 16384], BF16))
        b.y = E(nc.sbuf_tensor("ybuf", [128, 16384], BF16))
        b.wpool = E(nc.sbuf_tensor("wpool", [128, 16384], BF16))
        b.kT = E(nc.sbuf_tensor("kT_all", [128, 8192], BF16))
        b.qT = E(nc.sbuf_tensor("qT", [128, 2048], BF16))
        b.kTloc = E(nc.sbuf_tensor("kTloc", [128, 2048], BF16))
        b.vsb = [E(nc.sbuf_tensor(f"vsb{i}", [128, 2048], BF16)) for i in range(2)]
        b.usb = [E(nc.sbuf_tensor(f"usb{i}", [128, 2048], BF16)) for i in range(2)]
        b.tileAB = E(nc.sbuf_tensor("tileAB", [128, 4096], BF16))
        b.utile = b.tileAB[:, 0:2048]
        b.vtile = b.tileAB[:, 2048:4096]
        b.Bsb = b.tileAB
        b.BTsb = E(nc.sbuf_tensor("BTsb", [128, 4096], BF16))
        b.basebuf = E(nc.sbuf_tensor("basebuf", [128, 2048], F32))
        b.vsb += [b.basebuf[:, :].bitcast(BF16)[:, 0:2048],
                  b.basebuf[:, :].bitcast(BF16)[:, 2048:4096]]
        b.uvwb = E(nc.sbuf_tensor("uvwb", [128, 8 * 128], BF16))
        b.ident = E(nc.sbuf_tensor("identsb", [128, 128], BF16))
        b.diag = E(nc.sbuf_tensor("diagrs", [128, 8 * 128], BF16))
        b.corner = [E(nc.sbuf_tensor(f"corner{i}", [128, 256], F32))
                    for i in range(2)]
        b.costab = E(nc.sbuf_tensor("costab", [128, LOCP], F32))
        b.sintab = E(nc.sbuf_tensor("sintab", [128, LOCP], F32))
        b.gb = E(nc.sbuf_tensor("gbsb", [128, 4], F32))
        b.rsc = E(nc.sbuf_tensor("rscsb", [128, 8], F32))
        b.smalls = E(nc.sbuf_tensor("smalls", [128, 4], F32))
        b.gsb = E(nc.sbuf_tensor("gsb", [1, 2], F32))
        b.consts = E(nc.sbuf_tensor("consts_sb", [128, 2], F32))
        b.ones_col = b.consts[:, 0:1]
        b.eps_col = b.consts[:, 1:2]
        b.ones_row = E(nc.sbuf_tensor("ones_row", [1, 128], F32))
        b.xchunk = [E(nc.sbuf_tensor(f"xchunk{i}", [128, TOKS], F32))
                    for i in range(2)]
        b.scratch = E(nc.sbuf_tensor("scratch", [128, 2560], F32))
        b.xsq = [b.scratch[:, 0:512], b.scratch[:, 512:1024]]
        b.ln_tmp = b.scratch[:, 1024:1536]
        b.ropet = [b.scratch[:, 0:1024], b.scratch[:, 1024:2048]]
        b.rtile = b.scratch[:, 0:512]
        b.rtile4 = b.xchunk[0]
        b.xres = [b.scratch[:, 512:768], b.scratch[:, 768:1024]]
        b.ores = [b.scratch[:, 1024:1280], b.scratch[:, 1280:1536]]
        ktf = b.kT.bitcast(F32)
        b.m_row = ktf[0:1, 0:2048]
        b.rs_row = ktf[0:1, 2048:4096]
        b.v_row = b.BTsb.bitcast(F32)[0:1, 0:2048]
        b.t_row = b.v_row
        b.ps_a = E(nc.psum_tensor("ps_a", [128, 2048], F32))
        b.ps_b = E(nc.psum_tensor("ps_b", [128, 2048], F32))
        block = E(nc.Block())

        emit_attn0(em, b, xt_d, uvw_d, ow_d, gb_d, g_d, w2046_d, corner_d,
                   rsc_d, cos_d, sin_d, ident_d, consts_d, onesrow_d, out_d,
                   cc_in, cc_out, u_spill, dbg)
        em.run(block)
    return nc


def emit_attn0(em, b, xt_d, uvw_d, ow_d, gb_d, g_d, w2046_d, corner_d,
               rsc_d, cos_d, sin_d, ident_d, consts_d, onesrow_d, out_d,
               cc_in, cc_out, u_spill, dbg=None, cck0=None):
    em._tok_consts = em.dma("pool", [], b.consts[:, :], consts_d)
    tok_ones = em.dma("pool", [], b.ones_row[0:1, :], onesrow_d)
    tok_gb = em.dma("pool", [], b.gb[:, :], gb_d)
    tok_g = em.dma("pool", [], b.gsb[0:1, 0:1], g_d)
    tok_w2 = em.dma("pool", [], b.gsb[0:1, 1:2], w2046_d)
    tok_rsc = em.dma("pool", [], b.rsc[:, :], rsc_d)
    tok_cos = em.dma("pool", [], b.costab[:, :], cos_d)
    tok_sin = em.dma("pool", [], b.sintab[:, :], sin_d)
    tok_id = em.dma("pool", [], b.ident[:, :], ident_d)

    tok_w2b = em.push("pe", [tok_w2, tok_ones],
                      lambda t: t.matmul(b.ps_a[:, 0:1], b.ones_row[0:1, :],
                                         b.gsb[0:1, 1:2], start=True, stop=True),
                      inc="pe")
    em.push("dve", [tok_w2b],
            lambda v: v.tensor_scalar(b.smalls[:, 0:1], b.ps_a[:, 0:1],
                                      SQS, None, ALU.mult))
    tok_smalls = em.push("dve", [],
                         lambda v: v.tensor_copy(b.smalls[:, 1:2],
                                                 b.ps_a[:, 0:1]), inc="dve")
    tok_diag = None
    for dt in range(8):
        tok_diag = em.push("dve", [tok_id, tok_rsc],
                           lambda v, dt=dt: v.tensor_scalar(
                               b.diag[:, dt * 128:(dt + 1) * 128],
                               b.ident[:, :], b.rsc[:, dt:dt + 1], None,
                               ALU.mult), inc="dve")

    tok_y = emit_ln(em, b, TOKS, xt_d, b.y, 0,
                    extra_waits=(tok_ones, em._tok_consts), keep_var=True)
    tok_t0 = em.push("dve", [tok_y],
                     lambda v: v.tensor_scalar(b.t_row[0:1, :], b.v_row[0:1, :],
                                               0.0, None, ALU.max), inc="dve")
    tok_tr0 = em.push("act", [tok_t0],
                      lambda s: s.activation(b.t_row[0:1, :], b.t_row[0:1, :],
                                             AF.Sqrt), inc="act")
    em.push("dve", [tok_tr0],
            lambda v: v.tensor_tensor(b.t_row[0:1, :], b.t_row[0:1, :],
                                      b.rs_row[0:1, :], ALU.mult))
    em.push("dve", [],
            lambda v: v.tensor_scalar(b.t_row[0:1, :], b.t_row[0:1, :], EPS,
                                      None, ALU.max))
    em.push("dve", [], lambda v: v.reciprocal(b.t_row[0:1, :], b.t_row[0:1, :]))
    tok_tr = em.push("dve", [tok_g],
                     lambda v: v.tensor_scalar(b.t_row[0:1, :], b.t_row[0:1, :],
                                               b.gsb[0:1, 0:1], None, ALU.mult),
                     inc="dve")

    def bct(t):
        for cg in range(4):
            c0 = cg * 512
            ins = t.matmul(b.ps_a[:, c0:c0 + 512], b.ones_row[0:1, :],
                           b.t_row[0:1, c0:c0 + 512], start=True, stop=True)
        return ins
    tok_bct = em.push("pe", [tok_tr], bct, inc="pe")
    tok_xn2 = None
    for ic in range(8):
        sl = slice(ic * TOKS, (ic + 1) * TOKS)
        tok_xn2 = em.push("dve", [tok_bct],
                          lambda v, sl=sl: v.tensor_tensor(
                              b.bigbuf[:, sl], b.y[:, sl], b.ps_a[:, 0:TOKS],
                              ALU.mult), inc="dve")

    tok_xn2 = em.fence([tok_xn2], b.scratch[0:1, 0:1], b.consts[0:1, 0:1])

    uvw3 = uvw_d.rearrange("(ic p) f -> p ic f", p=128)
    psq = [b.ps_a[:, i * 512:(i + 1) * 512] for i in range(4)] + \
          [b.ps_b[:, i * 512:(i + 1) * 512] for i in range(4)]
    ps_free = [tok_bct] * 8
    qctr = 0
    spill_toks = []
    prev_pass_mm = None
    tok_uvw = None
    for fpass in range(2):
        f0 = fpass * 2048
        tok_uvw = em.dma(
            "pool", [prev_pass_mm],
            b.wpool[:, 0:16384].rearrange("p (ic f) -> p ic f", ic=8),
            uvw3[:, :, f0:f0 + 2048])
        tile = b.utile if fpass == 0 else b.vtile
        prev_spill = None
        for tt in range(16):
            atoks = []
            for fb in range(4):
                q = qctr % 8
                qctr += 1

                def f(t, tt=tt, fb=fb, q=q):
                    for ic in range(8):
                        ins = t.matmul(
                            psq[q],
                            b.bigbuf[:, ic * TOKS + tt * 128:
                                     ic * TOKS + tt * 128 + 128],
                            b.wpool[:, ic * 2048 + fb * 512:
                                    ic * 2048 + fb * 512 + 512],
                            start=(ic == 0), stop=(ic == 7))
                    return ins
                mmtok = em.push("pe", [tok_uvw, tok_xn2, ps_free[q]], f,
                                inc="pe")
                prev_pass_mm = mmtok
                atok = em.push("act", [mmtok, prev_spill],
                               lambda s, fb=fb, q=q, tile=tile: s.activation(
                                   tile[:, fb * 512:(fb + 1) * 512], psq[q],
                                   AF.Silu), inc="act")
                ps_free[q] = atok
                atoks.append(atok)
            dst_d = u_spill if fpass == 0 else cc_in
            row0 = tt * 128 if fpass == 0 else 128 + tt * 128
            prev_spill = em.dma("sync", atoks, dst_d[row0:row0 + 128, :],
                                tile)
            spill_toks.append(prev_spill)

    tok_bw = em.dma("pool", [],
                    b.uvwb[:, :].rearrange("p (ic f) -> p ic f", ic=8),
                    uvw3[:, :, 4096:4224])
    tok_basebuf = None
    for cg in range(4):
        c0 = cg * 512
        q = qctr % 8
        qctr += 1

        def f(t, c0=c0, q=q):
            for ic in range(8):
                ins = t.matmul(psq[q],
                               b.uvwb[:, ic * 128:(ic + 1) * 128],
                               b.bigbuf[:, ic * TOKS + c0:ic * TOKS + c0 + 512],
                               start=(ic == 0), stop=(ic == 7))
            return ins
        mmtok = em.push("pe", [tok_bw, tok_xn2, ps_free[q]], f, inc="pe")
        tok_basebuf = em.push("act", [mmtok],
                              lambda s, c0=c0, q=q: s.activation(
                                  b.basebuf[:, c0:c0 + 512], psq[q], AF.Silu),
                              inc="act")
        ps_free[q] = tok_basebuf

    qpre, kpre = b.xchunk[0], b.xchunk[1]
    em.push("dve", [tok_basebuf, tok_gb],
            lambda v: v.tensor_scalar(qpre[:, :], b.basebuf[:, :],
                                      b.gb[:, 0:1], b.gb[:, 1:2],
                                      ALU.mult, ALU.add))
    tok_pre = em.push("dve", [],
                      lambda v: v.tensor_scalar(kpre[:, :], b.basebuf[:, :],
                                                b.gb[:, 2:3], b.gb[:, 3:4],
                                                ALU.mult, ALU.add), inc="dve")
    if dbg:
        em.dma("sync", [tok_basebuf], dbg["base"], b.basebuf[:, :])
        em.dma("sync", [tok_bw], dbg["uvwb"], b.uvwb[:, :])
        em.dma("sync", [tok_uvw], dbg["wp0"], b.wpool[:, 0:2048])
    tok_rope = None
    sw = b.basebuf
    for (srcb, dstbuf) in ((qpre, b.qT), (kpre, b.kTloc)):
        em.dma("pool", [tok_pre, tok_rope], sw[0:64, :], srcb[64:128, :])
        tok_sw = em.dma("pool", [], sw[64:128, :], srcb[0:64, :])
        for bb in range(2):
            sl = slice(bb * LOCP, (bb + 1) * LOCP)
            em.push("dve", [tok_pre, tok_sw, tok_cos, tok_sin],
                    lambda v, srcb=srcb, sl=sl: v.tensor_tensor(
                        b.ropet[0][0:64, :], srcb[0:64, sl],
                        b.costab[0:64, :], ALU.mult))
            em.push("dve", [],
                    lambda v, sl=sl: v.tensor_tensor(
                        b.ropet[1][0:64, :], sw[0:64, sl],
                        b.sintab[0:64, :], ALU.mult))
            em.push("dve", [],
                    lambda v, dstbuf=dstbuf, sl=sl: v.tensor_tensor(
                        dstbuf[0:64, sl], b.ropet[0][0:64, :],
                        b.ropet[1][0:64, :], ALU.subtract))
            em.push("dve", [],
                    lambda v, srcb=srcb, sl=sl: v.tensor_tensor(
                        b.ropet[0][64:128, :], srcb[64:128, sl],
                        b.costab[64:128, :], ALU.mult))
            em.push("dve", [],
                    lambda v, sl=sl: v.tensor_tensor(
                        b.ropet[1][64:128, :], sw[64:128, sl],
                        b.sintab[64:128, :], ALU.mult))
            tok_rope = em.push("dve", [],
                               lambda v, dstbuf=dstbuf, sl=sl: v.tensor_tensor(
                                   dstbuf[64:128, sl], b.ropet[0][64:128, :],
                                   b.ropet[1][64:128, :], ALU.add), inc="dve")
    kdst = cck0[0] if cck0 is not None else cc_in[0:128, :]
    tok_kdma = em.dma("sync", [tok_rope], kdst, b.kTloc[:, :])
    if dbg:
        em.dma("sync", [tok_pre], dbg["qpre"], qpre[:, :])
        em.dma("sync", [tok_xn2],
               dbg["y"].rearrange("(ic p) t -> p ic t", p=128),
               b.y[:, :].rearrange("p (ic t) -> p ic t", ic=8))
        em.dma("sync", [tok_xn2],
               dbg["xn2"].rearrange("(ic p) t -> p ic t", p=128),
               b.bigbuf[:, :].rearrange("p (ic t) -> p ic t", ic=8))
        em.dma("sync", [tok_rope], dbg["qT"], b.qT[:, :])
        em.dma("sync", [tok_rope], dbg["kTloc"], b.kTloc[:, :])

    if cck0 is not None:
        tok_cck0 = em.push("pool", [tok_kdma],
                           lambda g: g.collective_compute(
                               "AllGather", ALU.bypass,
                               replica_groups=[list(range(8))],
                               ins=[cck0[0].opt()], outs=[cck0[1].opt()]),
                           inc=em.next_cc())
    tok_cc = em.push("pool", [tok_kdma] + spill_toks,
                     lambda g: g.collective_compute(
                         "AllGather", ALU.bypass,
                         replica_groups=[list(range(8))],
                         ins=[cc_in.opt()], outs=[cc_out.opt()]),
                     inc=em.next_cc())
    if cck0 is None:
        tok_cck0 = tok_cc

    ow3 = ow_d.rearrange("(ec p) dd -> p ec dd", p=128)
    tok_ow = em.dma("pool", [prev_pass_mm],
                    b.wpool[:, 0:16384].rearrange("p (ec dd) -> p ec dd", ec=16),
                    ow3)

    corner3 = corner_d.rearrange("(s p) i -> s p i", p=128)

    store_toks = []
    tok_score_done = None
    tok_ic_done = None
    tok_tr_done = None
    tok_evac = {0: None, 1: None}
    corner_war = [None, None]
    tok_uload = {0: None, 1: None}
    vwar = [None] * 4
    tok_st = [None, None]
    tok_xr = [None, None]
    owps_free = [None, None]

    for bb in range(2):
        tok_kta = None
        ksrc = cck0[1] if cck0 is not None else cc_out
        kstride = 128 if cck0 is not None else 2176
        for r in range(8):
            tok_kta = em.dma("sync", [tok_cck0, tok_score_done],
                             b.kT[:, r * 1024:(r + 1) * 1024],
                             ksrc[kstride * r:kstride * r + 128,
                                  bb * 1024:(bb + 1) * 1024])
        for ic in range(4):
            icg = bb * 4 + ic
            iofs = bb * 1024 + ic * 256
            sqt = None
            rt_hist = [None, None]
            for jp in range(16):
                g = jp % 2
                grp = b.ps_a[:, g * 1024:(g + 1) * 1024]
                deps = []
                for sl4 in range(4):
                    jt = jp * 4 + sl4
                    mmtok = em.push(
                        "pe", [tok_kta, tok_rope, ps_free[g], tok_ic_done,
                               tok_tr_done],
                        lambda t, jt=jt, sl4=sl4, grp=grp, iofs=iofs:
                        t.matmul(
                            grp[:, sl4 * 256:sl4 * 256 + 256],
                            b.kT[:, jt * 128:(jt + 1) * 128],
                            b.qT[:, iofs:iofs + 256], start=True, stop=True),
                        inc="pe")
                    dep = mmtok
                    if jt < 16:
                        cs = (ic * 16 + jt) % 2
                        ltok = em.dma("pool", [corner_war[cs], tok_smalls],
                                      b.corner[cs][:, :], corner3[ic * 16 + jt])
                        adjtok = em.push(
                            "dve", [ltok],
                            lambda v, cs=cs: v.tensor_scalar(
                                b.corner[cs][:, :], b.corner[cs][:, :],
                                b.smalls[:, 1:2], None, ALU.subtract),
                            inc="dve")
                        dep = em.push(
                            "dve", [mmtok, adjtok],
                            lambda v, sl4=sl4, grp=grp, cs=cs: v.tensor_tensor(
                                grp[:, sl4 * 256:sl4 * 256 + 256],
                                grp[:, sl4 * 256:sl4 * 256 + 256],
                                b.corner[cs][:, :], ALU.add),
                            inc="dve")
                        corner_war[cs] = dep
                    deps.append(dep)
                rh = g * 1024
                rt = em.push("act", [tok_smalls, rt_hist[g], *deps],
                             lambda s, grp=grp, rh=rh: s.activation(
                                 b.rtile4[:, rh:rh + 1024], grp,
                                 AF.Relu, bias=b.smalls[:, 0:1], scale=SQS),
                             inc="act")
                rt_hist[g] = rt
                sqt = em.push("dve", [rt],
                              lambda v, jp=jp, rh=rh: v.tensor_tensor(
                                  b.bigbuf[:, jp * 1024:(jp + 1) * 1024],
                                  b.rtile4[:, rh:rh + 1024],
                                  b.rtile4[:, rh:rh + 1024], ALU.mult),
                              inc="dve")
                ps_free[g] = sqt
            tok_score_done = em.fence([sqt], b.scratch[0:1, 0:1],
                                      b.consts[0:1, 0:1])
            if dbg and bb == 0 and ic == 0:
                em.dma("sync", [tok_score_done], dbg["kern"], b.bigbuf[:, :])
                em.dma("sync", [tok_kta], dbg["kTall"], b.kT[:, :])

            for it in range(2):
                trow = (icg * 2 + it) * 128
                tok_uload[it] = em.dma("sync", [tok_evac[it]],
                                       b.usb[it][:, :],
                                       u_spill[trow:trow + 128, :])
            Atok = None
            for jt in range(64):
                vb = jt % 4
                r, jr = jt // 8, jt % 8
                vrow = 2176 * r + 128 + bb * 1024 + jr * 128
                vtok = em.dma("sync" if jt % 2 == 0 else "pool",
                              [vwar[vb], tok_cc],
                              b.vsb[vb][:, :], cc_out[vrow:vrow + 128, :])

                def fA(t, jt=jt, vb=vb):
                    for it in range(2):
                        for es in range(4):
                            ins = t.matmul(
                                (b.ps_a if it == 0 else b.ps_b)[:, es * 512:
                                                                (es + 1) * 512],
                                b.bigbuf[:, jt * 256 + it * 128:
                                         jt * 256 + it * 128 + 128],
                                b.vsb[vb][:, es * 512:(es + 1) * 512],
                                start=(jt == 0), stop=(jt == 63))
                    return ins
                Atok = em.push("pe", [vtok, tok_score_done, tok_ic_done,
                                      tok_tr_done], fA, inc="pe")
                vwar[vb] = Atok
            for it in range(2):
                ps = b.ps_a if it == 0 else b.ps_b
                ev = None
                for es in range(4):
                    ev = em.push("dve", [Atok, tok_uload[it]],
                                 lambda v, it=it, es=es, ps=ps: v.tensor_tensor(
                                     b.Bsb[:, it * 2048 + es * 512:
                                           it * 2048 + (es + 1) * 512],
                                     ps[:, es * 512:(es + 1) * 512],
                                     b.usb[it][:, es * 512:(es + 1) * 512],
                                     ALU.mult), inc="dve")
                tok_evac[it] = ev

            tok_evac = {0: em.fence([tok_evac[0]], b.scratch[0:1, 0:1],
                                    b.consts[0:1, 0:1]),
                        1: em.fence([tok_evac[1]], b.scratch[0:1, 0:1],
                                    b.consts[0:1, 0:1])}
            if dbg and bb == 0 and ic == 0:
                dtok = em.dma("sync", [tok_evac[0], tok_evac[1]], dbg["Bsb"],
                              b.Bsb[:, :])
                tok_evac = {0: dtok, 1: dtok}
            cptok = None
            trq = 0
            for it in range(2):
                for et in range(16):
                    ps = (b.ps_a if trq % 2 == 0 else b.ps_b)
                    pt = ps[:, 0:64].bitcast(BF16)
                    trtok = em.push(
                        "pe", [tok_evac[it], tok_id, cptok],
                        lambda t, it=it, et=et, pt=pt: t.transpose(
                            pt[:, 0:128],
                            b.Bsb[:, it * 2048 + et * 128:
                                  it * 2048 + et * 128 + 128],
                            b.ident[:, :]), inc="pe")
                    cptok = em.push(
                        "act", [trtok],
                        lambda s, it=it, et=et, pt=pt: s.copy(
                            b.BTsb[:, et * 256 + it * 128:
                                   et * 256 + it * 128 + 128], pt[:, 0:128]),
                        inc="act")
                    trq += 1
            tok_tr_done = cptok
            if dbg and bb == 0 and ic == 0:
                tok_tr_done = em.dma("sync", [tok_tr_done], dbg["BTsb"],
                                     b.BTsb[:, :])

            last_stt = None
            for dt in range(8):
                slot = dt % 2
                psow = psq[4 + slot]
                rsl = (slice(dt * 128, (dt + 1) * 128),
                       slice(bb * 1024 + ic * 256, bb * 1024 + ic * 256 + 256))
                tok_xr[slot] = em.dma("sync", [tok_st[slot]],
                                      b.xres[slot], xt_d[rsl[0], rsl[1]])

                def fo(t, dt=dt, psow=psow, iofs=iofs):
                    for ec in range(16):
                        t.matmul(psow[:, 0:256],
                                 b.wpool[:, ec * 1024 + dt * 128:
                                         ec * 1024 + dt * 128 + 128],
                                 b.BTsb[:, ec * 256:(ec + 1) * 256],
                                 start=(ec == 0), stop=False)
                    ins = t.matmul(psow[:, 0:256],
                                   b.diag[:, dt * 128:(dt + 1) * 128],
                                   b.y[:, dt * TOKS + iofs:
                                       dt * TOKS + iofs + 256],
                                   start=False, stop=True)
                    return ins
                mmo = em.push("pe", [tok_tr_done, tok_ow, tok_diag,
                                     owps_free[slot]], fo, inc="pe")
                sttok = em.push("dve", [mmo, tok_xr[slot],
                                        store_toks[-2] if len(store_toks) >= 2
                                        else None],
                                lambda v, slot=slot, psow=psow: v.tensor_tensor(
                                    b.ores[slot], psow[:, 0:256],
                                    b.xres[slot], ALU.add),
                                inc="dve")
                tok_st[slot] = sttok
                owps_free[slot] = sttok
                last_stt = sttok
                store_toks.append(em.dma("sync", [sttok],
                                         out_d[rsl[0], rsl[1]],
                                         b.ores[slot]))
            tok_ic_done = last_stt

    em.push("sync", [store_toks[-1]], lambda sy: sy.nop())



def build_attn1(debug=False):
    nc = bass.Bass("TRN2", target_bir_lowering=False)
    xt_d = nc.dram_tensor("xt", [D, TOKS], F32, kind="ExternalInput").ap()
    wqa_d = nc.dram_tensor("wqa", [D, 64], F32, kind="ExternalInput").ap()
    wqb_d = nc.dram_tensor("wqbp", [64, D], F32, kind="ExternalInput").ap()
    wkva_d = nc.dram_tensor("wkva", [D, 32], F32, kind="ExternalInput").ap()
    wkvbk_d = nc.dram_tensor("wkvbk", [32, 256], F32, kind="ExternalInput").ap()
    wkvbv_d = nc.dram_tensor("wkvbv", [32, 256], F32, kind="ExternalInput").ap()
    wout_d = nc.dram_tensor("wout", [D, D], F32, kind="ExternalInput").ap()
    bout_d = nc.dram_tensor("boutc", [128, 8], F32, kind="ExternalInput").ap()
    cos_d = nc.dram_tensor("cosF", [128, LOCP], F32, kind="ExternalInput").ap()
    sin_d = nc.dram_tensor("sinF", [128, LOCP], F32, kind="ExternalInput").ap()
    mask_d = nc.dram_tensor("maskM", [128, 896], F32, kind="ExternalInput").ap()
    vtempl_d = nc.dram_tensor("vtempl", [8 * TOKS, 65], F32, kind="ExternalInput").ap()
    consts_d = nc.dram_tensor("consts", [128, 2], F32, kind="ExternalInput").ap()
    onesrow_d = nc.dram_tensor("onesrow", [1, 128], F32, kind="ExternalInput").ap()
    out_d = nc.dram_tensor("xo", [D, TOKS], F32, kind="ExternalOutput").ap()

    dbg = {}
    if debug:
        for nm, shape, dt in [("qTh", [128, 16384], BF16),
                              ("kTh0", [128, 8192], BF16),
                              ("vaug0", [128, 4160], BF16),
                              ("ccol", [128, 4], F32),
                              ("Asb", [64, 1024], F32),
                              ("dinv", [1, 512], F32),
                              ("exp0", [128, 2048], BF16)]:
            dbg[nm] = nc.dram_tensor("dbg_" + nm, shape, dt,
                                     kind="ExternalOutput").ap()
    ccq_in = nc.dram_tensor("cc1q_in", [1024, TOKS], BF16).ap()
    ccq_out = nc.dram_tensor("cc1q_out", [1024, TOKS], BF16).ap()
    cck_in = nc.dram_tensor("cc1k_in", [512, TOKS], BF16).ap()
    cck_out = nc.dram_tensor("cc1k_out", [512, TOKS], BF16).ap()
    ccv_in = nc.dram_tensor("cc1v_in", [8 * TOKS, 65], BF16).ap()
    ccv_out = nc.dram_tensor("cc1v_out", [8 * TOKS, 65], BF16).ap()
    cco_in = nc.dram_tensor("cc1o_in", [1024, TOKS], BF16).ap()
    cco_out = nc.dram_tensor("cc1o_out", [1024, TOKS], BF16).ap()

    with contextlib.ExitStack() as st:
        E = st.enter_context
        em = Em(nc, st)
        b = Bufs()
        b.big1 = E(nc.sbuf_tensor("big1", [128, 16384], BF16))
        b.big2 = E(nc.sbuf_tensor("big2", [128, 16384], BF16))
        b.big3 = E(nc.sbuf_tensor("big3", [128, 16384], BF16))
        b.xn = b.big1
        b.qTh = b.big1
        b.qTl = b.big3
        b.kTh = b.big3[:, 0:8192]
        b.vaugb = b.big3[:, 8192:12352]
        
        b.outT = b.big2
        b.aT = b.big2[:, 0:8192]
        b.woutsb = b.big2[:, 8192:16384]
        b.expbuf = b.big2[:, 0:4096]
        big2f = b.big2.bitcast(F32)
        b.m_row = big2f[0:1, 0:2048]
        b.rs_row = big2f[0:1, 2048:4096]
        b.kTloc = E(nc.sbuf_tensor("kTloc1", [128, 4096], BF16))
        b.vaugl = E(nc.sbuf_tensor("vaugl", [128, 16 * 260], BF16))
        b.qsT = E(nc.sbuf_tensor("qsT", [64, 2048], BF16))
        b.kvsT = E(nc.sbuf_tensor("kvsT", [32, 2048], BF16))
        b.swb = E(nc.sbuf_tensor("swb", [128, 2048], BF16))
        b.costab = E(nc.sbuf_tensor("costabF", [128, LOCP], F32))
        b.sintab = E(nc.sbuf_tensor("sintabF", [128, LOCP], F32))
        b.maskm = E(nc.sbuf_tensor("maskm", [128, 896], BF16))
        b.wqa = E(nc.sbuf_tensor("wqasb", [128, 8 * 64], BF16))
        b.wqb = E(nc.sbuf_tensor("wqbsb", [64, D], BF16))
        b.wkva = E(nc.sbuf_tensor("wkvasb", [128, 8 * 32], BF16))
        b.wkvbk = E(nc.sbuf_tensor("wkvbksb", [32, 256], BF16))
        b.wkvbv = E(nc.sbuf_tensor("wkvbvsb", [32, 256], BF16))
        b.boutc = E(nc.sbuf_tensor("boutcsb", [128, 8], F32))
        b.Asb = E(nc.sbuf_tensor("Asb", [64, 1024], F32))
        b.odiv = [E(nc.sbuf_tensor(f"odiv{i}", [64, 512], BF16))
                  for i in range(2)]
        b.dinv = E(nc.sbuf_tensor("dinv", [1, 512], F32))
        b.ccol = E(nc.sbuf_tensor("ccol", [128, 4], F32))
        b.crow = E(nc.sbuf_tensor("crow", [1, 4], F32))
        b.consts = E(nc.sbuf_tensor("consts_sb1", [128, 2], F32))
        b.ones_col = b.consts[:, 0:1]
        b.eps_col = b.consts[:, 1:2]
        b.ones_row = E(nc.sbuf_tensor("ones_row1", [1, 128], F32))
        b.xchunk = [E(nc.sbuf_tensor(f"xchunk1_{i}", [128, TOKS], F32))
                    for i in range(2)]
        b.scratch = E(nc.sbuf_tensor("scratch1", [128, 2560], F32))
        b.xsq = [b.scratch[:, 0:512], b.scratch[:, 512:1024]]
        b.ln_tmp = b.scratch[:, 1024:1536]
        b.xres = [b.scratch[:, 512:768], b.scratch[:, 768:1024]]
        b.ores = [b.scratch[:, 1024:1280], b.scratch[:, 1280:1536]]
        b.ps_a = E(nc.psum_tensor("ps_a1", [128, 2048], F32))
        b.ps_b = E(nc.psum_tensor("ps_b1", [128, 2048], F32))
        block = E(nc.Block())

        emit_attn1(em, b, xt_d, wqa_d, wqb_d, wkva_d, wkvbk_d, wkvbv_d,
                   wout_d, bout_d, cos_d, sin_d, mask_d, vtempl_d, consts_d,
                   onesrow_d, out_d, ccq_in, ccq_out, cck_in, cck_out,
                   ccv_in, ccv_out, cco_in, cco_out, dbg)
        em.run(block)
    return nc


def emit_attn1(em, b, xt_d, wqa_d, wqb_d, wkva_d, wkvbk_d, wkvbv_d,
               wout_d, bout_d, cos_d, sin_d, mask_d, vtempl_d, consts_d,
               onesrow_d, out_d, ccq_in, ccq_out, cck_in, cck_out,
               ccv_in, ccv_out, cco_in, cco_out, dbg=None):
    from concourse.aot_env import allow_ici_all_to_all
    SCL = 0.125
    em._tok_consts = em.dma("pool", [], b.consts[:, :], consts_d)
    tok_ones = em.dma("pool", [], b.ones_row[0:1, :], onesrow_d)
    tok_cos = em.dma("pool", [], b.costab[:, :], cos_d)
    tok_sin = em.dma("pool", [], b.sintab[:, :], sin_d)
    tok_mask = em.dma("pool", [], b.maskm[:, :], mask_d)
    tok_bout = em.dma("pool", [], b.boutc[:, :], bout_d)
    tok_wqa = em.dma("pool", [],
                     b.wqa[:, :].rearrange("p (ic f) -> p ic f", ic=8),
                     wqa_d.rearrange("(ic p) f -> p ic f", p=128))
    tok_wqb = em.dma("pool", [], b.wqb[:, :], wqb_d)
    tok_wkva = em.dma("pool", [],
                      b.wkva[:, :].rearrange("p (ic f) -> p ic f", ic=8),
                      wkva_d.rearrange("(ic p) f -> p ic f", p=128))
    tok_wkvbk = em.dma("pool", [], b.wkvbk[:, :], wkvbk_d)
    tok_wkvbv = em.dma("pool", [], b.wkvbv[:, :], wkvbv_d)
    tok_vt = em.dma("pool", [], ccv_in, vtempl_d)

    tok_xn = emit_ln(em, b, TOKS, xt_d, b.xn, 0,
                     extra_waits=(tok_ones, em._tok_consts))
    tok_xn = em.fence([tok_xn], b.scratch[0:1, 0:1], b.consts[0:1, 0:1])

    psq = [b.ps_a[:, i * 512:(i + 1) * 512] for i in range(4)] + \
          [b.ps_b[:, i * 512:(i + 1) * 512] for i in range(4)]

    def fqs(t):
        for cg in range(4):
            for ic in range(8):
                ins = t.matmul(b.ps_a[0:64, cg * 512:(cg + 1) * 512],
                               b.wqa[:, ic * 64:(ic + 1) * 64],
                               b.xn[:, ic * TOKS + cg * 512:
                                    ic * TOKS + cg * 512 + 512],
                               start=(ic == 0), stop=(ic == 7))
        return ins
    tok_qs_mm = em.push("pe", [tok_wqa, tok_xn], fqs, inc="pe")
    tok_qs = em.push("act", [tok_qs_mm],
                     lambda s: s.copy(b.qsT[:, :], b.ps_a[0:64, 0:2048]),
                     inc="act")

    def fkvs(t):
        for cg in range(4):
            for ic in range(8):
                ins = t.matmul(b.ps_b[0:32, cg * 512:(cg + 1) * 512],
                               b.wkva[:, ic * 32:(ic + 1) * 32],
                               b.xn[:, ic * TOKS + cg * 512:
                                    ic * TOKS + cg * 512 + 512],
                               start=(ic == 0), stop=(ic == 7))
        return ins
    tok_kvs_mm = em.push("pe", [tok_wkva, tok_xn], fkvs, inc="pe")
    tok_kvs = em.push("act", [tok_kvs_mm],
                      lambda s: s.copy(b.kvsT[:, :], b.ps_b[0:32, 0:2048]),
                      inc="act")
    tok_sfence = em.push("act", [tok_qs, tok_kvs],
                         lambda s: s.copy(b.crow[0:1, 3:4], b.consts[0:1, 0:1]),
                         inc="act")

    ps_free = [tok_qs_mm] * 8
    qctr = 0
    tok_qtl = None
    for et in range(8):
        for cg in range(4):
            q = qctr % 8
            qctr += 1
            mm = em.push("pe", [tok_wqb, tok_sfence, ps_free[q]],
                         lambda t, et=et, cg=cg, q=q: t.matmul(
                             psq[q], b.wqb[:, et * 128:(et + 1) * 128],
                             b.qsT[0:64, cg * 512:(cg + 1) * 512],
                             start=True, stop=True), inc="pe")
            tok_qtl = em.push("act", [mm],
                              lambda s, et=et, cg=cg, q=q: s.copy(
                                  b.qTl[:, et * 2048 + cg * 512:
                                        et * 2048 + cg * 512 + 512], psq[q]),
                              inc="act")
            ps_free[q] = tok_qtl
    tok_ktl = None
    for kt in range(2):
        for cg in range(4):
            q = qctr % 8
            qctr += 1
            mm = em.push("pe", [tok_wkvbk, tok_sfence, ps_free[q]],
                         lambda t, kt=kt, cg=cg, q=q: t.matmul(
                             psq[q], b.wkvbk[:, kt * 128:(kt + 1) * 128],
                             b.kvsT[0:32, cg * 512:(cg + 1) * 512],
                             start=True, stop=True), inc="pe")
            tok_ktl = em.push("act", [mm],
                              lambda s, kt=kt, cg=cg, q=q: s.copy(
                                  b.kTloc[:, kt * 2048 + cg * 512:
                                          kt * 2048 + cg * 512 + 512], psq[q]),
                              inc="act")
            ps_free[q] = tok_ktl
    tok_vl = None
    for tt in range(16):
        q = qctr % 8
        qctr += 1
        mm = em.push("pe", [tok_wkvbv, tok_sfence, ps_free[q]],
                     lambda t, tt=tt, q=q: t.matmul(
                         psq[q][:, 0:256], b.kvsT[0:32, tt * 128:(tt + 1) * 128],
                         b.wkvbv[:, :], start=True, stop=True), inc="pe")
        vsl = b.vaugl[:, tt * 260:(tt + 1) * 260].rearrange(
            "p (kh x) -> p kh x", kh=4)[:, :, 0:64]
        tok_vl = em.push("act", [mm],
                         lambda s, q=q, vsl=vsl: s.copy(
                             vsl, psq[q][:, 0:256].rearrange(
                                 "p (kh x) -> p kh x", kh=4)),
                         inc="act")
        ps_free[q] = tok_vl

    tok_rope = None
    tiles = [("q", et) for et in range(8)] + [("k", kt) for kt in range(2)]
    for kind, ti in tiles:
        srcb = b.qTl if kind == "q" else b.kTloc
        c0 = ti * 2048
        dep = tok_qtl if kind == "q" else tok_ktl
        sw_toks = []
        for (d0, s0) in ((0, 32), (32, 0), (64, 96), (96, 64)):
            sw_toks.append(em.dma(
                "pool", [dep, tok_rope],
                b.swb[d0:d0 + 32, :], srcb[s0:s0 + 32, c0:c0 + 2048]))
        for bb in range(2):
            sl = slice(c0 + bb * LOCP, c0 + bb * LOCP + LOCP)
            swsl = slice(bb * LOCP, bb * LOCP + LOCP)
            em.push("dve", [dep, *sw_toks, tok_cos, tok_sin],
                    lambda v, srcb=srcb, sl=sl: v.tensor_tensor(
                        b.scratch[:, 0:1024], srcb[:, sl], b.costab[:, :],
                        ALU.mult))
            em.push("dve", [],
                    lambda v, swsl=swsl: v.tensor_tensor(
                        b.scratch[:, 1024:2048], b.swb[:, swsl],
                        b.sintab[:, :], ALU.mult))
            tok_rope = em.push("dve", [],
                               lambda v, srcb=srcb, sl=sl: v.tensor_tensor(
                                   srcb[:, sl], b.scratch[:, 0:1024],
                                   b.scratch[:, 1024:2048], ALU.add),
                               inc="dve")

    ctoks = []
    for j in range(8):
        ctoks.append(em.dma("sync", [tok_rope],
                            ccq_in[j * 128:(j + 1) * 128, :],
                            b.qTl[:, j * 2048:(j + 1) * 2048]))
    for j in range(8):
        kh = j // 2
        kt, half = kh // 2, kh % 2
        ctoks.append(em.dma("sync", [tok_rope],
                            cck_in[j * 64:(j + 1) * 64, :],
                            b.kTloc[half * 64:(half + 1) * 64,
                                    kt * 2048:(kt + 1) * 2048]))
    for j in range(8):
        kh = j // 2
        vsl = b.vaugl[:, :].rearrange("p (tt r) -> p tt r", tt=16)[
            :, :, kh * 65:kh * 65 + 64]
        dsl = ccv_in[j * 2048:(j + 1) * 2048, 0:64].rearrange(
            "(tt p) r -> p tt r", p=128)
        ctoks.append(em.dma("sync", [tok_vl, tok_vt], dsl, vsl))

    def a2a(g, cin, cout):
        with allow_ici_all_to_all():
            return g.collective_compute(
                "AllToAll", ALU.bypass, replica_groups=[list(range(8))],
                ins=[cin.opt()], outs=[cout.opt()])
    tok_ccq = em.push("pool", ctoks, lambda g: a2a(g, ccq_in, ccq_out),
                      inc=em.next_cc())
    tok_cck = em.push("pool", [], lambda g: a2a(g, cck_in, cck_out),
                      inc=em.next_cc())
    tok_ccv = em.push("pool", [], lambda g: a2a(g, ccv_in, ccv_out),
                      inc=em.next_cc())

    tok_qth = None
    for bb in range(2):
        for s in range(8):
            tok_qth = em.dma("sync", [tok_ccq],
                             b.qTh[:, bb * 8192 + s * 1024:
                                   bb * 8192 + s * 1024 + 1024],
                             ccq_out[s * 128:(s + 1) * 128,
                                     bb * 1024:(bb + 1) * 1024])
    tok_out = None
    pvps = [b.ps_b[:, 0:512], b.ps_b[:, 512:1024]]
    dbps = [b.ps_b[:, 1024:1536], b.ps_b[:, 1024:1536]]
    junkps = b.ps_b[:, 1536:2048]
    scpair = [b.ps_a[:, 0:1024], b.ps_a[:, 1024:2048]]

    def _later(a, b_):
        if a is None:
            return b_
        if b_ is None:
            return a
        assert a[0] == b_[0]
        return a if a[1] >= b_[1] else b_
    sc_free = [_later(ps_free[0], ps_free[1]), _later(ps_free[2], ps_free[3])]
    pv_free = [ps_free[4], ps_free[5]]
    db_free = [ps_free[6], ps_free[7]]
    exp_free = [None] * 4
    tok_vaugb = None
    pvctr = 0
    otoks = []
    ostore = [None, None]
    ca_war = [None, None]
    tok_kth = None
    tok_kth_war = None
    for bb in range(2):
        for half in range(2):
            for s in range(8):
                tok_kth = em.dma("sync", [tok_cck, tok_kth_war],
                                 b.kTh[half * 64:(half + 1) * 64,
                                       s * 1024:(s + 1) * 1024],
                                 cck_out[s * 64:(s + 1) * 64,
                                         bb * 1024:(bb + 1) * 1024])
        em.push("dve", [tok_qth, tok_kth],
                lambda v, bb=bb: v.tensor_reduce(
                    b.ccol[:, 2:3], b.qTh[:, bb * 8192:(bb + 1) * 8192],
                    AX.X, ALU.max, apply_absolute_value=True))
        em.push("dve", [],
                lambda v: v.tensor_tensor(b.ccol[:, 2:3], b.ccol[:, 2:3],
                                          b.ccol[:, 2:3], ALU.mult))
        em.push("dve", [],
                lambda v: v.tensor_reduce(
                    b.ccol[0:64, 3:4], b.kTh[0:64, :],
                    AX.X, ALU.max, apply_absolute_value=True))
        tok_bnd = em.push("dve", [],
                          lambda v: v.tensor_tensor(
                              b.ccol[0:64, 3:4], b.ccol[0:64, 3:4],
                              b.ccol[0:64, 3:4], ALU.mult), inc="dve")
        mm1 = em.push("pe", [tok_bnd, ps_free[2], sc_free[1]],
                      lambda t: t.matmul(b.ps_a[0:1, 1024:1025],
                                         b.ones_col, b.ccol[:, 2:3],
                                         start=True, stop=True), inc="pe")
        mm2 = em.push("pe", [],
                      lambda t: t.matmul(b.ps_a[0:1, 1026:1027],
                                         b.ones_col[0:64, :], b.ccol[0:64, 3:4],
                                         start=True, stop=True), inc="pe")
        em.push("act", [mm1, mm2],
                lambda s: s.activation(b.crow[0:1, 0:1], b.ps_a[0:1, 1024:1025],
                                       AF.Sqrt))
        tok_c1 = em.push("act", [],
                         lambda s: s.activation(b.crow[0:1, 1:2],
                                                b.ps_a[0:1, 1026:1027],
                                                AF.Sqrt), inc="act")
        em.push("dve", [tok_c1],
                lambda v: v.tensor_tensor(b.crow[0:1, 0:1], b.crow[0:1, 0:1],
                                          b.crow[0:1, 1:2], ALU.mult))
        tok_c2 = em.push("dve", [],
                         lambda v: v.tensor_scalar(b.crow[0:1, 0:1],
                                                   b.crow[0:1, 0:1],
                                                   -SCL, None, ALU.mult),
                         inc="dve")
        mm3 = em.push("pe", [tok_c2, tok_ones],
                      lambda t: t.matmul(b.ps_a[0:128, 1028:1029],
                                         b.ones_row[0:1, :], b.crow[0:1, 0:1],
                                         start=True, stop=True), inc="pe")
        tok_negc = em.push("dve", [mm3],
                           lambda v, bb=bb: v.tensor_copy(
                               b.ccol[:, bb:bb + 1], b.ps_a[:, 1028:1029]),
                           inc="dve")
        va_toks = []
        for s in range(8):
            va_toks.append(em.dma(
                "sync", [tok_ccv, tok_vaugb],
                b.vaugb.rearrange("p (jt r) -> p jt r", jt=64)[:, s * 8:
                                                              (s + 1) * 8, :],
                ccv_out[s * 2048 + bb * 1024:s * 2048 + bb * 1024 + 1024,
                        :].rearrange("(jt p) r -> p jt r", p=128)))
        tok_va = va_toks[-1]
        for iblk in range(16):
            for hl in range(2):
                pvp = pvps[pvctr % 2]
                njt = 4 * (iblk + 1)
                npair = njt // 2
                ptoks = {}
                pvtok = None
                for jp in range(npair + 2):
                    if jp < npair:
                        pr = jp % 2
                        scp = scpair[pr]
                        mm = None
                        for half in range(2):
                            jt = 2 * jp + half
                            mm = em.push(
                                "pe", [tok_qth, tok_kth, sc_free[pr],
                                       tok_negc],
                                lambda t, jt=jt, bb=bb, hl=hl, iblk=iblk,
                                scp=scp, half=half: t.matmul(
                                    scp[:, half * 512:(half + 1) * 512],
                                    b.kTh[hl * 64:(hl + 1) * 64,
                                          jt * 128:(jt + 1) * 128],
                                    b.qTh[hl * 64:(hl + 1) * 64,
                                          bb * 8192 + iblk * 512:
                                          bb * 8192 + iblk * 512 + 512],
                                    start=True, stop=True), inc="pe")
                        eb = jp % 4
                        etok = em.push(
                            "act", [mm, tok_negc, exp_free[eb]],
                            lambda s, eb=eb, scp=scp, bb=bb: s.activation(
                                b.expbuf[:, eb * 1024:(eb + 1) * 1024], scp,
                                AF.Exp, bias=b.ccol[:, bb:bb + 1], scale=SCL),
                            inc="act")
                        sc_free[pr] = etok
                        mtoks = [etok, etok]
                        for half in range(2):
                            jt = 2 * jp + half
                            if jt >= 4 * iblk:
                                m0 = 384 - 128 * (jt - 4 * iblk)
                                col = eb * 1024 + half * 512
                                mtoks[half] = em.push(
                                    "dve", [etok, tok_mask],
                                    lambda v, col=col, m0=m0: v.tensor_tensor(
                                        b.expbuf[:, col:col + 512],
                                        b.expbuf[:, col:col + 512],
                                        b.maskm[:, m0:m0 + 512], ALU.mult),
                                    inc="dve")
                        ptoks[jp] = mtoks
                    jq = jp - 2
                    if jq >= 0:
                        ebq = jq % 4
                        mtoks_q = ptoks.pop(jq)
                        for half in range(2):
                            jt = 2 * jq + half
                            col = ebq * 1024 + half * 512
                            pvtok = em.push(
                                "pe", [mtoks_q[half], tok_va,
                                       pv_free[pvctr % 2]],
                                lambda t, jt=jt, col=col, pvp=pvp, njt=njt:
                                t.matmul(
                                    pvp[0:65, :],
                                    b.vaugb[:, jt * 65:jt * 65 + 65],
                                    b.expbuf[:, col:col + 512],
                                    start=(jt == 0), stop=(jt == njt - 1)),
                                inc="pe")
                        exp_free[ebq] = pvtok
                        em.push("pe", [],
                                lambda t: t.matmul(
                                    junkps[:, 0:512], b.maskm[:, 0:128],
                                    b.maskm[:, 0:512], start=True, stop=True))
                slot = pvctr % 2
                asl = slice(slot * 512, slot * 512 + 512)
                dtok = em.push("dve", [pvtok, db_free[slot]],
                               lambda v, pvp=pvp: v.reciprocal(
                                   b.dinv[0:1, :], pvp[64:65, :]), inc="dve")
                catok = em.push("dve", [pvtok, ca_war[slot]],
                                lambda v, pvp=pvp, asl=asl: v.tensor_copy(
                                    b.Asb[:, asl], pvp[0:64, :]),
                                inc="dve")
                pv_free[slot] = catok
                dbp = dbps[slot]
                mmD = em.push("pe", [dtok, db_free[slot]],
                              lambda t, dbp=dbp: t.matmul(
                                  dbp[0:64, :], b.ones_row[0:1, 0:64],
                                  b.dinv[0:1, :], start=True, stop=True),
                              inc="pe")
                divtok = em.push(
                    "dve", [mmD, catok],
                    lambda v, slot=slot, asl=asl, dbp=dbp: v.tensor_tensor(
                        b.odiv[slot][:, :], b.Asb[:, asl], dbp[0:64, :], ALU.mult),
                    inc="dve")
                db_free[slot] = divtok
                ca_war[slot] = divtok
                jdst = iblk // 2
                tok_out = em.dma(
                    "sync", [divtok, ostore[slot]],
                    cco_in[jdst * 128 + hl * 64:jdst * 128 + hl * 64 + 64,
                           bb * 1024 + (iblk % 2) * 512:
                           bb * 1024 + (iblk % 2) * 512 + 512],
                    b.odiv[slot][:, :])
                ostore[slot] = tok_out
                otoks.append(tok_out)
                if dbg and bb == 0 and hl == 0 and iblk == 0:
                    d1 = em.dma("sync", [divtok], dbg["Asb"], b.Asb[:, :])
                    d2 = em.dma("sync", [divtok], dbg["dinv"],
                                b.dinv[0:1, :])
                    d3 = em.dma("sync", [divtok], dbg["exp0"],
                                b.expbuf[:, :])
                    d4 = em.dma("sync", [divtok], dbg["qTh"], b.qTh[:, :])
                    d5 = em.dma("sync", [divtok], dbg["kTh0"],
                                b.kTh[:, :])
                    d6 = em.dma("sync", [divtok], dbg["vaug0"],
                                b.vaugb[:, :])
                    d7 = em.dma("sync", [divtok], dbg["ccol"], b.ccol[:, :])
                    ostore[slot] = d7
                pvctr += 1
        tok_vaugb = tok_out
        tok_kth_war = em.cnt_tok("pe")

    tok_cco = em.push("pool", otoks, lambda g: a2a(g, cco_in, cco_out),
                      inc=em.next_cc())

    tok_wout = em.dma("pool", [tok_cco],
                      b.woutsb.rearrange("p (ec dd) -> p ec dd", ec=8),
                      wout_d.rearrange("(ec p) dd -> p ec dd", p=128))
    at_toks = {}
    store_toks = []
    tok_st = [None, None]
    ow_free = [None, None]
    for bb in range(2):
        tok_at = None
        for ec in range(8):
            tok_at = em.dma("sync", [tok_cco, at_toks.get("war")],
                            b.aT[:, ec * 1024:(ec + 1) * 1024],
                            cco_out[ec * 128:(ec + 1) * 128,
                                    bb * 1024:(bb + 1) * 1024])
        at_toks[bb] = tok_at
        for ic in range(4):
            for dt in range(8):
                slot = dt % 2
                psow = psq[4 + slot]
                rsl = (slice(dt * 128, (dt + 1) * 128),
                       slice(bb * 1024 + ic * 256, bb * 1024 + ic * 256 + 256))
                xr = em.dma("sync", [tok_st[slot]], b.xres[slot],
                            xt_d[rsl[0], rsl[1]])

                def fo(t, dt=dt, ic=ic, psow=psow):
                    for ec in range(8):
                        ins = t.matmul(psow[:, 0:256],
                                       b.woutsb[:, ec * 1024 + dt * 128:
                                                ec * 1024 + dt * 128 + 128],
                                       b.aT[:, ec * 1024 + ic * 256:
                                            ec * 1024 + ic * 256 + 256],
                                       start=(ec == 0), stop=(ec == 7))
                    return ins
                mmo = em.push("pe", [tok_at, tok_wout, ow_free[slot]], fo,
                              inc="pe")
                sttok = em.push(
                    "dve", [mmo, xr, tok_bout,
                            store_toks[-2] if len(store_toks) >= 2 else None],
                    lambda v, slot=slot, psow=psow, dt=dt: v.scalar_tensor_tensor(
                        b.ores[slot], psow[:, 0:256], b.boutc[:, dt:dt + 1],
                        b.xres[slot], ALU.add, ALU.add),
                    inc="dve")
                tok_st[slot] = sttok
                ow_free[slot] = sttok
                store_toks.append(em.dma("sync", [sttok],
                                         out_d[rsl[0], rsl[1]],
                                         b.ores[slot]))
        at_toks["war"] = em.cnt_tok("pe")

    em.push("sync", [store_toks[-1]], lambda sy: sy.nop())



class Pool:
    def __init__(self, ap_f32):
        self.ap = ap_f32
        self.off = 0

    def reset(self):
        self.off = 0

    def f32(self, cols):
        s = self.ap[:, self.off:self.off + cols]
        self.off += cols
        return s

    def bf16(self, cols):
        need = (cols + 1) // 2
        return self.f32(need).bitcast(BF16)[:, 0:cols]


def mk_ffn_bufs(p, shared):
    b = Bufs()
    b.__dict__.update(shared)
    b.c32 = p.f32(8 * LOCP)
    b.c_bf = p.bf16(8 * LOCP)
    b.xn = p.bf16(8 * TOKS)
    b.gate = [p.bf16(8 * 512) for _ in range(3)]
    b.wbuf = [p.bf16(8 * D) for _ in range(4)]
    b.xchunk = [b.c32[:, 0:TOKS], b.c32[:, TOKS:2 * TOKS]]
    b.scratch = p.f32(2560)
    b.xsq = [b.scratch[:, 0:512], b.scratch[:, 512:1024]]
    b.ln_tmp = b.scratch[:, 1024:1536]
    b.t1 = b.scratch[:, 0:512]
    b.t2 = b.scratch[:, 512:1024]
    b.hbuf = [b.scratch[:, 1024:1280].bitcast(BF16),
              b.scratch[:, 1280:1536].bitcast(BF16)]
    b.xres = [b.scratch[:, 1536:2048], b.scratch[:, 2048:2560]]
    b.ores = [b.scratch[:, 0:512], b.scratch[:, 512:1024]]
    b.biases = p.f32(40)
    b.m_row = p.f32(2048)[0:1, :]
    b.rs_row = p.f32(2048)[0:1, :]
    return b


def mk_attn0_bufs(p, shared):
    b = Bufs()
    b.__dict__.update(shared)
    b.bigbuf = p.bf16(16384)
    b.y = p.bf16(16384)
    b.wpool = p.bf16(16384)
    b.kT = p.bf16(8192)
    b.qT = p.bf16(2048)
    b.kTloc = p.bf16(2048)
    b.vsb = [p.bf16(2048) for _ in range(2)]
    b.usb = [p.bf16(2048) for _ in range(2)]
    b.tileAB = p.bf16(4096)
    b.utile = b.tileAB[:, 0:2048]
    b.vtile = b.tileAB[:, 2048:4096]
    b.Bsb = b.tileAB
    b.BTsb = p.bf16(4096)
    b.basebuf = p.f32(2048)
    b.vsb += [b.basebuf.bitcast(BF16)[:, 0:2048],
              b.basebuf.bitcast(BF16)[:, 2048:4096]]
    b.uvwb = p.bf16(8 * 128)
    b.ident = p.bf16(128)
    b.diag = p.bf16(8 * 128)
    b.corner = [p.f32(256) for _ in range(2)]
    b.costab = p.f32(LOCP)
    b.sintab = p.f32(LOCP)
    b.gb = p.f32(4)
    b.rsc = p.f32(8)
    b.smalls = p.f32(4)
    b.gsb = p.f32(2)[0:1, :]
    b.xchunk = [p.f32(TOKS) for _ in range(2)]
    b.scratch = p.f32(2560)
    b.xsq = [b.scratch[:, 0:512], b.scratch[:, 512:1024]]
    b.ln_tmp = b.scratch[:, 1024:1536]
    b.ropet = [b.scratch[:, 0:1024], b.scratch[:, 1024:2048]]
    b.rtile = b.scratch[:, 0:512]
    b.xres = [b.scratch[:, 512:768], b.scratch[:, 768:1024]]
    b.ores = [b.scratch[:, 1024:1280], b.scratch[:, 1280:1536]]
    ktf = b.kT.bitcast(F32)
    b.m_row = ktf[0:1, 0:2048]
    b.rs_row = ktf[0:1, 2048:4096]
    b.v_row = b.BTsb.bitcast(F32)[0:1, 0:2048]
    b.t_row = b.v_row
    b.rtile4 = b.xchunk[0]
    return b


def mk_attn1_bufs(p, shared):
    b = Bufs()
    b.__dict__.update(shared)
    b.big1 = p.bf16(16384)
    b.big2 = p.bf16(16384)
    b.big3 = p.bf16(16384)
    b.xn = b.big1
    b.qTh = b.big1
    b.qTl = b.big3
    b.kTh = b.big3[:, 0:8192]
    b.vaugb = b.big3[:, 8192:12352]
    b.outT = b.big2
    b.aT = b.big2[:, 0:8192]
    b.woutsb = b.big2[:, 8192:16384]
    b.expbuf = b.big2[:, 0:4096]
    big2f = b.big2.bitcast(F32)
    b.m_row = big2f[0:1, 0:2048]
    b.rs_row = big2f[0:1, 2048:4096]
    b.kTloc = p.bf16(4096)
    b.vaugl = p.bf16(16 * 260)
    b.qsT = p.bf16(2048)[0:64, :]
    b.kvsT = p.bf16(2048)[0:32, :]
    b.swb = p.bf16(2048)
    b.costab = p.f32(LOCP)
    b.sintab = p.f32(LOCP)
    b.maskm = p.bf16(896)
    b.wqa = p.bf16(8 * 64)
    b.wqb = p.bf16(D)[0:64, :]
    b.wkva = p.bf16(8 * 32)
    b.wkvbk = p.bf16(256)[0:32, :]
    b.wkvbv = p.bf16(256)[0:32, :]
    b.boutc = p.f32(8)
    b.Asb = p.f32(1024)[0:64, :]
    b.odiv = [p.bf16(512)[0:64, :] for _ in range(2)]
    b.dinv = p.f32(512)[0:1, :]
    b.ccol = p.f32(4)
    b.crow = p.f32(4)[0:1, :]
    b.xchunk = [p.f32(TOKS) for _ in range(2)]
    b.scratch = p.f32(2560)
    b.xsq = [b.scratch[:, 0:512], b.scratch[:, 512:1024]]
    b.ln_tmp = b.scratch[:, 1024:1536]
    b.xres = [b.scratch[:, 512:768], b.scratch[:, 768:1024]]
    b.ores = [b.scratch[:, 1024:1280], b.scratch[:, 1280:1536]]
    return b


def build_fused():
    nc = bass.Bass("TRN2", target_bir_lowering=False)
    xt_d = nc.dram_tensor("xt", [D, TOKS], F32, kind="ExternalInput").ap()
    out_d = nc.dram_tensor("xo", [D, TOKS], F32, kind="ExternalOutput").ap()
    xs = [nc.dram_tensor(f"xs{i}", [D, TOKS], F32).ap() for i in range(5)]

    wd = {}
    wdb = {}
    for L in (0, 1):
        for n in ("wf", "uf", "wi", "ui", "wc", "uc", "wo", "uo"):
            wd[(L, n)] = nc.dram_tensor(f"l{L}_{n}T", [D, D], F32,
                                        kind="ExternalInput").ap()
            wdb[(L, n)] = nc.dram_tensor(f"l{L}_{n}b", [D, D], BF16).ap()
    bias_d = {L: nc.dram_tensor(f"l{L}_biases", [128, 40], F32,
                                kind="ExternalInput").ap() for L in (0, 1)}
    uvw_d = nc.dram_tensor("uvwT", [D, 4224], F32, kind="ExternalInput").ap()
    ow_d = nc.dram_tensor("owT", [2048, D], F32, kind="ExternalInput").ap()
    gb_d = nc.dram_tensor("gbcols", [128, 4], F32, kind="ExternalInput").ap()
    g_d = nc.dram_tensor("gval", [1, 1], F32, kind="ExternalInput").ap()
    w2046_d = nc.dram_tensor("w2046", [1, 1], F32, kind="ExternalInput").ap()
    corner_d = nc.dram_tensor("corner", [64 * 128, 256], F32,
                              kind="ExternalInput").ap()
    rsc_d = nc.dram_tensor("rsc", [128, 8], F32, kind="ExternalInput").ap()
    cos0_d = nc.dram_tensor("cosT", [128, LOCP], F32, kind="ExternalInput").ap()
    sin0_d = nc.dram_tensor("sinT", [128, LOCP], F32, kind="ExternalInput").ap()
    ident_d = nc.dram_tensor("identb", [128, 128], F32, kind="ExternalInput").ap()
    wqa_d = nc.dram_tensor("wqa", [D, 64], F32, kind="ExternalInput").ap()
    wqb_d = nc.dram_tensor("wqbp", [64, D], F32, kind="ExternalInput").ap()
    wkva_d = nc.dram_tensor("wkva", [D, 32], F32, kind="ExternalInput").ap()
    wkvbk_d = nc.dram_tensor("wkvbk", [32, 256], F32, kind="ExternalInput").ap()
    wkvbv_d = nc.dram_tensor("wkvbv", [32, 256], F32, kind="ExternalInput").ap()
    wout_d = nc.dram_tensor("wout", [D, D], F32, kind="ExternalInput").ap()
    bout_d = nc.dram_tensor("boutc", [128, 8], F32, kind="ExternalInput").ap()
    cos1_d = nc.dram_tensor("cosF", [128, LOCP], F32, kind="ExternalInput").ap()
    sin1_d = nc.dram_tensor("sinF", [128, LOCP], F32, kind="ExternalInput").ap()
    mask_d = nc.dram_tensor("maskM", [128, 896], F32, kind="ExternalInput").ap()
    vtempl_d = nc.dram_tensor("vtempl", [8 * TOKS, 65], F32,
                              kind="ExternalInput").ap()
    consts_d = nc.dram_tensor("consts", [128, 2], F32, kind="ExternalInput").ap()
    onesrow_d = nc.dram_tensor("onesrow", [1, 128], F32,
                               kind="ExternalInput").ap()
    cc_in = nc.dram_tensor("cc0_in", [2176, 2048], BF16).ap()
    cc_out = nc.dram_tensor("cc0_out", [8 * 2176, 2048], BF16).ap()
    cck0_in = nc.dram_tensor("cck0_in", [128, 2048], BF16).ap()
    cck0_out = nc.dram_tensor("cck0_out", [8 * 128, 2048], BF16).ap()
    u_spill = nc.dram_tensor("u_spill", [2048, 2048], BF16).ap()
    ccq_in = nc.dram_tensor("cc1q_in", [1024, TOKS], BF16).ap()
    ccq_out = nc.dram_tensor("cc1q_out", [1024, TOKS], BF16).ap()
    cck_in = nc.dram_tensor("cc1k_in", [512, TOKS], BF16).ap()
    cck_out = nc.dram_tensor("cc1k_out", [512, TOKS], BF16).ap()
    ccv_in = nc.dram_tensor("cc1v_in", [8 * TOKS, 65], BF16).ap()
    ccv_out = nc.dram_tensor("cc1v_out", [8 * TOKS, 65], BF16).ap()
    cco_in = nc.dram_tensor("cc1o_in", [1024, TOKS], BF16).ap()
    cco_out = nc.dram_tensor("cc1o_out", [1024, TOKS], BF16).ap()

    with contextlib.ExitStack() as st:
        E = st.enter_context
        em = Em(nc, st)
        shared = {}
        consts_sb = E(nc.sbuf_tensor("consts_sb", [128, 2], F32))
        onesrow_sb = E(nc.sbuf_tensor("ones_row", [1, 128], F32))
        tiny = E(nc.sbuf_tensor("tiny", [1, 4], F32))
        shared["consts"] = consts_sb[:, :]
        shared["ones_col"] = consts_sb[:, 0:1]
        shared["eps_col"] = consts_sb[:, 1:2]
        shared["ones_row"] = onesrow_sb[:, :]
        pool_t = E(nc.sbuf_tensor("pool", [128, 51456], F32))
        p = Pool(pool_t[:, :])
        shared["ps_a"] = E(nc.psum_tensor("ps_a", [128, 2048], F32))[:, :]
        shared["ps_b"] = E(nc.psum_tensor("ps_b", [128, 2048], F32))[:, :]
        block = E(nc.Block())

        em._tok_consts = em.dma("pool", [], shared["consts"], consts_d)
        em.dma("pool", [], shared["ones_row"], onesrow_d)
        wready = {}
        for L in (0, 1):
            for n in ("wf", "uf", "wi", "ui", "wc", "uc", "wo", "uo"):
                wready[L] = em.dma("pool", [], wdb[(L, n)], wd[(L, n)])

        def bar():
            em.barrier(tiny[0:1, 0:1], shared["consts"][0:1, 0:1])

        stages = [
            ("ffn", 0, xt_d, xs[0]),
            ("attn0", None, xs[0], xs[1]),
            ("ffn", 0, xs[1], xs[2]),
            ("ffn", 1, xs[2], xs[3]),
            ("attn1", None, xs[3], xs[4]),
            ("ffn", 1, xs[4], out_d),
        ]
        for kind, L, xin, xout in stages:
            p.reset()
            if kind == "ffn":
                b = mk_ffn_bufs(p, shared)
                w_d = {n: wdb[(L, n)] for n in
                       ("wf", "uf", "wi", "ui", "wc", "uc", "wo", "uo")}
                em._w_ready = wready[L]
                emit_ffn_body(em, b, xin, w_d, bias_d[L], xout,
                              consts_d=consts_d, onesrow_d=onesrow_d)
                em._w_ready = None
            elif kind == "attn0":
                b = mk_attn0_bufs(p, shared)
                emit_attn0(em, b, xin, uvw_d, ow_d, gb_d, g_d, w2046_d,
                           corner_d, rsc_d, cos0_d, sin0_d, ident_d,
                           consts_d, onesrow_d, xout, cc_in, cc_out, u_spill,
                           cck0=(cck0_in, cck0_out))
            else:
                b = mk_attn1_bufs(p, shared)
                emit_attn1(em, b, xin, wqa_d, wqb_d, wkva_d, wkvbk_d,
                           wkvbv_d, wout_d, bout_d, cos1_d, sin1_d, mask_d,
                           vtempl_d, consts_d, onesrow_d, xout,
                           ccq_in, ccq_out, cck_in, cck_out,
                           ccv_in, ccv_out, cco_in, cco_out)
            bar()
        em.run(block)
    return nc


def run_fused(inputs):
    nc = _CACHE.get("nc_fused")
    if nc is None:
        nc = _CACHE["nc_fused"] = build_fused()
    x = np.ascontiguousarray(inputs["L0_attn_x"], dtype=np.float32)
    shards = shard_xT(x)
    consts = np.stack([np.ones(128, np.float32),
                       np.full(128, EPS, np.float32)], axis=1)
    onesrow = np.ones((1, 128), np.float32)
    w0, bias0 = _ffn_inputs(inputs, 0)
    w1, bias1 = _ffn_inputs(inputs, 1)
    a0 = _attn0_inputs(inputs, 0)
    a1 = _attn1_inputs(inputs)
    in_maps = []
    for c in range(NCORE):
        m = dict(xt=shards[c], consts=consts, onesrow=onesrow,
                 l0_biases=bias0, l1_biases=bias1,
                 uvwT=a0["uvwT"], owT=a0["owT"], gbcols=a0["gbcols"],
                 gval=a0["gval"], w2046=a0["w2046"], corner=a0["corners"][c],
                 rsc=a0["rsc"], cosT=a0["cosT"][c], sinT=a0["sinT"][c],
                 identb=a0["identb"],
                 wqa=a1["wqa"], wqbp=a1["wqbp"], wkva=a1["wkva"],
                 wkvbk=a1["wkvbk"], wkvbv=a1["wkvbv"], wout=a1["wout"],
                 boutc=a1["boutc"], cosF=a1["cosF"][c], sinF=a1["sinF"][c],
                 maskM=a1["maskM"], vtempl=a1["vtempl"])
        for n in ("wf", "uf", "wi", "ui", "wc", "uc", "wo", "uo"):
            m[f"l0_{n}T"] = w0[n + "T"]
            m[f"l1_{n}T"] = w1[n + "T"]
        in_maps.append(m)
    res = run_bass_kernel_spmd(nc, in_maps, core_ids=list(range(NCORE)))
    return unshard_xT([r["xo"] for r in res.results])


_CACHE = {}


def _ffn_inputs(inputs, L):
    key = "ffn_w_" + str(L)
    if key not in _CACHE:
        p = f"L{L}_ffn_"
        w = {n + "T": transpose_w(inputs[p + n])
             for n in ("wf", "uf", "wi", "ui", "wc", "uc", "wo", "uo")}
        bias = np.concatenate([colpack(inputs[p + n])
                               for n in ("bf", "bi", "bc", "bo", "in_c")], axis=1)
        _CACHE[key] = (w, bias)
    return _CACHE[key]


def run_ffn(x, inputs, L):
    nc_ffn = _CACHE.get("nc_ffn")
    if nc_ffn is None:
        nc_ffn = _CACHE["nc_ffn"] = build_ffn()
    shards = shard_xT(x)
    w, bias = _ffn_inputs(inputs, L)
    consts = np.stack([np.ones(128, np.float32),
                       np.full(128, EPS, np.float32)], axis=1)
    onesrow = np.ones((1, 128), np.float32)
    in_maps = [dict(xt=shards[c], biases=bias, consts=consts,
                    onesrow=onesrow, **w) for c in range(NCORE)]
    res = run_bass_kernel_spmd(nc_ffn, in_maps, core_ids=list(range(NCORE)))
    return unshard_xT([r["xo"] for r in res.results])


def _attn0_inputs(inputs, core):
    key = "attn0_w"
    if key not in _CACHE:
        uvwT = transpose_w(inputs["L0_attn_uv_w"])
        owT = transpose_w(inputs["L0_attn_o_w"])
        gamma = np.asarray(inputs["L0_attn_gamma"], dtype=np.float32)
        beta = np.asarray(inputs["L0_attn_beta"], dtype=np.float32)
        gb = np.ascontiguousarray(
            np.stack([gamma[0], beta[0], gamma[1], beta[1]], axis=1))
        gval = np.asarray(inputs["L0_attn_g"], dtype=np.float32).reshape(1, 1)
        wrel = np.asarray(inputs["L0_attn_w_rel"], dtype=np.float32)
        w2046 = np.ascontiguousarray(wrel[2046:2047].reshape(1, 1))
        rsc = colpack(inputs["L0_attn_res_scale"])
        tabs = rope_tables()
        corners = []
        for c in range(NCORE):
            jg = np.arange(16 * 128)[:, None]
            tiles = []
            for icc in range(4):
                ig = LOCP * c + icc * 256 + np.arange(256)[None, :]
                idx = np.minimum(jg - ig + 8191, 2046)
                tiles.append(wrel[idx].reshape(16, 128, 256))
            corners.append(np.ascontiguousarray(
                np.stack(tiles, axis=0).reshape(64 * 128, 256)))
        cosT, sinT = [], []
        for c in range(NCORE):
            sl = slice(LOCP * c, LOCP * (c + 1))
            ct = tabs["l0_cos"][sl].T
            stt_ = tabs["l0_sin"][sl].T
            cosT.append(np.ascontiguousarray(np.vstack([ct, ct])))
            sinT.append(np.ascontiguousarray(np.vstack([stt_, stt_])))
        ident = np.eye(128, dtype=np.float32)
        _CACHE[key] = dict(uvwT=uvwT, owT=owT, gbcols=gb, gval=gval,
                           w2046=w2046, rsc=rsc, corners=corners,
                           cosT=cosT, sinT=sinT, identb=ident)
    return _CACHE[key]


def run_attn0(x, inputs):
    nc = _CACHE.get("nc_attn0")
    if nc is None:
        nc = _CACHE["nc_attn0"] = build_attn0()
    shards = shard_xT(x)
    w = _attn0_inputs(inputs, 0)
    consts = np.stack([np.ones(128, np.float32),
                       np.full(128, EPS, np.float32)], axis=1)
    onesrow = np.ones((1, 128), np.float32)
    in_maps = [dict(xt=shards[c], uvwT=w["uvwT"], owT=w["owT"],
                    gbcols=w["gbcols"], gval=w["gval"], w2046=w["w2046"],
                    corner=w["corners"][c], rsc=w["rsc"], cosT=w["cosT"][c],
                    sinT=w["sinT"][c], identb=w["identb"], consts=consts,
                    onesrow=onesrow) for c in range(NCORE)]
    res = run_bass_kernel_spmd(nc, in_maps, core_ids=list(range(NCORE)))
    return unshard_xT([r["xo"] for r in res.results])


def _attn1_inputs(inputs):
    key = "attn1_w"
    if key not in _CACHE:
        wqb = np.asarray(inputs["L1_attn_Wqb"], dtype=np.float32)
        perm = np.empty(1024, np.int64)
        for h in range(16):
            for j in range(32):
                perm[h * 64 + j] = h * 64 + 2 * j
                perm[h * 64 + 32 + j] = h * 64 + 2 * j + 1
        wqbp = np.ascontiguousarray(wqb[:, perm])
        wkvb = np.asarray(inputs["L1_attn_Wkvb"], dtype=np.float32)
        permk = np.empty(256, np.int64)
        permv = np.empty(256, np.int64)
        for kh in range(4):
            for j in range(32):
                permk[kh * 64 + j] = kh * 128 + 4 * j
                permk[kh * 64 + 32 + j] = kh * 128 + 4 * j + 2
            for dd in range(64):
                permv[kh * 64 + dd] = kh * 128 + 2 * dd + 1
        wkvbk = np.ascontiguousarray(wkvb[:, permk])
        wkvbv = np.ascontiguousarray(wkvb[:, permv])
        tabs = rope_tables()
        cosF, sinF = [], []
        for c in range(NCORE):
            sl = slice(LOCP * c, LOCP * (c + 1))
            c32 = tabs["l1_cos"][sl].T
            s32 = tabs["l1_sin"][sl].T
            cosF.append(np.ascontiguousarray(np.vstack([c32] * 4)))
            sinF.append(np.ascontiguousarray(
                np.vstack([-s32, s32, -s32, s32])))
        p_ = np.arange(128)[:, None]
        m_ = np.arange(896)[None, :]
        maskM = (p_ <= m_ - 384).astype(np.float32)
        vtempl = np.zeros((8 * TOKS, 65), np.float32)
        vtempl[:, 64] = 1.0
        _CACHE[key] = dict(
            wqa=np.ascontiguousarray(np.asarray(inputs["L1_attn_Wqa"],
                                                dtype=np.float32)),
            wqbp=wqbp,
            wkva=np.ascontiguousarray(np.asarray(inputs["L1_attn_Wkva"],
                                                 dtype=np.float32)),
            wkvbk=wkvbk, wkvbv=wkvbv,
            wout=np.ascontiguousarray(np.asarray(inputs["L1_attn_Wout"],
                                                 dtype=np.float32)),
            boutc=colpack(inputs["L1_attn_bout"]),
            cosF=cosF, sinF=sinF, maskM=maskM, vtempl=vtempl)
    return _CACHE[key]


def run_attn1(x, inputs):
    nc = _CACHE.get("nc_attn1")
    if nc is None:
        nc = _CACHE["nc_attn1"] = build_attn1()
    shards = shard_xT(x)
    w = _attn1_inputs(inputs)
    consts = np.stack([np.ones(128, np.float32),
                       np.full(128, EPS, np.float32)], axis=1)
    onesrow = np.ones((1, 128), np.float32)
    in_maps = [dict(xt=shards[c], wqa=w["wqa"], wqbp=w["wqbp"],
                    wkva=w["wkva"], wkvbk=w["wkvbk"], wkvbv=w["wkvbv"],
                    wout=w["wout"], boutc=w["boutc"], cosF=w["cosF"][c],
                    sinF=w["sinF"][c], maskM=w["maskM"], vtempl=w["vtempl"],
                    consts=consts, onesrow=onesrow) for c in range(NCORE)]
    res = run_bass_kernel_spmd(nc, in_maps, core_ids=list(range(NCORE)))
    return unshard_xT([r["xo"] for r in res.results])


def kernel(**inputs):
    return np.ascontiguousarray(run_fused(inputs), dtype=np.float32)


# version 13
# speedup vs baseline: 1.0049x; 1.0049x over previous
import contextlib
import numpy as np
import ml_dtypes
import concourse.bass as bass
import concourse.mybir as mybir
from concourse.bass_utils import run_bass_kernel_spmd

F32 = mybir.dt.float32
BF16 = mybir.dt.bfloat16
AF = mybir.ActivationFunctionType
ALU = mybir.AluOpType
AX = mybir.AxisListType

NCORE = 8
BLK = 512
NPOS = 8192
NB = 2
D = 1024
TOKS = 2048
LOCP = 1024
EPS = 1e-5
BF = ml_dtypes.bfloat16



def zz_rows(c):
    return np.arange(LOCP * c, LOCP * (c + 1))


def shard_xT(x):
    out = []
    for c in range(NCORE):
        xs = x[:, zz_rows(c), :]
        out.append(np.ascontiguousarray(xs.transpose(2, 0, 1).reshape(D, TOKS)))
    return out


def unshard_xT(shards):
    out = np.empty((NB, NPOS, D), dtype=np.float32)
    for c in range(NCORE):
        xT = np.asarray(shards[c], dtype=np.float32).reshape(D, NB, LOCP)
        out[:, zz_rows(c), :] = xT.transpose(1, 2, 0)
    return out


def colpack(v):
    return np.ascontiguousarray(np.asarray(v, dtype=np.float32).reshape(8, 128).T)


def transpose_w(w):
    return np.ascontiguousarray(np.asarray(w, dtype=np.float32).T)


_JAX_TABLES = {}


def rope_tables():
    if _JAX_TABLES:
        return _JAX_TABLES
    import jax
    try:
        cpu = jax.devices("cpu")[0]
    except Exception:
        jax.config.update("jax_platforms", "cpu")
        cpu = jax.devices("cpu")[0]
    import jax.numpy as jnp
    with jax.default_device(cpu):
        half = 64
        pos = jnp.arange(NPOS, dtype=jnp.float32)[:, None]
        invf = 10000.0 ** (jnp.arange(half, dtype=jnp.float32) / half)
        arg = pos * invf[None, :]
        l0_sin = np.asarray(jnp.sin(arg))
        l0_cos = np.asarray(jnp.cos(arg))
        invf1 = 1.0 / (10000.0 ** (jnp.arange(0, 64, 2, dtype=jnp.float32) / 64))
        t = jnp.arange(NPOS, dtype=jnp.float32)
        fr = jnp.outer(t, invf1)
        l1_sin = np.asarray(jnp.sin(fr))
        l1_cos = np.asarray(jnp.cos(fr))
    _JAX_TABLES.update(l0_sin=l0_sin, l0_cos=l0_cos, l1_sin=l1_sin, l1_cos=l1_cos)
    return _JAX_TABLES



class Em:

    COMPUTE_SEM = {"pe": "pe", "act": "act", "dve": "dve", "pool": "pool"}
    _w_ready = None

    def __init__(self, nc, stack):
        self.nc = nc
        E = stack.enter_context
        self.sem = {n: E(nc.semaphore("sem_" + n))
                    for n in ("pe", "act", "dve", "pool", "qs", "qg",
                              "cc0", "cc1", "cc2", "cc3", "cc4", "cc5",
                              "cc6", "cc7", "cc8", "cc9")}
        self.cc_i = 0
        self.barrier_toks = []
        self.cnt = {n: 0 for n in self.sem}
        self.seen = {}
        self.q = {n: [] for n in ("pe", "act", "dve", "pool", "sync")}

    def next_cc(self):
        n = f"cc{self.cc_i}"
        self.cc_i += 1
        return n

    def barrier(self, tiny_out, tiny_in):
        self.push("pe", [], lambda t: t.nop(), inc="pe")
        self.push("act", [], lambda s: s.copy(tiny_out, tiny_in), inc="act")
        self.push("dve", [], lambda v: v.tensor_copy(tiny_out, tiny_in),
                  inc="dve")
        self.push("pool", [], lambda g: g.tensor_copy(tiny_out, tiny_in),
                  inc="pool")
        self.barrier_toks = [(s, c) for s, c in self.cnt.items() if c > 0]

    def push(self, engq, waits, build, inc=None, incn=1):
        rw = []
        for w in list(waits) + self.barrier_toks:
            if w is None:
                continue
            s, v = w
            if self.COMPUTE_SEM.get(engq) == s:
                continue
            if self.seen.get((engq, s), 0) >= v:
                continue
            self.seen[(engq, s)] = v
            rw.append((s, v))
        tok = None
        if inc is not None:
            self.cnt[inc] += incn
            tok = (inc, self.cnt[inc])
        sems = self.sem

        def thunk(eng, rw=rw, build=build, tok=tok, incn=incn):
            for s, v in rw:
                eng.wait_ge(sems[s], v)
            ins = build(eng)
            if tok is not None:
                ins.then_inc(sems[tok[0]], incn)
        self.q[engq].append(thunk)
        return tok

    def cnt_tok(self, sem):
        return (sem, self.cnt[sem])

    def fence(self, waits, tiny_out, tiny_in):
        return self.push("dve", waits,
                         lambda v: v.tensor_copy(tiny_out, tiny_in), inc="dve")

    def dma(self, engq, waits, out, in_, **kw):
        qsem = "qs" if engq == "sync" else "qg"
        return self.push(engq, waits,
                         lambda e: e.dma_start(out=out, in_=in_, **kw),
                         inc=qsem, incn=16)

    def run(self, block):
        q = self.q

        @block.tensor
        def _(t):
            for f in q["pe"]:
                f(t)

        @block.scalar
        def _(s):
            for f in q["act"]:
                f(s)

        @block.vector
        def _(v):
            for f in q["dve"]:
                f(v)

        @block.gpsimd
        def _(g):
            for f in q["pool"]:
                f(g)

        @block.sync
        def _(sy):
            for f in q["sync"]:
                f(sy)


class Bufs:
    pass



def emit_ln(em, b, T, xsrc, xn, nc0, extra_waits=(), keep_var=False):
    ncg = T // 512
    xsrc3 = xsrc.rearrange("(ic p) t -> ic p t", p=128)
    tok_st = [None] * 8
    tok_cgst = {}
    tok_ld = [None] * 8
    for ic in range(8):
        s = ic % 2
        tok_ld[ic] = em.dma("pool",
                            [tok_st[ic - 2] if ic >= 2 else None],
                            b.xchunk[s][:, 0:T], xsrc3[ic])
        last = None
        for cg in range(ncg):
            c0 = cg * 512
            sq = b.xsq[cg % 2]
            prev_sq = tok_cgst.get((ic, cg - 2)) or (
                tok_cgst.get((ic - 1, ncg - 2 + cg)) if cg < 2 else None)
            tok_sq = em.push("act", [tok_ld[ic], prev_sq],
                             lambda e, s=s, c0=c0, sq=sq: e.activation(
                                 sq, b.xchunk[s][:, c0:c0 + 512], AF.Square),
                             inc="act")

            def stats(t, ic=ic, s=s, c0=c0, sq=sq):
                t.matmul(b.ps_a[0:1, c0:c0 + 512], b.ones_col,
                         b.xchunk[s][:, c0:c0 + 512], start=(ic == 0), stop=(ic == 7))
                return t.matmul(b.ps_b[0:1, c0:c0 + 512], b.ones_col,
                                sq, start=(ic == 0), stop=(ic == 7))
            last = em.push("pe", [tok_sq, *extra_waits], stats, inc="pe")
            tok_cgst[(ic, cg)] = last
        tok_st[ic] = last
    em.push("dve", [tok_st[7]],
            lambda v: v.tensor_scalar(b.m_row[0:1, 0:T], b.ps_a[0:1, 0:T],
                                      1.0 / D, None, ALU.mult))
    em.push("dve", [],
            lambda v: v.tensor_tensor(b.rs_row[0:1, 0:T], b.m_row[0:1, 0:T],
                                      b.m_row[0:1, 0:T], ALU.mult))
    tok_v = em.push("dve", [],
                    lambda v: v.scalar_tensor_tensor(
                        b.rs_row[0:1, 0:T], b.ps_b[0:1, 0:T], 1.0 / D,
                        b.rs_row[0:1, 0:T], ALU.mult, ALU.subtract), inc="dve")
    if keep_var:
        tok_v = em.push("dve", [],
                        lambda v: v.tensor_copy(b.v_row[0:1, 0:T],
                                                b.rs_row[0:1, 0:T]), inc="dve")
    em.push("act", [tok_v],
            lambda s: s.activation(b.rs_row[0:1, 0:T], b.rs_row[0:1, 0:T],
                                   AF.Ln, bias=b.eps_col[0:1, :]))
    tok_rs = em.push("act", [],
                     lambda s: s.activation(b.rs_row[0:1, 0:T], b.rs_row[0:1, 0:T],
                                            AF.Exp, scale=-0.5), inc="act")
    def bcast(t):
        for cg in range(ncg):
            c0 = cg * 512
            t.matmul(b.ps_a[:, c0:c0 + 512], b.ones_row[0:1, :],
                     b.m_row[0:1, c0:c0 + 512], start=True, stop=True)
            ins = t.matmul(b.ps_b[:, c0:c0 + 512], b.ones_row[0:1, :],
                           b.rs_row[0:1, c0:c0 + 512], start=True, stop=True)
        return ins
    tok_bc = em.push("pe", [tok_rs], bcast, inc="pe")

    tok_xn = None
    tok_n = [None] * 8
    for ic in range(8):
        s = ic % 2
        war = tok_n[ic - 2] if ic >= 2 else tok_st[7]
        tok_l = em.dma("pool", [war],
                       b.xchunk[s][:, 0:T], xsrc3[ic])
        for cg in range(ncg):
            c0 = cg * 512
            osl = slice(nc0 + ic * T + c0, nc0 + ic * T + c0 + 512)
            em.push("dve", [tok_bc, tok_l],
                    lambda v, s=s, c0=c0: v.tensor_tensor(
                        b.ln_tmp, b.xchunk[s][:, c0:c0 + 512],
                        b.ps_a[:, c0:c0 + 512], ALU.subtract))
            tok_xn = em.push("dve", [],
                             lambda v, osl=osl, c0=c0: v.tensor_tensor(
                                 xn[:, osl], b.ln_tmp,
                                 b.ps_b[:, c0:c0 + 512], ALU.mult),
                             inc="dve")
        tok_n[ic] = tok_xn
    return tok_xn



def build_ffn(debug=False):
    nc = bass.Bass("TRN2", target_bir_lowering=False)
    xt_d = nc.dram_tensor("xt", [D, TOKS], F32, kind="ExternalInput").ap()
    wnames = ("wf", "uf", "wi", "ui", "wc", "uc", "wo", "uo")
    w_d = {n: nc.dram_tensor(n + "T", [D, D], F32, kind="ExternalInput").ap()
           for n in wnames}
    bias_d = nc.dram_tensor("biases", [128, 40], F32, kind="ExternalInput").ap()
    consts_d = nc.dram_tensor("consts", [128, 2], F32, kind="ExternalInput").ap()
    onesrow_d = nc.dram_tensor("onesrow", [1, 128], F32, kind="ExternalInput").ap()
    out_d = nc.dram_tensor("xo", [D, TOKS], F32, kind="ExternalOutput").ap()
    dbg = {}
    if debug:
        dbg["xn"] = nc.dram_tensor("dbg_xn", [8 * 128, TOKS], BF16,
                                   kind="ExternalOutput").ap()
        dbg["c32"] = nc.dram_tensor("dbg_c32", [8 * 128, LOCP], F32,
                                    kind="ExternalOutput").ap()
        dbg["cbf"] = nc.dram_tensor("dbg_cbf", [8 * 128, LOCP], BF16,
                                    kind="ExternalOutput").ap()
        dbg["c1"] = nc.dram_tensor("dbg_c1", [8 * 128, LOCP], F32,
                                   kind="ExternalOutput").ap()
        dbg["c0"] = nc.dram_tensor("dbg_c0", [8 * 128, LOCP], F32,
                                   kind="ExternalOutput").ap()
        for gi in range(3):
            dbg[f"g{gi}"] = nc.dram_tensor(f"dbg_g{gi}", [8 * 128, 512], BF16,
                                           kind="ExternalOutput").ap()

    with contextlib.ExitStack() as st:
        E = st.enter_context
        em = Em(nc, st)
        b = Bufs()
        b.c32 = E(nc.sbuf_tensor("c32", [128, 8 * LOCP], F32))
        b.c_bf = E(nc.sbuf_tensor("c_bf", [128, 8 * LOCP], BF16))
        b.xn = E(nc.sbuf_tensor("xn", [128, 8 * TOKS], BF16))
        b.gate = [E(nc.sbuf_tensor(f"gate{i}", [128, 8 * 512], BF16))
                  for i in range(3)]
        b.wbuf = [E(nc.sbuf_tensor(f"wbuf{i}", [128, 8 * D], BF16))
                  for i in range(4)]
        b.xchunk = [b.c32[:, 0:TOKS], b.c32[:, TOKS:2 * TOKS]]
        b.scratch = E(nc.sbuf_tensor("scratch", [128, 2560], F32))
        b.xsq = [b.scratch[:, 0:512], b.scratch[:, 512:1024]]
        b.ln_tmp = b.scratch[:, 1024:1536]
        b.t1 = b.scratch[:, 0:512]
        b.t2 = b.scratch[:, 512:1024]
        b.hbuf = [b.scratch[:, 1024:1280].bitcast(BF16),
                  b.scratch[:, 1280:1536].bitcast(BF16)]
        b.xres = [b.scratch[:, 1536:2048], b.scratch[:, 2048:2560]]
        b.ores = [b.scratch[:, 0:512], b.scratch[:, 512:1024]]
        b.biases = E(nc.sbuf_tensor("biases_sb", [128, 40], F32))
        b.m_row = E(nc.sbuf_tensor("m_row", [1, TOKS], F32))
        b.rs_row = E(nc.sbuf_tensor("rs_row", [1, TOKS], F32))
        b.consts = E(nc.sbuf_tensor("consts_sb", [128, 2], F32))
        b.ones_col = b.consts[:, 0:1]
        b.eps_col = b.consts[:, 1:2]
        b.ones_row = E(nc.sbuf_tensor("ones_row", [1, 128], F32))
        b.ps_a = E(nc.psum_tensor("ps_a", [128, 2048], F32))
        b.ps_b = E(nc.psum_tensor("ps_b", [128, 2048], F32))
        block = E(nc.Block())

        emit_ffn_body(em, b, xt_d, w_d, bias_d, out_d, dbg,
                      consts_d, onesrow_d)
        em.run(block)
    return nc


def emit_ffn_body(em, b, xt_d, w_d, bias_d, out_d, dbg=None,
                  consts_d=None, onesrow_d=None):
    em._tok_consts = em.dma("pool", [], b.consts[:, :], consts_d)
    tok_ones = em.dma("pool", [], b.ones_row[0:1, :], onesrow_d)
    tok_bias = em.dma("pool", [], b.biases[:, :], bias_d)

    tok_xn = emit_ln(em, b, TOKS, xt_d, b.xn, 0,
                     extra_waits=(tok_ones, em._tok_consts))

    tok_c0 = None
    for mc in range(8):
        sl = slice(mc * LOCP, (mc + 1) * LOCP)
        tok_c0 = em.push("dve", [tok_bias, tok_xn],
                         lambda v, sl=sl, mc=mc: v.tensor_scalar(
                             b.c32[:, sl], b.xn[:, 0:LOCP],
                             0.0, b.biases[:, 32 + mc:33 + mc],
                             ALU.mult, ALU.add),
                         inc="dve")
    tok_cbf = None
    for mc in range(8):
        sl = slice(mc * LOCP, (mc + 1) * LOCP)
        tok_cbf = em.push("dve", [],
                          lambda v, sl=sl: v.tensor_copy(b.c_bf[:, sl],
                                                         b.c32[:, sl]),
                          inc="dve")

    if dbg:
        em.dma("sync", [tok_cbf],
               dbg["c0"].rearrange("(ic p) t -> p ic t", p=128),
               b.c32[:, :].rearrange("p (ic t) -> p ic t", ic=8))
    order = ["wf", "uf", "wi", "ui", "wc", "uc"] * 4 + ["wo", "uo"]
    NL = len(order)
    wtok = [None] * NL
    wfree = [None] * NL
    emitted_load = [False] * NL

    def need(k):
        if k >= NL or emitted_load[k]:
            return
        emitted_load[k] = True
        wait = wfree[k - 4] if k >= 4 else None
        wtok[k] = em.dma(
            "pool", [wait, em._w_ready],
            b.wbuf[k % 4][:, :].rearrange("p (ic m) -> p ic m", ic=8),
            w_d[order[k]].rearrange("(ic p) m -> p ic m", p=128))

    for k in range(4):
        need(k)

    psq = [b.ps_a[:, i * 512:(i + 1) * 512] for i in range(4)] + \
          [b.ps_b[:, i * 512:(i + 1) * 512] for i in range(4)]
    ps_free = [tok_xn] * 8
    qctr = 0


    tok_upd_all = None
    for tstep in range(2):
        for half in range(2):
            base = (tstep * 2 + half) * 6
            gate_act_tok = {}
            for gi in range(3):
                kw, ku = base + 2 * gi, base + 2 * gi + 1
                need(kw), need(ku)
                fn = AF.Sigmoid if gi < 2 else AF.Tanh
                last_mm = None
                for mt in range(8):
                    q = qctr % 8
                    qctr += 1
                    waits = [wtok[kw], wtok[ku], tok_cbf, tok_xn, ps_free[q]]
                    def f(t, kw=kw, ku=ku, q=q, mt=mt, half=half, tstep=tstep):
                        ps = psq[q]
                        bw, bu = b.wbuf[kw % 4], b.wbuf[ku % 4]
                        xc = tstep * LOCP + half * 512
                        cc = half * 512
                        for ic in range(8):
                            t.matmul(ps, bw[:, ic * D + mt * 128:
                                            ic * D + mt * 128 + 128],
                                     b.xn[:, ic * TOKS + xc:ic * TOKS + xc + 512],
                                     start=(ic == 0), stop=False)
                        for ic in range(8):
                            ins = t.matmul(ps, bu[:, ic * D + mt * 128:
                                                  ic * D + mt * 128 + 128],
                                           b.c_bf[:, ic * LOCP + cc:
                                                  ic * LOCP + cc + 512],
                                           start=False, stop=(ic == 7))
                        return ins
                    mmtok = em.push("pe", waits, f, inc="pe")
                    last_mm = mmtok
                    atok = em.push(
                        "act", [mmtok, tok_upd_all],
                        lambda s, gi=gi, q=q, fn=fn, mt=mt:
                        s.activation(
                            b.gate[gi][:, mt * 512:(mt + 1) * 512],
                            psq[q], fn, bias=b.biases[:, gi * 8 + mt:
                                                      gi * 8 + mt + 1]),
                        inc="act")
                    ps_free[q] = atok
                    gate_act_tok[(gi, mt)] = atok
                wfree[kw] = last_mm
                wfree[ku] = last_mm
                need(kw + 4)
                need(ku + 4)
            for mt in range(8):
                gsl = slice(mt * 512, (mt + 1) * 512)
                csl = slice(mt * LOCP + half * 512, mt * LOCP + half * 512 + 512)
                em.push("dve", [gate_act_tok[(0, mt)]],
                        lambda v, gsl=gsl, csl=csl: v.tensor_tensor(
                            b.t1, b.gate[0][:, gsl], b.c32[:, csl], ALU.mult))
                em.push("dve", [gate_act_tok[(1, mt)],
                                gate_act_tok[(2, mt)]],
                        lambda v, gsl=gsl: v.tensor_tensor(
                            b.t2, b.gate[1][:, gsl], b.gate[2][:, gsl],
                            ALU.mult))
                tok_upd_all = em.push(
                    "dve", [],
                    lambda v, csl=csl: v.tensor_tensor(
                        b.c32[:, csl], b.t1, b.t2, ALU.add),
                    inc="dve")
        if dbg and tstep == 0:
            em.dma("sync", [tok_upd_all],
                   dbg["c1"].rearrange("(ic p) t -> p ic t", p=128),
                   b.c32[:, :].rearrange("p (ic t) -> p ic t", ic=8))
        for mc in range(8):
            sl = slice(mc * LOCP, (mc + 1) * LOCP)
            tok_cbf = em.push("dve", [tok_upd_all],
                              lambda v, sl=sl: v.tensor_copy(
                                  b.c_bf[:, sl], b.c32[:, sl]),
                              inc="dve")

    kwo, kuo = 24, 25
    need(kwo), need(kuo)
    store_toks = []
    stt_tok = [None, None]
    slot_i = 0
    last_mm = None
    for tstep in range(2):
        for half in range(2):
            for mt in range(8):
                slot = slot_i % 2
                slot_i += 1
                rsl = (slice(mt * 128, (mt + 1) * 128),
                       slice(tstep * LOCP + half * 512,
                             tstep * LOCP + half * 512 + 512))
                xr_tok = em.dma("sync", [stt_tok[slot]],
                                b.xres[slot], xt_d[rsl[0], rsl[1]])
                q = qctr % 8
                qctr += 1

                def f(t, q=q, mt=mt, half=half, tstep=tstep):
                    ps = psq[q]
                    bw, bu = b.wbuf[kwo % 4], b.wbuf[kuo % 4]
                    xc = tstep * LOCP + half * 512
                    cc = half * 512
                    for ic in range(8):
                        t.matmul(ps, bw[:, ic * D + mt * 128:
                                        ic * D + mt * 128 + 128],
                                 b.xn[:, ic * TOKS + xc:ic * TOKS + xc + 512],
                                 start=(ic == 0), stop=False)
                    for ic in range(8):
                        ins = t.matmul(ps, bu[:, ic * D + mt * 128:
                                              ic * D + mt * 128 + 128],
                                       b.c_bf[:, ic * LOCP + cc:
                                              ic * LOCP + cc + 512],
                                       start=False, stop=(ic == 7))
                    return ins
                mmtok = em.push("pe", [wtok[kwo], wtok[kuo], tok_cbf,
                                       ps_free[q]], f, inc="pe")
                last_mm = mmtok
                atok = em.push("act", [mmtok, stt_tok[slot]],
                               lambda s, q=q, mt=mt, slot=slot: s.activation(
                                   b.hbuf[slot], psq[q], AF.Tanh,
                                   bias=b.biases[:, 24 + mt:25 + mt]),
                               inc="act")
                ps_free[q] = atok
                prev_store = store_toks[-2] if len(store_toks) >= 2 else None
                stt_tok[slot] = em.push(
                    "dve", [atok, xr_tok, prev_store],
                    lambda v, slot=slot: v.scalar_tensor_tensor(
                        b.ores[slot], b.hbuf[slot], 0.5,
                        b.xres[slot], ALU.mult, ALU.add),
                    inc="dve")
                store_toks.append(em.dma("sync", [stt_tok[slot]],
                                         out_d[rsl[0], rsl[1]],
                                         b.ores[slot]))
    wfree[kwo] = last_mm
    wfree[kuo] = last_mm

    if dbg:
        for gi in range(3):
            em.dma("sync", [tok_upd_all],
                   dbg[f"g{gi}"].rearrange("(ic p) t -> p ic t", p=128),
                   b.gate[gi][:, :].rearrange("p (ic t) -> p ic t", ic=8))
        em.dma("sync", [tok_xn],
               dbg["xn"].rearrange("(ic p) t -> p ic t", p=128),
               b.xn[:, :].rearrange("p (ic t) -> p ic t", ic=8))
        em.dma("sync", [tok_upd_all],
               dbg["c32"].rearrange("(ic p) t -> p ic t", p=128),
               b.c32[:, :].rearrange("p (ic t) -> p ic t", ic=8))
        store_toks.append(em.dma(
            "sync", [tok_cbf],
            dbg["cbf"].rearrange("(ic p) t -> p ic t", p=128),
            b.c_bf[:, :].rearrange("p (ic t) -> p ic t", ic=8)))
    em.push("sync", [store_toks[-1]], lambda sy: sy.nop())




SQS = 0.08838834764831845


def build_attn0(debug=False):
    nc = bass.Bass("TRN2", target_bir_lowering=False)
    xt_d = nc.dram_tensor("xt", [D, TOKS], F32, kind="ExternalInput").ap()
    uvw_d = nc.dram_tensor("uvwT", [D, 4224], F32, kind="ExternalInput").ap()
    ow_d = nc.dram_tensor("owT", [2048, D], F32, kind="ExternalInput").ap()
    gb_d = nc.dram_tensor("gbcols", [128, 4], F32, kind="ExternalInput").ap()
    g_d = nc.dram_tensor("gval", [1, 1], F32, kind="ExternalInput").ap()
    w2046_d = nc.dram_tensor("w2046", [1, 1], F32, kind="ExternalInput").ap()
    corner_d = nc.dram_tensor("corner", [64 * 128, 256], F32,
                              kind="ExternalInput").ap()
    rsc_d = nc.dram_tensor("rsc", [128, 8], F32, kind="ExternalInput").ap()
    cos_d = nc.dram_tensor("cosT", [128, LOCP], F32, kind="ExternalInput").ap()
    sin_d = nc.dram_tensor("sinT", [128, LOCP], F32, kind="ExternalInput").ap()
    ident_d = nc.dram_tensor("identb", [128, 128], F32, kind="ExternalInput").ap()
    consts_d = nc.dram_tensor("consts", [128, 2], F32, kind="ExternalInput").ap()
    onesrow_d = nc.dram_tensor("onesrow", [1, 128], F32, kind="ExternalInput").ap()
    out_d = nc.dram_tensor("xo", [D, TOKS], F32, kind="ExternalOutput").ap()

    dbg = {}
    if debug:
        for nm, shape, dt in [("y", [1024, TOKS], BF16), ("xn2", [1024, TOKS], BF16),
                              ("qT", [128, TOKS], BF16), ("kTloc", [128, TOKS], BF16),
                              ("kTall", [128, 8192], BF16),
                              ("kern", [128, 16384], BF16),
                              ("Bsb", [128, 4096], BF16),
                              ("base", [128, TOKS], F32),
                              ("uvwb", [128, 1024], BF16),
                              ("wp0", [128, 2048], BF16),
                              ("qpre", [128, TOKS], F32),
                              ("BTsb", [128, 4096], BF16)]:
            dbg[nm] = nc.dram_tensor("dbg_" + nm, shape, dt,
                                     kind="ExternalOutput").ap()
    cc_in = nc.dram_tensor("cc0_in", [2176, 2048], BF16).ap()
    cc_out = nc.dram_tensor("cc0_out", [8 * 2176, 2048], BF16).ap()
    u_spill = nc.dram_tensor("u_spill", [2048, 2048], BF16).ap()

    with contextlib.ExitStack() as st:
        E = st.enter_context
        em = Em(nc, st)
        b = Bufs()
        b.bigbuf = E(nc.sbuf_tensor("bigbuf", [128, 16384], BF16))
        b.y = E(nc.sbuf_tensor("ybuf", [128, 16384], BF16))
        b.wpool = E(nc.sbuf_tensor("wpool", [128, 16384], BF16))
        b.kT = E(nc.sbuf_tensor("kT_all", [128, 8192], BF16))
        b.qT = E(nc.sbuf_tensor("qT", [128, 2048], BF16))
        b.kTloc = E(nc.sbuf_tensor("kTloc", [128, 2048], BF16))
        b.vsb = [E(nc.sbuf_tensor(f"vsb{i}", [128, 2048], BF16)) for i in range(2)]
        b.usb = [E(nc.sbuf_tensor(f"usb{i}", [128, 2048], BF16)) for i in range(2)]
        b.tileAB = E(nc.sbuf_tensor("tileAB", [128, 4096], BF16))
        b.utile = b.tileAB[:, 0:2048]
        b.vtile = b.tileAB[:, 2048:4096]
        b.Bsb = b.tileAB
        b.BTsb = E(nc.sbuf_tensor("BTsb", [128, 4096], BF16))
        b.basebuf = E(nc.sbuf_tensor("basebuf", [128, 2048], F32))
        b.vsb += [b.basebuf[:, :].bitcast(BF16)[:, 0:2048],
                  b.basebuf[:, :].bitcast(BF16)[:, 2048:4096]]
        b.uvwb = E(nc.sbuf_tensor("uvwb", [128, 8 * 128], BF16))
        b.ident = E(nc.sbuf_tensor("identsb", [128, 128], BF16))
        b.diag = E(nc.sbuf_tensor("diagrs", [128, 8 * 128], BF16))
        b.corner = [E(nc.sbuf_tensor(f"corner{i}", [128, 256], F32))
                    for i in range(2)]
        b.costab = E(nc.sbuf_tensor("costab", [128, LOCP], F32))
        b.sintab = E(nc.sbuf_tensor("sintab", [128, LOCP], F32))
        b.gb = E(nc.sbuf_tensor("gbsb", [128, 4], F32))
        b.rsc = E(nc.sbuf_tensor("rscsb", [128, 8], F32))
        b.smalls = E(nc.sbuf_tensor("smalls", [128, 4], F32))
        b.gsb = E(nc.sbuf_tensor("gsb", [1, 2], F32))
        b.consts = E(nc.sbuf_tensor("consts_sb", [128, 2], F32))
        b.ones_col = b.consts[:, 0:1]
        b.eps_col = b.consts[:, 1:2]
        b.ones_row = E(nc.sbuf_tensor("ones_row", [1, 128], F32))
        b.xchunk = [E(nc.sbuf_tensor(f"xchunk{i}", [128, TOKS], F32))
                    for i in range(2)]
        b.scratch = E(nc.sbuf_tensor("scratch", [128, 2560], F32))
        b.xsq = [b.scratch[:, 0:512], b.scratch[:, 512:1024]]
        b.ln_tmp = b.scratch[:, 1024:1536]
        b.ropet = [b.scratch[:, 0:1024], b.scratch[:, 1024:2048]]
        b.rtile = b.scratch[:, 0:512]
        b.rtile4 = b.xchunk[0]
        b.xres = [b.scratch[:, 512:768], b.scratch[:, 768:1024]]
        b.ores = [b.scratch[:, 1024:1280], b.scratch[:, 1280:1536]]
        ktf = b.kT.bitcast(F32)
        b.m_row = ktf[0:1, 0:2048]
        b.rs_row = ktf[0:1, 2048:4096]
        b.v_row = b.BTsb.bitcast(F32)[0:1, 0:2048]
        b.t_row = b.v_row
        b.ps_a = E(nc.psum_tensor("ps_a", [128, 2048], F32))
        b.ps_b = E(nc.psum_tensor("ps_b", [128, 2048], F32))
        block = E(nc.Block())

        emit_attn0(em, b, xt_d, uvw_d, ow_d, gb_d, g_d, w2046_d, corner_d,
                   rsc_d, cos_d, sin_d, ident_d, consts_d, onesrow_d, out_d,
                   cc_in, cc_out, u_spill, dbg)
        em.run(block)
    return nc


def emit_attn0(em, b, xt_d, uvw_d, ow_d, gb_d, g_d, w2046_d, corner_d,
               rsc_d, cos_d, sin_d, ident_d, consts_d, onesrow_d, out_d,
               cc_in, cc_out, u_spill, dbg=None, cck0=None,
               ccv0_out=None, ccv1_out=None):
    em._tok_consts = em.dma("pool", [], b.consts[:, :], consts_d)
    tok_ones = em.dma("pool", [], b.ones_row[0:1, :], onesrow_d)
    tok_gb = em.dma("pool", [], b.gb[:, :], gb_d)
    tok_g = em.dma("pool", [], b.gsb[0:1, 0:1], g_d)
    tok_w2 = em.dma("pool", [], b.gsb[0:1, 1:2], w2046_d)
    tok_rsc = em.dma("pool", [], b.rsc[:, :], rsc_d)
    tok_cos = em.dma("pool", [], b.costab[:, :], cos_d)
    tok_sin = em.dma("pool", [], b.sintab[:, :], sin_d)
    tok_id = em.dma("pool", [], b.ident[:, :], ident_d)

    tok_w2b = em.push("pe", [tok_w2, tok_ones],
                      lambda t: t.matmul(b.ps_a[:, 0:1], b.ones_row[0:1, :],
                                         b.gsb[0:1, 1:2], start=True, stop=True),
                      inc="pe")
    em.push("dve", [tok_w2b],
            lambda v: v.tensor_scalar(b.smalls[:, 0:1], b.ps_a[:, 0:1],
                                      SQS, None, ALU.mult))
    tok_smalls = em.push("dve", [],
                         lambda v: v.tensor_copy(b.smalls[:, 1:2],
                                                 b.ps_a[:, 0:1]), inc="dve")
    tok_diag = None
    for dt in range(8):
        tok_diag = em.push("dve", [tok_id, tok_rsc],
                           lambda v, dt=dt: v.tensor_scalar(
                               b.diag[:, dt * 128:(dt + 1) * 128],
                               b.ident[:, :], b.rsc[:, dt:dt + 1], None,
                               ALU.mult), inc="dve")

    tok_y = emit_ln(em, b, TOKS, xt_d, b.y, 0,
                    extra_waits=(tok_ones, em._tok_consts), keep_var=True)
    tok_t0 = em.push("dve", [tok_y],
                     lambda v: v.tensor_scalar(b.t_row[0:1, :], b.v_row[0:1, :],
                                               0.0, None, ALU.max), inc="dve")
    tok_tr0 = em.push("act", [tok_t0],
                      lambda s: s.activation(b.t_row[0:1, :], b.t_row[0:1, :],
                                             AF.Sqrt), inc="act")
    em.push("dve", [tok_tr0],
            lambda v: v.tensor_tensor(b.t_row[0:1, :], b.t_row[0:1, :],
                                      b.rs_row[0:1, :], ALU.mult))
    em.push("dve", [],
            lambda v: v.tensor_scalar(b.t_row[0:1, :], b.t_row[0:1, :], EPS,
                                      None, ALU.max))
    em.push("dve", [], lambda v: v.reciprocal(b.t_row[0:1, :], b.t_row[0:1, :]))
    tok_tr = em.push("dve", [tok_g],
                     lambda v: v.tensor_scalar(b.t_row[0:1, :], b.t_row[0:1, :],
                                               b.gsb[0:1, 0:1], None, ALU.mult),
                     inc="dve")

    def bct(t):
        for cg in range(4):
            c0 = cg * 512
            ins = t.matmul(b.ps_a[:, c0:c0 + 512], b.ones_row[0:1, :],
                           b.t_row[0:1, c0:c0 + 512], start=True, stop=True)
        return ins
    tok_bct = em.push("pe", [tok_tr], bct, inc="pe")
    tok_xn2 = None
    for ic in range(8):
        sl = slice(ic * TOKS, (ic + 1) * TOKS)
        tok_xn2 = em.push("dve", [tok_bct],
                          lambda v, sl=sl: v.tensor_tensor(
                              b.bigbuf[:, sl], b.y[:, sl], b.ps_a[:, 0:TOKS],
                              ALU.mult), inc="dve")

    tok_xn2 = em.fence([tok_xn2], b.scratch[0:1, 0:1], b.consts[0:1, 0:1])

    uvw3 = uvw_d.rearrange("(ic p) f -> p ic f", p=128)
    psq = [b.ps_a[:, i * 512:(i + 1) * 512] for i in range(4)] + \
          [b.ps_b[:, i * 512:(i + 1) * 512] for i in range(4)]
    ps_free = [tok_bct] * 8
    qctr = 0
    spill_toks = []
    prev_pass_mm = None
    tok_uvw = None
    for fpass in range(2):
        f0 = fpass * 2048
        tok_uvw = em.dma(
            "pool", [prev_pass_mm],
            b.wpool[:, 0:16384].rearrange("p (ic f) -> p ic f", ic=8),
            uvw3[:, :, f0:f0 + 2048])
        tile = b.utile if fpass == 0 else b.vtile
        prev_spill = None
        for tt in range(16):
            atoks = []
            for fb in range(4):
                q = qctr % 8
                qctr += 1

                def f(t, tt=tt, fb=fb, q=q):
                    for ic in range(8):
                        ins = t.matmul(
                            psq[q],
                            b.bigbuf[:, ic * TOKS + tt * 128:
                                     ic * TOKS + tt * 128 + 128],
                            b.wpool[:, ic * 2048 + fb * 512:
                                    ic * 2048 + fb * 512 + 512],
                            start=(ic == 0), stop=(ic == 7))
                    return ins
                mmtok = em.push("pe", [tok_uvw, tok_xn2, ps_free[q]], f,
                                inc="pe")
                prev_pass_mm = mmtok
                atok = em.push("act", [mmtok, prev_spill],
                               lambda s, fb=fb, q=q, tile=tile: s.activation(
                                   tile[:, fb * 512:(fb + 1) * 512], psq[q],
                                   AF.Silu), inc="act")
                ps_free[q] = atok
                atoks.append(atok)
            dst_d = u_spill if fpass == 0 else cc_in
            row0 = tt * 128 if fpass == 0 else 128 + tt * 128
            prev_spill = em.dma("sync", atoks, dst_d[row0:row0 + 128, :],
                                tile)
            spill_toks.append(prev_spill)

    tok_bw = em.dma("pool", [],
                    b.uvwb[:, :].rearrange("p (ic f) -> p ic f", ic=8),
                    uvw3[:, :, 4096:4224])
    tok_basebuf = None
    for cg in range(4):
        c0 = cg * 512
        q = qctr % 8
        qctr += 1

        def f(t, c0=c0, q=q):
            for ic in range(8):
                ins = t.matmul(psq[q],
                               b.uvwb[:, ic * 128:(ic + 1) * 128],
                               b.bigbuf[:, ic * TOKS + c0:ic * TOKS + c0 + 512],
                               start=(ic == 0), stop=(ic == 7))
            return ins
        mmtok = em.push("pe", [tok_bw, tok_xn2, ps_free[q]], f, inc="pe")
        tok_basebuf = em.push("act", [mmtok],
                              lambda s, c0=c0, q=q: s.activation(
                                  b.basebuf[:, c0:c0 + 512], psq[q], AF.Silu),
                              inc="act")
        ps_free[q] = tok_basebuf

    qpre, kpre = b.xchunk[0], b.xchunk[1]
    em.push("dve", [tok_basebuf, tok_gb],
            lambda v: v.tensor_scalar(qpre[:, :], b.basebuf[:, :],
                                      b.gb[:, 0:1], b.gb[:, 1:2],
                                      ALU.mult, ALU.add))
    tok_pre = em.push("dve", [],
                      lambda v: v.tensor_scalar(kpre[:, :], b.basebuf[:, :],
                                                b.gb[:, 2:3], b.gb[:, 3:4],
                                                ALU.mult, ALU.add), inc="dve")
    if dbg:
        em.dma("sync", [tok_basebuf], dbg["base"], b.basebuf[:, :])
        em.dma("sync", [tok_bw], dbg["uvwb"], b.uvwb[:, :])
        em.dma("sync", [tok_uvw], dbg["wp0"], b.wpool[:, 0:2048])
    tok_rope = None
    sw = b.basebuf
    for (srcb, dstbuf) in ((qpre, b.qT), (kpre, b.kTloc)):
        em.dma("pool", [tok_pre, tok_rope], sw[0:64, :], srcb[64:128, :])
        tok_sw = em.dma("pool", [], sw[64:128, :], srcb[0:64, :])
        for bb in range(2):
            sl = slice(bb * LOCP, (bb + 1) * LOCP)
            em.push("dve", [tok_pre, tok_sw, tok_cos, tok_sin],
                    lambda v, srcb=srcb, sl=sl: v.tensor_tensor(
                        b.ropet[0][0:64, :], srcb[0:64, sl],
                        b.costab[0:64, :], ALU.mult))
            em.push("dve", [],
                    lambda v, sl=sl: v.tensor_tensor(
                        b.ropet[1][0:64, :], sw[0:64, sl],
                        b.sintab[0:64, :], ALU.mult))
            em.push("dve", [],
                    lambda v, dstbuf=dstbuf, sl=sl: v.tensor_tensor(
                        dstbuf[0:64, sl], b.ropet[0][0:64, :],
                        b.ropet[1][0:64, :], ALU.subtract))
            em.push("dve", [],
                    lambda v, srcb=srcb, sl=sl: v.tensor_tensor(
                        b.ropet[0][64:128, :], srcb[64:128, sl],
                        b.costab[64:128, :], ALU.mult))
            em.push("dve", [],
                    lambda v, sl=sl: v.tensor_tensor(
                        b.ropet[1][64:128, :], sw[64:128, sl],
                        b.sintab[64:128, :], ALU.mult))
            tok_rope = em.push("dve", [],
                               lambda v, dstbuf=dstbuf, sl=sl: v.tensor_tensor(
                                   dstbuf[64:128, sl], b.ropet[0][64:128, :],
                                   b.ropet[1][64:128, :], ALU.add), inc="dve")
    kdst = cck0[0] if cck0 is not None else cc_in[0:128, :]
    tok_kdma = em.dma("sync", [tok_rope], kdst, b.kTloc[:, :])
    if dbg:
        em.dma("sync", [tok_pre], dbg["qpre"], qpre[:, :])
        em.dma("sync", [tok_xn2],
               dbg["y"].rearrange("(ic p) t -> p ic t", p=128),
               b.y[:, :].rearrange("p (ic t) -> p ic t", ic=8))
        em.dma("sync", [tok_xn2],
               dbg["xn2"].rearrange("(ic p) t -> p ic t", p=128),
               b.bigbuf[:, :].rearrange("p (ic t) -> p ic t", ic=8))
        em.dma("sync", [tok_rope], dbg["qT"], b.qT[:, :])
        em.dma("sync", [tok_rope], dbg["kTloc"], b.kTloc[:, :])

    if cck0 is not None:
        tok_cck0 = em.push("pool", [tok_kdma],
                           lambda g: g.collective_compute(
                               "AllGather", ALU.bypass,
                               replica_groups=[list(range(8))],
                               ins=[cck0[0].opt()], outs=[cck0[1].opt()]),
                           inc=em.next_cc())
        tok_ccv0 = em.push("pool", spill_toks[16:24],
                           lambda g: g.collective_compute(
                               "AllGather", ALU.bypass,
                               replica_groups=[list(range(8))],
                               ins=[cc_in[128:1152, :].opt()],
                               outs=[ccv0_out.opt()]),
                           inc=em.next_cc())
        tok_ccv1 = em.push("pool", spill_toks,
                           lambda g: g.collective_compute(
                               "AllGather", ALU.bypass,
                               replica_groups=[list(range(8))],
                               ins=[cc_in[1152:2176, :].opt()],
                               outs=[ccv1_out.opt()]),
                           inc=em.next_cc())
        tok_ccv = [tok_ccv0, tok_ccv1]
    else:
        tok_cc = em.push("pool", [tok_kdma] + spill_toks,
                         lambda g: g.collective_compute(
                             "AllGather", ALU.bypass,
                             replica_groups=[list(range(8))],
                             ins=[cc_in.opt()], outs=[cc_out.opt()]),
                         inc=em.next_cc())
        tok_cck0 = tok_cc
        tok_ccv = [tok_cc, tok_cc]

    ow3 = ow_d.rearrange("(ec p) dd -> p ec dd", p=128)
    tok_ow = em.dma("pool", [prev_pass_mm],
                    b.wpool[:, 0:16384].rearrange("p (ec dd) -> p ec dd", ec=16),
                    ow3)

    corner3 = corner_d.rearrange("(s p) i -> s p i", p=128)

    store_toks = []
    tok_score_done = None
    tok_ic_done = None
    tok_tr_done = None
    tok_evac = {0: None, 1: None}
    corner_war = [None, None]
    tok_uload = {0: None, 1: None}
    vwar = [None] * 4
    tok_st = [None, None]
    tok_xr = [None, None]
    owps_free = [None, None]

    for bb in range(2):
        tok_kta = None
        ksrc = cck0[1] if cck0 is not None else cc_out
        kstride = 128 if cck0 is not None else 2176
        for r in range(8):
            tok_kta = em.dma("sync", [tok_cck0, tok_score_done],
                             b.kT[:, r * 1024:(r + 1) * 1024],
                             ksrc[kstride * r:kstride * r + 128,
                                  bb * 1024:(bb + 1) * 1024])
        for ic in range(4):
            icg = bb * 4 + ic
            iofs = bb * 1024 + ic * 256
            sqt = None
            rt_hist = [None, None]
            for jp in range(16):
                g = jp % 2
                grp = b.ps_a[:, g * 1024:(g + 1) * 1024]
                deps = []
                for sl4 in range(4):
                    jt = jp * 4 + sl4
                    mmtok = em.push(
                        "pe", [tok_kta, tok_rope, ps_free[g], tok_ic_done,
                               tok_tr_done],
                        lambda t, jt=jt, sl4=sl4, grp=grp, iofs=iofs:
                        t.matmul(
                            grp[:, sl4 * 256:sl4 * 256 + 256],
                            b.kT[:, jt * 128:(jt + 1) * 128],
                            b.qT[:, iofs:iofs + 256], start=True, stop=True),
                        inc="pe")
                    dep = mmtok
                    if jt < 16:
                        cs = (ic * 16 + jt) % 2
                        ltok = em.dma("pool", [corner_war[cs], tok_smalls],
                                      b.corner[cs][:, :], corner3[ic * 16 + jt])
                        adjtok = em.push(
                            "dve", [ltok],
                            lambda v, cs=cs: v.tensor_scalar(
                                b.corner[cs][:, :], b.corner[cs][:, :],
                                b.smalls[:, 1:2], None, ALU.subtract),
                            inc="dve")
                        dep = em.push(
                            "dve", [mmtok, adjtok],
                            lambda v, sl4=sl4, grp=grp, cs=cs: v.tensor_tensor(
                                grp[:, sl4 * 256:sl4 * 256 + 256],
                                grp[:, sl4 * 256:sl4 * 256 + 256],
                                b.corner[cs][:, :], ALU.add),
                            inc="dve")
                        corner_war[cs] = dep
                    deps.append(dep)
                rh = g * 1024
                rt = em.push("act", [tok_smalls, rt_hist[g], *deps],
                             lambda s, grp=grp, rh=rh: s.activation(
                                 b.rtile4[:, rh:rh + 1024], grp,
                                 AF.Relu, bias=b.smalls[:, 0:1], scale=SQS),
                             inc="act")
                rt_hist[g] = rt
                sqt = em.push("dve", [rt],
                              lambda v, jp=jp, rh=rh: v.tensor_tensor(
                                  b.bigbuf[:, jp * 1024:(jp + 1) * 1024],
                                  b.rtile4[:, rh:rh + 1024],
                                  b.rtile4[:, rh:rh + 1024], ALU.mult),
                              inc="dve")
                ps_free[g] = sqt
            tok_score_done = em.fence([sqt], b.scratch[0:1, 0:1],
                                      b.consts[0:1, 0:1])
            if dbg and bb == 0 and ic == 0:
                em.dma("sync", [tok_score_done], dbg["kern"], b.bigbuf[:, :])
                em.dma("sync", [tok_kta], dbg["kTall"], b.kT[:, :])

            for it in range(2):
                trow = (icg * 2 + it) * 128
                tok_uload[it] = em.dma("sync", [tok_evac[it]],
                                       b.usb[it][:, :],
                                       u_spill[trow:trow + 128, :])
            Atok = None
            for jt in range(64):
                vb = jt % 4
                r, jr = jt // 8, jt % 8
                if cck0 is not None:
                    vsrcT = ccv0_out if bb == 0 else ccv1_out
                    vrow = 1024 * r + jr * 128
                else:
                    vsrcT = cc_out
                    vrow = 2176 * r + 128 + bb * 1024 + jr * 128
                vtok = em.dma("sync" if jt % 2 == 0 else "pool",
                              [vwar[vb], tok_ccv[bb]],
                              b.vsb[vb][:, :], vsrcT[vrow:vrow + 128, :])

                def fA(t, jt=jt, vb=vb):
                    for it in range(2):
                        for es in range(4):
                            ins = t.matmul(
                                (b.ps_a if it == 0 else b.ps_b)[:, es * 512:
                                                                (es + 1) * 512],
                                b.bigbuf[:, jt * 256 + it * 128:
                                         jt * 256 + it * 128 + 128],
                                b.vsb[vb][:, es * 512:(es + 1) * 512],
                                start=(jt == 0), stop=(jt == 63))
                    return ins
                Atok = em.push("pe", [vtok, tok_score_done, tok_ic_done,
                                      tok_tr_done], fA, inc="pe")
                vwar[vb] = Atok
            for it in range(2):
                ps = b.ps_a if it == 0 else b.ps_b
                ev = None
                for es in range(4):
                    ev = em.push("dve", [Atok, tok_uload[it]],
                                 lambda v, it=it, es=es, ps=ps: v.tensor_tensor(
                                     b.Bsb[:, it * 2048 + es * 512:
                                           it * 2048 + (es + 1) * 512],
                                     ps[:, es * 512:(es + 1) * 512],
                                     b.usb[it][:, es * 512:(es + 1) * 512],
                                     ALU.mult), inc="dve")
                tok_evac[it] = ev

            tok_evac = {0: em.fence([tok_evac[0]], b.scratch[0:1, 0:1],
                                    b.consts[0:1, 0:1]),
                        1: em.fence([tok_evac[1]], b.scratch[0:1, 0:1],
                                    b.consts[0:1, 0:1])}
            if dbg and bb == 0 and ic == 0:
                dtok = em.dma("sync", [tok_evac[0], tok_evac[1]], dbg["Bsb"],
                              b.Bsb[:, :])
                tok_evac = {0: dtok, 1: dtok}
            cptok = None
            trq = 0
            for it in range(2):
                for et in range(16):
                    ps = (b.ps_a if trq % 2 == 0 else b.ps_b)
                    pt = ps[:, 0:64].bitcast(BF16)
                    trtok = em.push(
                        "pe", [tok_evac[it], tok_id, cptok],
                        lambda t, it=it, et=et, pt=pt: t.transpose(
                            pt[:, 0:128],
                            b.Bsb[:, it * 2048 + et * 128:
                                  it * 2048 + et * 128 + 128],
                            b.ident[:, :]), inc="pe")
                    cptok = em.push(
                        "act", [trtok],
                        lambda s, it=it, et=et, pt=pt: s.copy(
                            b.BTsb[:, et * 256 + it * 128:
                                   et * 256 + it * 128 + 128], pt[:, 0:128]),
                        inc="act")
                    trq += 1
            tok_tr_done = cptok
            if dbg and bb == 0 and ic == 0:
                tok_tr_done = em.dma("sync", [tok_tr_done], dbg["BTsb"],
                                     b.BTsb[:, :])

            last_stt = None
            for dt in range(8):
                slot = dt % 2
                psow = psq[4 + slot]
                rsl = (slice(dt * 128, (dt + 1) * 128),
                       slice(bb * 1024 + ic * 256, bb * 1024 + ic * 256 + 256))
                tok_xr[slot] = em.dma("sync", [tok_st[slot]],
                                      b.xres[slot], xt_d[rsl[0], rsl[1]])

                def fo(t, dt=dt, psow=psow, iofs=iofs):
                    for ec in range(16):
                        t.matmul(psow[:, 0:256],
                                 b.wpool[:, ec * 1024 + dt * 128:
                                         ec * 1024 + dt * 128 + 128],
                                 b.BTsb[:, ec * 256:(ec + 1) * 256],
                                 start=(ec == 0), stop=False)
                    ins = t.matmul(psow[:, 0:256],
                                   b.diag[:, dt * 128:(dt + 1) * 128],
                                   b.y[:, dt * TOKS + iofs:
                                       dt * TOKS + iofs + 256],
                                   start=False, stop=True)
                    return ins
                mmo = em.push("pe", [tok_tr_done, tok_ow, tok_diag,
                                     owps_free[slot]], fo, inc="pe")
                sttok = em.push("dve", [mmo, tok_xr[slot],
                                        store_toks[-2] if len(store_toks) >= 2
                                        else None],
                                lambda v, slot=slot, psow=psow: v.tensor_tensor(
                                    b.ores[slot], psow[:, 0:256],
                                    b.xres[slot], ALU.add),
                                inc="dve")
                tok_st[slot] = sttok
                owps_free[slot] = sttok
                last_stt = sttok
                store_toks.append(em.dma("sync", [sttok],
                                         out_d[rsl[0], rsl[1]],
                                         b.ores[slot]))
            tok_ic_done = last_stt

    em.push("sync", [store_toks[-1]], lambda sy: sy.nop())



def build_attn1(debug=False):
    nc = bass.Bass("TRN2", target_bir_lowering=False)
    xt_d = nc.dram_tensor("xt", [D, TOKS], F32, kind="ExternalInput").ap()
    wqa_d = nc.dram_tensor("wqa", [D, 64], F32, kind="ExternalInput").ap()
    wqb_d = nc.dram_tensor("wqbp", [64, D], F32, kind="ExternalInput").ap()
    wkva_d = nc.dram_tensor("wkva", [D, 32], F32, kind="ExternalInput").ap()
    wkvbk_d = nc.dram_tensor("wkvbk", [32, 256], F32, kind="ExternalInput").ap()
    wkvbv_d = nc.dram_tensor("wkvbv", [32, 256], F32, kind="ExternalInput").ap()
    wout_d = nc.dram_tensor("wout", [D, D], F32, kind="ExternalInput").ap()
    bout_d = nc.dram_tensor("boutc", [128, 8], F32, kind="ExternalInput").ap()
    cos_d = nc.dram_tensor("cosF", [128, LOCP], F32, kind="ExternalInput").ap()
    sin_d = nc.dram_tensor("sinF", [128, LOCP], F32, kind="ExternalInput").ap()
    mask_d = nc.dram_tensor("maskM", [128, 896], F32, kind="ExternalInput").ap()
    vtempl_d = nc.dram_tensor("vtempl", [8 * TOKS, 65], F32, kind="ExternalInput").ap()
    consts_d = nc.dram_tensor("consts", [128, 2], F32, kind="ExternalInput").ap()
    onesrow_d = nc.dram_tensor("onesrow", [1, 128], F32, kind="ExternalInput").ap()
    out_d = nc.dram_tensor("xo", [D, TOKS], F32, kind="ExternalOutput").ap()

    dbg = {}
    if debug:
        for nm, shape, dt in [("qTh", [128, 16384], BF16),
                              ("kTh0", [128, 8192], BF16),
                              ("vaug0", [128, 4160], BF16),
                              ("ccol", [128, 4], F32),
                              ("Asb", [64, 1024], F32),
                              ("dinv", [1, 512], F32),
                              ("exp0", [128, 2048], BF16)]:
            dbg[nm] = nc.dram_tensor("dbg_" + nm, shape, dt,
                                     kind="ExternalOutput").ap()
    ccq_in = nc.dram_tensor("cc1q_in", [1024, TOKS], BF16).ap()
    ccq_out = nc.dram_tensor("cc1q_out", [1024, TOKS], BF16).ap()
    cck_in = nc.dram_tensor("cc1k_in", [512, TOKS], BF16).ap()
    cck_out = nc.dram_tensor("cc1k_out", [512, TOKS], BF16).ap()
    ccv_in = nc.dram_tensor("cc1v_in", [8 * TOKS, 65], BF16).ap()
    ccv_out = nc.dram_tensor("cc1v_out", [8 * TOKS, 65], BF16).ap()
    cco_in = nc.dram_tensor("cc1o_in", [1024, TOKS], BF16).ap()
    cco_out = nc.dram_tensor("cc1o_out", [1024, TOKS], BF16).ap()

    with contextlib.ExitStack() as st:
        E = st.enter_context
        em = Em(nc, st)
        b = Bufs()
        b.big1 = E(nc.sbuf_tensor("big1", [128, 16384], BF16))
        b.big2 = E(nc.sbuf_tensor("big2", [128, 16384], BF16))
        b.big3 = E(nc.sbuf_tensor("big3", [128, 16384], BF16))
        b.xn = b.big1
        b.qTh = b.big1
        b.qTl = b.big3
        b.kTh = b.big3[:, 0:8192]
        b.vaugb = b.big3[:, 8192:12352]
        
        b.outT = b.big2
        b.aT = b.big2[:, 0:8192]
        b.woutsb = b.big2[:, 8192:16384]
        b.expbuf = b.big2[:, 0:4096]
        big2f = b.big2.bitcast(F32)
        b.m_row = big2f[0:1, 0:2048]
        b.rs_row = big2f[0:1, 2048:4096]
        b.kTloc = E(nc.sbuf_tensor("kTloc1", [128, 4096], BF16))
        b.vaugl = E(nc.sbuf_tensor("vaugl", [128, 16 * 260], BF16))
        b.qsT = E(nc.sbuf_tensor("qsT", [64, 2048], BF16))
        b.kvsT = E(nc.sbuf_tensor("kvsT", [32, 2048], BF16))
        b.swb = E(nc.sbuf_tensor("swb", [128, 2048], BF16))
        b.costab = E(nc.sbuf_tensor("costabF", [128, LOCP], F32))
        b.sintab = E(nc.sbuf_tensor("sintabF", [128, LOCP], F32))
        b.maskm = E(nc.sbuf_tensor("maskm", [128, 896], BF16))
        b.wqa = E(nc.sbuf_tensor("wqasb", [128, 8 * 64], BF16))
        b.wqb = E(nc.sbuf_tensor("wqbsb", [64, D], BF16))
        b.wkva = E(nc.sbuf_tensor("wkvasb", [128, 8 * 32], BF16))
        b.wkvbk = E(nc.sbuf_tensor("wkvbksb", [32, 256], BF16))
        b.wkvbv = E(nc.sbuf_tensor("wkvbvsb", [32, 256], BF16))
        b.boutc = E(nc.sbuf_tensor("boutcsb", [128, 8], F32))
        b.Asb = E(nc.sbuf_tensor("Asb", [64, 1024], F32))
        b.odiv = [E(nc.sbuf_tensor(f"odiv{i}", [64, 512], BF16))
                  for i in range(2)]
        b.dinv = E(nc.sbuf_tensor("dinv", [1, 512], F32))
        b.ccol = E(nc.sbuf_tensor("ccol", [128, 4], F32))
        b.crow = E(nc.sbuf_tensor("crow", [1, 4], F32))
        b.consts = E(nc.sbuf_tensor("consts_sb1", [128, 2], F32))
        b.ones_col = b.consts[:, 0:1]
        b.eps_col = b.consts[:, 1:2]
        b.ones_row = E(nc.sbuf_tensor("ones_row1", [1, 128], F32))
        b.xchunk = [E(nc.sbuf_tensor(f"xchunk1_{i}", [128, TOKS], F32))
                    for i in range(2)]
        b.scratch = E(nc.sbuf_tensor("scratch1", [128, 2560], F32))
        b.xsq = [b.scratch[:, 0:512], b.scratch[:, 512:1024]]
        b.ln_tmp = b.scratch[:, 1024:1536]
        b.xres = [b.scratch[:, 512:768], b.scratch[:, 768:1024]]
        b.ores = [b.scratch[:, 1024:1280], b.scratch[:, 1280:1536]]
        b.ps_a = E(nc.psum_tensor("ps_a1", [128, 2048], F32))
        b.ps_b = E(nc.psum_tensor("ps_b1", [128, 2048], F32))
        block = E(nc.Block())

        emit_attn1(em, b, xt_d, wqa_d, wqb_d, wkva_d, wkvbk_d, wkvbv_d,
                   wout_d, bout_d, cos_d, sin_d, mask_d, vtempl_d, consts_d,
                   onesrow_d, out_d, ccq_in, ccq_out, cck_in, cck_out,
                   ccv_in, ccv_out, cco_in, cco_out, dbg)
        em.run(block)
    return nc


def emit_attn1(em, b, xt_d, wqa_d, wqb_d, wkva_d, wkvbk_d, wkvbv_d,
               wout_d, bout_d, cos_d, sin_d, mask_d, vtempl_d, consts_d,
               onesrow_d, out_d, ccq_in, ccq_out, cck_in, cck_out,
               ccv_in, ccv_out, cco_in, cco_out, dbg=None):
    from concourse.aot_env import allow_ici_all_to_all
    SCL = 0.125
    em._tok_consts = em.dma("pool", [], b.consts[:, :], consts_d)
    tok_ones = em.dma("pool", [], b.ones_row[0:1, :], onesrow_d)
    tok_cos = em.dma("pool", [], b.costab[:, :], cos_d)
    tok_sin = em.dma("pool", [], b.sintab[:, :], sin_d)
    tok_mask = em.dma("pool", [], b.maskm[:, :], mask_d)
    tok_bout = em.dma("pool", [], b.boutc[:, :], bout_d)
    tok_wqa = em.dma("pool", [],
                     b.wqa[:, :].rearrange("p (ic f) -> p ic f", ic=8),
                     wqa_d.rearrange("(ic p) f -> p ic f", p=128))
    tok_wqb = em.dma("pool", [], b.wqb[:, :], wqb_d)
    tok_wkva = em.dma("pool", [],
                      b.wkva[:, :].rearrange("p (ic f) -> p ic f", ic=8),
                      wkva_d.rearrange("(ic p) f -> p ic f", p=128))
    tok_wkvbk = em.dma("pool", [], b.wkvbk[:, :], wkvbk_d)
    tok_wkvbv = em.dma("pool", [], b.wkvbv[:, :], wkvbv_d)
    tok_vt = em.dma("pool", [], ccv_in, vtempl_d)

    tok_xn = emit_ln(em, b, TOKS, xt_d, b.xn, 0,
                     extra_waits=(tok_ones, em._tok_consts))
    tok_xn = em.fence([tok_xn], b.scratch[0:1, 0:1], b.consts[0:1, 0:1])

    psq = [b.ps_a[:, i * 512:(i + 1) * 512] for i in range(4)] + \
          [b.ps_b[:, i * 512:(i + 1) * 512] for i in range(4)]

    def fqs(t):
        for cg in range(4):
            for ic in range(8):
                ins = t.matmul(b.ps_a[0:64, cg * 512:(cg + 1) * 512],
                               b.wqa[:, ic * 64:(ic + 1) * 64],
                               b.xn[:, ic * TOKS + cg * 512:
                                    ic * TOKS + cg * 512 + 512],
                               start=(ic == 0), stop=(ic == 7))
        return ins
    tok_qs_mm = em.push("pe", [tok_wqa, tok_xn], fqs, inc="pe")
    tok_qs = em.push("act", [tok_qs_mm],
                     lambda s: s.copy(b.qsT[:, :], b.ps_a[0:64, 0:2048]),
                     inc="act")

    def fkvs(t):
        for cg in range(4):
            for ic in range(8):
                ins = t.matmul(b.ps_b[0:32, cg * 512:(cg + 1) * 512],
                               b.wkva[:, ic * 32:(ic + 1) * 32],
                               b.xn[:, ic * TOKS + cg * 512:
                                    ic * TOKS + cg * 512 + 512],
                               start=(ic == 0), stop=(ic == 7))
        return ins
    tok_kvs_mm = em.push("pe", [tok_wkva, tok_xn], fkvs, inc="pe")
    tok_kvs = em.push("act", [tok_kvs_mm],
                      lambda s: s.copy(b.kvsT[:, :], b.ps_b[0:32, 0:2048]),
                      inc="act")
    tok_sfence = em.push("act", [tok_qs, tok_kvs],
                         lambda s: s.copy(b.crow[0:1, 3:4], b.consts[0:1, 0:1]),
                         inc="act")

    ps_free = [tok_qs_mm] * 8
    qctr = 0
    tok_qtl = None
    for et in range(8):
        for cg in range(4):
            q = qctr % 8
            qctr += 1
            mm = em.push("pe", [tok_wqb, tok_sfence, ps_free[q]],
                         lambda t, et=et, cg=cg, q=q: t.matmul(
                             psq[q], b.wqb[:, et * 128:(et + 1) * 128],
                             b.qsT[0:64, cg * 512:(cg + 1) * 512],
                             start=True, stop=True), inc="pe")
            tok_qtl = em.push("act", [mm],
                              lambda s, et=et, cg=cg, q=q: s.copy(
                                  b.qTl[:, et * 2048 + cg * 512:
                                        et * 2048 + cg * 512 + 512], psq[q]),
                              inc="act")
            ps_free[q] = tok_qtl
    tok_ktl = None
    for kt in range(2):
        for cg in range(4):
            q = qctr % 8
            qctr += 1
            mm = em.push("pe", [tok_wkvbk, tok_sfence, ps_free[q]],
                         lambda t, kt=kt, cg=cg, q=q: t.matmul(
                             psq[q], b.wkvbk[:, kt * 128:(kt + 1) * 128],
                             b.kvsT[0:32, cg * 512:(cg + 1) * 512],
                             start=True, stop=True), inc="pe")
            tok_ktl = em.push("act", [mm],
                              lambda s, kt=kt, cg=cg, q=q: s.copy(
                                  b.kTloc[:, kt * 2048 + cg * 512:
                                          kt * 2048 + cg * 512 + 512], psq[q]),
                              inc="act")
            ps_free[q] = tok_ktl
    tok_vl = None
    for tt in range(16):
        q = qctr % 8
        qctr += 1
        mm = em.push("pe", [tok_wkvbv, tok_sfence, ps_free[q]],
                     lambda t, tt=tt, q=q: t.matmul(
                         psq[q][:, 0:256], b.kvsT[0:32, tt * 128:(tt + 1) * 128],
                         b.wkvbv[:, :], start=True, stop=True), inc="pe")
        vsl = b.vaugl[:, tt * 260:(tt + 1) * 260].rearrange(
            "p (kh x) -> p kh x", kh=4)[:, :, 0:64]
        tok_vl = em.push("act", [mm],
                         lambda s, q=q, vsl=vsl: s.copy(
                             vsl, psq[q][:, 0:256].rearrange(
                                 "p (kh x) -> p kh x", kh=4)),
                         inc="act")
        ps_free[q] = tok_vl

    tok_rope = None
    tiles = [("q", et) for et in range(8)] + [("k", kt) for kt in range(2)]
    for kind, ti in tiles:
        srcb = b.qTl if kind == "q" else b.kTloc
        c0 = ti * 2048
        dep = tok_qtl if kind == "q" else tok_ktl
        sw_toks = []
        for (d0, s0) in ((0, 32), (32, 0), (64, 96), (96, 64)):
            sw_toks.append(em.dma(
                "pool", [dep, tok_rope],
                b.swb[d0:d0 + 32, :], srcb[s0:s0 + 32, c0:c0 + 2048]))
        for bb in range(2):
            sl = slice(c0 + bb * LOCP, c0 + bb * LOCP + LOCP)
            swsl = slice(bb * LOCP, bb * LOCP + LOCP)
            em.push("dve", [dep, *sw_toks, tok_cos, tok_sin],
                    lambda v, srcb=srcb, sl=sl: v.tensor_tensor(
                        b.scratch[:, 0:1024], srcb[:, sl], b.costab[:, :],
                        ALU.mult))
            em.push("dve", [],
                    lambda v, swsl=swsl: v.tensor_tensor(
                        b.scratch[:, 1024:2048], b.swb[:, swsl],
                        b.sintab[:, :], ALU.mult))
            tok_rope = em.push("dve", [],
                               lambda v, srcb=srcb, sl=sl: v.tensor_tensor(
                                   srcb[:, sl], b.scratch[:, 0:1024],
                                   b.scratch[:, 1024:2048], ALU.add),
                               inc="dve")

    ctoks = []
    for j in range(8):
        ctoks.append(em.dma("sync", [tok_rope],
                            ccq_in[j * 128:(j + 1) * 128, :],
                            b.qTl[:, j * 2048:(j + 1) * 2048]))
    for j in range(8):
        kh = j // 2
        kt, half = kh // 2, kh % 2
        ctoks.append(em.dma("sync", [tok_rope],
                            cck_in[j * 64:(j + 1) * 64, :],
                            b.kTloc[half * 64:(half + 1) * 64,
                                    kt * 2048:(kt + 1) * 2048]))
    for j in range(8):
        kh = j // 2
        vsl = b.vaugl[:, :].rearrange("p (tt r) -> p tt r", tt=16)[
            :, :, kh * 65:kh * 65 + 64]
        dsl = ccv_in[j * 2048:(j + 1) * 2048, 0:64].rearrange(
            "(tt p) r -> p tt r", p=128)
        ctoks.append(em.dma("sync", [tok_vl, tok_vt], dsl, vsl))

    def a2a(g, cin, cout):
        with allow_ici_all_to_all():
            return g.collective_compute(
                "AllToAll", ALU.bypass, replica_groups=[list(range(8))],
                ins=[cin.opt()], outs=[cout.opt()])
    tok_ccq = em.push("pool", ctoks, lambda g: a2a(g, ccq_in, ccq_out),
                      inc=em.next_cc())
    tok_cck = em.push("pool", [], lambda g: a2a(g, cck_in, cck_out),
                      inc=em.next_cc())
    tok_ccv = em.push("pool", [], lambda g: a2a(g, ccv_in, ccv_out),
                      inc=em.next_cc())

    tok_qth = None
    for bb in range(2):
        for s in range(8):
            tok_qth = em.dma("sync", [tok_ccq],
                             b.qTh[:, bb * 8192 + s * 1024:
                                   bb * 8192 + s * 1024 + 1024],
                             ccq_out[s * 128:(s + 1) * 128,
                                     bb * 1024:(bb + 1) * 1024])
    tok_out = None
    pvps = [b.ps_b[:, 0:512], b.ps_b[:, 512:1024]]
    dbps = [b.ps_b[:, 1024:1536], b.ps_b[:, 1024:1536]]
    junkps = b.ps_b[:, 1536:2048]
    scpair = [b.ps_a[:, 0:1024], b.ps_a[:, 1024:2048]]

    def _later(a, b_):
        if a is None:
            return b_
        if b_ is None:
            return a
        assert a[0] == b_[0]
        return a if a[1] >= b_[1] else b_
    sc_free = [_later(ps_free[0], ps_free[1]), _later(ps_free[2], ps_free[3])]
    pv_free = [ps_free[4], ps_free[5]]
    db_free = [ps_free[6], ps_free[7]]
    exp_free = [None] * 4
    tok_vaugb = None
    pvctr = 0
    otoks = []
    ostore = [None, None]
    ca_war = [None, None]
    tok_kth = None
    tok_kth_war = None
    for bb in range(2):
        for half in range(2):
            for s in range(8):
                tok_kth = em.dma("sync", [tok_cck, tok_kth_war],
                                 b.kTh[half * 64:(half + 1) * 64,
                                       s * 1024:(s + 1) * 1024],
                                 cck_out[s * 64:(s + 1) * 64,
                                         bb * 1024:(bb + 1) * 1024])
        em.push("dve", [tok_qth, tok_kth],
                lambda v, bb=bb: v.tensor_reduce(
                    b.ccol[:, 2:3], b.qTh[:, bb * 8192:(bb + 1) * 8192],
                    AX.X, ALU.max, apply_absolute_value=True))
        em.push("dve", [],
                lambda v: v.tensor_tensor(b.ccol[:, 2:3], b.ccol[:, 2:3],
                                          b.ccol[:, 2:3], ALU.mult))
        em.push("dve", [],
                lambda v: v.tensor_reduce(
                    b.ccol[0:64, 3:4], b.kTh[0:64, :],
                    AX.X, ALU.max, apply_absolute_value=True))
        tok_bnd = em.push("dve", [],
                          lambda v: v.tensor_tensor(
                              b.ccol[0:64, 3:4], b.ccol[0:64, 3:4],
                              b.ccol[0:64, 3:4], ALU.mult), inc="dve")
        mm1 = em.push("pe", [tok_bnd, ps_free[2], sc_free[1]],
                      lambda t: t.matmul(b.ps_a[0:1, 1024:1025],
                                         b.ones_col, b.ccol[:, 2:3],
                                         start=True, stop=True), inc="pe")
        mm2 = em.push("pe", [],
                      lambda t: t.matmul(b.ps_a[0:1, 1026:1027],
                                         b.ones_col[0:64, :], b.ccol[0:64, 3:4],
                                         start=True, stop=True), inc="pe")
        em.push("act", [mm1, mm2],
                lambda s: s.activation(b.crow[0:1, 0:1], b.ps_a[0:1, 1024:1025],
                                       AF.Sqrt))
        tok_c1 = em.push("act", [],
                         lambda s: s.activation(b.crow[0:1, 1:2],
                                                b.ps_a[0:1, 1026:1027],
                                                AF.Sqrt), inc="act")
        em.push("dve", [tok_c1],
                lambda v: v.tensor_tensor(b.crow[0:1, 0:1], b.crow[0:1, 0:1],
                                          b.crow[0:1, 1:2], ALU.mult))
        tok_c2 = em.push("dve", [],
                         lambda v: v.tensor_scalar(b.crow[0:1, 0:1],
                                                   b.crow[0:1, 0:1],
                                                   -SCL, None, ALU.mult),
                         inc="dve")
        mm3 = em.push("pe", [tok_c2, tok_ones],
                      lambda t: t.matmul(b.ps_a[0:128, 1028:1029],
                                         b.ones_row[0:1, :], b.crow[0:1, 0:1],
                                         start=True, stop=True), inc="pe")
        tok_negc = em.push("dve", [mm3],
                           lambda v, bb=bb: v.tensor_copy(
                               b.ccol[:, bb:bb + 1], b.ps_a[:, 1028:1029]),
                           inc="dve")
        va_toks = []
        for s in range(8):
            va_toks.append(em.dma(
                "sync", [tok_ccv, tok_vaugb],
                b.vaugb.rearrange("p (jt r) -> p jt r", jt=64)[:, s * 8:
                                                              (s + 1) * 8, :],
                ccv_out[s * 2048 + bb * 1024:s * 2048 + bb * 1024 + 1024,
                        :].rearrange("(jt p) r -> p jt r", p=128)))
        tok_va = va_toks[-1]
        for iblk in range(16):
            for hl in range(2):
                pvp = pvps[pvctr % 2]
                njt = 4 * (iblk + 1)
                npair = njt // 2
                ptoks = {}
                pvtok = None
                for jp in range(npair + 2):
                    if jp < npair:
                        pr = jp % 2
                        scp = scpair[pr]
                        mm = None
                        for half in range(2):
                            jt = 2 * jp + half
                            mm = em.push(
                                "pe", [tok_qth, tok_kth, sc_free[pr],
                                       tok_negc],
                                lambda t, jt=jt, bb=bb, hl=hl, iblk=iblk,
                                scp=scp, half=half: t.matmul(
                                    scp[:, half * 512:(half + 1) * 512],
                                    b.kTh[hl * 64:(hl + 1) * 64,
                                          jt * 128:(jt + 1) * 128],
                                    b.qTh[hl * 64:(hl + 1) * 64,
                                          bb * 8192 + iblk * 512:
                                          bb * 8192 + iblk * 512 + 512],
                                    start=True, stop=True), inc="pe")
                        eb = jp % 4
                        etok = em.push(
                            "act", [mm, tok_negc, exp_free[eb]],
                            lambda s, eb=eb, scp=scp, bb=bb: s.activation(
                                b.expbuf[:, eb * 1024:(eb + 1) * 1024], scp,
                                AF.Exp, bias=b.ccol[:, bb:bb + 1], scale=SCL),
                            inc="act")
                        sc_free[pr] = etok
                        mtoks = [etok, etok]
                        for half in range(2):
                            jt = 2 * jp + half
                            if jt >= 4 * iblk:
                                m0 = 384 - 128 * (jt - 4 * iblk)
                                col = eb * 1024 + half * 512
                                mtoks[half] = em.push(
                                    "dve", [etok, tok_mask],
                                    lambda v, col=col, m0=m0: v.tensor_tensor(
                                        b.expbuf[:, col:col + 512],
                                        b.expbuf[:, col:col + 512],
                                        b.maskm[:, m0:m0 + 512], ALU.mult),
                                    inc="dve")
                        ptoks[jp] = mtoks
                    jq = jp - 2
                    if jq >= 0:
                        ebq = jq % 4
                        mtoks_q = ptoks.pop(jq)
                        for half in range(2):
                            jt = 2 * jq + half
                            col = ebq * 1024 + half * 512
                            pvtok = em.push(
                                "pe", [mtoks_q[half], tok_va,
                                       pv_free[pvctr % 2]],
                                lambda t, jt=jt, col=col, pvp=pvp, njt=njt:
                                t.matmul(
                                    pvp[0:65, :],
                                    b.vaugb[:, jt * 65:jt * 65 + 65],
                                    b.expbuf[:, col:col + 512],
                                    start=(jt == 0), stop=(jt == njt - 1)),
                                inc="pe")
                        exp_free[ebq] = pvtok
                        em.push("pe", [],
                                lambda t: t.matmul(
                                    junkps[:, 0:512], b.maskm[:, 0:128],
                                    b.maskm[:, 0:512], start=True, stop=True))
                slot = pvctr % 2
                asl = slice(slot * 512, slot * 512 + 512)
                dtok = em.push("dve", [pvtok, db_free[slot]],
                               lambda v, pvp=pvp: v.reciprocal(
                                   b.dinv[0:1, :], pvp[64:65, :]), inc="dve")
                catok = em.push("dve", [pvtok, ca_war[slot]],
                                lambda v, pvp=pvp, asl=asl: v.tensor_copy(
                                    b.Asb[:, asl], pvp[0:64, :]),
                                inc="dve")
                pv_free[slot] = catok
                dbp = dbps[slot]
                mmD = em.push("pe", [dtok, db_free[slot]],
                              lambda t, dbp=dbp: t.matmul(
                                  dbp[0:64, :], b.ones_row[0:1, 0:64],
                                  b.dinv[0:1, :], start=True, stop=True),
                              inc="pe")
                divtok = em.push(
                    "dve", [mmD, catok],
                    lambda v, slot=slot, asl=asl, dbp=dbp: v.tensor_tensor(
                        b.odiv[slot][:, :], b.Asb[:, asl], dbp[0:64, :], ALU.mult),
                    inc="dve")
                db_free[slot] = divtok
                ca_war[slot] = divtok
                jdst = iblk // 2
                tok_out = em.dma(
                    "sync", [divtok, ostore[slot]],
                    cco_in[jdst * 128 + hl * 64:jdst * 128 + hl * 64 + 64,
                           bb * 1024 + (iblk % 2) * 512:
                           bb * 1024 + (iblk % 2) * 512 + 512],
                    b.odiv[slot][:, :])
                ostore[slot] = tok_out
                otoks.append(tok_out)
                if dbg and bb == 0 and hl == 0 and iblk == 0:
                    d1 = em.dma("sync", [divtok], dbg["Asb"], b.Asb[:, :])
                    d2 = em.dma("sync", [divtok], dbg["dinv"],
                                b.dinv[0:1, :])
                    d3 = em.dma("sync", [divtok], dbg["exp0"],
                                b.expbuf[:, :])
                    d4 = em.dma("sync", [divtok], dbg["qTh"], b.qTh[:, :])
                    d5 = em.dma("sync", [divtok], dbg["kTh0"],
                                b.kTh[:, :])
                    d6 = em.dma("sync", [divtok], dbg["vaug0"],
                                b.vaugb[:, :])
                    d7 = em.dma("sync", [divtok], dbg["ccol"], b.ccol[:, :])
                    ostore[slot] = d7
                pvctr += 1
        tok_vaugb = tok_out
        tok_kth_war = em.cnt_tok("pe")

    tok_cco = em.push("pool", otoks, lambda g: a2a(g, cco_in, cco_out),
                      inc=em.next_cc())

    tok_wout = em.dma("pool", [tok_cco],
                      b.woutsb.rearrange("p (ec dd) -> p ec dd", ec=8),
                      wout_d.rearrange("(ec p) dd -> p ec dd", p=128))
    at_toks = {}
    store_toks = []
    tok_st = [None, None]
    ow_free = [None, None]
    for bb in range(2):
        tok_at = None
        for ec in range(8):
            tok_at = em.dma("sync", [tok_cco, at_toks.get("war")],
                            b.aT[:, ec * 1024:(ec + 1) * 1024],
                            cco_out[ec * 128:(ec + 1) * 128,
                                    bb * 1024:(bb + 1) * 1024])
        at_toks[bb] = tok_at
        for ic in range(4):
            for dt in range(8):
                slot = dt % 2
                psow = psq[4 + slot]
                rsl = (slice(dt * 128, (dt + 1) * 128),
                       slice(bb * 1024 + ic * 256, bb * 1024 + ic * 256 + 256))
                xr = em.dma("sync", [tok_st[slot]], b.xres[slot],
                            xt_d[rsl[0], rsl[1]])

                def fo(t, dt=dt, ic=ic, psow=psow):
                    for ec in range(8):
                        ins = t.matmul(psow[:, 0:256],
                                       b.woutsb[:, ec * 1024 + dt * 128:
                                                ec * 1024 + dt * 128 + 128],
                                       b.aT[:, ec * 1024 + ic * 256:
                                            ec * 1024 + ic * 256 + 256],
                                       start=(ec == 0), stop=(ec == 7))
                    return ins
                mmo = em.push("pe", [tok_at, tok_wout, ow_free[slot]], fo,
                              inc="pe")
                sttok = em.push(
                    "dve", [mmo, xr, tok_bout,
                            store_toks[-2] if len(store_toks) >= 2 else None],
                    lambda v, slot=slot, psow=psow, dt=dt: v.scalar_tensor_tensor(
                        b.ores[slot], psow[:, 0:256], b.boutc[:, dt:dt + 1],
                        b.xres[slot], ALU.add, ALU.add),
                    inc="dve")
                tok_st[slot] = sttok
                ow_free[slot] = sttok
                store_toks.append(em.dma("sync", [sttok],
                                         out_d[rsl[0], rsl[1]],
                                         b.ores[slot]))
        at_toks["war"] = em.cnt_tok("pe")

    em.push("sync", [store_toks[-1]], lambda sy: sy.nop())



class Pool:
    def __init__(self, ap_f32):
        self.ap = ap_f32
        self.off = 0

    def reset(self):
        self.off = 0

    def f32(self, cols):
        s = self.ap[:, self.off:self.off + cols]
        self.off += cols
        return s

    def bf16(self, cols):
        need = (cols + 1) // 2
        return self.f32(need).bitcast(BF16)[:, 0:cols]


def mk_ffn_bufs(p, shared):
    b = Bufs()
    b.__dict__.update(shared)
    b.c32 = p.f32(8 * LOCP)
    b.c_bf = p.bf16(8 * LOCP)
    b.xn = p.bf16(8 * TOKS)
    b.gate = [p.bf16(8 * 512) for _ in range(3)]
    b.wbuf = [p.bf16(8 * D) for _ in range(4)]
    b.xchunk = [b.c32[:, 0:TOKS], b.c32[:, TOKS:2 * TOKS]]
    b.scratch = p.f32(2560)
    b.xsq = [b.scratch[:, 0:512], b.scratch[:, 512:1024]]
    b.ln_tmp = b.scratch[:, 1024:1536]
    b.t1 = b.scratch[:, 0:512]
    b.t2 = b.scratch[:, 512:1024]
    b.hbuf = [b.scratch[:, 1024:1280].bitcast(BF16),
              b.scratch[:, 1280:1536].bitcast(BF16)]
    b.xres = [b.scratch[:, 1536:2048], b.scratch[:, 2048:2560]]
    b.ores = [b.scratch[:, 0:512], b.scratch[:, 512:1024]]
    b.biases = p.f32(40)
    b.m_row = p.f32(2048)[0:1, :]
    b.rs_row = p.f32(2048)[0:1, :]
    return b


def mk_attn0_bufs(p, shared):
    b = Bufs()
    b.__dict__.update(shared)
    b.bigbuf = p.bf16(16384)
    b.y = p.bf16(16384)
    b.wpool = p.bf16(16384)
    b.kT = p.bf16(8192)
    b.qT = p.bf16(2048)
    b.kTloc = p.bf16(2048)
    b.vsb = [p.bf16(2048) for _ in range(2)]
    b.usb = [p.bf16(2048) for _ in range(2)]
    b.tileAB = p.bf16(4096)
    b.utile = b.tileAB[:, 0:2048]
    b.vtile = b.tileAB[:, 2048:4096]
    b.Bsb = b.tileAB
    b.BTsb = p.bf16(4096)
    b.basebuf = p.f32(2048)
    b.vsb += [b.basebuf.bitcast(BF16)[:, 0:2048],
              b.basebuf.bitcast(BF16)[:, 2048:4096]]
    b.uvwb = p.bf16(8 * 128)
    b.ident = p.bf16(128)
    b.diag = p.bf16(8 * 128)
    b.corner = [p.f32(256) for _ in range(2)]
    b.costab = p.f32(LOCP)
    b.sintab = p.f32(LOCP)
    b.gb = p.f32(4)
    b.rsc = p.f32(8)
    b.smalls = p.f32(4)
    b.gsb = p.f32(2)[0:1, :]
    b.xchunk = [p.f32(TOKS) for _ in range(2)]
    b.scratch = p.f32(2560)
    b.xsq = [b.scratch[:, 0:512], b.scratch[:, 512:1024]]
    b.ln_tmp = b.scratch[:, 1024:1536]
    b.ropet = [b.scratch[:, 0:1024], b.scratch[:, 1024:2048]]
    b.rtile = b.scratch[:, 0:512]
    b.xres = [b.scratch[:, 512:768], b.scratch[:, 768:1024]]
    b.ores = [b.scratch[:, 1024:1280], b.scratch[:, 1280:1536]]
    ktf = b.kT.bitcast(F32)
    b.m_row = ktf[0:1, 0:2048]
    b.rs_row = ktf[0:1, 2048:4096]
    b.v_row = b.BTsb.bitcast(F32)[0:1, 0:2048]
    b.t_row = b.v_row
    b.rtile4 = b.xchunk[0]
    return b


def mk_attn1_bufs(p, shared):
    b = Bufs()
    b.__dict__.update(shared)
    b.big1 = p.bf16(16384)
    b.big2 = p.bf16(16384)
    b.big3 = p.bf16(16384)
    b.xn = b.big1
    b.qTh = b.big1
    b.qTl = b.big3
    b.kTh = b.big3[:, 0:8192]
    b.vaugb = b.big3[:, 8192:12352]
    b.outT = b.big2
    b.aT = b.big2[:, 0:8192]
    b.woutsb = b.big2[:, 8192:16384]
    b.expbuf = b.big2[:, 0:4096]
    big2f = b.big2.bitcast(F32)
    b.m_row = big2f[0:1, 0:2048]
    b.rs_row = big2f[0:1, 2048:4096]
    b.kTloc = p.bf16(4096)
    b.vaugl = p.bf16(16 * 260)
    b.qsT = p.bf16(2048)[0:64, :]
    b.kvsT = p.bf16(2048)[0:32, :]
    b.swb = p.bf16(2048)
    b.costab = p.f32(LOCP)
    b.sintab = p.f32(LOCP)
    b.maskm = p.bf16(896)
    b.wqa = p.bf16(8 * 64)
    b.wqb = p.bf16(D)[0:64, :]
    b.wkva = p.bf16(8 * 32)
    b.wkvbk = p.bf16(256)[0:32, :]
    b.wkvbv = p.bf16(256)[0:32, :]
    b.boutc = p.f32(8)
    b.Asb = p.f32(1024)[0:64, :]
    b.odiv = [p.bf16(512)[0:64, :] for _ in range(2)]
    b.dinv = p.f32(512)[0:1, :]
    b.ccol = p.f32(4)
    b.crow = p.f32(4)[0:1, :]
    b.xchunk = [p.f32(TOKS) for _ in range(2)]
    b.scratch = p.f32(2560)
    b.xsq = [b.scratch[:, 0:512], b.scratch[:, 512:1024]]
    b.ln_tmp = b.scratch[:, 1024:1536]
    b.xres = [b.scratch[:, 512:768], b.scratch[:, 768:1024]]
    b.ores = [b.scratch[:, 1024:1280], b.scratch[:, 1280:1536]]
    return b


def build_fused():
    nc = bass.Bass("TRN2", target_bir_lowering=False)
    xt_d = nc.dram_tensor("xt", [D, TOKS], F32, kind="ExternalInput").ap()
    out_d = nc.dram_tensor("xo", [D, TOKS], F32, kind="ExternalOutput").ap()
    xs = [nc.dram_tensor(f"xs{i}", [D, TOKS], F32).ap() for i in range(5)]

    wd = {}
    wdb = {}
    for L in (0, 1):
        for n in ("wf", "uf", "wi", "ui", "wc", "uc", "wo", "uo"):
            wd[(L, n)] = nc.dram_tensor(f"l{L}_{n}T", [D, D], F32,
                                        kind="ExternalInput").ap()
            wdb[(L, n)] = nc.dram_tensor(f"l{L}_{n}b", [D, D], BF16).ap()
    bias_d = {L: nc.dram_tensor(f"l{L}_biases", [128, 40], F32,
                                kind="ExternalInput").ap() for L in (0, 1)}
    uvw_d = nc.dram_tensor("uvwT", [D, 4224], F32, kind="ExternalInput").ap()
    ow_d = nc.dram_tensor("owT", [2048, D], F32, kind="ExternalInput").ap()
    gb_d = nc.dram_tensor("gbcols", [128, 4], F32, kind="ExternalInput").ap()
    g_d = nc.dram_tensor("gval", [1, 1], F32, kind="ExternalInput").ap()
    w2046_d = nc.dram_tensor("w2046", [1, 1], F32, kind="ExternalInput").ap()
    corner_d = nc.dram_tensor("corner", [64 * 128, 256], F32,
                              kind="ExternalInput").ap()
    rsc_d = nc.dram_tensor("rsc", [128, 8], F32, kind="ExternalInput").ap()
    cos0_d = nc.dram_tensor("cosT", [128, LOCP], F32, kind="ExternalInput").ap()
    sin0_d = nc.dram_tensor("sinT", [128, LOCP], F32, kind="ExternalInput").ap()
    ident_d = nc.dram_tensor("identb", [128, 128], F32, kind="ExternalInput").ap()
    wqa_d = nc.dram_tensor("wqa", [D, 64], F32, kind="ExternalInput").ap()
    wqb_d = nc.dram_tensor("wqbp", [64, D], F32, kind="ExternalInput").ap()
    wkva_d = nc.dram_tensor("wkva", [D, 32], F32, kind="ExternalInput").ap()
    wkvbk_d = nc.dram_tensor("wkvbk", [32, 256], F32, kind="ExternalInput").ap()
    wkvbv_d = nc.dram_tensor("wkvbv", [32, 256], F32, kind="ExternalInput").ap()
    wout_d = nc.dram_tensor("wout", [D, D], F32, kind="ExternalInput").ap()
    bout_d = nc.dram_tensor("boutc", [128, 8], F32, kind="ExternalInput").ap()
    cos1_d = nc.dram_tensor("cosF", [128, LOCP], F32, kind="ExternalInput").ap()
    sin1_d = nc.dram_tensor("sinF", [128, LOCP], F32, kind="ExternalInput").ap()
    mask_d = nc.dram_tensor("maskM", [128, 896], F32, kind="ExternalInput").ap()
    vtempl_d = nc.dram_tensor("vtempl", [8 * TOKS, 65], F32,
                              kind="ExternalInput").ap()
    consts_d = nc.dram_tensor("consts", [128, 2], F32, kind="ExternalInput").ap()
    onesrow_d = nc.dram_tensor("onesrow", [1, 128], F32,
                               kind="ExternalInput").ap()
    cc_in = nc.dram_tensor("cc0_in", [2176, 2048], BF16).ap()
    cc_out = nc.dram_tensor("cc0_out", [8 * 2176, 2048], BF16).ap()
    cck0_in = nc.dram_tensor("cck0_in", [128, 2048], BF16).ap()
    cck0_out = nc.dram_tensor("cck0_out", [8 * 128, 2048], BF16).ap()
    ccv0_out = nc.dram_tensor("ccv0_out", [8 * 1024, 2048], BF16).ap()
    ccv1_out = nc.dram_tensor("ccv1_out", [8 * 1024, 2048], BF16).ap()
    u_spill = nc.dram_tensor("u_spill", [2048, 2048], BF16).ap()
    ccq_in = nc.dram_tensor("cc1q_in", [1024, TOKS], BF16).ap()
    ccq_out = nc.dram_tensor("cc1q_out", [1024, TOKS], BF16).ap()
    cck_in = nc.dram_tensor("cc1k_in", [512, TOKS], BF16).ap()
    cck_out = nc.dram_tensor("cc1k_out", [512, TOKS], BF16).ap()
    ccv_in = nc.dram_tensor("cc1v_in", [8 * TOKS, 65], BF16).ap()
    ccv_out = nc.dram_tensor("cc1v_out", [8 * TOKS, 65], BF16).ap()
    cco_in = nc.dram_tensor("cc1o_in", [1024, TOKS], BF16).ap()
    cco_out = nc.dram_tensor("cc1o_out", [1024, TOKS], BF16).ap()

    with contextlib.ExitStack() as st:
        E = st.enter_context
        em = Em(nc, st)
        shared = {}
        consts_sb = E(nc.sbuf_tensor("consts_sb", [128, 2], F32))
        onesrow_sb = E(nc.sbuf_tensor("ones_row", [1, 128], F32))
        tiny = E(nc.sbuf_tensor("tiny", [1, 4], F32))
        shared["consts"] = consts_sb[:, :]
        shared["ones_col"] = consts_sb[:, 0:1]
        shared["eps_col"] = consts_sb[:, 1:2]
        shared["ones_row"] = onesrow_sb[:, :]
        pool_t = E(nc.sbuf_tensor("pool", [128, 51456], F32))
        p = Pool(pool_t[:, :])
        shared["ps_a"] = E(nc.psum_tensor("ps_a", [128, 2048], F32))[:, :]
        shared["ps_b"] = E(nc.psum_tensor("ps_b", [128, 2048], F32))[:, :]
        block = E(nc.Block())

        em._tok_consts = em.dma("pool", [], shared["consts"], consts_d)
        em.dma("pool", [], shared["ones_row"], onesrow_d)
        wready = {}
        for L in (0, 1):
            for n in ("wf", "uf", "wi", "ui", "wc", "uc", "wo", "uo"):
                wready[L] = em.dma("pool", [], wdb[(L, n)], wd[(L, n)])

        def bar():
            em.barrier(tiny[0:1, 0:1], shared["consts"][0:1, 0:1])

        stages = [
            ("ffn", 0, xt_d, xs[0]),
            ("attn0", None, xs[0], xs[1]),
            ("ffn", 0, xs[1], xs[2]),
            ("ffn", 1, xs[2], xs[3]),
            ("attn1", None, xs[3], xs[4]),
            ("ffn", 1, xs[4], out_d),
        ]
        for kind, L, xin, xout in stages:
            p.reset()
            if kind == "ffn":
                b = mk_ffn_bufs(p, shared)
                w_d = {n: wdb[(L, n)] for n in
                       ("wf", "uf", "wi", "ui", "wc", "uc", "wo", "uo")}
                em._w_ready = wready[L]
                emit_ffn_body(em, b, xin, w_d, bias_d[L], xout,
                              consts_d=consts_d, onesrow_d=onesrow_d)
                em._w_ready = None
            elif kind == "attn0":
                b = mk_attn0_bufs(p, shared)
                emit_attn0(em, b, xin, uvw_d, ow_d, gb_d, g_d, w2046_d,
                           corner_d, rsc_d, cos0_d, sin0_d, ident_d,
                           consts_d, onesrow_d, xout, cc_in, cc_out, u_spill,
                           cck0=(cck0_in, cck0_out), ccv0_out=ccv0_out,
                           ccv1_out=ccv1_out)
            else:
                b = mk_attn1_bufs(p, shared)
                emit_attn1(em, b, xin, wqa_d, wqb_d, wkva_d, wkvbk_d,
                           wkvbv_d, wout_d, bout_d, cos1_d, sin1_d, mask_d,
                           vtempl_d, consts_d, onesrow_d, xout,
                           ccq_in, ccq_out, cck_in, cck_out,
                           ccv_in, ccv_out, cco_in, cco_out)
            bar()
        em.run(block)
    return nc


def run_fused(inputs):
    nc = _CACHE.get("nc_fused")
    if nc is None:
        nc = _CACHE["nc_fused"] = build_fused()
    x = np.ascontiguousarray(inputs["L0_attn_x"], dtype=np.float32)
    shards = shard_xT(x)
    consts = np.stack([np.ones(128, np.float32),
                       np.full(128, EPS, np.float32)], axis=1)
    onesrow = np.ones((1, 128), np.float32)
    w0, bias0 = _ffn_inputs(inputs, 0)
    w1, bias1 = _ffn_inputs(inputs, 1)
    a0 = _attn0_inputs(inputs, 0)
    a1 = _attn1_inputs(inputs)
    in_maps = []
    for c in range(NCORE):
        m = dict(xt=shards[c], consts=consts, onesrow=onesrow,
                 l0_biases=bias0, l1_biases=bias1,
                 uvwT=a0["uvwT"], owT=a0["owT"], gbcols=a0["gbcols"],
                 gval=a0["gval"], w2046=a0["w2046"], corner=a0["corners"][c],
                 rsc=a0["rsc"], cosT=a0["cosT"][c], sinT=a0["sinT"][c],
                 identb=a0["identb"],
                 wqa=a1["wqa"], wqbp=a1["wqbp"], wkva=a1["wkva"],
                 wkvbk=a1["wkvbk"], wkvbv=a1["wkvbv"], wout=a1["wout"],
                 boutc=a1["boutc"], cosF=a1["cosF"][c], sinF=a1["sinF"][c],
                 maskM=a1["maskM"], vtempl=a1["vtempl"])
        for n in ("wf", "uf", "wi", "ui", "wc", "uc", "wo", "uo"):
            m[f"l0_{n}T"] = w0[n + "T"]
            m[f"l1_{n}T"] = w1[n + "T"]
        in_maps.append(m)
    res = run_bass_kernel_spmd(nc, in_maps, core_ids=list(range(NCORE)))
    return unshard_xT([r["xo"] for r in res.results])


_CACHE = {}


def _ffn_inputs(inputs, L):
    key = "ffn_w_" + str(L)
    if key not in _CACHE:
        p = f"L{L}_ffn_"
        w = {n + "T": transpose_w(inputs[p + n])
             for n in ("wf", "uf", "wi", "ui", "wc", "uc", "wo", "uo")}
        bias = np.concatenate([colpack(inputs[p + n])
                               for n in ("bf", "bi", "bc", "bo", "in_c")], axis=1)
        _CACHE[key] = (w, bias)
    return _CACHE[key]


def run_ffn(x, inputs, L):
    nc_ffn = _CACHE.get("nc_ffn")
    if nc_ffn is None:
        nc_ffn = _CACHE["nc_ffn"] = build_ffn()
    shards = shard_xT(x)
    w, bias = _ffn_inputs(inputs, L)
    consts = np.stack([np.ones(128, np.float32),
                       np.full(128, EPS, np.float32)], axis=1)
    onesrow = np.ones((1, 128), np.float32)
    in_maps = [dict(xt=shards[c], biases=bias, consts=consts,
                    onesrow=onesrow, **w) for c in range(NCORE)]
    res = run_bass_kernel_spmd(nc_ffn, in_maps, core_ids=list(range(NCORE)))
    return unshard_xT([r["xo"] for r in res.results])


def _attn0_inputs(inputs, core):
    key = "attn0_w"
    if key not in _CACHE:
        uvwT = transpose_w(inputs["L0_attn_uv_w"])
        owT = transpose_w(inputs["L0_attn_o_w"])
        gamma = np.asarray(inputs["L0_attn_gamma"], dtype=np.float32)
        beta = np.asarray(inputs["L0_attn_beta"], dtype=np.float32)
        gb = np.ascontiguousarray(
            np.stack([gamma[0], beta[0], gamma[1], beta[1]], axis=1))
        gval = np.asarray(inputs["L0_attn_g"], dtype=np.float32).reshape(1, 1)
        wrel = np.asarray(inputs["L0_attn_w_rel"], dtype=np.float32)
        w2046 = np.ascontiguousarray(wrel[2046:2047].reshape(1, 1))
        rsc = colpack(inputs["L0_attn_res_scale"])
        tabs = rope_tables()
        corners = []
        for c in range(NCORE):
            jg = np.arange(16 * 128)[:, None]
            tiles = []
            for icc in range(4):
                ig = LOCP * c + icc * 256 + np.arange(256)[None, :]
                idx = np.minimum(jg - ig + 8191, 2046)
                tiles.append(wrel[idx].reshape(16, 128, 256))
            corners.append(np.ascontiguousarray(
                np.stack(tiles, axis=0).reshape(64 * 128, 256)))
        cosT, sinT = [], []
        for c in range(NCORE):
            sl = slice(LOCP * c, LOCP * (c + 1))
            ct = tabs["l0_cos"][sl].T
            stt_ = tabs["l0_sin"][sl].T
            cosT.append(np.ascontiguousarray(np.vstack([ct, ct])))
            sinT.append(np.ascontiguousarray(np.vstack([stt_, stt_])))
        ident = np.eye(128, dtype=np.float32)
        _CACHE[key] = dict(uvwT=uvwT, owT=owT, gbcols=gb, gval=gval,
                           w2046=w2046, rsc=rsc, corners=corners,
                           cosT=cosT, sinT=sinT, identb=ident)
    return _CACHE[key]


def run_attn0(x, inputs):
    nc = _CACHE.get("nc_attn0")
    if nc is None:
        nc = _CACHE["nc_attn0"] = build_attn0()
    shards = shard_xT(x)
    w = _attn0_inputs(inputs, 0)
    consts = np.stack([np.ones(128, np.float32),
                       np.full(128, EPS, np.float32)], axis=1)
    onesrow = np.ones((1, 128), np.float32)
    in_maps = [dict(xt=shards[c], uvwT=w["uvwT"], owT=w["owT"],
                    gbcols=w["gbcols"], gval=w["gval"], w2046=w["w2046"],
                    corner=w["corners"][c], rsc=w["rsc"], cosT=w["cosT"][c],
                    sinT=w["sinT"][c], identb=w["identb"], consts=consts,
                    onesrow=onesrow) for c in range(NCORE)]
    res = run_bass_kernel_spmd(nc, in_maps, core_ids=list(range(NCORE)))
    return unshard_xT([r["xo"] for r in res.results])


def _attn1_inputs(inputs):
    key = "attn1_w"
    if key not in _CACHE:
        wqb = np.asarray(inputs["L1_attn_Wqb"], dtype=np.float32)
        perm = np.empty(1024, np.int64)
        for h in range(16):
            for j in range(32):
                perm[h * 64 + j] = h * 64 + 2 * j
                perm[h * 64 + 32 + j] = h * 64 + 2 * j + 1
        wqbp = np.ascontiguousarray(wqb[:, perm])
        wkvb = np.asarray(inputs["L1_attn_Wkvb"], dtype=np.float32)
        permk = np.empty(256, np.int64)
        permv = np.empty(256, np.int64)
        for kh in range(4):
            for j in range(32):
                permk[kh * 64 + j] = kh * 128 + 4 * j
                permk[kh * 64 + 32 + j] = kh * 128 + 4 * j + 2
            for dd in range(64):
                permv[kh * 64 + dd] = kh * 128 + 2 * dd + 1
        wkvbk = np.ascontiguousarray(wkvb[:, permk])
        wkvbv = np.ascontiguousarray(wkvb[:, permv])
        tabs = rope_tables()
        cosF, sinF = [], []
        for c in range(NCORE):
            sl = slice(LOCP * c, LOCP * (c + 1))
            c32 = tabs["l1_cos"][sl].T
            s32 = tabs["l1_sin"][sl].T
            cosF.append(np.ascontiguousarray(np.vstack([c32] * 4)))
            sinF.append(np.ascontiguousarray(
                np.vstack([-s32, s32, -s32, s32])))
        p_ = np.arange(128)[:, None]
        m_ = np.arange(896)[None, :]
        maskM = (p_ <= m_ - 384).astype(np.float32)
        vtempl = np.zeros((8 * TOKS, 65), np.float32)
        vtempl[:, 64] = 1.0
        _CACHE[key] = dict(
            wqa=np.ascontiguousarray(np.asarray(inputs["L1_attn_Wqa"],
                                                dtype=np.float32)),
            wqbp=wqbp,
            wkva=np.ascontiguousarray(np.asarray(inputs["L1_attn_Wkva"],
                                                 dtype=np.float32)),
            wkvbk=wkvbk, wkvbv=wkvbv,
            wout=np.ascontiguousarray(np.asarray(inputs["L1_attn_Wout"],
                                                 dtype=np.float32)),
            boutc=colpack(inputs["L1_attn_bout"]),
            cosF=cosF, sinF=sinF, maskM=maskM, vtempl=vtempl)
    return _CACHE[key]


def run_attn1(x, inputs):
    nc = _CACHE.get("nc_attn1")
    if nc is None:
        nc = _CACHE["nc_attn1"] = build_attn1()
    shards = shard_xT(x)
    w = _attn1_inputs(inputs)
    consts = np.stack([np.ones(128, np.float32),
                       np.full(128, EPS, np.float32)], axis=1)
    onesrow = np.ones((1, 128), np.float32)
    in_maps = [dict(xt=shards[c], wqa=w["wqa"], wqbp=w["wqbp"],
                    wkva=w["wkva"], wkvbk=w["wkvbk"], wkvbv=w["wkvbv"],
                    wout=w["wout"], boutc=w["boutc"], cosF=w["cosF"][c],
                    sinF=w["sinF"][c], maskM=w["maskM"], vtempl=w["vtempl"],
                    consts=consts, onesrow=onesrow) for c in range(NCORE)]
    res = run_bass_kernel_spmd(nc, in_maps, core_ids=list(range(NCORE)))
    return unshard_xT([r["xo"] for r in res.results])


def kernel(**inputs):
    return np.ascontiguousarray(run_fused(inputs), dtype=np.float32)


# version 14
# speedup vs baseline: 1.0096x; 1.0048x over previous
import contextlib
import numpy as np
import ml_dtypes
import concourse.bass as bass
import concourse.mybir as mybir
from concourse.bass_utils import run_bass_kernel_spmd

F32 = mybir.dt.float32
BF16 = mybir.dt.bfloat16
AF = mybir.ActivationFunctionType
ALU = mybir.AluOpType
AX = mybir.AxisListType

NCORE = 8
BLK = 512
NPOS = 8192
NB = 2
D = 1024
TOKS = 2048
LOCP = 1024
EPS = 1e-5
BF = ml_dtypes.bfloat16



def zz_rows(c):
    return np.arange(LOCP * c, LOCP * (c + 1))


def shard_xT(x):
    out = []
    for c in range(NCORE):
        xs = x[:, zz_rows(c), :]
        out.append(np.ascontiguousarray(xs.transpose(2, 0, 1).reshape(D, TOKS)))
    return out


def unshard_xT(shards):
    out = np.empty((NB, NPOS, D), dtype=np.float32)
    for c in range(NCORE):
        xT = np.asarray(shards[c], dtype=np.float32).reshape(D, NB, LOCP)
        out[:, zz_rows(c), :] = xT.transpose(1, 2, 0)
    return out


def colpack(v):
    return np.ascontiguousarray(np.asarray(v, dtype=np.float32).reshape(8, 128).T)


def transpose_w(w):
    return np.ascontiguousarray(np.asarray(w, dtype=np.float32).T)


_JAX_TABLES = {}


def rope_tables():
    if _JAX_TABLES:
        return _JAX_TABLES
    import jax
    try:
        cpu = jax.devices("cpu")[0]
    except Exception:
        jax.config.update("jax_platforms", "cpu")
        cpu = jax.devices("cpu")[0]
    import jax.numpy as jnp
    with jax.default_device(cpu):
        half = 64
        pos = jnp.arange(NPOS, dtype=jnp.float32)[:, None]
        invf = 10000.0 ** (jnp.arange(half, dtype=jnp.float32) / half)
        arg = pos * invf[None, :]
        l0_sin = np.asarray(jnp.sin(arg))
        l0_cos = np.asarray(jnp.cos(arg))
        invf1 = 1.0 / (10000.0 ** (jnp.arange(0, 64, 2, dtype=jnp.float32) / 64))
        t = jnp.arange(NPOS, dtype=jnp.float32)
        fr = jnp.outer(t, invf1)
        l1_sin = np.asarray(jnp.sin(fr))
        l1_cos = np.asarray(jnp.cos(fr))
    _JAX_TABLES.update(l0_sin=l0_sin, l0_cos=l0_cos, l1_sin=l1_sin, l1_cos=l1_cos)
    return _JAX_TABLES



class Em:

    COMPUTE_SEM = {"pe": "pe", "act": "act", "dve": "dve", "pool": "pool"}
    _w_ready = None

    def __init__(self, nc, stack):
        self.nc = nc
        E = stack.enter_context
        self.sem = {n: E(nc.semaphore("sem_" + n))
                    for n in ("pe", "act", "dve", "pool", "qs", "qg",
                              "cc0", "cc1", "cc2", "cc3", "cc4", "cc5",
                              "cc6", "cc7", "cc8", "cc9")}
        self.cc_i = 0
        self.barrier_toks = []
        self.cnt = {n: 0 for n in self.sem}
        self.seen = {}
        self.q = {n: [] for n in ("pe", "act", "dve", "pool", "sync")}

    def next_cc(self):
        n = f"cc{self.cc_i}"
        self.cc_i += 1
        return n

    def barrier(self, tiny_out, tiny_in):
        self.push("pe", [], lambda t: t.nop(), inc="pe")
        self.push("act", [], lambda s: s.copy(tiny_out, tiny_in), inc="act")
        self.push("dve", [], lambda v: v.tensor_copy(tiny_out, tiny_in),
                  inc="dve")
        self.push("pool", [], lambda g: g.tensor_copy(tiny_out, tiny_in),
                  inc="pool")
        self.barrier_toks = [(s, c) for s, c in self.cnt.items() if c > 0]

    def push(self, engq, waits, build, inc=None, incn=1):
        rw = []
        for w in list(waits) + self.barrier_toks:
            if w is None:
                continue
            s, v = w
            if self.COMPUTE_SEM.get(engq) == s:
                continue
            if self.seen.get((engq, s), 0) >= v:
                continue
            self.seen[(engq, s)] = v
            rw.append((s, v))
        tok = None
        if inc is not None:
            self.cnt[inc] += incn
            tok = (inc, self.cnt[inc])
        sems = self.sem

        def thunk(eng, rw=rw, build=build, tok=tok, incn=incn):
            for s, v in rw:
                eng.wait_ge(sems[s], v)
            ins = build(eng)
            if tok is not None:
                ins.then_inc(sems[tok[0]], incn)
        self.q[engq].append(thunk)
        return tok

    def cnt_tok(self, sem):
        return (sem, self.cnt[sem])

    def fence(self, waits, tiny_out, tiny_in):
        return self.push("dve", waits,
                         lambda v: v.tensor_copy(tiny_out, tiny_in), inc="dve")

    def dma(self, engq, waits, out, in_, **kw):
        qsem = "qs" if engq == "sync" else "qg"
        return self.push(engq, waits,
                         lambda e: e.dma_start(out=out, in_=in_, **kw),
                         inc=qsem, incn=16)

    def run(self, block):
        q = self.q

        @block.tensor
        def _(t):
            for f in q["pe"]:
                f(t)

        @block.scalar
        def _(s):
            for f in q["act"]:
                f(s)

        @block.vector
        def _(v):
            for f in q["dve"]:
                f(v)

        @block.gpsimd
        def _(g):
            for f in q["pool"]:
                f(g)

        @block.sync
        def _(sy):
            for f in q["sync"]:
                f(sy)


class Bufs:
    pass



def emit_ln(em, b, T, xsrc, xn, nc0, extra_waits=(), keep_var=False):
    ncg = T // 512
    xsrc3 = xsrc.rearrange("(ic p) t -> ic p t", p=128)
    tok_st = [None] * 8
    tok_cgst = {}
    tok_ld = [None] * 8
    for ic in range(8):
        s = ic % 2
        tok_ld[ic] = em.dma("pool",
                            [tok_st[ic - 2] if ic >= 2 else None],
                            b.xchunk[s][:, 0:T], xsrc3[ic])
        last = None
        for cg in range(ncg):
            c0 = cg * 512
            sq = b.xsq[cg % 2]
            prev_sq = tok_cgst.get((ic, cg - 2)) or (
                tok_cgst.get((ic - 1, ncg - 2 + cg)) if cg < 2 else None)
            tok_sq = em.push("act", [tok_ld[ic], prev_sq],
                             lambda e, s=s, c0=c0, sq=sq: e.activation(
                                 sq, b.xchunk[s][:, c0:c0 + 512], AF.Square),
                             inc="act")

            def stats(t, ic=ic, s=s, c0=c0, sq=sq):
                t.matmul(b.ps_a[0:1, c0:c0 + 512], b.ones_col,
                         b.xchunk[s][:, c0:c0 + 512], start=(ic == 0), stop=(ic == 7))
                return t.matmul(b.ps_b[0:1, c0:c0 + 512], b.ones_col,
                                sq, start=(ic == 0), stop=(ic == 7))
            last = em.push("pe", [tok_sq, *extra_waits], stats, inc="pe")
            tok_cgst[(ic, cg)] = last
        tok_st[ic] = last
    em.push("dve", [tok_st[7]],
            lambda v: v.tensor_scalar(b.m_row[0:1, 0:T], b.ps_a[0:1, 0:T],
                                      1.0 / D, None, ALU.mult))
    em.push("dve", [],
            lambda v: v.tensor_tensor(b.rs_row[0:1, 0:T], b.m_row[0:1, 0:T],
                                      b.m_row[0:1, 0:T], ALU.mult))
    tok_v = em.push("dve", [],
                    lambda v: v.scalar_tensor_tensor(
                        b.rs_row[0:1, 0:T], b.ps_b[0:1, 0:T], 1.0 / D,
                        b.rs_row[0:1, 0:T], ALU.mult, ALU.subtract), inc="dve")
    if keep_var:
        tok_v = em.push("dve", [],
                        lambda v: v.tensor_copy(b.v_row[0:1, 0:T],
                                                b.rs_row[0:1, 0:T]), inc="dve")
    em.push("act", [tok_v],
            lambda s: s.activation(b.rs_row[0:1, 0:T], b.rs_row[0:1, 0:T],
                                   AF.Ln, bias=b.eps_col[0:1, :]))
    tok_rs = em.push("act", [],
                     lambda s: s.activation(b.rs_row[0:1, 0:T], b.rs_row[0:1, 0:T],
                                            AF.Exp, scale=-0.5), inc="act")
    def bcast(t):
        for cg in range(ncg):
            c0 = cg * 512
            t.matmul(b.ps_a[:, c0:c0 + 512], b.ones_row[0:1, :],
                     b.m_row[0:1, c0:c0 + 512], start=True, stop=True)
            ins = t.matmul(b.ps_b[:, c0:c0 + 512], b.ones_row[0:1, :],
                           b.rs_row[0:1, c0:c0 + 512], start=True, stop=True)
        return ins
    tok_bc = em.push("pe", [tok_rs], bcast, inc="pe")

    tok_xn = None
    tok_n = [None] * 8
    for ic in range(8):
        s = ic % 2
        war = tok_n[ic - 2] if ic >= 2 else tok_st[7]
        tok_l = em.dma("pool", [war],
                       b.xchunk[s][:, 0:T], xsrc3[ic])
        for cg in range(ncg):
            c0 = cg * 512
            osl = slice(nc0 + ic * T + c0, nc0 + ic * T + c0 + 512)
            em.push("dve", [tok_bc, tok_l],
                    lambda v, s=s, c0=c0: v.tensor_tensor(
                        b.ln_tmp, b.xchunk[s][:, c0:c0 + 512],
                        b.ps_a[:, c0:c0 + 512], ALU.subtract))
            tok_xn = em.push("dve", [],
                             lambda v, osl=osl, c0=c0: v.tensor_tensor(
                                 xn[:, osl], b.ln_tmp,
                                 b.ps_b[:, c0:c0 + 512], ALU.mult),
                             inc="dve")
        tok_n[ic] = tok_xn
    return tok_xn



def build_ffn(debug=False):
    nc = bass.Bass("TRN2", target_bir_lowering=False)
    xt_d = nc.dram_tensor("xt", [D, TOKS], F32, kind="ExternalInput").ap()
    wnames = ("wf", "uf", "wi", "ui", "wc", "uc", "wo", "uo")
    w_d = {n: nc.dram_tensor(n + "T", [D, D], F32, kind="ExternalInput").ap()
           for n in wnames}
    bias_d = nc.dram_tensor("biases", [128, 40], F32, kind="ExternalInput").ap()
    consts_d = nc.dram_tensor("consts", [128, 2], F32, kind="ExternalInput").ap()
    onesrow_d = nc.dram_tensor("onesrow", [1, 128], F32, kind="ExternalInput").ap()
    out_d = nc.dram_tensor("xo", [D, TOKS], F32, kind="ExternalOutput").ap()
    dbg = {}
    if debug:
        dbg["xn"] = nc.dram_tensor("dbg_xn", [8 * 128, TOKS], BF16,
                                   kind="ExternalOutput").ap()
        dbg["c32"] = nc.dram_tensor("dbg_c32", [8 * 128, LOCP], F32,
                                    kind="ExternalOutput").ap()
        dbg["cbf"] = nc.dram_tensor("dbg_cbf", [8 * 128, LOCP], BF16,
                                    kind="ExternalOutput").ap()
        dbg["c1"] = nc.dram_tensor("dbg_c1", [8 * 128, LOCP], F32,
                                   kind="ExternalOutput").ap()
        dbg["c0"] = nc.dram_tensor("dbg_c0", [8 * 128, LOCP], F32,
                                   kind="ExternalOutput").ap()
        for gi in range(3):
            dbg[f"g{gi}"] = nc.dram_tensor(f"dbg_g{gi}", [8 * 128, 512], BF16,
                                           kind="ExternalOutput").ap()

    with contextlib.ExitStack() as st:
        E = st.enter_context
        em = Em(nc, st)
        b = Bufs()
        b.c32 = E(nc.sbuf_tensor("c32", [128, 8 * LOCP], F32))
        b.c_bf = E(nc.sbuf_tensor("c_bf", [128, 8 * LOCP], BF16))
        b.xn = E(nc.sbuf_tensor("xn", [128, 8 * TOKS], BF16))
        b.gate = [E(nc.sbuf_tensor(f"gate{i}", [128, 8 * 512], BF16))
                  for i in range(3)]
        b.wbuf = [E(nc.sbuf_tensor(f"wbuf{i}", [128, 8 * D], BF16))
                  for i in range(4)]
        b.xchunk = [b.c32[:, 0:TOKS], b.c32[:, TOKS:2 * TOKS]]
        b.scratch = E(nc.sbuf_tensor("scratch", [128, 2560], F32))
        b.xsq = [b.scratch[:, 0:512], b.scratch[:, 512:1024]]
        b.ln_tmp = b.scratch[:, 1024:1536]
        b.t1 = b.scratch[:, 0:512]
        b.t2 = b.scratch[:, 512:1024]
        b.hbuf = [b.scratch[:, 1024:1280].bitcast(BF16),
                  b.scratch[:, 1280:1536].bitcast(BF16)]
        b.xres = [b.scratch[:, 1536:2048], b.scratch[:, 2048:2560]]
        b.ores = [b.scratch[:, 0:512], b.scratch[:, 512:1024]]
        b.biases = E(nc.sbuf_tensor("biases_sb", [128, 40], F32))
        b.m_row = E(nc.sbuf_tensor("m_row", [1, TOKS], F32))
        b.rs_row = E(nc.sbuf_tensor("rs_row", [1, TOKS], F32))
        b.consts = E(nc.sbuf_tensor("consts_sb", [128, 2], F32))
        b.ones_col = b.consts[:, 0:1]
        b.eps_col = b.consts[:, 1:2]
        b.ones_row = E(nc.sbuf_tensor("ones_row", [1, 128], F32))
        b.ps_a = E(nc.psum_tensor("ps_a", [128, 2048], F32))
        b.ps_b = E(nc.psum_tensor("ps_b", [128, 2048], F32))
        block = E(nc.Block())

        emit_ffn_body(em, b, xt_d, w_d, bias_d, out_d, dbg,
                      consts_d, onesrow_d)
        em.run(block)
    return nc


def emit_ffn_body(em, b, xt_d, w_d, bias_d, out_d, dbg=None,
                  consts_d=None, onesrow_d=None):
    em._tok_consts = em.dma("pool", [], b.consts[:, :], consts_d)
    tok_ones = em.dma("pool", [], b.ones_row[0:1, :], onesrow_d)
    tok_bias = em.dma("pool", [], b.biases[:, :], bias_d)

    tok_xn = emit_ln(em, b, TOKS, xt_d, b.xn, 0,
                     extra_waits=(tok_ones, em._tok_consts))

    tok_c0 = None
    for mc in range(8):
        sl = slice(mc * LOCP, (mc + 1) * LOCP)
        tok_c0 = em.push("dve", [tok_bias, tok_xn],
                         lambda v, sl=sl, mc=mc: v.tensor_scalar(
                             b.c32[:, sl], b.xn[:, 0:LOCP],
                             0.0, b.biases[:, 32 + mc:33 + mc],
                             ALU.mult, ALU.add),
                         inc="dve")
    tok_cbf = None
    for mc in range(8):
        sl = slice(mc * LOCP, (mc + 1) * LOCP)
        tok_cbf = em.push("dve", [],
                          lambda v, sl=sl: v.tensor_copy(b.c_bf[:, sl],
                                                         b.c32[:, sl]),
                          inc="dve")

    if dbg:
        em.dma("sync", [tok_cbf],
               dbg["c0"].rearrange("(ic p) t -> p ic t", p=128),
               b.c32[:, :].rearrange("p (ic t) -> p ic t", ic=8))
    order = ["wf", "uf", "wi", "ui", "wc", "uc"] * 4 + ["wo", "uo"]
    NL = len(order)
    wtok = [None] * NL
    wfree = [None] * NL
    emitted_load = [False] * NL

    def need(k):
        if k >= NL or emitted_load[k]:
            return
        emitted_load[k] = True
        wait = wfree[k - 4] if k >= 4 else None
        wtok[k] = em.dma(
            "pool", [wait, em._w_ready],
            b.wbuf[k % 4][:, :].rearrange("p (ic m) -> p ic m", ic=8),
            w_d[order[k]].rearrange("(ic p) m -> p ic m", p=128))

    for k in range(4):
        need(k)

    psq = [b.ps_a[:, i * 512:(i + 1) * 512] for i in range(4)] + \
          [b.ps_b[:, i * 512:(i + 1) * 512] for i in range(4)]
    ps_free = [tok_xn] * 8
    qctr = 0


    tok_upd_all = None
    for tstep in range(2):
        for half in range(2):
            base = (tstep * 2 + half) * 6
            gate_act_tok = {}
            for gi in range(3):
                kw, ku = base + 2 * gi, base + 2 * gi + 1
                need(kw), need(ku)
                fn = AF.Sigmoid if gi < 2 else AF.Tanh
                last_mm = None
                for mt in range(8):
                    q = qctr % 8
                    qctr += 1
                    waits = [wtok[kw], wtok[ku], tok_cbf, tok_xn, ps_free[q]]
                    def f(t, kw=kw, ku=ku, q=q, mt=mt, half=half, tstep=tstep):
                        ps = psq[q]
                        bw, bu = b.wbuf[kw % 4], b.wbuf[ku % 4]
                        xc = tstep * LOCP + half * 512
                        cc = half * 512
                        for ic in range(8):
                            t.matmul(ps, bw[:, ic * D + mt * 128:
                                            ic * D + mt * 128 + 128],
                                     b.xn[:, ic * TOKS + xc:ic * TOKS + xc + 512],
                                     start=(ic == 0), stop=False)
                        for ic in range(8):
                            ins = t.matmul(ps, bu[:, ic * D + mt * 128:
                                                  ic * D + mt * 128 + 128],
                                           b.c_bf[:, ic * LOCP + cc:
                                                  ic * LOCP + cc + 512],
                                           start=False, stop=(ic == 7))
                        return ins
                    mmtok = em.push("pe", waits, f, inc="pe")
                    last_mm = mmtok
                    atok = em.push(
                        "act", [mmtok, tok_upd_all],
                        lambda s, gi=gi, q=q, fn=fn, mt=mt:
                        s.activation(
                            b.gate[gi][:, mt * 512:(mt + 1) * 512],
                            psq[q], fn, bias=b.biases[:, gi * 8 + mt:
                                                      gi * 8 + mt + 1]),
                        inc="act")
                    ps_free[q] = atok
                    gate_act_tok[(gi, mt)] = atok
                wfree[kw] = last_mm
                wfree[ku] = last_mm
                need(kw + 4)
                need(ku + 4)
            for mt in range(8):
                gsl = slice(mt * 512, (mt + 1) * 512)
                csl = slice(mt * LOCP + half * 512, mt * LOCP + half * 512 + 512)
                em.push("dve", [gate_act_tok[(0, mt)]],
                        lambda v, gsl=gsl, csl=csl: v.tensor_tensor(
                            b.t1, b.gate[0][:, gsl], b.c32[:, csl], ALU.mult))
                em.push("dve", [gate_act_tok[(1, mt)],
                                gate_act_tok[(2, mt)]],
                        lambda v, gsl=gsl: v.tensor_tensor(
                            b.t2, b.gate[1][:, gsl], b.gate[2][:, gsl],
                            ALU.mult))
                tok_upd_all = em.push(
                    "dve", [],
                    lambda v, csl=csl: v.tensor_tensor(
                        b.c32[:, csl], b.t1, b.t2, ALU.add),
                    inc="dve")
        if dbg and tstep == 0:
            em.dma("sync", [tok_upd_all],
                   dbg["c1"].rearrange("(ic p) t -> p ic t", p=128),
                   b.c32[:, :].rearrange("p (ic t) -> p ic t", ic=8))
        for mc in range(8):
            sl = slice(mc * LOCP, (mc + 1) * LOCP)
            tok_cbf = em.push("dve", [tok_upd_all],
                              lambda v, sl=sl: v.tensor_copy(
                                  b.c_bf[:, sl], b.c32[:, sl]),
                              inc="dve")

    kwo, kuo = 24, 25
    need(kwo), need(kuo)
    store_toks = []
    stt_tok = [None, None]
    slot_i = 0
    last_mm = None
    for tstep in range(2):
        for half in range(2):
            for mt in range(8):
                slot = slot_i % 2
                slot_i += 1
                rsl = (slice(mt * 128, (mt + 1) * 128),
                       slice(tstep * LOCP + half * 512,
                             tstep * LOCP + half * 512 + 512))
                xr_tok = em.dma("sync", [stt_tok[slot]],
                                b.xres[slot], xt_d[rsl[0], rsl[1]])
                q = qctr % 8
                qctr += 1

                def f(t, q=q, mt=mt, half=half, tstep=tstep):
                    ps = psq[q]
                    bw, bu = b.wbuf[kwo % 4], b.wbuf[kuo % 4]
                    xc = tstep * LOCP + half * 512
                    cc = half * 512
                    for ic in range(8):
                        t.matmul(ps, bw[:, ic * D + mt * 128:
                                        ic * D + mt * 128 + 128],
                                 b.xn[:, ic * TOKS + xc:ic * TOKS + xc + 512],
                                 start=(ic == 0), stop=False)
                    for ic in range(8):
                        ins = t.matmul(ps, bu[:, ic * D + mt * 128:
                                              ic * D + mt * 128 + 128],
                                       b.c_bf[:, ic * LOCP + cc:
                                              ic * LOCP + cc + 512],
                                       start=False, stop=(ic == 7))
                    return ins
                mmtok = em.push("pe", [wtok[kwo], wtok[kuo], tok_cbf,
                                       ps_free[q]], f, inc="pe")
                last_mm = mmtok
                atok = em.push("act", [mmtok, stt_tok[slot]],
                               lambda s, q=q, mt=mt, slot=slot: s.activation(
                                   b.hbuf[slot], psq[q], AF.Tanh,
                                   bias=b.biases[:, 24 + mt:25 + mt]),
                               inc="act")
                ps_free[q] = atok
                prev_store = store_toks[-2] if len(store_toks) >= 2 else None
                stt_tok[slot] = em.push(
                    "dve", [atok, xr_tok, prev_store],
                    lambda v, slot=slot: v.scalar_tensor_tensor(
                        b.ores[slot], b.hbuf[slot], 0.5,
                        b.xres[slot], ALU.mult, ALU.add),
                    inc="dve")
                store_toks.append(em.dma("sync", [stt_tok[slot]],
                                         out_d[rsl[0], rsl[1]],
                                         b.ores[slot]))
    wfree[kwo] = last_mm
    wfree[kuo] = last_mm

    if dbg:
        for gi in range(3):
            em.dma("sync", [tok_upd_all],
                   dbg[f"g{gi}"].rearrange("(ic p) t -> p ic t", p=128),
                   b.gate[gi][:, :].rearrange("p (ic t) -> p ic t", ic=8))
        em.dma("sync", [tok_xn],
               dbg["xn"].rearrange("(ic p) t -> p ic t", p=128),
               b.xn[:, :].rearrange("p (ic t) -> p ic t", ic=8))
        em.dma("sync", [tok_upd_all],
               dbg["c32"].rearrange("(ic p) t -> p ic t", p=128),
               b.c32[:, :].rearrange("p (ic t) -> p ic t", ic=8))
        store_toks.append(em.dma(
            "sync", [tok_cbf],
            dbg["cbf"].rearrange("(ic p) t -> p ic t", p=128),
            b.c_bf[:, :].rearrange("p (ic t) -> p ic t", ic=8)))
    em.push("sync", [store_toks[-1]], lambda sy: sy.nop())




SQS = 0.08838834764831845


def build_attn0(debug=False):
    nc = bass.Bass("TRN2", target_bir_lowering=False)
    xt_d = nc.dram_tensor("xt", [D, TOKS], F32, kind="ExternalInput").ap()
    uvw_d = nc.dram_tensor("uvwT", [D, 4224], F32, kind="ExternalInput").ap()
    ow_d = nc.dram_tensor("owT", [2048, D], F32, kind="ExternalInput").ap()
    gb_d = nc.dram_tensor("gbcols", [128, 4], F32, kind="ExternalInput").ap()
    g_d = nc.dram_tensor("gval", [1, 1], F32, kind="ExternalInput").ap()
    w2046_d = nc.dram_tensor("w2046", [1, 1], F32, kind="ExternalInput").ap()
    corner_d = nc.dram_tensor("corner", [64 * 128, 256], F32,
                              kind="ExternalInput").ap()
    rsc_d = nc.dram_tensor("rsc", [128, 8], F32, kind="ExternalInput").ap()
    cos_d = nc.dram_tensor("cosT", [128, LOCP], F32, kind="ExternalInput").ap()
    sin_d = nc.dram_tensor("sinT", [128, LOCP], F32, kind="ExternalInput").ap()
    ident_d = nc.dram_tensor("identb", [128, 128], F32, kind="ExternalInput").ap()
    consts_d = nc.dram_tensor("consts", [128, 2], F32, kind="ExternalInput").ap()
    onesrow_d = nc.dram_tensor("onesrow", [1, 128], F32, kind="ExternalInput").ap()
    out_d = nc.dram_tensor("xo", [D, TOKS], F32, kind="ExternalOutput").ap()

    dbg = {}
    if debug:
        for nm, shape, dt in [("y", [1024, TOKS], BF16), ("xn2", [1024, TOKS], BF16),
                              ("qT", [128, TOKS], BF16), ("kTloc", [128, TOKS], BF16),
                              ("kTall", [128, 8192], BF16),
                              ("kern", [128, 16384], BF16),
                              ("Bsb", [128, 4096], BF16),
                              ("base", [128, TOKS], F32),
                              ("uvwb", [128, 1024], BF16),
                              ("wp0", [128, 2048], BF16),
                              ("qpre", [128, TOKS], F32),
                              ("BTsb", [128, 4096], BF16)]:
            dbg[nm] = nc.dram_tensor("dbg_" + nm, shape, dt,
                                     kind="ExternalOutput").ap()
    cc_in = nc.dram_tensor("cc0_in", [2176, 2048], BF16).ap()
    cc_out = nc.dram_tensor("cc0_out", [8 * 2176, 2048], BF16).ap()
    u_spill = nc.dram_tensor("u_spill", [2048, 2048], BF16).ap()

    with contextlib.ExitStack() as st:
        E = st.enter_context
        em = Em(nc, st)
        b = Bufs()
        b.bigbuf = E(nc.sbuf_tensor("bigbuf", [128, 16384], BF16))
        b.y = E(nc.sbuf_tensor("ybuf", [128, 16384], BF16))
        b.wpool = E(nc.sbuf_tensor("wpool", [128, 16384], BF16))
        b.kT = E(nc.sbuf_tensor("kT_all", [128, 8192], BF16))
        b.qT = E(nc.sbuf_tensor("qT", [128, 2048], BF16))
        b.kTloc = E(nc.sbuf_tensor("kTloc", [128, 2048], BF16))
        b.vsb = [E(nc.sbuf_tensor(f"vsb{i}", [128, 2048], BF16)) for i in range(2)]
        b.usb = [E(nc.sbuf_tensor(f"usb{i}", [128, 2048], BF16)) for i in range(2)]
        b.tileAB = E(nc.sbuf_tensor("tileAB", [128, 4096], BF16))
        b.utile = b.tileAB[:, 0:2048]
        b.vtile = b.tileAB[:, 2048:4096]
        b.Bsb = b.tileAB
        b.BTsb = E(nc.sbuf_tensor("BTsb", [128, 4096], BF16))
        b.basebuf = E(nc.sbuf_tensor("basebuf", [128, 2048], F32))
        b.vsb += [b.basebuf[:, :].bitcast(BF16)[:, 0:2048],
                  b.basebuf[:, :].bitcast(BF16)[:, 2048:4096]]
        b.uvwb = E(nc.sbuf_tensor("uvwb", [128, 8 * 128], BF16))
        b.ident = E(nc.sbuf_tensor("identsb", [128, 128], BF16))
        b.diag = E(nc.sbuf_tensor("diagrs", [128, 8 * 128], BF16))
        b.corner = [E(nc.sbuf_tensor(f"corner{i}", [128, 256], F32))
                    for i in range(2)]
        b.costab = E(nc.sbuf_tensor("costab", [128, LOCP], F32))
        b.sintab = E(nc.sbuf_tensor("sintab", [128, LOCP], F32))
        b.gb = E(nc.sbuf_tensor("gbsb", [128, 4], F32))
        b.rsc = E(nc.sbuf_tensor("rscsb", [128, 8], F32))
        b.smalls = E(nc.sbuf_tensor("smalls", [128, 4], F32))
        b.gsb = E(nc.sbuf_tensor("gsb", [1, 2], F32))
        b.consts = E(nc.sbuf_tensor("consts_sb", [128, 2], F32))
        b.ones_col = b.consts[:, 0:1]
        b.eps_col = b.consts[:, 1:2]
        b.ones_row = E(nc.sbuf_tensor("ones_row", [1, 128], F32))
        b.xchunk = [E(nc.sbuf_tensor(f"xchunk{i}", [128, TOKS], F32))
                    for i in range(2)]
        b.scratch = E(nc.sbuf_tensor("scratch", [128, 2560], F32))
        b.xsq = [b.scratch[:, 0:512], b.scratch[:, 512:1024]]
        b.ln_tmp = b.scratch[:, 1024:1536]
        b.ropet = [b.scratch[:, 0:1024], b.scratch[:, 1024:2048]]
        b.rtile = b.scratch[:, 0:512]
        b.rtile4 = b.xchunk[0]
        b.xres = [b.scratch[:, 512:768], b.scratch[:, 768:1024]]
        b.ores = [b.scratch[:, 1024:1280], b.scratch[:, 1280:1536]]
        ktf = b.kT.bitcast(F32)
        b.m_row = ktf[0:1, 0:2048]
        b.rs_row = ktf[0:1, 2048:4096]
        b.v_row = b.BTsb.bitcast(F32)[0:1, 0:2048]
        b.t_row = b.v_row
        b.ps_a = E(nc.psum_tensor("ps_a", [128, 2048], F32))
        b.ps_b = E(nc.psum_tensor("ps_b", [128, 2048], F32))
        block = E(nc.Block())

        emit_attn0(em, b, xt_d, uvw_d, ow_d, gb_d, g_d, w2046_d, corner_d,
                   rsc_d, cos_d, sin_d, ident_d, consts_d, onesrow_d, out_d,
                   cc_in, cc_out, u_spill, dbg)
        em.run(block)
    return nc


def emit_attn0(em, b, xt_d, uvw_d, ow_d, gb_d, g_d, w2046_d, corner_d,
               rsc_d, cos_d, sin_d, ident_d, consts_d, onesrow_d, out_d,
               cc_in, cc_out, u_spill, dbg=None, cck0=None,
               ccv0_out=None, ccv1_out=None, ccv0_in=None, ccv1_in=None):
    em._tok_consts = em.dma("pool", [], b.consts[:, :], consts_d)
    tok_ones = em.dma("pool", [], b.ones_row[0:1, :], onesrow_d)
    tok_gb = em.dma("pool", [], b.gb[:, :], gb_d)
    tok_g = em.dma("pool", [], b.gsb[0:1, 0:1], g_d)
    tok_w2 = em.dma("pool", [], b.gsb[0:1, 1:2], w2046_d)
    tok_rsc = em.dma("pool", [], b.rsc[:, :], rsc_d)
    tok_cos = em.dma("pool", [], b.costab[:, :], cos_d)
    tok_sin = em.dma("pool", [], b.sintab[:, :], sin_d)
    tok_id = em.dma("pool", [], b.ident[:, :], ident_d)

    tok_w2b = em.push("pe", [tok_w2, tok_ones],
                      lambda t: t.matmul(b.ps_a[:, 0:1], b.ones_row[0:1, :],
                                         b.gsb[0:1, 1:2], start=True, stop=True),
                      inc="pe")
    em.push("dve", [tok_w2b],
            lambda v: v.tensor_scalar(b.smalls[:, 0:1], b.ps_a[:, 0:1],
                                      SQS, None, ALU.mult))
    tok_smalls = em.push("dve", [],
                         lambda v: v.tensor_copy(b.smalls[:, 1:2],
                                                 b.ps_a[:, 0:1]), inc="dve")
    tok_diag = None
    for dt in range(8):
        tok_diag = em.push("dve", [tok_id, tok_rsc],
                           lambda v, dt=dt: v.tensor_scalar(
                               b.diag[:, dt * 128:(dt + 1) * 128],
                               b.ident[:, :], b.rsc[:, dt:dt + 1], None,
                               ALU.mult), inc="dve")

    tok_y = emit_ln(em, b, TOKS, xt_d, b.y, 0,
                    extra_waits=(tok_ones, em._tok_consts), keep_var=True)
    tok_t0 = em.push("dve", [tok_y],
                     lambda v: v.tensor_scalar(b.t_row[0:1, :], b.v_row[0:1, :],
                                               0.0, None, ALU.max), inc="dve")
    tok_tr0 = em.push("act", [tok_t0],
                      lambda s: s.activation(b.t_row[0:1, :], b.t_row[0:1, :],
                                             AF.Sqrt), inc="act")
    em.push("dve", [tok_tr0],
            lambda v: v.tensor_tensor(b.t_row[0:1, :], b.t_row[0:1, :],
                                      b.rs_row[0:1, :], ALU.mult))
    em.push("dve", [],
            lambda v: v.tensor_scalar(b.t_row[0:1, :], b.t_row[0:1, :], EPS,
                                      None, ALU.max))
    em.push("dve", [], lambda v: v.reciprocal(b.t_row[0:1, :], b.t_row[0:1, :]))
    tok_tr = em.push("dve", [tok_g],
                     lambda v: v.tensor_scalar(b.t_row[0:1, :], b.t_row[0:1, :],
                                               b.gsb[0:1, 0:1], None, ALU.mult),
                     inc="dve")

    def bct(t):
        for cg in range(4):
            c0 = cg * 512
            ins = t.matmul(b.ps_a[:, c0:c0 + 512], b.ones_row[0:1, :],
                           b.t_row[0:1, c0:c0 + 512], start=True, stop=True)
        return ins
    tok_bct = em.push("pe", [tok_tr], bct, inc="pe")
    tok_xn2 = None
    for ic in range(8):
        sl = slice(ic * TOKS, (ic + 1) * TOKS)
        tok_xn2 = em.push("dve", [tok_bct],
                          lambda v, sl=sl: v.tensor_tensor(
                              b.bigbuf[:, sl], b.y[:, sl], b.ps_a[:, 0:TOKS],
                              ALU.mult), inc="dve")

    tok_xn2 = em.fence([tok_xn2], b.scratch[0:1, 0:1], b.consts[0:1, 0:1])

    uvw3 = uvw_d.rearrange("(ic p) f -> p ic f", p=128)
    psq = [b.ps_a[:, i * 512:(i + 1) * 512] for i in range(4)] + \
          [b.ps_b[:, i * 512:(i + 1) * 512] for i in range(4)]
    ps_free = [tok_bct] * 8
    qctr = 0
    spill_toks = []
    prev_pass_mm = None
    tok_uvw = None
    for fpass in range(2):
        f0 = fpass * 2048
        tok_uvw = em.dma(
            "pool", [prev_pass_mm],
            b.wpool[:, 0:16384].rearrange("p (ic f) -> p ic f", ic=8),
            uvw3[:, :, f0:f0 + 2048])
        tile = b.utile if fpass == 0 else b.vtile
        prev_spill = None
        for tt in range(16):
            atoks = []
            for fb in range(4):
                q = qctr % 8
                qctr += 1

                def f(t, tt=tt, fb=fb, q=q):
                    for ic in range(8):
                        ins = t.matmul(
                            psq[q],
                            b.bigbuf[:, ic * TOKS + tt * 128:
                                     ic * TOKS + tt * 128 + 128],
                            b.wpool[:, ic * 2048 + fb * 512:
                                    ic * 2048 + fb * 512 + 512],
                            start=(ic == 0), stop=(ic == 7))
                    return ins
                mmtok = em.push("pe", [tok_uvw, tok_xn2, ps_free[q]], f,
                                inc="pe")
                prev_pass_mm = mmtok
                atok = em.push("act", [mmtok, prev_spill],
                               lambda s, fb=fb, q=q, tile=tile: s.activation(
                                   tile[:, fb * 512:(fb + 1) * 512], psq[q],
                                   AF.Silu), inc="act")
                ps_free[q] = atok
                atoks.append(atok)
            if fpass == 0:
                dst_d, row0 = u_spill, tt * 128
            elif ccv0_in is not None:
                dst_d = ccv0_in if tt < 8 else ccv1_in
                row0 = (tt % 8) * 128
            else:
                dst_d, row0 = cc_in, 128 + tt * 128
            prev_spill = em.dma("sync", atoks, dst_d[row0:row0 + 128, :],
                                tile)
            spill_toks.append(prev_spill)

    tok_bw = em.dma("pool", [],
                    b.uvwb[:, :].rearrange("p (ic f) -> p ic f", ic=8),
                    uvw3[:, :, 4096:4224])
    tok_basebuf = None
    for cg in range(4):
        c0 = cg * 512
        q = qctr % 8
        qctr += 1

        def f(t, c0=c0, q=q):
            for ic in range(8):
                ins = t.matmul(psq[q],
                               b.uvwb[:, ic * 128:(ic + 1) * 128],
                               b.bigbuf[:, ic * TOKS + c0:ic * TOKS + c0 + 512],
                               start=(ic == 0), stop=(ic == 7))
            return ins
        mmtok = em.push("pe", [tok_bw, tok_xn2, ps_free[q]], f, inc="pe")
        tok_basebuf = em.push("act", [mmtok],
                              lambda s, c0=c0, q=q: s.activation(
                                  b.basebuf[:, c0:c0 + 512], psq[q], AF.Silu),
                              inc="act")
        ps_free[q] = tok_basebuf

    qpre, kpre = b.xchunk[0], b.xchunk[1]
    em.push("dve", [tok_basebuf, tok_gb],
            lambda v: v.tensor_scalar(qpre[:, :], b.basebuf[:, :],
                                      b.gb[:, 0:1], b.gb[:, 1:2],
                                      ALU.mult, ALU.add))
    tok_pre = em.push("dve", [],
                      lambda v: v.tensor_scalar(kpre[:, :], b.basebuf[:, :],
                                                b.gb[:, 2:3], b.gb[:, 3:4],
                                                ALU.mult, ALU.add), inc="dve")
    if dbg:
        em.dma("sync", [tok_basebuf], dbg["base"], b.basebuf[:, :])
        em.dma("sync", [tok_bw], dbg["uvwb"], b.uvwb[:, :])
        em.dma("sync", [tok_uvw], dbg["wp0"], b.wpool[:, 0:2048])
    tok_rope = None
    sw = b.basebuf
    for (srcb, dstbuf) in ((qpre, b.qT), (kpre, b.kTloc)):
        em.dma("pool", [tok_pre, tok_rope], sw[0:64, :], srcb[64:128, :])
        tok_sw = em.dma("pool", [], sw[64:128, :], srcb[0:64, :])
        for bb in range(2):
            sl = slice(bb * LOCP, (bb + 1) * LOCP)
            em.push("dve", [tok_pre, tok_sw, tok_cos, tok_sin],
                    lambda v, srcb=srcb, sl=sl: v.tensor_tensor(
                        b.ropet[0][0:64, :], srcb[0:64, sl],
                        b.costab[0:64, :], ALU.mult))
            em.push("dve", [],
                    lambda v, sl=sl: v.tensor_tensor(
                        b.ropet[1][0:64, :], sw[0:64, sl],
                        b.sintab[0:64, :], ALU.mult))
            em.push("dve", [],
                    lambda v, dstbuf=dstbuf, sl=sl: v.tensor_tensor(
                        dstbuf[0:64, sl], b.ropet[0][0:64, :],
                        b.ropet[1][0:64, :], ALU.subtract))
            em.push("dve", [],
                    lambda v, srcb=srcb, sl=sl: v.tensor_tensor(
                        b.ropet[0][64:128, :], srcb[64:128, sl],
                        b.costab[64:128, :], ALU.mult))
            em.push("dve", [],
                    lambda v, sl=sl: v.tensor_tensor(
                        b.ropet[1][64:128, :], sw[64:128, sl],
                        b.sintab[64:128, :], ALU.mult))
            tok_rope = em.push("dve", [],
                               lambda v, dstbuf=dstbuf, sl=sl: v.tensor_tensor(
                                   dstbuf[64:128, sl], b.ropet[0][64:128, :],
                                   b.ropet[1][64:128, :], ALU.add), inc="dve")
    kdst = cck0[0] if cck0 is not None else cc_in[0:128, :]
    tok_kdma = em.dma("sync", [tok_rope], kdst, b.kTloc[:, :])
    if dbg:
        em.dma("sync", [tok_pre], dbg["qpre"], qpre[:, :])
        em.dma("sync", [tok_xn2],
               dbg["y"].rearrange("(ic p) t -> p ic t", p=128),
               b.y[:, :].rearrange("p (ic t) -> p ic t", ic=8))
        em.dma("sync", [tok_xn2],
               dbg["xn2"].rearrange("(ic p) t -> p ic t", p=128),
               b.bigbuf[:, :].rearrange("p (ic t) -> p ic t", ic=8))
        em.dma("sync", [tok_rope], dbg["qT"], b.qT[:, :])
        em.dma("sync", [tok_rope], dbg["kTloc"], b.kTloc[:, :])

    if cck0 is not None:
        tok_cck0 = em.push("pool", [tok_kdma],
                           lambda g: g.collective_compute(
                               "AllGather", ALU.bypass,
                               replica_groups=[list(range(8))],
                               ins=[cck0[0].opt()], outs=[cck0[1].opt()]),
                           inc=em.next_cc())
        tok_ccv0 = em.push("pool", spill_toks[16:24],
                           lambda g: g.collective_compute(
                               "AllGather", ALU.bypass,
                               replica_groups=[list(range(8))],
                               ins=[ccv0_in.opt()],
                               outs=[ccv0_out.opt()]),
                           inc=em.next_cc())
        tok_ccv1 = em.push("pool", spill_toks,
                           lambda g: g.collective_compute(
                               "AllGather", ALU.bypass,
                               replica_groups=[list(range(8))],
                               ins=[ccv1_in.opt()],
                               outs=[ccv1_out.opt()]),
                           inc=em.next_cc())
        tok_ccv = [tok_ccv0, tok_ccv1]
    else:
        tok_cc = em.push("pool", [tok_kdma] + spill_toks,
                         lambda g: g.collective_compute(
                             "AllGather", ALU.bypass,
                             replica_groups=[list(range(8))],
                             ins=[cc_in.opt()], outs=[cc_out.opt()]),
                         inc=em.next_cc())
        tok_cck0 = tok_cc
        tok_ccv = [tok_cc, tok_cc]

    ow3 = ow_d.rearrange("(ec p) dd -> p ec dd", p=128)
    tok_ow = em.dma("pool", [prev_pass_mm],
                    b.wpool[:, 0:16384].rearrange("p (ec dd) -> p ec dd", ec=16),
                    ow3)

    corner3 = corner_d.rearrange("(s p) i -> s p i", p=128)

    store_toks = []
    tok_score_done = None
    tok_ic_done = None
    tok_tr_done = None
    tok_evac = {0: None, 1: None}
    corner_war = [None, None]
    tok_uload = {0: None, 1: None}
    vwar = [None] * 4
    tok_st = [None, None]
    tok_xr = [None, None]
    owps_free = [None, None]

    for bb in range(2):
        tok_kta = None
        ksrc = cck0[1] if cck0 is not None else cc_out
        kstride = 128 if cck0 is not None else 2176
        for r in range(8):
            tok_kta = em.dma("sync", [tok_cck0, tok_score_done],
                             b.kT[:, r * 1024:(r + 1) * 1024],
                             ksrc[kstride * r:kstride * r + 128,
                                  bb * 1024:(bb + 1) * 1024])
        for ic in range(4):
            icg = bb * 4 + ic
            iofs = bb * 1024 + ic * 256
            sqt = None
            rt_hist = [None, None]
            for jp in range(16):
                g = jp % 2
                grp = b.ps_a[:, g * 1024:(g + 1) * 1024]
                deps = []
                for sl4 in range(4):
                    jt = jp * 4 + sl4
                    mmtok = em.push(
                        "pe", [tok_kta, tok_rope, ps_free[g], tok_ic_done,
                               tok_tr_done],
                        lambda t, jt=jt, sl4=sl4, grp=grp, iofs=iofs:
                        t.matmul(
                            grp[:, sl4 * 256:sl4 * 256 + 256],
                            b.kT[:, jt * 128:(jt + 1) * 128],
                            b.qT[:, iofs:iofs + 256], start=True, stop=True),
                        inc="pe")
                    dep = mmtok
                    if jt < 16:
                        cs = (ic * 16 + jt) % 2
                        ltok = em.dma("pool", [corner_war[cs], tok_smalls],
                                      b.corner[cs][:, :], corner3[ic * 16 + jt])
                        adjtok = em.push(
                            "dve", [ltok],
                            lambda v, cs=cs: v.tensor_scalar(
                                b.corner[cs][:, :], b.corner[cs][:, :],
                                b.smalls[:, 1:2], None, ALU.subtract),
                            inc="dve")
                        dep = em.push(
                            "dve", [mmtok, adjtok],
                            lambda v, sl4=sl4, grp=grp, cs=cs: v.tensor_tensor(
                                grp[:, sl4 * 256:sl4 * 256 + 256],
                                grp[:, sl4 * 256:sl4 * 256 + 256],
                                b.corner[cs][:, :], ALU.add),
                            inc="dve")
                        corner_war[cs] = dep
                    deps.append(dep)
                rh = g * 1024
                rt = em.push("act", [tok_smalls, rt_hist[g], *deps],
                             lambda s, grp=grp, rh=rh: s.activation(
                                 b.rtile4[:, rh:rh + 1024], grp,
                                 AF.Relu, bias=b.smalls[:, 0:1], scale=SQS),
                             inc="act")
                rt_hist[g] = rt
                sqt = em.push("dve", [rt],
                              lambda v, jp=jp, rh=rh: v.tensor_tensor(
                                  b.bigbuf[:, jp * 1024:(jp + 1) * 1024],
                                  b.rtile4[:, rh:rh + 1024],
                                  b.rtile4[:, rh:rh + 1024], ALU.mult),
                              inc="dve")
                ps_free[g] = sqt
            tok_score_done = em.fence([sqt], b.scratch[0:1, 0:1],
                                      b.consts[0:1, 0:1])
            if dbg and bb == 0 and ic == 0:
                em.dma("sync", [tok_score_done], dbg["kern"], b.bigbuf[:, :])
                em.dma("sync", [tok_kta], dbg["kTall"], b.kT[:, :])

            for it in range(2):
                trow = (icg * 2 + it) * 128
                tok_uload[it] = em.dma("sync", [tok_evac[it]],
                                       b.usb[it][:, :],
                                       u_spill[trow:trow + 128, :])
            Atok = None
            for jt in range(64):
                vb = jt % 4
                r, jr = jt // 8, jt % 8
                if cck0 is not None:
                    vsrcT = ccv0_out if bb == 0 else ccv1_out
                    vrow = 1024 * r + jr * 128
                else:
                    vsrcT = cc_out
                    vrow = 2176 * r + 128 + bb * 1024 + jr * 128
                vtok = em.dma("sync" if jt % 2 == 0 else "pool",
                              [vwar[vb], tok_ccv[bb]],
                              b.vsb[vb][:, :], vsrcT[vrow:vrow + 128, :])

                def fA(t, jt=jt, vb=vb):
                    for it in range(2):
                        for es in range(4):
                            ins = t.matmul(
                                (b.ps_a if it == 0 else b.ps_b)[:, es * 512:
                                                                (es + 1) * 512],
                                b.bigbuf[:, jt * 256 + it * 128:
                                         jt * 256 + it * 128 + 128],
                                b.vsb[vb][:, es * 512:(es + 1) * 512],
                                start=(jt == 0), stop=(jt == 63))
                    return ins
                Atok = em.push("pe", [vtok, tok_score_done, tok_ic_done,
                                      tok_tr_done], fA, inc="pe")
                vwar[vb] = Atok
            for it in range(2):
                ps = b.ps_a if it == 0 else b.ps_b
                ev = None
                for es in range(4):
                    ev = em.push("dve", [Atok, tok_uload[it]],
                                 lambda v, it=it, es=es, ps=ps: v.tensor_tensor(
                                     b.Bsb[:, it * 2048 + es * 512:
                                           it * 2048 + (es + 1) * 512],
                                     ps[:, es * 512:(es + 1) * 512],
                                     b.usb[it][:, es * 512:(es + 1) * 512],
                                     ALU.mult), inc="dve")
                tok_evac[it] = ev

            tok_evac = {0: em.fence([tok_evac[0]], b.scratch[0:1, 0:1],
                                    b.consts[0:1, 0:1]),
                        1: em.fence([tok_evac[1]], b.scratch[0:1, 0:1],
                                    b.consts[0:1, 0:1])}
            if dbg and bb == 0 and ic == 0:
                dtok = em.dma("sync", [tok_evac[0], tok_evac[1]], dbg["Bsb"],
                              b.Bsb[:, :])
                tok_evac = {0: dtok, 1: dtok}
            cptok = None
            trq = 0
            for it in range(2):
                for et in range(16):
                    ps = (b.ps_a if trq % 2 == 0 else b.ps_b)
                    pt = ps[:, 0:64].bitcast(BF16)
                    trtok = em.push(
                        "pe", [tok_evac[it], tok_id, cptok],
                        lambda t, it=it, et=et, pt=pt: t.transpose(
                            pt[:, 0:128],
                            b.Bsb[:, it * 2048 + et * 128:
                                  it * 2048 + et * 128 + 128],
                            b.ident[:, :]), inc="pe")
                    cptok = em.push(
                        "act", [trtok],
                        lambda s, it=it, et=et, pt=pt: s.copy(
                            b.BTsb[:, et * 256 + it * 128:
                                   et * 256 + it * 128 + 128], pt[:, 0:128]),
                        inc="act")
                    trq += 1
            tok_tr_done = cptok
            if dbg and bb == 0 and ic == 0:
                tok_tr_done = em.dma("sync", [tok_tr_done], dbg["BTsb"],
                                     b.BTsb[:, :])

            last_stt = None
            for dt in range(8):
                slot = dt % 2
                psow = psq[4 + slot]
                rsl = (slice(dt * 128, (dt + 1) * 128),
                       slice(bb * 1024 + ic * 256, bb * 1024 + ic * 256 + 256))
                tok_xr[slot] = em.dma("sync", [tok_st[slot]],
                                      b.xres[slot], xt_d[rsl[0], rsl[1]])

                def fo(t, dt=dt, psow=psow, iofs=iofs):
                    for ec in range(16):
                        t.matmul(psow[:, 0:256],
                                 b.wpool[:, ec * 1024 + dt * 128:
                                         ec * 1024 + dt * 128 + 128],
                                 b.BTsb[:, ec * 256:(ec + 1) * 256],
                                 start=(ec == 0), stop=False)
                    ins = t.matmul(psow[:, 0:256],
                                   b.diag[:, dt * 128:(dt + 1) * 128],
                                   b.y[:, dt * TOKS + iofs:
                                       dt * TOKS + iofs + 256],
                                   start=False, stop=True)
                    return ins
                mmo = em.push("pe", [tok_tr_done, tok_ow, tok_diag,
                                     owps_free[slot]], fo, inc="pe")
                sttok = em.push("dve", [mmo, tok_xr[slot],
                                        store_toks[-2] if len(store_toks) >= 2
                                        else None],
                                lambda v, slot=slot, psow=psow: v.tensor_tensor(
                                    b.ores[slot], psow[:, 0:256],
                                    b.xres[slot], ALU.add),
                                inc="dve")
                tok_st[slot] = sttok
                owps_free[slot] = sttok
                last_stt = sttok
                store_toks.append(em.dma("sync", [sttok],
                                         out_d[rsl[0], rsl[1]],
                                         b.ores[slot]))
            tok_ic_done = last_stt

    em.push("sync", [store_toks[-1]], lambda sy: sy.nop())



def build_attn1(debug=False):
    nc = bass.Bass("TRN2", target_bir_lowering=False)
    xt_d = nc.dram_tensor("xt", [D, TOKS], F32, kind="ExternalInput").ap()
    wqa_d = nc.dram_tensor("wqa", [D, 64], F32, kind="ExternalInput").ap()
    wqb_d = nc.dram_tensor("wqbp", [64, D], F32, kind="ExternalInput").ap()
    wkva_d = nc.dram_tensor("wkva", [D, 32], F32, kind="ExternalInput").ap()
    wkvbk_d = nc.dram_tensor("wkvbk", [32, 256], F32, kind="ExternalInput").ap()
    wkvbv_d = nc.dram_tensor("wkvbv", [32, 256], F32, kind="ExternalInput").ap()
    wout_d = nc.dram_tensor("wout", [D, D], F32, kind="ExternalInput").ap()
    bout_d = nc.dram_tensor("boutc", [128, 8], F32, kind="ExternalInput").ap()
    cos_d = nc.dram_tensor("cosF", [128, LOCP], F32, kind="ExternalInput").ap()
    sin_d = nc.dram_tensor("sinF", [128, LOCP], F32, kind="ExternalInput").ap()
    mask_d = nc.dram_tensor("maskM", [128, 896], F32, kind="ExternalInput").ap()
    vtempl_d = nc.dram_tensor("vtempl", [8 * TOKS, 65], F32, kind="ExternalInput").ap()
    consts_d = nc.dram_tensor("consts", [128, 2], F32, kind="ExternalInput").ap()
    onesrow_d = nc.dram_tensor("onesrow", [1, 128], F32, kind="ExternalInput").ap()
    out_d = nc.dram_tensor("xo", [D, TOKS], F32, kind="ExternalOutput").ap()

    dbg = {}
    if debug:
        for nm, shape, dt in [("qTh", [128, 16384], BF16),
                              ("kTh0", [128, 8192], BF16),
                              ("vaug0", [128, 4160], BF16),
                              ("ccol", [128, 4], F32),
                              ("Asb", [64, 1024], F32),
                              ("dinv", [1, 512], F32),
                              ("exp0", [128, 2048], BF16)]:
            dbg[nm] = nc.dram_tensor("dbg_" + nm, shape, dt,
                                     kind="ExternalOutput").ap()
    ccq_in = nc.dram_tensor("cc1q_in", [1024, TOKS], BF16).ap()
    ccq_out = nc.dram_tensor("cc1q_out", [1024, TOKS], BF16).ap()
    cck_in = nc.dram_tensor("cc1k_in", [512, TOKS], BF16).ap()
    cck_out = nc.dram_tensor("cc1k_out", [512, TOKS], BF16).ap()
    ccv_in = nc.dram_tensor("cc1v_in", [8 * TOKS, 65], BF16).ap()
    ccv_out = nc.dram_tensor("cc1v_out", [8 * TOKS, 65], BF16).ap()
    cco_in = nc.dram_tensor("cc1o_in", [1024, TOKS], BF16).ap()
    cco_out = nc.dram_tensor("cc1o_out", [1024, TOKS], BF16).ap()

    with contextlib.ExitStack() as st:
        E = st.enter_context
        em = Em(nc, st)
        b = Bufs()
        b.big1 = E(nc.sbuf_tensor("big1", [128, 16384], BF16))
        b.big2 = E(nc.sbuf_tensor("big2", [128, 16384], BF16))
        b.big3 = E(nc.sbuf_tensor("big3", [128, 16384], BF16))
        b.xn = b.big1
        b.qTh = b.big1
        b.qTl = b.big3
        b.kTh = b.big3[:, 0:8192]
        b.vaugb = b.big3[:, 8192:12352]
        
        b.outT = b.big2
        b.aT = b.big2[:, 0:8192]
        b.woutsb = b.big2[:, 8192:16384]
        b.expbuf = b.big2[:, 0:4096]
        big2f = b.big2.bitcast(F32)
        b.m_row = big2f[0:1, 0:2048]
        b.rs_row = big2f[0:1, 2048:4096]
        b.kTloc = E(nc.sbuf_tensor("kTloc1", [128, 4096], BF16))
        b.vaugl = E(nc.sbuf_tensor("vaugl", [128, 16 * 260], BF16))
        b.qsT = E(nc.sbuf_tensor("qsT", [64, 2048], BF16))
        b.kvsT = E(nc.sbuf_tensor("kvsT", [32, 2048], BF16))
        b.swb = E(nc.sbuf_tensor("swb", [128, 2048], BF16))
        b.costab = E(nc.sbuf_tensor("costabF", [128, LOCP], F32))
        b.sintab = E(nc.sbuf_tensor("sintabF", [128, LOCP], F32))
        b.maskm = E(nc.sbuf_tensor("maskm", [128, 896], BF16))
        b.wqa = E(nc.sbuf_tensor("wqasb", [128, 8 * 64], BF16))
        b.wqb = E(nc.sbuf_tensor("wqbsb", [64, D], BF16))
        b.wkva = E(nc.sbuf_tensor("wkvasb", [128, 8 * 32], BF16))
        b.wkvbk = E(nc.sbuf_tensor("wkvbksb", [32, 256], BF16))
        b.wkvbv = E(nc.sbuf_tensor("wkvbvsb", [32, 256], BF16))
        b.boutc = E(nc.sbuf_tensor("boutcsb", [128, 8], F32))
        b.Asb = E(nc.sbuf_tensor("Asb", [64, 1024], F32))
        b.odiv = [E(nc.sbuf_tensor(f"odiv{i}", [64, 512], BF16))
                  for i in range(2)]
        b.dinv = E(nc.sbuf_tensor("dinv", [1, 512], F32))
        b.ccol = E(nc.sbuf_tensor("ccol", [128, 4], F32))
        b.crow = E(nc.sbuf_tensor("crow", [1, 4], F32))
        b.consts = E(nc.sbuf_tensor("consts_sb1", [128, 2], F32))
        b.ones_col = b.consts[:, 0:1]
        b.eps_col = b.consts[:, 1:2]
        b.ones_row = E(nc.sbuf_tensor("ones_row1", [1, 128], F32))
        b.xchunk = [E(nc.sbuf_tensor(f"xchunk1_{i}", [128, TOKS], F32))
                    for i in range(2)]
        b.scratch = E(nc.sbuf_tensor("scratch1", [128, 2560], F32))
        b.xsq = [b.scratch[:, 0:512], b.scratch[:, 512:1024]]
        b.ln_tmp = b.scratch[:, 1024:1536]
        b.xres = [b.scratch[:, 512:768], b.scratch[:, 768:1024]]
        b.ores = [b.scratch[:, 1024:1280], b.scratch[:, 1280:1536]]
        b.ps_a = E(nc.psum_tensor("ps_a1", [128, 2048], F32))
        b.ps_b = E(nc.psum_tensor("ps_b1", [128, 2048], F32))
        block = E(nc.Block())

        emit_attn1(em, b, xt_d, wqa_d, wqb_d, wkva_d, wkvbk_d, wkvbv_d,
                   wout_d, bout_d, cos_d, sin_d, mask_d, vtempl_d, consts_d,
                   onesrow_d, out_d, ccq_in, ccq_out, cck_in, cck_out,
                   ccv_in, ccv_out, cco_in, cco_out, dbg)
        em.run(block)
    return nc


def emit_attn1(em, b, xt_d, wqa_d, wqb_d, wkva_d, wkvbk_d, wkvbv_d,
               wout_d, bout_d, cos_d, sin_d, mask_d, vtempl_d, consts_d,
               onesrow_d, out_d, ccq_in, ccq_out, cck_in, cck_out,
               ccv_in, ccv_out, cco_in, cco_out, dbg=None):
    from concourse.aot_env import allow_ici_all_to_all
    SCL = 0.125
    em._tok_consts = em.dma("pool", [], b.consts[:, :], consts_d)
    tok_ones = em.dma("pool", [], b.ones_row[0:1, :], onesrow_d)
    tok_cos = em.dma("pool", [], b.costab[:, :], cos_d)
    tok_sin = em.dma("pool", [], b.sintab[:, :], sin_d)
    tok_mask = em.dma("pool", [], b.maskm[:, :], mask_d)
    tok_bout = em.dma("pool", [], b.boutc[:, :], bout_d)
    tok_wqa = em.dma("pool", [],
                     b.wqa[:, :].rearrange("p (ic f) -> p ic f", ic=8),
                     wqa_d.rearrange("(ic p) f -> p ic f", p=128))
    tok_wqb = em.dma("pool", [], b.wqb[:, :], wqb_d)
    tok_wkva = em.dma("pool", [],
                      b.wkva[:, :].rearrange("p (ic f) -> p ic f", ic=8),
                      wkva_d.rearrange("(ic p) f -> p ic f", p=128))
    tok_wkvbk = em.dma("pool", [], b.wkvbk[:, :], wkvbk_d)
    tok_wkvbv = em.dma("pool", [], b.wkvbv[:, :], wkvbv_d)
    tok_vt = em.dma("pool", [], ccv_in, vtempl_d)

    tok_xn = emit_ln(em, b, TOKS, xt_d, b.xn, 0,
                     extra_waits=(tok_ones, em._tok_consts))
    tok_xn = em.fence([tok_xn], b.scratch[0:1, 0:1], b.consts[0:1, 0:1])

    psq = [b.ps_a[:, i * 512:(i + 1) * 512] for i in range(4)] + \
          [b.ps_b[:, i * 512:(i + 1) * 512] for i in range(4)]

    def fqs(t):
        for cg in range(4):
            for ic in range(8):
                ins = t.matmul(b.ps_a[0:64, cg * 512:(cg + 1) * 512],
                               b.wqa[:, ic * 64:(ic + 1) * 64],
                               b.xn[:, ic * TOKS + cg * 512:
                                    ic * TOKS + cg * 512 + 512],
                               start=(ic == 0), stop=(ic == 7))
        return ins
    tok_qs_mm = em.push("pe", [tok_wqa, tok_xn], fqs, inc="pe")
    tok_qs = em.push("act", [tok_qs_mm],
                     lambda s: s.copy(b.qsT[:, :], b.ps_a[0:64, 0:2048]),
                     inc="act")

    def fkvs(t):
        for cg in range(4):
            for ic in range(8):
                ins = t.matmul(b.ps_b[0:32, cg * 512:(cg + 1) * 512],
                               b.wkva[:, ic * 32:(ic + 1) * 32],
                               b.xn[:, ic * TOKS + cg * 512:
                                    ic * TOKS + cg * 512 + 512],
                               start=(ic == 0), stop=(ic == 7))
        return ins
    tok_kvs_mm = em.push("pe", [tok_wkva, tok_xn], fkvs, inc="pe")
    tok_kvs = em.push("act", [tok_kvs_mm],
                      lambda s: s.copy(b.kvsT[:, :], b.ps_b[0:32, 0:2048]),
                      inc="act")
    tok_sfence = em.push("act", [tok_qs, tok_kvs],
                         lambda s: s.copy(b.crow[0:1, 3:4], b.consts[0:1, 0:1]),
                         inc="act")

    ps_free = [tok_qs_mm] * 8
    qctr = 0
    tok_qtl = None
    for et in range(8):
        for cg in range(4):
            q = qctr % 8
            qctr += 1
            mm = em.push("pe", [tok_wqb, tok_sfence, ps_free[q]],
                         lambda t, et=et, cg=cg, q=q: t.matmul(
                             psq[q], b.wqb[:, et * 128:(et + 1) * 128],
                             b.qsT[0:64, cg * 512:(cg + 1) * 512],
                             start=True, stop=True), inc="pe")
            tok_qtl = em.push("act", [mm],
                              lambda s, et=et, cg=cg, q=q: s.copy(
                                  b.qTl[:, et * 2048 + cg * 512:
                                        et * 2048 + cg * 512 + 512], psq[q]),
                              inc="act")
            ps_free[q] = tok_qtl
    tok_ktl = None
    for kt in range(2):
        for cg in range(4):
            q = qctr % 8
            qctr += 1
            mm = em.push("pe", [tok_wkvbk, tok_sfence, ps_free[q]],
                         lambda t, kt=kt, cg=cg, q=q: t.matmul(
                             psq[q], b.wkvbk[:, kt * 128:(kt + 1) * 128],
                             b.kvsT[0:32, cg * 512:(cg + 1) * 512],
                             start=True, stop=True), inc="pe")
            tok_ktl = em.push("act", [mm],
                              lambda s, kt=kt, cg=cg, q=q: s.copy(
                                  b.kTloc[:, kt * 2048 + cg * 512:
                                          kt * 2048 + cg * 512 + 512], psq[q]),
                              inc="act")
            ps_free[q] = tok_ktl
    tok_vl = None
    for tt in range(16):
        q = qctr % 8
        qctr += 1
        mm = em.push("pe", [tok_wkvbv, tok_sfence, ps_free[q]],
                     lambda t, tt=tt, q=q: t.matmul(
                         psq[q][:, 0:256], b.kvsT[0:32, tt * 128:(tt + 1) * 128],
                         b.wkvbv[:, :], start=True, stop=True), inc="pe")
        vsl = b.vaugl[:, tt * 260:(tt + 1) * 260].rearrange(
            "p (kh x) -> p kh x", kh=4)[:, :, 0:64]
        tok_vl = em.push("act", [mm],
                         lambda s, q=q, vsl=vsl: s.copy(
                             vsl, psq[q][:, 0:256].rearrange(
                                 "p (kh x) -> p kh x", kh=4)),
                         inc="act")
        ps_free[q] = tok_vl

    tok_rope = None
    tiles = [("q", et) for et in range(8)] + [("k", kt) for kt in range(2)]
    for kind, ti in tiles:
        srcb = b.qTl if kind == "q" else b.kTloc
        c0 = ti * 2048
        dep = tok_qtl if kind == "q" else tok_ktl
        sw_toks = []
        for (d0, s0) in ((0, 32), (32, 0), (64, 96), (96, 64)):
            sw_toks.append(em.dma(
                "pool", [dep, tok_rope],
                b.swb[d0:d0 + 32, :], srcb[s0:s0 + 32, c0:c0 + 2048]))
        for bb in range(2):
            sl = slice(c0 + bb * LOCP, c0 + bb * LOCP + LOCP)
            swsl = slice(bb * LOCP, bb * LOCP + LOCP)
            em.push("dve", [dep, *sw_toks, tok_cos, tok_sin],
                    lambda v, srcb=srcb, sl=sl: v.tensor_tensor(
                        b.scratch[:, 0:1024], srcb[:, sl], b.costab[:, :],
                        ALU.mult))
            em.push("dve", [],
                    lambda v, swsl=swsl: v.tensor_tensor(
                        b.scratch[:, 1024:2048], b.swb[:, swsl],
                        b.sintab[:, :], ALU.mult))
            tok_rope = em.push("dve", [],
                               lambda v, srcb=srcb, sl=sl: v.tensor_tensor(
                                   srcb[:, sl], b.scratch[:, 0:1024],
                                   b.scratch[:, 1024:2048], ALU.add),
                               inc="dve")

    ctoks = []
    for j in range(8):
        ctoks.append(em.dma("sync", [tok_rope],
                            ccq_in[j * 128:(j + 1) * 128, :],
                            b.qTl[:, j * 2048:(j + 1) * 2048]))
    for j in range(8):
        kh = j // 2
        kt, half = kh // 2, kh % 2
        ctoks.append(em.dma("sync", [tok_rope],
                            cck_in[j * 64:(j + 1) * 64, :],
                            b.kTloc[half * 64:(half + 1) * 64,
                                    kt * 2048:(kt + 1) * 2048]))
    for j in range(8):
        kh = j // 2
        vsl = b.vaugl[:, :].rearrange("p (tt r) -> p tt r", tt=16)[
            :, :, kh * 65:kh * 65 + 64]
        dsl = ccv_in[j * 2048:(j + 1) * 2048, 0:64].rearrange(
            "(tt p) r -> p tt r", p=128)
        ctoks.append(em.dma("sync", [tok_vl, tok_vt], dsl, vsl))

    def a2a(g, cin, cout):
        with allow_ici_all_to_all():
            return g.collective_compute(
                "AllToAll", ALU.bypass, replica_groups=[list(range(8))],
                ins=[cin.opt()], outs=[cout.opt()])
    tok_ccq = em.push("pool", ctoks, lambda g: a2a(g, ccq_in, ccq_out),
                      inc=em.next_cc())
    tok_cck = em.push("pool", [], lambda g: a2a(g, cck_in, cck_out),
                      inc=em.next_cc())
    tok_ccv = em.push("pool", [], lambda g: a2a(g, ccv_in, ccv_out),
                      inc=em.next_cc())

    tok_qth = None
    for bb in range(2):
        for s in range(8):
            tok_qth = em.dma("sync", [tok_ccq],
                             b.qTh[:, bb * 8192 + s * 1024:
                                   bb * 8192 + s * 1024 + 1024],
                             ccq_out[s * 128:(s + 1) * 128,
                                     bb * 1024:(bb + 1) * 1024])
    tok_out = None
    pvps = [b.ps_b[:, 0:512], b.ps_b[:, 512:1024]]
    dbps = [b.ps_b[:, 1024:1536], b.ps_b[:, 1024:1536]]
    junkps = b.ps_b[:, 1536:2048]
    scpair = [b.ps_a[:, 0:1024], b.ps_a[:, 1024:2048]]

    def _later(a, b_):
        if a is None:
            return b_
        if b_ is None:
            return a
        assert a[0] == b_[0]
        return a if a[1] >= b_[1] else b_
    sc_free = [_later(ps_free[0], ps_free[1]), _later(ps_free[2], ps_free[3])]
    pv_free = [ps_free[4], ps_free[5]]
    db_free = [ps_free[6], ps_free[7]]
    exp_free = [None] * 4
    tok_vaugb = None
    pvctr = 0
    otoks = []
    ostore = [None, None]
    ca_war = [None, None]
    tok_kth = None
    tok_kth_war = None
    for bb in range(2):
        for half in range(2):
            for s in range(8):
                tok_kth = em.dma("sync", [tok_cck, tok_kth_war],
                                 b.kTh[half * 64:(half + 1) * 64,
                                       s * 1024:(s + 1) * 1024],
                                 cck_out[s * 64:(s + 1) * 64,
                                         bb * 1024:(bb + 1) * 1024])
        em.push("dve", [tok_qth, tok_kth],
                lambda v, bb=bb: v.tensor_reduce(
                    b.ccol[:, 2:3], b.qTh[:, bb * 8192:(bb + 1) * 8192],
                    AX.X, ALU.max, apply_absolute_value=True))
        em.push("dve", [],
                lambda v: v.tensor_tensor(b.ccol[:, 2:3], b.ccol[:, 2:3],
                                          b.ccol[:, 2:3], ALU.mult))
        em.push("dve", [],
                lambda v: v.tensor_reduce(
                    b.ccol[0:64, 3:4], b.kTh[0:64, :],
                    AX.X, ALU.max, apply_absolute_value=True))
        tok_bnd = em.push("dve", [],
                          lambda v: v.tensor_tensor(
                              b.ccol[0:64, 3:4], b.ccol[0:64, 3:4],
                              b.ccol[0:64, 3:4], ALU.mult), inc="dve")
        mm1 = em.push("pe", [tok_bnd, ps_free[2], sc_free[1]],
                      lambda t: t.matmul(b.ps_a[0:1, 1024:1025],
                                         b.ones_col, b.ccol[:, 2:3],
                                         start=True, stop=True), inc="pe")
        mm2 = em.push("pe", [],
                      lambda t: t.matmul(b.ps_a[0:1, 1026:1027],
                                         b.ones_col[0:64, :], b.ccol[0:64, 3:4],
                                         start=True, stop=True), inc="pe")
        em.push("act", [mm1, mm2],
                lambda s: s.activation(b.crow[0:1, 0:1], b.ps_a[0:1, 1024:1025],
                                       AF.Sqrt))
        tok_c1 = em.push("act", [],
                         lambda s: s.activation(b.crow[0:1, 1:2],
                                                b.ps_a[0:1, 1026:1027],
                                                AF.Sqrt), inc="act")
        em.push("dve", [tok_c1],
                lambda v: v.tensor_tensor(b.crow[0:1, 0:1], b.crow[0:1, 0:1],
                                          b.crow[0:1, 1:2], ALU.mult))
        tok_c2 = em.push("dve", [],
                         lambda v: v.tensor_scalar(b.crow[0:1, 0:1],
                                                   b.crow[0:1, 0:1],
                                                   -SCL, None, ALU.mult),
                         inc="dve")
        mm3 = em.push("pe", [tok_c2, tok_ones],
                      lambda t: t.matmul(b.ps_a[0:128, 1028:1029],
                                         b.ones_row[0:1, :], b.crow[0:1, 0:1],
                                         start=True, stop=True), inc="pe")
        tok_negc = em.push("dve", [mm3],
                           lambda v, bb=bb: v.tensor_copy(
                               b.ccol[:, bb:bb + 1], b.ps_a[:, 1028:1029]),
                           inc="dve")
        va_toks = []
        for s in range(8):
            va_toks.append(em.dma(
                "sync", [tok_ccv, tok_vaugb],
                b.vaugb.rearrange("p (jt r) -> p jt r", jt=64)[:, s * 8:
                                                              (s + 1) * 8, :],
                ccv_out[s * 2048 + bb * 1024:s * 2048 + bb * 1024 + 1024,
                        :].rearrange("(jt p) r -> p jt r", p=128)))
        tok_va = va_toks[-1]
        for iblk in range(16):
            for hl in range(2):
                pvp = pvps[pvctr % 2]
                njt = 4 * (iblk + 1)
                npair = njt // 2
                ptoks = {}
                pvtok = None
                for jp in range(npair + 2):
                    if jp < npair:
                        pr = jp % 2
                        scp = scpair[pr]
                        mm = None
                        for half in range(2):
                            jt = 2 * jp + half
                            mm = em.push(
                                "pe", [tok_qth, tok_kth, sc_free[pr],
                                       tok_negc],
                                lambda t, jt=jt, bb=bb, hl=hl, iblk=iblk,
                                scp=scp, half=half: t.matmul(
                                    scp[:, half * 512:(half + 1) * 512],
                                    b.kTh[hl * 64:(hl + 1) * 64,
                                          jt * 128:(jt + 1) * 128],
                                    b.qTh[hl * 64:(hl + 1) * 64,
                                          bb * 8192 + iblk * 512:
                                          bb * 8192 + iblk * 512 + 512],
                                    start=True, stop=True), inc="pe")
                        eb = jp % 4
                        etok = em.push(
                            "act", [mm, tok_negc, exp_free[eb]],
                            lambda s, eb=eb, scp=scp, bb=bb: s.activation(
                                b.expbuf[:, eb * 1024:(eb + 1) * 1024], scp,
                                AF.Exp, bias=b.ccol[:, bb:bb + 1], scale=SCL),
                            inc="act")
                        sc_free[pr] = etok
                        mtoks = [etok, etok]
                        for half in range(2):
                            jt = 2 * jp + half
                            if jt >= 4 * iblk:
                                m0 = 384 - 128 * (jt - 4 * iblk)
                                col = eb * 1024 + half * 512
                                mtoks[half] = em.push(
                                    "dve", [etok, tok_mask],
                                    lambda v, col=col, m0=m0: v.tensor_tensor(
                                        b.expbuf[:, col:col + 512],
                                        b.expbuf[:, col:col + 512],
                                        b.maskm[:, m0:m0 + 512], ALU.mult),
                                    inc="dve")
                        ptoks[jp] = mtoks
                    jq = jp - 2
                    if jq >= 0:
                        ebq = jq % 4
                        mtoks_q = ptoks.pop(jq)
                        for half in range(2):
                            jt = 2 * jq + half
                            col = ebq * 1024 + half * 512
                            pvtok = em.push(
                                "pe", [mtoks_q[half], tok_va,
                                       pv_free[pvctr % 2]],
                                lambda t, jt=jt, col=col, pvp=pvp, njt=njt:
                                t.matmul(
                                    pvp[0:65, :],
                                    b.vaugb[:, jt * 65:jt * 65 + 65],
                                    b.expbuf[:, col:col + 512],
                                    start=(jt == 0), stop=(jt == njt - 1)),
                                inc="pe")
                        exp_free[ebq] = pvtok
                        em.push("pe", [],
                                lambda t: t.matmul(
                                    junkps[:, 0:512], b.maskm[:, 0:128],
                                    b.maskm[:, 0:512], start=True, stop=True))
                slot = pvctr % 2
                asl = slice(slot * 512, slot * 512 + 512)
                dtok = em.push("dve", [pvtok, db_free[slot]],
                               lambda v, pvp=pvp: v.reciprocal(
                                   b.dinv[0:1, :], pvp[64:65, :]), inc="dve")
                catok = em.push("dve", [pvtok, ca_war[slot]],
                                lambda v, pvp=pvp, asl=asl: v.tensor_copy(
                                    b.Asb[:, asl], pvp[0:64, :]),
                                inc="dve")
                pv_free[slot] = catok
                dbp = dbps[slot]
                mmD = em.push("pe", [dtok, db_free[slot]],
                              lambda t, dbp=dbp: t.matmul(
                                  dbp[0:64, :], b.ones_row[0:1, 0:64],
                                  b.dinv[0:1, :], start=True, stop=True),
                              inc="pe")
                divtok = em.push(
                    "dve", [mmD, catok],
                    lambda v, slot=slot, asl=asl, dbp=dbp: v.tensor_tensor(
                        b.odiv[slot][:, :], b.Asb[:, asl], dbp[0:64, :], ALU.mult),
                    inc="dve")
                db_free[slot] = divtok
                ca_war[slot] = divtok
                jdst = iblk // 2
                tok_out = em.dma(
                    "sync", [divtok, ostore[slot]],
                    cco_in[jdst * 128 + hl * 64:jdst * 128 + hl * 64 + 64,
                           bb * 1024 + (iblk % 2) * 512:
                           bb * 1024 + (iblk % 2) * 512 + 512],
                    b.odiv[slot][:, :])
                ostore[slot] = tok_out
                otoks.append(tok_out)
                if dbg and bb == 0 and hl == 0 and iblk == 0:
                    d1 = em.dma("sync", [divtok], dbg["Asb"], b.Asb[:, :])
                    d2 = em.dma("sync", [divtok], dbg["dinv"],
                                b.dinv[0:1, :])
                    d3 = em.dma("sync", [divtok], dbg["exp0"],
                                b.expbuf[:, :])
                    d4 = em.dma("sync", [divtok], dbg["qTh"], b.qTh[:, :])
                    d5 = em.dma("sync", [divtok], dbg["kTh0"],
                                b.kTh[:, :])
                    d6 = em.dma("sync", [divtok], dbg["vaug0"],
                                b.vaugb[:, :])
                    d7 = em.dma("sync", [divtok], dbg["ccol"], b.ccol[:, :])
                    ostore[slot] = d7
                pvctr += 1
        tok_vaugb = tok_out
        tok_kth_war = em.cnt_tok("pe")

    tok_cco = em.push("pool", otoks, lambda g: a2a(g, cco_in, cco_out),
                      inc=em.next_cc())

    tok_wout = em.dma("pool", [tok_cco],
                      b.woutsb.rearrange("p (ec dd) -> p ec dd", ec=8),
                      wout_d.rearrange("(ec p) dd -> p ec dd", p=128))
    at_toks = {}
    store_toks = []
    tok_st = [None, None]
    ow_free = [None, None]
    for bb in range(2):
        tok_at = None
        for ec in range(8):
            tok_at = em.dma("sync", [tok_cco, at_toks.get("war")],
                            b.aT[:, ec * 1024:(ec + 1) * 1024],
                            cco_out[ec * 128:(ec + 1) * 128,
                                    bb * 1024:(bb + 1) * 1024])
        at_toks[bb] = tok_at
        for ic in range(4):
            for dt in range(8):
                slot = dt % 2
                psow = psq[4 + slot]
                rsl = (slice(dt * 128, (dt + 1) * 128),
                       slice(bb * 1024 + ic * 256, bb * 1024 + ic * 256 + 256))
                xr = em.dma("sync", [tok_st[slot]], b.xres[slot],
                            xt_d[rsl[0], rsl[1]])

                def fo(t, dt=dt, ic=ic, psow=psow):
                    for ec in range(8):
                        ins = t.matmul(psow[:, 0:256],
                                       b.woutsb[:, ec * 1024 + dt * 128:
                                                ec * 1024 + dt * 128 + 128],
                                       b.aT[:, ec * 1024 + ic * 256:
                                            ec * 1024 + ic * 256 + 256],
                                       start=(ec == 0), stop=(ec == 7))
                    return ins
                mmo = em.push("pe", [tok_at, tok_wout, ow_free[slot]], fo,
                              inc="pe")
                sttok = em.push(
                    "dve", [mmo, xr, tok_bout,
                            store_toks[-2] if len(store_toks) >= 2 else None],
                    lambda v, slot=slot, psow=psow, dt=dt: v.scalar_tensor_tensor(
                        b.ores[slot], psow[:, 0:256], b.boutc[:, dt:dt + 1],
                        b.xres[slot], ALU.add, ALU.add),
                    inc="dve")
                tok_st[slot] = sttok
                ow_free[slot] = sttok
                store_toks.append(em.dma("sync", [sttok],
                                         out_d[rsl[0], rsl[1]],
                                         b.ores[slot]))
        at_toks["war"] = em.cnt_tok("pe")

    em.push("sync", [store_toks[-1]], lambda sy: sy.nop())



class Pool:
    def __init__(self, ap_f32):
        self.ap = ap_f32
        self.off = 0

    def reset(self):
        self.off = 0

    def f32(self, cols):
        s = self.ap[:, self.off:self.off + cols]
        self.off += cols
        return s

    def bf16(self, cols):
        need = (cols + 1) // 2
        return self.f32(need).bitcast(BF16)[:, 0:cols]


def mk_ffn_bufs(p, shared):
    b = Bufs()
    b.__dict__.update(shared)
    b.c32 = p.f32(8 * LOCP)
    b.c_bf = p.bf16(8 * LOCP)
    b.xn = p.bf16(8 * TOKS)
    b.gate = [p.bf16(8 * 512) for _ in range(3)]
    b.wbuf = [p.bf16(8 * D) for _ in range(4)]
    b.xchunk = [b.c32[:, 0:TOKS], b.c32[:, TOKS:2 * TOKS]]
    b.scratch = p.f32(2560)
    b.xsq = [b.scratch[:, 0:512], b.scratch[:, 512:1024]]
    b.ln_tmp = b.scratch[:, 1024:1536]
    b.t1 = b.scratch[:, 0:512]
    b.t2 = b.scratch[:, 512:1024]
    b.hbuf = [b.scratch[:, 1024:1280].bitcast(BF16),
              b.scratch[:, 1280:1536].bitcast(BF16)]
    b.xres = [b.scratch[:, 1536:2048], b.scratch[:, 2048:2560]]
    b.ores = [b.scratch[:, 0:512], b.scratch[:, 512:1024]]
    b.biases = p.f32(40)
    b.m_row = p.f32(2048)[0:1, :]
    b.rs_row = p.f32(2048)[0:1, :]
    return b


def mk_attn0_bufs(p, shared):
    b = Bufs()
    b.__dict__.update(shared)
    b.bigbuf = p.bf16(16384)
    b.y = p.bf16(16384)
    b.wpool = p.bf16(16384)
    b.kT = p.bf16(8192)
    b.qT = p.bf16(2048)
    b.kTloc = p.bf16(2048)
    b.vsb = [p.bf16(2048) for _ in range(2)]
    b.usb = [p.bf16(2048) for _ in range(2)]
    b.tileAB = p.bf16(4096)
    b.utile = b.tileAB[:, 0:2048]
    b.vtile = b.tileAB[:, 2048:4096]
    b.Bsb = b.tileAB
    b.BTsb = p.bf16(4096)
    b.basebuf = p.f32(2048)
    b.vsb += [b.basebuf.bitcast(BF16)[:, 0:2048],
              b.basebuf.bitcast(BF16)[:, 2048:4096]]
    b.uvwb = p.bf16(8 * 128)
    b.ident = p.bf16(128)
    b.diag = p.bf16(8 * 128)
    b.corner = [p.f32(256) for _ in range(2)]
    b.costab = p.f32(LOCP)
    b.sintab = p.f32(LOCP)
    b.gb = p.f32(4)
    b.rsc = p.f32(8)
    b.smalls = p.f32(4)
    b.gsb = p.f32(2)[0:1, :]
    b.xchunk = [p.f32(TOKS) for _ in range(2)]
    b.scratch = p.f32(2560)
    b.xsq = [b.scratch[:, 0:512], b.scratch[:, 512:1024]]
    b.ln_tmp = b.scratch[:, 1024:1536]
    b.ropet = [b.scratch[:, 0:1024], b.scratch[:, 1024:2048]]
    b.rtile = b.scratch[:, 0:512]
    b.xres = [b.scratch[:, 512:768], b.scratch[:, 768:1024]]
    b.ores = [b.scratch[:, 1024:1280], b.scratch[:, 1280:1536]]
    ktf = b.kT.bitcast(F32)
    b.m_row = ktf[0:1, 0:2048]
    b.rs_row = ktf[0:1, 2048:4096]
    b.v_row = b.BTsb.bitcast(F32)[0:1, 0:2048]
    b.t_row = b.v_row
    b.rtile4 = b.xchunk[0]
    return b


def mk_attn1_bufs(p, shared):
    b = Bufs()
    b.__dict__.update(shared)
    b.big1 = p.bf16(16384)
    b.big2 = p.bf16(16384)
    b.big3 = p.bf16(16384)
    b.xn = b.big1
    b.qTh = b.big1
    b.qTl = b.big3
    b.kTh = b.big3[:, 0:8192]
    b.vaugb = b.big3[:, 8192:12352]
    b.outT = b.big2
    b.aT = b.big2[:, 0:8192]
    b.woutsb = b.big2[:, 8192:16384]
    b.expbuf = b.big2[:, 0:4096]
    big2f = b.big2.bitcast(F32)
    b.m_row = big2f[0:1, 0:2048]
    b.rs_row = big2f[0:1, 2048:4096]
    b.kTloc = p.bf16(4096)
    b.vaugl = p.bf16(16 * 260)
    b.qsT = p.bf16(2048)[0:64, :]
    b.kvsT = p.bf16(2048)[0:32, :]
    b.swb = p.bf16(2048)
    b.costab = p.f32(LOCP)
    b.sintab = p.f32(LOCP)
    b.maskm = p.bf16(896)
    b.wqa = p.bf16(8 * 64)
    b.wqb = p.bf16(D)[0:64, :]
    b.wkva = p.bf16(8 * 32)
    b.wkvbk = p.bf16(256)[0:32, :]
    b.wkvbv = p.bf16(256)[0:32, :]
    b.boutc = p.f32(8)
    b.Asb = p.f32(1024)[0:64, :]
    b.odiv = [p.bf16(512)[0:64, :] for _ in range(2)]
    b.dinv = p.f32(512)[0:1, :]
    b.ccol = p.f32(4)
    b.crow = p.f32(4)[0:1, :]
    b.xchunk = [p.f32(TOKS) for _ in range(2)]
    b.scratch = p.f32(2560)
    b.xsq = [b.scratch[:, 0:512], b.scratch[:, 512:1024]]
    b.ln_tmp = b.scratch[:, 1024:1536]
    b.xres = [b.scratch[:, 512:768], b.scratch[:, 768:1024]]
    b.ores = [b.scratch[:, 1024:1280], b.scratch[:, 1280:1536]]
    return b


def build_fused():
    nc = bass.Bass("TRN2", target_bir_lowering=False)
    xt_d = nc.dram_tensor("xt", [D, TOKS], F32, kind="ExternalInput").ap()
    out_d = nc.dram_tensor("xo", [D, TOKS], F32, kind="ExternalOutput").ap()
    xs = [nc.dram_tensor(f"xs{i}", [D, TOKS], F32).ap() for i in range(5)]

    wd = {}
    wdb = {}
    for L in (0, 1):
        for n in ("wf", "uf", "wi", "ui", "wc", "uc", "wo", "uo"):
            wd[(L, n)] = nc.dram_tensor(f"l{L}_{n}T", [D, D], F32,
                                        kind="ExternalInput").ap()
            wdb[(L, n)] = nc.dram_tensor(f"l{L}_{n}b", [D, D], BF16).ap()
    bias_d = {L: nc.dram_tensor(f"l{L}_biases", [128, 40], F32,
                                kind="ExternalInput").ap() for L in (0, 1)}
    uvw_d = nc.dram_tensor("uvwT", [D, 4224], F32, kind="ExternalInput").ap()
    ow_d = nc.dram_tensor("owT", [2048, D], F32, kind="ExternalInput").ap()
    gb_d = nc.dram_tensor("gbcols", [128, 4], F32, kind="ExternalInput").ap()
    g_d = nc.dram_tensor("gval", [1, 1], F32, kind="ExternalInput").ap()
    w2046_d = nc.dram_tensor("w2046", [1, 1], F32, kind="ExternalInput").ap()
    corner_d = nc.dram_tensor("corner", [64 * 128, 256], F32,
                              kind="ExternalInput").ap()
    rsc_d = nc.dram_tensor("rsc", [128, 8], F32, kind="ExternalInput").ap()
    cos0_d = nc.dram_tensor("cosT", [128, LOCP], F32, kind="ExternalInput").ap()
    sin0_d = nc.dram_tensor("sinT", [128, LOCP], F32, kind="ExternalInput").ap()
    ident_d = nc.dram_tensor("identb", [128, 128], F32, kind="ExternalInput").ap()
    wqa_d = nc.dram_tensor("wqa", [D, 64], F32, kind="ExternalInput").ap()
    wqb_d = nc.dram_tensor("wqbp", [64, D], F32, kind="ExternalInput").ap()
    wkva_d = nc.dram_tensor("wkva", [D, 32], F32, kind="ExternalInput").ap()
    wkvbk_d = nc.dram_tensor("wkvbk", [32, 256], F32, kind="ExternalInput").ap()
    wkvbv_d = nc.dram_tensor("wkvbv", [32, 256], F32, kind="ExternalInput").ap()
    wout_d = nc.dram_tensor("wout", [D, D], F32, kind="ExternalInput").ap()
    bout_d = nc.dram_tensor("boutc", [128, 8], F32, kind="ExternalInput").ap()
    cos1_d = nc.dram_tensor("cosF", [128, LOCP], F32, kind="ExternalInput").ap()
    sin1_d = nc.dram_tensor("sinF", [128, LOCP], F32, kind="ExternalInput").ap()
    mask_d = nc.dram_tensor("maskM", [128, 896], F32, kind="ExternalInput").ap()
    vtempl_d = nc.dram_tensor("vtempl", [8 * TOKS, 65], F32,
                              kind="ExternalInput").ap()
    consts_d = nc.dram_tensor("consts", [128, 2], F32, kind="ExternalInput").ap()
    onesrow_d = nc.dram_tensor("onesrow", [1, 128], F32,
                               kind="ExternalInput").ap()
    cc_in = nc.dram_tensor("cc0_in", [2176, 2048], BF16).ap()
    cc_out = nc.dram_tensor("cc0_out", [8 * 2176, 2048], BF16).ap()
    cck0_in = nc.dram_tensor("cck0_in", [128, 2048], BF16).ap()
    cck0_out = nc.dram_tensor("cck0_out", [8 * 128, 2048], BF16).ap()
    ccv0_in = nc.dram_tensor("ccv0_in", [1024, 2048], BF16).ap()
    ccv1_in = nc.dram_tensor("ccv1_in", [1024, 2048], BF16).ap()
    ccv0_out = nc.dram_tensor("ccv0_out", [8 * 1024, 2048], BF16).ap()
    ccv1_out = nc.dram_tensor("ccv1_out", [8 * 1024, 2048], BF16).ap()
    u_spill = nc.dram_tensor("u_spill", [2048, 2048], BF16).ap()
    ccq_in = nc.dram_tensor("cc1q_in", [1024, TOKS], BF16).ap()
    ccq_out = nc.dram_tensor("cc1q_out", [1024, TOKS], BF16).ap()
    cck_in = nc.dram_tensor("cc1k_in", [512, TOKS], BF16).ap()
    cck_out = nc.dram_tensor("cc1k_out", [512, TOKS], BF16).ap()
    ccv_in = nc.dram_tensor("cc1v_in", [8 * TOKS, 65], BF16).ap()
    ccv_out = nc.dram_tensor("cc1v_out", [8 * TOKS, 65], BF16).ap()
    cco_in = nc.dram_tensor("cc1o_in", [1024, TOKS], BF16).ap()
    cco_out = nc.dram_tensor("cc1o_out", [1024, TOKS], BF16).ap()

    with contextlib.ExitStack() as st:
        E = st.enter_context
        em = Em(nc, st)
        shared = {}
        consts_sb = E(nc.sbuf_tensor("consts_sb", [128, 2], F32))
        onesrow_sb = E(nc.sbuf_tensor("ones_row", [1, 128], F32))
        tiny = E(nc.sbuf_tensor("tiny", [1, 4], F32))
        shared["consts"] = consts_sb[:, :]
        shared["ones_col"] = consts_sb[:, 0:1]
        shared["eps_col"] = consts_sb[:, 1:2]
        shared["ones_row"] = onesrow_sb[:, :]
        pool_t = E(nc.sbuf_tensor("pool", [128, 51456], F32))
        p = Pool(pool_t[:, :])
        shared["ps_a"] = E(nc.psum_tensor("ps_a", [128, 2048], F32))[:, :]
        shared["ps_b"] = E(nc.psum_tensor("ps_b", [128, 2048], F32))[:, :]
        block = E(nc.Block())

        em._tok_consts = em.dma("pool", [], shared["consts"], consts_d)
        em.dma("pool", [], shared["ones_row"], onesrow_d)
        wready = {}
        for L in (0, 1):
            for n in ("wf", "uf", "wi", "ui", "wc", "uc", "wo", "uo"):
                wready[L] = em.dma("pool", [], wdb[(L, n)], wd[(L, n)])

        def bar():
            em.barrier(tiny[0:1, 0:1], shared["consts"][0:1, 0:1])

        stages = [
            ("ffn", 0, xt_d, xs[0]),
            ("attn0", None, xs[0], xs[1]),
            ("ffn", 0, xs[1], xs[2]),
            ("ffn", 1, xs[2], xs[3]),
            ("attn1", None, xs[3], xs[4]),
            ("ffn", 1, xs[4], out_d),
        ]
        for kind, L, xin, xout in stages:
            p.reset()
            if kind == "ffn":
                b = mk_ffn_bufs(p, shared)
                w_d = {n: wdb[(L, n)] for n in
                       ("wf", "uf", "wi", "ui", "wc", "uc", "wo", "uo")}
                em._w_ready = wready[L]
                emit_ffn_body(em, b, xin, w_d, bias_d[L], xout,
                              consts_d=consts_d, onesrow_d=onesrow_d)
                em._w_ready = None
            elif kind == "attn0":
                b = mk_attn0_bufs(p, shared)
                emit_attn0(em, b, xin, uvw_d, ow_d, gb_d, g_d, w2046_d,
                           corner_d, rsc_d, cos0_d, sin0_d, ident_d,
                           consts_d, onesrow_d, xout, cc_in, cc_out, u_spill,
                           cck0=(cck0_in, cck0_out), ccv0_out=ccv0_out,
                           ccv1_out=ccv1_out, ccv0_in=ccv0_in,
                           ccv1_in=ccv1_in)
            else:
                b = mk_attn1_bufs(p, shared)
                emit_attn1(em, b, xin, wqa_d, wqb_d, wkva_d, wkvbk_d,
                           wkvbv_d, wout_d, bout_d, cos1_d, sin1_d, mask_d,
                           vtempl_d, consts_d, onesrow_d, xout,
                           ccq_in, ccq_out, cck_in, cck_out,
                           ccv_in, ccv_out, cco_in, cco_out)
            bar()
        em.run(block)
    return nc


def run_fused(inputs):
    nc = _CACHE.get("nc_fused")
    if nc is None:
        nc = _CACHE["nc_fused"] = build_fused()
    x = np.ascontiguousarray(inputs["L0_attn_x"], dtype=np.float32)
    shards = shard_xT(x)
    consts = np.stack([np.ones(128, np.float32),
                       np.full(128, EPS, np.float32)], axis=1)
    onesrow = np.ones((1, 128), np.float32)
    w0, bias0 = _ffn_inputs(inputs, 0)
    w1, bias1 = _ffn_inputs(inputs, 1)
    a0 = _attn0_inputs(inputs, 0)
    a1 = _attn1_inputs(inputs)
    in_maps = []
    for c in range(NCORE):
        m = dict(xt=shards[c], consts=consts, onesrow=onesrow,
                 l0_biases=bias0, l1_biases=bias1,
                 uvwT=a0["uvwT"], owT=a0["owT"], gbcols=a0["gbcols"],
                 gval=a0["gval"], w2046=a0["w2046"], corner=a0["corners"][c],
                 rsc=a0["rsc"], cosT=a0["cosT"][c], sinT=a0["sinT"][c],
                 identb=a0["identb"],
                 wqa=a1["wqa"], wqbp=a1["wqbp"], wkva=a1["wkva"],
                 wkvbk=a1["wkvbk"], wkvbv=a1["wkvbv"], wout=a1["wout"],
                 boutc=a1["boutc"], cosF=a1["cosF"][c], sinF=a1["sinF"][c],
                 maskM=a1["maskM"], vtempl=a1["vtempl"])
        for n in ("wf", "uf", "wi", "ui", "wc", "uc", "wo", "uo"):
            m[f"l0_{n}T"] = w0[n + "T"]
            m[f"l1_{n}T"] = w1[n + "T"]
        in_maps.append(m)
    res = run_bass_kernel_spmd(nc, in_maps, core_ids=list(range(NCORE)))
    return unshard_xT([r["xo"] for r in res.results])


_CACHE = {}


def _ffn_inputs(inputs, L):
    key = "ffn_w_" + str(L)
    if key not in _CACHE:
        p = f"L{L}_ffn_"
        w = {n + "T": transpose_w(inputs[p + n])
             for n in ("wf", "uf", "wi", "ui", "wc", "uc", "wo", "uo")}
        bias = np.concatenate([colpack(inputs[p + n])
                               for n in ("bf", "bi", "bc", "bo", "in_c")], axis=1)
        _CACHE[key] = (w, bias)
    return _CACHE[key]


def run_ffn(x, inputs, L):
    nc_ffn = _CACHE.get("nc_ffn")
    if nc_ffn is None:
        nc_ffn = _CACHE["nc_ffn"] = build_ffn()
    shards = shard_xT(x)
    w, bias = _ffn_inputs(inputs, L)
    consts = np.stack([np.ones(128, np.float32),
                       np.full(128, EPS, np.float32)], axis=1)
    onesrow = np.ones((1, 128), np.float32)
    in_maps = [dict(xt=shards[c], biases=bias, consts=consts,
                    onesrow=onesrow, **w) for c in range(NCORE)]
    res = run_bass_kernel_spmd(nc_ffn, in_maps, core_ids=list(range(NCORE)))
    return unshard_xT([r["xo"] for r in res.results])


def _attn0_inputs(inputs, core):
    key = "attn0_w"
    if key not in _CACHE:
        uvwT = transpose_w(inputs["L0_attn_uv_w"])
        owT = transpose_w(inputs["L0_attn_o_w"])
        gamma = np.asarray(inputs["L0_attn_gamma"], dtype=np.float32)
        beta = np.asarray(inputs["L0_attn_beta"], dtype=np.float32)
        gb = np.ascontiguousarray(
            np.stack([gamma[0], beta[0], gamma[1], beta[1]], axis=1))
        gval = np.asarray(inputs["L0_attn_g"], dtype=np.float32).reshape(1, 1)
        wrel = np.asarray(inputs["L0_attn_w_rel"], dtype=np.float32)
        w2046 = np.ascontiguousarray(wrel[2046:2047].reshape(1, 1))
        rsc = colpack(inputs["L0_attn_res_scale"])
        tabs = rope_tables()
        corners = []
        for c in range(NCORE):
            jg = np.arange(16 * 128)[:, None]
            tiles = []
            for icc in range(4):
                ig = LOCP * c + icc * 256 + np.arange(256)[None, :]
                idx = np.minimum(jg - ig + 8191, 2046)
                tiles.append(wrel[idx].reshape(16, 128, 256))
            corners.append(np.ascontiguousarray(
                np.stack(tiles, axis=0).reshape(64 * 128, 256)))
        cosT, sinT = [], []
        for c in range(NCORE):
            sl = slice(LOCP * c, LOCP * (c + 1))
            ct = tabs["l0_cos"][sl].T
            stt_ = tabs["l0_sin"][sl].T
            cosT.append(np.ascontiguousarray(np.vstack([ct, ct])))
            sinT.append(np.ascontiguousarray(np.vstack([stt_, stt_])))
        ident = np.eye(128, dtype=np.float32)
        _CACHE[key] = dict(uvwT=uvwT, owT=owT, gbcols=gb, gval=gval,
                           w2046=w2046, rsc=rsc, corners=corners,
                           cosT=cosT, sinT=sinT, identb=ident)
    return _CACHE[key]


def run_attn0(x, inputs):
    nc = _CACHE.get("nc_attn0")
    if nc is None:
        nc = _CACHE["nc_attn0"] = build_attn0()
    shards = shard_xT(x)
    w = _attn0_inputs(inputs, 0)
    consts = np.stack([np.ones(128, np.float32),
                       np.full(128, EPS, np.float32)], axis=1)
    onesrow = np.ones((1, 128), np.float32)
    in_maps = [dict(xt=shards[c], uvwT=w["uvwT"], owT=w["owT"],
                    gbcols=w["gbcols"], gval=w["gval"], w2046=w["w2046"],
                    corner=w["corners"][c], rsc=w["rsc"], cosT=w["cosT"][c],
                    sinT=w["sinT"][c], identb=w["identb"], consts=consts,
                    onesrow=onesrow) for c in range(NCORE)]
    res = run_bass_kernel_spmd(nc, in_maps, core_ids=list(range(NCORE)))
    return unshard_xT([r["xo"] for r in res.results])


def _attn1_inputs(inputs):
    key = "attn1_w"
    if key not in _CACHE:
        wqb = np.asarray(inputs["L1_attn_Wqb"], dtype=np.float32)
        perm = np.empty(1024, np.int64)
        for h in range(16):
            for j in range(32):
                perm[h * 64 + j] = h * 64 + 2 * j
                perm[h * 64 + 32 + j] = h * 64 + 2 * j + 1
        wqbp = np.ascontiguousarray(wqb[:, perm])
        wkvb = np.asarray(inputs["L1_attn_Wkvb"], dtype=np.float32)
        permk = np.empty(256, np.int64)
        permv = np.empty(256, np.int64)
        for kh in range(4):
            for j in range(32):
                permk[kh * 64 + j] = kh * 128 + 4 * j
                permk[kh * 64 + 32 + j] = kh * 128 + 4 * j + 2
            for dd in range(64):
                permv[kh * 64 + dd] = kh * 128 + 2 * dd + 1
        wkvbk = np.ascontiguousarray(wkvb[:, permk])
        wkvbv = np.ascontiguousarray(wkvb[:, permv])
        tabs = rope_tables()
        cosF, sinF = [], []
        for c in range(NCORE):
            sl = slice(LOCP * c, LOCP * (c + 1))
            c32 = tabs["l1_cos"][sl].T
            s32 = tabs["l1_sin"][sl].T
            cosF.append(np.ascontiguousarray(np.vstack([c32] * 4)))
            sinF.append(np.ascontiguousarray(
                np.vstack([-s32, s32, -s32, s32])))
        p_ = np.arange(128)[:, None]
        m_ = np.arange(896)[None, :]
        maskM = (p_ <= m_ - 384).astype(np.float32)
        vtempl = np.zeros((8 * TOKS, 65), np.float32)
        vtempl[:, 64] = 1.0
        _CACHE[key] = dict(
            wqa=np.ascontiguousarray(np.asarray(inputs["L1_attn_Wqa"],
                                                dtype=np.float32)),
            wqbp=wqbp,
            wkva=np.ascontiguousarray(np.asarray(inputs["L1_attn_Wkva"],
                                                 dtype=np.float32)),
            wkvbk=wkvbk, wkvbv=wkvbv,
            wout=np.ascontiguousarray(np.asarray(inputs["L1_attn_Wout"],
                                                 dtype=np.float32)),
            boutc=colpack(inputs["L1_attn_bout"]),
            cosF=cosF, sinF=sinF, maskM=maskM, vtempl=vtempl)
    return _CACHE[key]


def run_attn1(x, inputs):
    nc = _CACHE.get("nc_attn1")
    if nc is None:
        nc = _CACHE["nc_attn1"] = build_attn1()
    shards = shard_xT(x)
    w = _attn1_inputs(inputs)
    consts = np.stack([np.ones(128, np.float32),
                       np.full(128, EPS, np.float32)], axis=1)
    onesrow = np.ones((1, 128), np.float32)
    in_maps = [dict(xt=shards[c], wqa=w["wqa"], wqbp=w["wqbp"],
                    wkva=w["wkva"], wkvbk=w["wkvbk"], wkvbv=w["wkvbv"],
                    wout=w["wout"], boutc=w["boutc"], cosF=w["cosF"][c],
                    sinF=w["sinF"][c], maskM=w["maskM"], vtempl=w["vtempl"],
                    consts=consts, onesrow=onesrow) for c in range(NCORE)]
    res = run_bass_kernel_spmd(nc, in_maps, core_ids=list(range(NCORE)))
    return unshard_xT([r["xo"] for r in res.results])


def kernel(**inputs):
    return np.ascontiguousarray(run_fused(inputs), dtype=np.float32)


# version 15
# speedup vs baseline: 1.0098x; 1.0002x over previous
import contextlib
import numpy as np
import ml_dtypes
import concourse.bass as bass
import concourse.mybir as mybir
from concourse.bass_utils import run_bass_kernel_spmd

F32 = mybir.dt.float32
BF16 = mybir.dt.bfloat16
AF = mybir.ActivationFunctionType
ALU = mybir.AluOpType
AX = mybir.AxisListType

NCORE = 8
BLK = 512
NPOS = 8192
NB = 2
D = 1024
TOKS = 2048
LOCP = 1024
EPS = 1e-5
BF = ml_dtypes.bfloat16



def zz_rows(c):
    return np.arange(LOCP * c, LOCP * (c + 1))


def shard_xT(x):
    out = []
    for c in range(NCORE):
        xs = x[:, zz_rows(c), :]
        out.append(np.ascontiguousarray(xs.transpose(2, 0, 1).reshape(D, TOKS)))
    return out


def unshard_xT(shards):
    out = np.empty((NB, NPOS, D), dtype=np.float32)
    for c in range(NCORE):
        xT = np.asarray(shards[c], dtype=np.float32).reshape(D, NB, LOCP)
        out[:, zz_rows(c), :] = xT.transpose(1, 2, 0)
    return out


def colpack(v):
    return np.ascontiguousarray(np.asarray(v, dtype=np.float32).reshape(8, 128).T)


def transpose_w(w):
    return np.ascontiguousarray(np.asarray(w, dtype=np.float32).T)


_JAX_TABLES = {}


def rope_tables():
    if _JAX_TABLES:
        return _JAX_TABLES
    import jax
    try:
        cpu = jax.devices("cpu")[0]
    except Exception:
        jax.config.update("jax_platforms", "cpu")
        cpu = jax.devices("cpu")[0]
    import jax.numpy as jnp
    with jax.default_device(cpu):
        half = 64
        pos = jnp.arange(NPOS, dtype=jnp.float32)[:, None]
        invf = 10000.0 ** (jnp.arange(half, dtype=jnp.float32) / half)
        arg = pos * invf[None, :]
        l0_sin = np.asarray(jnp.sin(arg))
        l0_cos = np.asarray(jnp.cos(arg))
        invf1 = 1.0 / (10000.0 ** (jnp.arange(0, 64, 2, dtype=jnp.float32) / 64))
        t = jnp.arange(NPOS, dtype=jnp.float32)
        fr = jnp.outer(t, invf1)
        l1_sin = np.asarray(jnp.sin(fr))
        l1_cos = np.asarray(jnp.cos(fr))
    _JAX_TABLES.update(l0_sin=l0_sin, l0_cos=l0_cos, l1_sin=l1_sin, l1_cos=l1_cos)
    return _JAX_TABLES



class Em:

    COMPUTE_SEM = {"pe": "pe", "act": "act", "dve": "dve", "pool": "pool"}
    _w_ready = None

    def __init__(self, nc, stack):
        self.nc = nc
        E = stack.enter_context
        self.sem = {n: E(nc.semaphore("sem_" + n))
                    for n in ("pe", "act", "dve", "pool", "qs", "qg",
                              "cc0", "cc1", "cc2", "cc3", "cc4", "cc5",
                              "cc6", "cc7", "cc8", "cc9")}
        self.cc_i = 0
        self.barrier_toks = []
        self.cnt = {n: 0 for n in self.sem}
        self.seen = {}
        self.q = {n: [] for n in ("pe", "act", "dve", "pool", "sync")}

    def next_cc(self):
        n = f"cc{self.cc_i}"
        self.cc_i += 1
        return n

    def barrier(self, tiny_out, tiny_in):
        self.push("pe", [], lambda t: t.nop(), inc="pe")
        self.push("act", [], lambda s: s.copy(tiny_out, tiny_in), inc="act")
        self.push("dve", [], lambda v: v.tensor_copy(tiny_out, tiny_in),
                  inc="dve")
        self.push("pool", [], lambda g: g.tensor_copy(tiny_out, tiny_in),
                  inc="pool")
        self.barrier_toks = [(s, c) for s, c in self.cnt.items() if c > 0]

    def push(self, engq, waits, build, inc=None, incn=1):
        rw = []
        for w in list(waits) + self.barrier_toks:
            if w is None:
                continue
            s, v = w
            if self.COMPUTE_SEM.get(engq) == s:
                continue
            if self.seen.get((engq, s), 0) >= v:
                continue
            self.seen[(engq, s)] = v
            rw.append((s, v))
        tok = None
        if inc is not None:
            self.cnt[inc] += incn
            tok = (inc, self.cnt[inc])
        sems = self.sem

        def thunk(eng, rw=rw, build=build, tok=tok, incn=incn):
            for s, v in rw:
                eng.wait_ge(sems[s], v)
            ins = build(eng)
            if tok is not None:
                ins.then_inc(sems[tok[0]], incn)
        self.q[engq].append(thunk)
        return tok

    def cnt_tok(self, sem):
        return (sem, self.cnt[sem])

    def fence(self, waits, tiny_out, tiny_in):
        return self.push("dve", waits,
                         lambda v: v.tensor_copy(tiny_out, tiny_in), inc="dve")

    def dma(self, engq, waits, out, in_, **kw):
        qsem = "qs" if engq == "sync" else "qg"
        return self.push(engq, waits,
                         lambda e: e.dma_start(out=out, in_=in_, **kw),
                         inc=qsem, incn=16)

    def run(self, block):
        q = self.q

        @block.tensor
        def _(t):
            for f in q["pe"]:
                f(t)

        @block.scalar
        def _(s):
            for f in q["act"]:
                f(s)

        @block.vector
        def _(v):
            for f in q["dve"]:
                f(v)

        @block.gpsimd
        def _(g):
            for f in q["pool"]:
                f(g)

        @block.sync
        def _(sy):
            for f in q["sync"]:
                f(sy)


class Bufs:
    pass



def emit_ln(em, b, T, xsrc, xn, nc0, extra_waits=(), keep_var=False):
    ncg = T // 512
    xsrc3 = xsrc.rearrange("(ic p) t -> ic p t", p=128)
    tok_st = [None] * 8
    tok_cgst = {}
    tok_ld = [None] * 8
    for ic in range(8):
        s = ic % 2
        tok_ld[ic] = em.dma("pool",
                            [tok_st[ic - 2] if ic >= 2 else None],
                            b.xchunk[s][:, 0:T], xsrc3[ic])
        last = None
        for cg in range(ncg):
            c0 = cg * 512
            sq = b.xsq[cg % 2]
            prev_sq = tok_cgst.get((ic, cg - 2)) or (
                tok_cgst.get((ic - 1, ncg - 2 + cg)) if cg < 2 else None)
            tok_sq = em.push("act", [tok_ld[ic], prev_sq],
                             lambda e, s=s, c0=c0, sq=sq: e.activation(
                                 sq, b.xchunk[s][:, c0:c0 + 512], AF.Square),
                             inc="act")

            def stats(t, ic=ic, s=s, c0=c0, sq=sq):
                t.matmul(b.ps_a[0:1, c0:c0 + 512], b.ones_col,
                         b.xchunk[s][:, c0:c0 + 512], start=(ic == 0), stop=(ic == 7))
                return t.matmul(b.ps_b[0:1, c0:c0 + 512], b.ones_col,
                                sq, start=(ic == 0), stop=(ic == 7))
            last = em.push("pe", [tok_sq, *extra_waits], stats, inc="pe")
            tok_cgst[(ic, cg)] = last
        tok_st[ic] = last
    em.push("dve", [tok_st[7]],
            lambda v: v.tensor_scalar(b.m_row[0:1, 0:T], b.ps_a[0:1, 0:T],
                                      1.0 / D, None, ALU.mult))
    em.push("dve", [],
            lambda v: v.tensor_tensor(b.rs_row[0:1, 0:T], b.m_row[0:1, 0:T],
                                      b.m_row[0:1, 0:T], ALU.mult))
    tok_v = em.push("dve", [],
                    lambda v: v.scalar_tensor_tensor(
                        b.rs_row[0:1, 0:T], b.ps_b[0:1, 0:T], 1.0 / D,
                        b.rs_row[0:1, 0:T], ALU.mult, ALU.subtract), inc="dve")
    if keep_var:
        tok_v = em.push("dve", [],
                        lambda v: v.tensor_copy(b.v_row[0:1, 0:T],
                                                b.rs_row[0:1, 0:T]), inc="dve")
    em.push("act", [tok_v],
            lambda s: s.activation(b.rs_row[0:1, 0:T], b.rs_row[0:1, 0:T],
                                   AF.Ln, bias=b.eps_col[0:1, :]))
    tok_rs = em.push("act", [],
                     lambda s: s.activation(b.rs_row[0:1, 0:T], b.rs_row[0:1, 0:T],
                                            AF.Exp, scale=-0.5), inc="act")
    def bcast(t):
        for cg in range(ncg):
            c0 = cg * 512
            t.matmul(b.ps_a[:, c0:c0 + 512], b.ones_row[0:1, :],
                     b.m_row[0:1, c0:c0 + 512], start=True, stop=True)
            ins = t.matmul(b.ps_b[:, c0:c0 + 512], b.ones_row[0:1, :],
                           b.rs_row[0:1, c0:c0 + 512], start=True, stop=True)
        return ins
    tok_bc = em.push("pe", [tok_rs], bcast, inc="pe")

    tok_xn = None
    tok_n = [None] * 8
    for ic in range(8):
        s = ic % 2
        war = tok_n[ic - 2] if ic >= 2 else tok_st[7]
        tok_l = em.dma("pool", [war],
                       b.xchunk[s][:, 0:T], xsrc3[ic])
        for cg in range(ncg):
            c0 = cg * 512
            osl = slice(nc0 + ic * T + c0, nc0 + ic * T + c0 + 512)
            em.push("dve", [tok_bc, tok_l],
                    lambda v, s=s, c0=c0: v.tensor_tensor(
                        b.ln_tmp, b.xchunk[s][:, c0:c0 + 512],
                        b.ps_a[:, c0:c0 + 512], ALU.subtract))
            tok_xn = em.push("dve", [],
                             lambda v, osl=osl, c0=c0: v.tensor_tensor(
                                 xn[:, osl], b.ln_tmp,
                                 b.ps_b[:, c0:c0 + 512], ALU.mult),
                             inc="dve")
        tok_n[ic] = tok_xn
    return tok_xn



def build_ffn(debug=False):
    nc = bass.Bass("TRN2", target_bir_lowering=False)
    xt_d = nc.dram_tensor("xt", [D, TOKS], F32, kind="ExternalInput").ap()
    wnames = ("wf", "uf", "wi", "ui", "wc", "uc", "wo", "uo")
    w_d = {n: nc.dram_tensor(n + "T", [D, D], F32, kind="ExternalInput").ap()
           for n in wnames}
    bias_d = nc.dram_tensor("biases", [128, 40], F32, kind="ExternalInput").ap()
    consts_d = nc.dram_tensor("consts", [128, 2], F32, kind="ExternalInput").ap()
    onesrow_d = nc.dram_tensor("onesrow", [1, 128], F32, kind="ExternalInput").ap()
    out_d = nc.dram_tensor("xo", [D, TOKS], F32, kind="ExternalOutput").ap()
    dbg = {}
    if debug:
        dbg["xn"] = nc.dram_tensor("dbg_xn", [8 * 128, TOKS], BF16,
                                   kind="ExternalOutput").ap()
        dbg["c32"] = nc.dram_tensor("dbg_c32", [8 * 128, LOCP], F32,
                                    kind="ExternalOutput").ap()
        dbg["cbf"] = nc.dram_tensor("dbg_cbf", [8 * 128, LOCP], BF16,
                                    kind="ExternalOutput").ap()
        dbg["c1"] = nc.dram_tensor("dbg_c1", [8 * 128, LOCP], F32,
                                   kind="ExternalOutput").ap()
        dbg["c0"] = nc.dram_tensor("dbg_c0", [8 * 128, LOCP], F32,
                                   kind="ExternalOutput").ap()
        for gi in range(3):
            dbg[f"g{gi}"] = nc.dram_tensor(f"dbg_g{gi}", [8 * 128, 512], BF16,
                                           kind="ExternalOutput").ap()

    with contextlib.ExitStack() as st:
        E = st.enter_context
        em = Em(nc, st)
        b = Bufs()
        b.c32 = E(nc.sbuf_tensor("c32", [128, 8 * LOCP], F32))
        b.c_bf = E(nc.sbuf_tensor("c_bf", [128, 8 * LOCP], BF16))
        b.xn = E(nc.sbuf_tensor("xn", [128, 8 * TOKS], BF16))
        b.gate = [E(nc.sbuf_tensor(f"gate{i}", [128, 8 * 512], BF16))
                  for i in range(3)]
        b.wbuf = [E(nc.sbuf_tensor(f"wbuf{i}", [128, 8 * D], BF16))
                  for i in range(4)]
        b.xchunk = [b.c32[:, 0:TOKS], b.c32[:, TOKS:2 * TOKS]]
        b.scratch = E(nc.sbuf_tensor("scratch", [128, 2560], F32))
        b.xsq = [b.scratch[:, 0:512], b.scratch[:, 512:1024]]
        b.ln_tmp = b.scratch[:, 1024:1536]
        b.t1 = b.scratch[:, 0:512]
        b.t2 = b.scratch[:, 512:1024]
        b.hbuf = [b.scratch[:, 1024:1280].bitcast(BF16),
                  b.scratch[:, 1280:1536].bitcast(BF16)]
        b.xres = [b.scratch[:, 1536:2048], b.scratch[:, 2048:2560]]
        b.ores = [b.scratch[:, 0:512], b.scratch[:, 512:1024]]
        b.biases = E(nc.sbuf_tensor("biases_sb", [128, 40], F32))
        b.m_row = E(nc.sbuf_tensor("m_row", [1, TOKS], F32))
        b.rs_row = E(nc.sbuf_tensor("rs_row", [1, TOKS], F32))
        b.consts = E(nc.sbuf_tensor("consts_sb", [128, 2], F32))
        b.ones_col = b.consts[:, 0:1]
        b.eps_col = b.consts[:, 1:2]
        b.ones_row = E(nc.sbuf_tensor("ones_row", [1, 128], F32))
        b.ps_a = E(nc.psum_tensor("ps_a", [128, 2048], F32))
        b.ps_b = E(nc.psum_tensor("ps_b", [128, 2048], F32))
        block = E(nc.Block())

        emit_ffn_body(em, b, xt_d, w_d, bias_d, out_d, dbg,
                      consts_d, onesrow_d)
        em.run(block)
    return nc


def emit_ffn_body(em, b, xt_d, w_d, bias_d, out_d, dbg=None,
                  consts_d=None, onesrow_d=None):
    em._tok_consts = em.dma("pool", [], b.consts[:, :], consts_d)
    tok_ones = em.dma("pool", [], b.ones_row[0:1, :], onesrow_d)
    tok_bias = em.dma("pool", [], b.biases[:, :], bias_d)

    tok_xn = emit_ln(em, b, TOKS, xt_d, b.xn, 0,
                     extra_waits=(tok_ones, em._tok_consts))

    tok_c0 = None
    for mc in range(8):
        sl = slice(mc * LOCP, (mc + 1) * LOCP)
        tok_c0 = em.push("dve", [tok_bias, tok_xn],
                         lambda v, sl=sl, mc=mc: v.tensor_scalar(
                             b.c32[:, sl], b.xn[:, 0:LOCP],
                             0.0, b.biases[:, 32 + mc:33 + mc],
                             ALU.mult, ALU.add),
                         inc="dve")
    tok_cbf = None
    for mc in range(8):
        sl = slice(mc * LOCP, (mc + 1) * LOCP)
        tok_cbf = em.push("dve", [],
                          lambda v, sl=sl: v.tensor_copy(b.c_bf[:, sl],
                                                         b.c32[:, sl]),
                          inc="dve")

    if dbg:
        em.dma("sync", [tok_cbf],
               dbg["c0"].rearrange("(ic p) t -> p ic t", p=128),
               b.c32[:, :].rearrange("p (ic t) -> p ic t", ic=8))
    order = ["wf", "uf", "wi", "ui", "wc", "uc"] * 4 + ["wo", "uo"]
    NL = len(order)
    wtok = [None] * NL
    wfree = [None] * NL
    emitted_load = [False] * NL

    def need(k):
        if k >= NL or emitted_load[k]:
            return
        emitted_load[k] = True
        wait = wfree[k - 4] if k >= 4 else None
        wtok[k] = em.dma(
            "pool", [wait, em._w_ready],
            b.wbuf[k % 4][:, :].rearrange("p (ic m) -> p ic m", ic=8),
            w_d[order[k]].rearrange("(ic p) m -> p ic m", p=128))

    for k in range(4):
        need(k)

    psq = [b.ps_a[:, i * 512:(i + 1) * 512] for i in range(4)] + \
          [b.ps_b[:, i * 512:(i + 1) * 512] for i in range(4)]
    ps_free = [tok_xn] * 8
    qctr = 0


    tok_upd_all = None
    for tstep in range(2):
        for half in range(2):
            base = (tstep * 2 + half) * 6
            gate_act_tok = {}
            for gi in range(3):
                kw, ku = base + 2 * gi, base + 2 * gi + 1
                need(kw), need(ku)
                fn = AF.Sigmoid if gi < 2 else AF.Tanh
                last_mm = None
                for mt in range(8):
                    q = qctr % 8
                    qctr += 1
                    waits = [wtok[kw], wtok[ku], tok_cbf, tok_xn, ps_free[q]]
                    def f(t, kw=kw, ku=ku, q=q, mt=mt, half=half, tstep=tstep):
                        ps = psq[q]
                        bw, bu = b.wbuf[kw % 4], b.wbuf[ku % 4]
                        xc = tstep * LOCP + half * 512
                        cc = half * 512
                        for ic in range(8):
                            t.matmul(ps, bw[:, ic * D + mt * 128:
                                            ic * D + mt * 128 + 128],
                                     b.xn[:, ic * TOKS + xc:ic * TOKS + xc + 512],
                                     start=(ic == 0), stop=False)
                        for ic in range(8):
                            ins = t.matmul(ps, bu[:, ic * D + mt * 128:
                                                  ic * D + mt * 128 + 128],
                                           b.c_bf[:, ic * LOCP + cc:
                                                  ic * LOCP + cc + 512],
                                           start=False, stop=(ic == 7))
                        return ins
                    mmtok = em.push("pe", waits, f, inc="pe")
                    last_mm = mmtok
                    atok = em.push(
                        "act", [mmtok, tok_upd_all],
                        lambda s, gi=gi, q=q, fn=fn, mt=mt:
                        s.activation(
                            b.gate[gi][:, mt * 512:(mt + 1) * 512],
                            psq[q], fn, bias=b.biases[:, gi * 8 + mt:
                                                      gi * 8 + mt + 1]),
                        inc="act")
                    ps_free[q] = atok
                    gate_act_tok[(gi, mt)] = atok
                wfree[kw] = last_mm
                wfree[ku] = last_mm
                need(kw + 4)
                need(ku + 4)
            for mt in range(8):
                gsl = slice(mt * 512, (mt + 1) * 512)
                csl = slice(mt * LOCP + half * 512, mt * LOCP + half * 512 + 512)
                em.push("dve", [gate_act_tok[(0, mt)]],
                        lambda v, gsl=gsl, csl=csl: v.tensor_tensor(
                            b.t1, b.gate[0][:, gsl], b.c32[:, csl], ALU.mult))
                em.push("dve", [gate_act_tok[(1, mt)],
                                gate_act_tok[(2, mt)]],
                        lambda v, gsl=gsl: v.tensor_tensor(
                            b.t2, b.gate[1][:, gsl], b.gate[2][:, gsl],
                            ALU.mult))
                tok_upd_all = em.push(
                    "dve", [],
                    lambda v, csl=csl: v.tensor_tensor(
                        b.c32[:, csl], b.t1, b.t2, ALU.add),
                    inc="dve")
        if dbg and tstep == 0:
            em.dma("sync", [tok_upd_all],
                   dbg["c1"].rearrange("(ic p) t -> p ic t", p=128),
                   b.c32[:, :].rearrange("p (ic t) -> p ic t", ic=8))
        for mc in range(8):
            sl = slice(mc * LOCP, (mc + 1) * LOCP)
            tok_cbf = em.push("dve", [tok_upd_all],
                              lambda v, sl=sl: v.tensor_copy(
                                  b.c_bf[:, sl], b.c32[:, sl]),
                              inc="dve")

    kwo, kuo = 24, 25
    need(kwo), need(kuo)
    store_toks = []
    stt_tok = [None, None]
    slot_i = 0
    last_mm = None
    for tstep in range(2):
        for half in range(2):
            for mt in range(8):
                slot = slot_i % 2
                slot_i += 1
                rsl = (slice(mt * 128, (mt + 1) * 128),
                       slice(tstep * LOCP + half * 512,
                             tstep * LOCP + half * 512 + 512))
                xr_tok = em.dma("sync", [stt_tok[slot]],
                                b.xres[slot], xt_d[rsl[0], rsl[1]])
                q = qctr % 8
                qctr += 1

                def f(t, q=q, mt=mt, half=half, tstep=tstep):
                    ps = psq[q]
                    bw, bu = b.wbuf[kwo % 4], b.wbuf[kuo % 4]
                    xc = tstep * LOCP + half * 512
                    cc = half * 512
                    for ic in range(8):
                        t.matmul(ps, bw[:, ic * D + mt * 128:
                                        ic * D + mt * 128 + 128],
                                 b.xn[:, ic * TOKS + xc:ic * TOKS + xc + 512],
                                 start=(ic == 0), stop=False)
                    for ic in range(8):
                        ins = t.matmul(ps, bu[:, ic * D + mt * 128:
                                              ic * D + mt * 128 + 128],
                                       b.c_bf[:, ic * LOCP + cc:
                                              ic * LOCP + cc + 512],
                                       start=False, stop=(ic == 7))
                    return ins
                mmtok = em.push("pe", [wtok[kwo], wtok[kuo], tok_cbf,
                                       ps_free[q]], f, inc="pe")
                last_mm = mmtok
                atok = em.push("act", [mmtok, stt_tok[slot]],
                               lambda s, q=q, mt=mt, slot=slot: s.activation(
                                   b.hbuf[slot], psq[q], AF.Tanh,
                                   bias=b.biases[:, 24 + mt:25 + mt]),
                               inc="act")
                ps_free[q] = atok
                prev_store = store_toks[-2] if len(store_toks) >= 2 else None
                stt_tok[slot] = em.push(
                    "dve", [atok, xr_tok, prev_store],
                    lambda v, slot=slot: v.scalar_tensor_tensor(
                        b.ores[slot], b.hbuf[slot], 0.5,
                        b.xres[slot], ALU.mult, ALU.add),
                    inc="dve")
                store_toks.append(em.dma("sync", [stt_tok[slot]],
                                         out_d[rsl[0], rsl[1]],
                                         b.ores[slot]))
    wfree[kwo] = last_mm
    wfree[kuo] = last_mm

    if dbg:
        for gi in range(3):
            em.dma("sync", [tok_upd_all],
                   dbg[f"g{gi}"].rearrange("(ic p) t -> p ic t", p=128),
                   b.gate[gi][:, :].rearrange("p (ic t) -> p ic t", ic=8))
        em.dma("sync", [tok_xn],
               dbg["xn"].rearrange("(ic p) t -> p ic t", p=128),
               b.xn[:, :].rearrange("p (ic t) -> p ic t", ic=8))
        em.dma("sync", [tok_upd_all],
               dbg["c32"].rearrange("(ic p) t -> p ic t", p=128),
               b.c32[:, :].rearrange("p (ic t) -> p ic t", ic=8))
        store_toks.append(em.dma(
            "sync", [tok_cbf],
            dbg["cbf"].rearrange("(ic p) t -> p ic t", p=128),
            b.c_bf[:, :].rearrange("p (ic t) -> p ic t", ic=8)))
    em.push("sync", [store_toks[-1]], lambda sy: sy.nop())




SQS = 0.08838834764831845


def build_attn0(debug=False):
    nc = bass.Bass("TRN2", target_bir_lowering=False)
    xt_d = nc.dram_tensor("xt", [D, TOKS], F32, kind="ExternalInput").ap()
    uvw_d = nc.dram_tensor("uvwT", [D, 4224], F32, kind="ExternalInput").ap()
    ow_d = nc.dram_tensor("owT", [2048, D], F32, kind="ExternalInput").ap()
    gb_d = nc.dram_tensor("gbcols", [128, 4], F32, kind="ExternalInput").ap()
    g_d = nc.dram_tensor("gval", [1, 1], F32, kind="ExternalInput").ap()
    w2046_d = nc.dram_tensor("w2046", [1, 1], F32, kind="ExternalInput").ap()
    corner_d = nc.dram_tensor("corner", [64 * 128, 256], F32,
                              kind="ExternalInput").ap()
    rsc_d = nc.dram_tensor("rsc", [128, 8], F32, kind="ExternalInput").ap()
    cos_d = nc.dram_tensor("cosT", [128, LOCP], F32, kind="ExternalInput").ap()
    sin_d = nc.dram_tensor("sinT", [128, LOCP], F32, kind="ExternalInput").ap()
    ident_d = nc.dram_tensor("identb", [128, 128], F32, kind="ExternalInput").ap()
    consts_d = nc.dram_tensor("consts", [128, 2], F32, kind="ExternalInput").ap()
    onesrow_d = nc.dram_tensor("onesrow", [1, 128], F32, kind="ExternalInput").ap()
    out_d = nc.dram_tensor("xo", [D, TOKS], F32, kind="ExternalOutput").ap()

    dbg = {}
    if debug:
        for nm, shape, dt in [("y", [1024, TOKS], BF16), ("xn2", [1024, TOKS], BF16),
                              ("qT", [128, TOKS], BF16), ("kTloc", [128, TOKS], BF16),
                              ("kTall", [128, 8192], BF16),
                              ("kern", [128, 16384], BF16),
                              ("Bsb", [128, 4096], BF16),
                              ("base", [128, TOKS], F32),
                              ("uvwb", [128, 1024], BF16),
                              ("wp0", [128, 2048], BF16),
                              ("qpre", [128, TOKS], F32),
                              ("BTsb", [128, 4096], BF16)]:
            dbg[nm] = nc.dram_tensor("dbg_" + nm, shape, dt,
                                     kind="ExternalOutput").ap()
    cc_in = nc.dram_tensor("cc0_in", [2176, 2048], BF16).ap()
    cc_out = nc.dram_tensor("cc0_out", [8 * 2176, 2048], BF16).ap()
    u_spill = nc.dram_tensor("u_spill", [2048, 2048], BF16).ap()

    with contextlib.ExitStack() as st:
        E = st.enter_context
        em = Em(nc, st)
        b = Bufs()
        b.bigbuf = E(nc.sbuf_tensor("bigbuf", [128, 16384], BF16))
        b.y = E(nc.sbuf_tensor("ybuf", [128, 16384], BF16))
        b.wpool = E(nc.sbuf_tensor("wpool", [128, 16384], BF16))
        b.kT = E(nc.sbuf_tensor("kT_all", [128, 8192], BF16))
        b.qT = E(nc.sbuf_tensor("qT", [128, 2048], BF16))
        b.kTloc = E(nc.sbuf_tensor("kTloc", [128, 2048], BF16))
        b.vsb = [E(nc.sbuf_tensor(f"vsb{i}", [128, 2048], BF16)) for i in range(2)]
        b.usb = [E(nc.sbuf_tensor(f"usb{i}", [128, 2048], BF16)) for i in range(2)]
        b.tileAB = E(nc.sbuf_tensor("tileAB", [128, 4096], BF16))
        b.utile = b.tileAB[:, 0:2048]
        b.vtile = b.tileAB[:, 2048:4096]
        b.Bsb = b.tileAB
        b.BTsb = E(nc.sbuf_tensor("BTsb", [128, 4096], BF16))
        b.basebuf = E(nc.sbuf_tensor("basebuf", [128, 2048], F32))
        b.vsb += [b.basebuf[:, :].bitcast(BF16)[:, 0:2048],
                  b.basebuf[:, :].bitcast(BF16)[:, 2048:4096]]
        b.uvwb = E(nc.sbuf_tensor("uvwb", [128, 8 * 128], BF16))
        b.ident = E(nc.sbuf_tensor("identsb", [128, 128], BF16))
        b.diag = E(nc.sbuf_tensor("diagrs", [128, 8 * 128], BF16))
        b.corner = [E(nc.sbuf_tensor(f"corner{i}", [128, 256], F32))
                    for i in range(2)]
        b.costab = E(nc.sbuf_tensor("costab", [128, LOCP], F32))
        b.sintab = E(nc.sbuf_tensor("sintab", [128, LOCP], F32))
        b.gb = E(nc.sbuf_tensor("gbsb", [128, 4], F32))
        b.rsc = E(nc.sbuf_tensor("rscsb", [128, 8], F32))
        b.smalls = E(nc.sbuf_tensor("smalls", [128, 4], F32))
        b.gsb = E(nc.sbuf_tensor("gsb", [1, 2], F32))
        b.consts = E(nc.sbuf_tensor("consts_sb", [128, 2], F32))
        b.ones_col = b.consts[:, 0:1]
        b.eps_col = b.consts[:, 1:2]
        b.ones_row = E(nc.sbuf_tensor("ones_row", [1, 128], F32))
        b.xchunk = [E(nc.sbuf_tensor(f"xchunk{i}", [128, TOKS], F32))
                    for i in range(2)]
        b.scratch = E(nc.sbuf_tensor("scratch", [128, 2560], F32))
        b.xsq = [b.scratch[:, 0:512], b.scratch[:, 512:1024]]
        b.ln_tmp = b.scratch[:, 1024:1536]
        b.ropet = [b.scratch[:, 0:1024], b.scratch[:, 1024:2048]]
        b.rtile = b.scratch[:, 0:512]
        b.rtile4 = b.xchunk[0]
        b.xres = [b.scratch[:, 512:768], b.scratch[:, 768:1024]]
        b.ores = [b.scratch[:, 1024:1280], b.scratch[:, 1280:1536]]
        ktf = b.kT.bitcast(F32)
        b.m_row = ktf[0:1, 0:2048]
        b.rs_row = ktf[0:1, 2048:4096]
        b.v_row = b.BTsb.bitcast(F32)[0:1, 0:2048]
        b.t_row = b.v_row
        b.ps_a = E(nc.psum_tensor("ps_a", [128, 2048], F32))
        b.ps_b = E(nc.psum_tensor("ps_b", [128, 2048], F32))
        block = E(nc.Block())

        emit_attn0(em, b, xt_d, uvw_d, ow_d, gb_d, g_d, w2046_d, corner_d,
                   rsc_d, cos_d, sin_d, ident_d, consts_d, onesrow_d, out_d,
                   cc_in, cc_out, u_spill, dbg)
        em.run(block)
    return nc


def emit_attn0(em, b, xt_d, uvw_d, ow_d, gb_d, g_d, w2046_d, corner_d,
               rsc_d, cos_d, sin_d, ident_d, consts_d, onesrow_d, out_d,
               cc_in, cc_out, u_spill, dbg=None, cck0=None,
               ccv0_out=None, ccv1_out=None, ccv0_in=None, ccv1_in=None):
    em._tok_consts = em.dma("pool", [], b.consts[:, :], consts_d)
    tok_ones = em.dma("pool", [], b.ones_row[0:1, :], onesrow_d)
    tok_gb = em.dma("pool", [], b.gb[:, :], gb_d)
    tok_g = em.dma("pool", [], b.gsb[0:1, 0:1], g_d)
    tok_w2 = em.dma("pool", [], b.gsb[0:1, 1:2], w2046_d)
    tok_rsc = em.dma("pool", [], b.rsc[:, :], rsc_d)
    tok_cos = em.dma("pool", [], b.costab[:, :], cos_d)
    tok_sin = em.dma("pool", [], b.sintab[:, :], sin_d)
    tok_id = em.dma("pool", [], b.ident[:, :], ident_d)

    tok_w2b = em.push("pe", [tok_w2, tok_ones],
                      lambda t: t.matmul(b.ps_a[:, 0:1], b.ones_row[0:1, :],
                                         b.gsb[0:1, 1:2], start=True, stop=True),
                      inc="pe")
    em.push("dve", [tok_w2b],
            lambda v: v.tensor_scalar(b.smalls[:, 0:1], b.ps_a[:, 0:1],
                                      SQS, None, ALU.mult))
    tok_smalls = em.push("dve", [],
                         lambda v: v.tensor_copy(b.smalls[:, 1:2],
                                                 b.ps_a[:, 0:1]), inc="dve")
    tok_diag = None
    for dt in range(8):
        tok_diag = em.push("dve", [tok_id, tok_rsc],
                           lambda v, dt=dt: v.tensor_scalar(
                               b.diag[:, dt * 128:(dt + 1) * 128],
                               b.ident[:, :], b.rsc[:, dt:dt + 1], None,
                               ALU.mult), inc="dve")

    tok_y = emit_ln(em, b, TOKS, xt_d, b.y, 0,
                    extra_waits=(tok_ones, em._tok_consts), keep_var=True)
    tok_t0 = em.push("dve", [tok_y],
                     lambda v: v.tensor_scalar(b.t_row[0:1, :], b.v_row[0:1, :],
                                               0.0, None, ALU.max), inc="dve")
    tok_tr0 = em.push("act", [tok_t0],
                      lambda s: s.activation(b.t_row[0:1, :], b.t_row[0:1, :],
                                             AF.Sqrt), inc="act")
    em.push("dve", [tok_tr0],
            lambda v: v.tensor_tensor(b.t_row[0:1, :], b.t_row[0:1, :],
                                      b.rs_row[0:1, :], ALU.mult))
    em.push("dve", [],
            lambda v: v.tensor_scalar(b.t_row[0:1, :], b.t_row[0:1, :], EPS,
                                      None, ALU.max))
    em.push("dve", [], lambda v: v.reciprocal(b.t_row[0:1, :], b.t_row[0:1, :]))
    tok_tr = em.push("dve", [tok_g],
                     lambda v: v.tensor_scalar(b.t_row[0:1, :], b.t_row[0:1, :],
                                               b.gsb[0:1, 0:1], None, ALU.mult),
                     inc="dve")

    def bct(t):
        for cg in range(4):
            c0 = cg * 512
            ins = t.matmul(b.ps_a[:, c0:c0 + 512], b.ones_row[0:1, :],
                           b.t_row[0:1, c0:c0 + 512], start=True, stop=True)
        return ins
    tok_bct = em.push("pe", [tok_tr], bct, inc="pe")
    tok_xn2 = None
    for ic in range(8):
        sl = slice(ic * TOKS, (ic + 1) * TOKS)
        tok_xn2 = em.push("dve", [tok_bct],
                          lambda v, sl=sl: v.tensor_tensor(
                              b.bigbuf[:, sl], b.y[:, sl], b.ps_a[:, 0:TOKS],
                              ALU.mult), inc="dve")

    tok_xn2 = em.fence([tok_xn2], b.scratch[0:1, 0:1], b.consts[0:1, 0:1])

    uvw3 = uvw_d.rearrange("(ic p) f -> p ic f", p=128)
    psq = [b.ps_a[:, i * 512:(i + 1) * 512] for i in range(4)] + \
          [b.ps_b[:, i * 512:(i + 1) * 512] for i in range(4)]
    ps_free = [tok_bct] * 8
    qctr = 0
    spill_toks = []
    prev_pass_mm = None
    tok_uvw = None
    for fpass in range(2):
        f0 = fpass * 2048
        tok_uvw = em.dma(
            "pool", [prev_pass_mm],
            b.wpool[:, 0:16384].rearrange("p (ic f) -> p ic f", ic=8),
            uvw3[:, :, f0:f0 + 2048])
        tile = b.utile if fpass == 0 else b.vtile
        prev_spill = None
        for tt in range(16):
            atoks = []
            for fb in range(4):
                q = qctr % 8
                qctr += 1

                def f(t, tt=tt, fb=fb, q=q):
                    for ic in range(8):
                        ins = t.matmul(
                            psq[q],
                            b.bigbuf[:, ic * TOKS + tt * 128:
                                     ic * TOKS + tt * 128 + 128],
                            b.wpool[:, ic * 2048 + fb * 512:
                                    ic * 2048 + fb * 512 + 512],
                            start=(ic == 0), stop=(ic == 7))
                    return ins
                mmtok = em.push("pe", [tok_uvw, tok_xn2, ps_free[q]], f,
                                inc="pe")
                prev_pass_mm = mmtok
                atok = em.push("act", [mmtok, prev_spill],
                               lambda s, fb=fb, q=q, tile=tile: s.activation(
                                   tile[:, fb * 512:(fb + 1) * 512], psq[q],
                                   AF.Silu), inc="act")
                ps_free[q] = atok
                atoks.append(atok)
            if fpass == 0:
                dst_d, row0 = u_spill, tt * 128
            elif ccv0_in is not None:
                dst_d = ccv0_in if tt < 8 else ccv1_in
                row0 = (tt % 8) * 128
            else:
                dst_d, row0 = cc_in, 128 + tt * 128
            prev_spill = em.dma("sync", atoks, dst_d[row0:row0 + 128, :],
                                tile)
            spill_toks.append(prev_spill)

    tok_bw = em.dma("pool", [],
                    b.uvwb[:, :].rearrange("p (ic f) -> p ic f", ic=8),
                    uvw3[:, :, 4096:4224])
    tok_basebuf = None
    for cg in range(4):
        c0 = cg * 512
        q = qctr % 8
        qctr += 1

        def f(t, c0=c0, q=q):
            for ic in range(8):
                ins = t.matmul(psq[q],
                               b.uvwb[:, ic * 128:(ic + 1) * 128],
                               b.bigbuf[:, ic * TOKS + c0:ic * TOKS + c0 + 512],
                               start=(ic == 0), stop=(ic == 7))
            return ins
        mmtok = em.push("pe", [tok_bw, tok_xn2, ps_free[q]], f, inc="pe")
        tok_basebuf = em.push("act", [mmtok],
                              lambda s, c0=c0, q=q: s.activation(
                                  b.basebuf[:, c0:c0 + 512], psq[q], AF.Silu),
                              inc="act")
        ps_free[q] = tok_basebuf

    qpre, kpre = b.xchunk[0], b.xchunk[1]
    em.push("dve", [tok_basebuf, tok_gb],
            lambda v: v.tensor_scalar(qpre[:, :], b.basebuf[:, :],
                                      b.gb[:, 0:1], b.gb[:, 1:2],
                                      ALU.mult, ALU.add))
    tok_pre = em.push("dve", [],
                      lambda v: v.tensor_scalar(kpre[:, :], b.basebuf[:, :],
                                                b.gb[:, 2:3], b.gb[:, 3:4],
                                                ALU.mult, ALU.add), inc="dve")
    if dbg:
        em.dma("sync", [tok_basebuf], dbg["base"], b.basebuf[:, :])
        em.dma("sync", [tok_bw], dbg["uvwb"], b.uvwb[:, :])
        em.dma("sync", [tok_uvw], dbg["wp0"], b.wpool[:, 0:2048])
    tok_rope = None
    sw = b.basebuf
    for (srcb, dstbuf) in ((qpre, b.qT), (kpre, b.kTloc)):
        em.dma("pool", [tok_pre, tok_rope], sw[0:64, :], srcb[64:128, :])
        tok_sw = em.dma("pool", [], sw[64:128, :], srcb[0:64, :])
        for bb in range(2):
            sl = slice(bb * LOCP, (bb + 1) * LOCP)
            em.push("dve", [tok_pre, tok_sw, tok_cos, tok_sin],
                    lambda v, srcb=srcb, sl=sl: v.tensor_tensor(
                        b.ropet[0][0:64, :], srcb[0:64, sl],
                        b.costab[0:64, :], ALU.mult))
            em.push("dve", [],
                    lambda v, sl=sl: v.tensor_tensor(
                        b.ropet[1][0:64, :], sw[0:64, sl],
                        b.sintab[0:64, :], ALU.mult))
            em.push("dve", [],
                    lambda v, dstbuf=dstbuf, sl=sl: v.tensor_tensor(
                        dstbuf[0:64, sl], b.ropet[0][0:64, :],
                        b.ropet[1][0:64, :], ALU.subtract))
            em.push("dve", [],
                    lambda v, srcb=srcb, sl=sl: v.tensor_tensor(
                        b.ropet[0][64:128, :], srcb[64:128, sl],
                        b.costab[64:128, :], ALU.mult))
            em.push("dve", [],
                    lambda v, sl=sl: v.tensor_tensor(
                        b.ropet[1][64:128, :], sw[64:128, sl],
                        b.sintab[64:128, :], ALU.mult))
            tok_rope = em.push("dve", [],
                               lambda v, dstbuf=dstbuf, sl=sl: v.tensor_tensor(
                                   dstbuf[64:128, sl], b.ropet[0][64:128, :],
                                   b.ropet[1][64:128, :], ALU.add), inc="dve")
    kdst = cck0[0] if cck0 is not None else cc_in[0:128, :]
    tok_kdma = em.dma("sync", [tok_rope], kdst, b.kTloc[:, :])
    if dbg:
        em.dma("sync", [tok_pre], dbg["qpre"], qpre[:, :])
        em.dma("sync", [tok_xn2],
               dbg["y"].rearrange("(ic p) t -> p ic t", p=128),
               b.y[:, :].rearrange("p (ic t) -> p ic t", ic=8))
        em.dma("sync", [tok_xn2],
               dbg["xn2"].rearrange("(ic p) t -> p ic t", p=128),
               b.bigbuf[:, :].rearrange("p (ic t) -> p ic t", ic=8))
        em.dma("sync", [tok_rope], dbg["qT"], b.qT[:, :])
        em.dma("sync", [tok_rope], dbg["kTloc"], b.kTloc[:, :])

    if cck0 is not None:
        tok_cck0 = em.push("pool", [tok_kdma],
                           lambda g: g.collective_compute(
                               "AllGather", ALU.bypass,
                               replica_groups=[list(range(8))],
                               ins=[cck0[0].opt()], outs=[cck0[1].opt()]),
                           inc=em.next_cc())
        tok_ccv0 = em.push("pool", spill_toks[16:24],
                           lambda g: g.collective_compute(
                               "AllGather", ALU.bypass,
                               replica_groups=[list(range(8))],
                               ins=[ccv0_in.opt()],
                               outs=[ccv0_out.opt()]),
                           inc=em.next_cc())
        tok_ccv1 = em.push("pool", spill_toks + [tok_cck0],
                           lambda g: g.collective_compute(
                               "AllGather", ALU.bypass,
                               replica_groups=[list(range(8))],
                               ins=[ccv1_in.opt()],
                               outs=[ccv1_out.opt()]),
                           inc=em.next_cc())
        tok_ccv = [tok_ccv0, tok_ccv1]
    else:
        tok_cc = em.push("pool", [tok_kdma] + spill_toks,
                         lambda g: g.collective_compute(
                             "AllGather", ALU.bypass,
                             replica_groups=[list(range(8))],
                             ins=[cc_in.opt()], outs=[cc_out.opt()]),
                         inc=em.next_cc())
        tok_cck0 = tok_cc
        tok_ccv = [tok_cc, tok_cc]

    ow3 = ow_d.rearrange("(ec p) dd -> p ec dd", p=128)
    tok_ow = em.dma("pool", [prev_pass_mm],
                    b.wpool[:, 0:16384].rearrange("p (ec dd) -> p ec dd", ec=16),
                    ow3)

    corner3 = corner_d.rearrange("(s p) i -> s p i", p=128)

    store_toks = []
    tok_score_done = None
    tok_ic_done = None
    tok_tr_done = None
    tok_evac = {0: None, 1: None}
    corner_war = [None, None]
    tok_uload = {0: None, 1: None}
    vwar = [None] * 4
    tok_st = [None, None]
    tok_xr = [None, None]
    owps_free = [None, None]

    for bb in range(2):
        tok_kta = None
        ksrc = cck0[1] if cck0 is not None else cc_out
        kstride = 128 if cck0 is not None else 2176
        for r in range(8):
            tok_kta = em.dma("sync", [tok_cck0, tok_score_done],
                             b.kT[:, r * 1024:(r + 1) * 1024],
                             ksrc[kstride * r:kstride * r + 128,
                                  bb * 1024:(bb + 1) * 1024])
        for ic in range(4):
            icg = bb * 4 + ic
            iofs = bb * 1024 + ic * 256
            sqt = None
            rt_hist = [None, None]
            for jp in range(16):
                g = jp % 2
                grp = b.ps_a[:, g * 1024:(g + 1) * 1024]
                deps = []
                for sl4 in range(4):
                    jt = jp * 4 + sl4
                    mmtok = em.push(
                        "pe", [tok_kta, tok_rope, ps_free[g], tok_ic_done,
                               tok_tr_done],
                        lambda t, jt=jt, sl4=sl4, grp=grp, iofs=iofs:
                        t.matmul(
                            grp[:, sl4 * 256:sl4 * 256 + 256],
                            b.kT[:, jt * 128:(jt + 1) * 128],
                            b.qT[:, iofs:iofs + 256], start=True, stop=True),
                        inc="pe")
                    dep = mmtok
                    if jt < 16:
                        cs = (ic * 16 + jt) % 2
                        ltok = em.dma("pool", [corner_war[cs], tok_smalls],
                                      b.corner[cs][:, :], corner3[ic * 16 + jt])
                        adjtok = em.push(
                            "dve", [ltok],
                            lambda v, cs=cs: v.tensor_scalar(
                                b.corner[cs][:, :], b.corner[cs][:, :],
                                b.smalls[:, 1:2], None, ALU.subtract),
                            inc="dve")
                        dep = em.push(
                            "dve", [mmtok, adjtok],
                            lambda v, sl4=sl4, grp=grp, cs=cs: v.tensor_tensor(
                                grp[:, sl4 * 256:sl4 * 256 + 256],
                                grp[:, sl4 * 256:sl4 * 256 + 256],
                                b.corner[cs][:, :], ALU.add),
                            inc="dve")
                        corner_war[cs] = dep
                    deps.append(dep)
                rh = g * 1024
                rt = em.push("act", [tok_smalls, rt_hist[g], *deps],
                             lambda s, grp=grp, rh=rh: s.activation(
                                 b.rtile4[:, rh:rh + 1024], grp,
                                 AF.Relu, bias=b.smalls[:, 0:1], scale=SQS),
                             inc="act")
                rt_hist[g] = rt
                sqt = em.push("dve", [rt],
                              lambda v, jp=jp, rh=rh: v.tensor_tensor(
                                  b.bigbuf[:, jp * 1024:(jp + 1) * 1024],
                                  b.rtile4[:, rh:rh + 1024],
                                  b.rtile4[:, rh:rh + 1024], ALU.mult),
                              inc="dve")
                ps_free[g] = sqt
            tok_score_done = em.fence([sqt], b.scratch[0:1, 0:1],
                                      b.consts[0:1, 0:1])
            if dbg and bb == 0 and ic == 0:
                em.dma("sync", [tok_score_done], dbg["kern"], b.bigbuf[:, :])
                em.dma("sync", [tok_kta], dbg["kTall"], b.kT[:, :])

            for it in range(2):
                trow = (icg * 2 + it) * 128
                tok_uload[it] = em.dma("sync", [tok_evac[it]],
                                       b.usb[it][:, :],
                                       u_spill[trow:trow + 128, :])
            Atok = None
            for jt in range(64):
                vb = jt % 4
                r, jr = jt // 8, jt % 8
                if cck0 is not None:
                    vsrcT = ccv0_out if bb == 0 else ccv1_out
                    vrow = 1024 * r + jr * 128
                else:
                    vsrcT = cc_out
                    vrow = 2176 * r + 128 + bb * 1024 + jr * 128
                vtok = em.dma("sync" if jt % 2 == 0 else "pool",
                              [vwar[vb], tok_ccv[bb]],
                              b.vsb[vb][:, :], vsrcT[vrow:vrow + 128, :])

                def fA(t, jt=jt, vb=vb):
                    for it in range(2):
                        for es in range(4):
                            ins = t.matmul(
                                (b.ps_a if it == 0 else b.ps_b)[:, es * 512:
                                                                (es + 1) * 512],
                                b.bigbuf[:, jt * 256 + it * 128:
                                         jt * 256 + it * 128 + 128],
                                b.vsb[vb][:, es * 512:(es + 1) * 512],
                                start=(jt == 0), stop=(jt == 63))
                    return ins
                Atok = em.push("pe", [vtok, tok_score_done, tok_ic_done,
                                      tok_tr_done], fA, inc="pe")
                vwar[vb] = Atok
            for it in range(2):
                ps = b.ps_a if it == 0 else b.ps_b
                ev = None
                for es in range(4):
                    ev = em.push("dve", [Atok, tok_uload[it]],
                                 lambda v, it=it, es=es, ps=ps: v.tensor_tensor(
                                     b.Bsb[:, it * 2048 + es * 512:
                                           it * 2048 + (es + 1) * 512],
                                     ps[:, es * 512:(es + 1) * 512],
                                     b.usb[it][:, es * 512:(es + 1) * 512],
                                     ALU.mult), inc="dve")
                tok_evac[it] = ev

            tok_evac = {0: em.fence([tok_evac[0]], b.scratch[0:1, 0:1],
                                    b.consts[0:1, 0:1]),
                        1: em.fence([tok_evac[1]], b.scratch[0:1, 0:1],
                                    b.consts[0:1, 0:1])}
            if dbg and bb == 0 and ic == 0:
                dtok = em.dma("sync", [tok_evac[0], tok_evac[1]], dbg["Bsb"],
                              b.Bsb[:, :])
                tok_evac = {0: dtok, 1: dtok}
            cptok = None
            trq = 0
            for it in range(2):
                for et in range(16):
                    ps = (b.ps_a if trq % 2 == 0 else b.ps_b)
                    pt = ps[:, 0:64].bitcast(BF16)
                    trtok = em.push(
                        "pe", [tok_evac[it], tok_id, cptok],
                        lambda t, it=it, et=et, pt=pt: t.transpose(
                            pt[:, 0:128],
                            b.Bsb[:, it * 2048 + et * 128:
                                  it * 2048 + et * 128 + 128],
                            b.ident[:, :]), inc="pe")
                    cptok = em.push(
                        "act", [trtok],
                        lambda s, it=it, et=et, pt=pt: s.copy(
                            b.BTsb[:, et * 256 + it * 128:
                                   et * 256 + it * 128 + 128], pt[:, 0:128]),
                        inc="act")
                    trq += 1
            tok_tr_done = cptok
            if dbg and bb == 0 and ic == 0:
                tok_tr_done = em.dma("sync", [tok_tr_done], dbg["BTsb"],
                                     b.BTsb[:, :])

            last_stt = None
            for dt in range(8):
                slot = dt % 2
                psow = psq[4 + slot]
                rsl = (slice(dt * 128, (dt + 1) * 128),
                       slice(bb * 1024 + ic * 256, bb * 1024 + ic * 256 + 256))
                tok_xr[slot] = em.dma("sync", [tok_st[slot]],
                                      b.xres[slot], xt_d[rsl[0], rsl[1]])

                def fo(t, dt=dt, psow=psow, iofs=iofs):
                    for ec in range(16):
                        t.matmul(psow[:, 0:256],
                                 b.wpool[:, ec * 1024 + dt * 128:
                                         ec * 1024 + dt * 128 + 128],
                                 b.BTsb[:, ec * 256:(ec + 1) * 256],
                                 start=(ec == 0), stop=False)
                    ins = t.matmul(psow[:, 0:256],
                                   b.diag[:, dt * 128:(dt + 1) * 128],
                                   b.y[:, dt * TOKS + iofs:
                                       dt * TOKS + iofs + 256],
                                   start=False, stop=True)
                    return ins
                mmo = em.push("pe", [tok_tr_done, tok_ow, tok_diag,
                                     owps_free[slot]], fo, inc="pe")
                sttok = em.push("dve", [mmo, tok_xr[slot],
                                        store_toks[-2] if len(store_toks) >= 2
                                        else None],
                                lambda v, slot=slot, psow=psow: v.tensor_tensor(
                                    b.ores[slot], psow[:, 0:256],
                                    b.xres[slot], ALU.add),
                                inc="dve")
                tok_st[slot] = sttok
                owps_free[slot] = sttok
                last_stt = sttok
                store_toks.append(em.dma("sync", [sttok],
                                         out_d[rsl[0], rsl[1]],
                                         b.ores[slot]))
            tok_ic_done = last_stt

    em.push("sync", [store_toks[-1]], lambda sy: sy.nop())



def build_attn1(debug=False):
    nc = bass.Bass("TRN2", target_bir_lowering=False)
    xt_d = nc.dram_tensor("xt", [D, TOKS], F32, kind="ExternalInput").ap()
    wqa_d = nc.dram_tensor("wqa", [D, 64], F32, kind="ExternalInput").ap()
    wqb_d = nc.dram_tensor("wqbp", [64, D], F32, kind="ExternalInput").ap()
    wkva_d = nc.dram_tensor("wkva", [D, 32], F32, kind="ExternalInput").ap()
    wkvbk_d = nc.dram_tensor("wkvbk", [32, 256], F32, kind="ExternalInput").ap()
    wkvbv_d = nc.dram_tensor("wkvbv", [32, 256], F32, kind="ExternalInput").ap()
    wout_d = nc.dram_tensor("wout", [D, D], F32, kind="ExternalInput").ap()
    bout_d = nc.dram_tensor("boutc", [128, 8], F32, kind="ExternalInput").ap()
    cos_d = nc.dram_tensor("cosF", [128, LOCP], F32, kind="ExternalInput").ap()
    sin_d = nc.dram_tensor("sinF", [128, LOCP], F32, kind="ExternalInput").ap()
    mask_d = nc.dram_tensor("maskM", [128, 896], F32, kind="ExternalInput").ap()
    vtempl_d = nc.dram_tensor("vtempl", [8 * TOKS, 65], F32, kind="ExternalInput").ap()
    consts_d = nc.dram_tensor("consts", [128, 2], F32, kind="ExternalInput").ap()
    onesrow_d = nc.dram_tensor("onesrow", [1, 128], F32, kind="ExternalInput").ap()
    out_d = nc.dram_tensor("xo", [D, TOKS], F32, kind="ExternalOutput").ap()

    dbg = {}
    if debug:
        for nm, shape, dt in [("qTh", [128, 16384], BF16),
                              ("kTh0", [128, 8192], BF16),
                              ("vaug0", [128, 4160], BF16),
                              ("ccol", [128, 4], F32),
                              ("Asb", [64, 1024], F32),
                              ("dinv", [1, 512], F32),
                              ("exp0", [128, 2048], BF16)]:
            dbg[nm] = nc.dram_tensor("dbg_" + nm, shape, dt,
                                     kind="ExternalOutput").ap()
    ccq_in = nc.dram_tensor("cc1q_in", [1024, TOKS], BF16).ap()
    ccq_out = nc.dram_tensor("cc1q_out", [1024, TOKS], BF16).ap()
    cck_in = nc.dram_tensor("cc1k_in", [512, TOKS], BF16).ap()
    cck_out = nc.dram_tensor("cc1k_out", [512, TOKS], BF16).ap()
    ccv_in = nc.dram_tensor("cc1v_in", [8 * TOKS, 65], BF16).ap()
    ccv_out = nc.dram_tensor("cc1v_out", [8 * TOKS, 65], BF16).ap()
    cco_in = nc.dram_tensor("cc1o_in", [1024, TOKS], BF16).ap()
    cco_out = nc.dram_tensor("cc1o_out", [1024, TOKS], BF16).ap()

    with contextlib.ExitStack() as st:
        E = st.enter_context
        em = Em(nc, st)
        b = Bufs()
        b.big1 = E(nc.sbuf_tensor("big1", [128, 16384], BF16))
        b.big2 = E(nc.sbuf_tensor("big2", [128, 16384], BF16))
        b.big3 = E(nc.sbuf_tensor("big3", [128, 16384], BF16))
        b.xn = b.big1
        b.qTh = b.big1
        b.qTl = b.big3
        b.kTh = b.big3[:, 0:8192]
        b.vaugb = b.big3[:, 8192:12352]
        
        b.outT = b.big2
        b.aT = b.big2[:, 0:8192]
        b.woutsb = b.big2[:, 8192:16384]
        b.expbuf = b.big2[:, 0:4096]
        big2f = b.big2.bitcast(F32)
        b.m_row = big2f[0:1, 0:2048]
        b.rs_row = big2f[0:1, 2048:4096]
        b.kTloc = E(nc.sbuf_tensor("kTloc1", [128, 4096], BF16))
        b.vaugl = E(nc.sbuf_tensor("vaugl", [128, 16 * 260], BF16))
        b.qsT = E(nc.sbuf_tensor("qsT", [64, 2048], BF16))
        b.kvsT = E(nc.sbuf_tensor("kvsT", [32, 2048], BF16))
        b.swb = E(nc.sbuf_tensor("swb", [128, 2048], BF16))
        b.costab = E(nc.sbuf_tensor("costabF", [128, LOCP], F32))
        b.sintab = E(nc.sbuf_tensor("sintabF", [128, LOCP], F32))
        b.maskm = E(nc.sbuf_tensor("maskm", [128, 896], BF16))
        b.wqa = E(nc.sbuf_tensor("wqasb", [128, 8 * 64], BF16))
        b.wqb = E(nc.sbuf_tensor("wqbsb", [64, D], BF16))
        b.wkva = E(nc.sbuf_tensor("wkvasb", [128, 8 * 32], BF16))
        b.wkvbk = E(nc.sbuf_tensor("wkvbksb", [32, 256], BF16))
        b.wkvbv = E(nc.sbuf_tensor("wkvbvsb", [32, 256], BF16))
        b.boutc = E(nc.sbuf_tensor("boutcsb", [128, 8], F32))
        b.Asb = E(nc.sbuf_tensor("Asb", [64, 1024], F32))
        b.odiv = [E(nc.sbuf_tensor(f"odiv{i}", [64, 512], BF16))
                  for i in range(2)]
        b.dinv = E(nc.sbuf_tensor("dinv", [1, 512], F32))
        b.ccol = E(nc.sbuf_tensor("ccol", [128, 4], F32))
        b.crow = E(nc.sbuf_tensor("crow", [1, 4], F32))
        b.consts = E(nc.sbuf_tensor("consts_sb1", [128, 2], F32))
        b.ones_col = b.consts[:, 0:1]
        b.eps_col = b.consts[:, 1:2]
        b.ones_row = E(nc.sbuf_tensor("ones_row1", [1, 128], F32))
        b.xchunk = [E(nc.sbuf_tensor(f"xchunk1_{i}", [128, TOKS], F32))
                    for i in range(2)]
        b.scratch = E(nc.sbuf_tensor("scratch1", [128, 2560], F32))
        b.xsq = [b.scratch[:, 0:512], b.scratch[:, 512:1024]]
        b.ln_tmp = b.scratch[:, 1024:1536]
        b.xres = [b.scratch[:, 512:768], b.scratch[:, 768:1024]]
        b.ores = [b.scratch[:, 1024:1280], b.scratch[:, 1280:1536]]
        b.ps_a = E(nc.psum_tensor("ps_a1", [128, 2048], F32))
        b.ps_b = E(nc.psum_tensor("ps_b1", [128, 2048], F32))
        block = E(nc.Block())

        emit_attn1(em, b, xt_d, wqa_d, wqb_d, wkva_d, wkvbk_d, wkvbv_d,
                   wout_d, bout_d, cos_d, sin_d, mask_d, vtempl_d, consts_d,
                   onesrow_d, out_d, ccq_in, ccq_out, cck_in, cck_out,
                   ccv_in, ccv_out, cco_in, cco_out, dbg)
        em.run(block)
    return nc


def emit_attn1(em, b, xt_d, wqa_d, wqb_d, wkva_d, wkvbk_d, wkvbv_d,
               wout_d, bout_d, cos_d, sin_d, mask_d, vtempl_d, consts_d,
               onesrow_d, out_d, ccq_in, ccq_out, cck_in, cck_out,
               ccv_in, ccv_out, cco_in, cco_out, dbg=None):
    from concourse.aot_env import allow_ici_all_to_all
    SCL = 0.125
    em._tok_consts = em.dma("pool", [], b.consts[:, :], consts_d)
    tok_ones = em.dma("pool", [], b.ones_row[0:1, :], onesrow_d)
    tok_cos = em.dma("pool", [], b.costab[:, :], cos_d)
    tok_sin = em.dma("pool", [], b.sintab[:, :], sin_d)
    tok_mask = em.dma("pool", [], b.maskm[:, :], mask_d)
    tok_bout = em.dma("pool", [], b.boutc[:, :], bout_d)
    tok_wqa = em.dma("pool", [],
                     b.wqa[:, :].rearrange("p (ic f) -> p ic f", ic=8),
                     wqa_d.rearrange("(ic p) f -> p ic f", p=128))
    tok_wqb = em.dma("pool", [], b.wqb[:, :], wqb_d)
    tok_wkva = em.dma("pool", [],
                      b.wkva[:, :].rearrange("p (ic f) -> p ic f", ic=8),
                      wkva_d.rearrange("(ic p) f -> p ic f", p=128))
    tok_wkvbk = em.dma("pool", [], b.wkvbk[:, :], wkvbk_d)
    tok_wkvbv = em.dma("pool", [], b.wkvbv[:, :], wkvbv_d)
    tok_vt = em.dma("pool", [], ccv_in, vtempl_d)

    tok_xn = emit_ln(em, b, TOKS, xt_d, b.xn, 0,
                     extra_waits=(tok_ones, em._tok_consts))
    tok_xn = em.fence([tok_xn], b.scratch[0:1, 0:1], b.consts[0:1, 0:1])

    psq = [b.ps_a[:, i * 512:(i + 1) * 512] for i in range(4)] + \
          [b.ps_b[:, i * 512:(i + 1) * 512] for i in range(4)]

    def fqs(t):
        for cg in range(4):
            for ic in range(8):
                ins = t.matmul(b.ps_a[0:64, cg * 512:(cg + 1) * 512],
                               b.wqa[:, ic * 64:(ic + 1) * 64],
                               b.xn[:, ic * TOKS + cg * 512:
                                    ic * TOKS + cg * 512 + 512],
                               start=(ic == 0), stop=(ic == 7))
        return ins
    tok_qs_mm = em.push("pe", [tok_wqa, tok_xn], fqs, inc="pe")
    tok_qs = em.push("act", [tok_qs_mm],
                     lambda s: s.copy(b.qsT[:, :], b.ps_a[0:64, 0:2048]),
                     inc="act")

    def fkvs(t):
        for cg in range(4):
            for ic in range(8):
                ins = t.matmul(b.ps_b[0:32, cg * 512:(cg + 1) * 512],
                               b.wkva[:, ic * 32:(ic + 1) * 32],
                               b.xn[:, ic * TOKS + cg * 512:
                                    ic * TOKS + cg * 512 + 512],
                               start=(ic == 0), stop=(ic == 7))
        return ins
    tok_kvs_mm = em.push("pe", [tok_wkva, tok_xn], fkvs, inc="pe")
    tok_kvs = em.push("act", [tok_kvs_mm],
                      lambda s: s.copy(b.kvsT[:, :], b.ps_b[0:32, 0:2048]),
                      inc="act")
    tok_sfence = em.push("act", [tok_qs, tok_kvs],
                         lambda s: s.copy(b.crow[0:1, 3:4], b.consts[0:1, 0:1]),
                         inc="act")

    ps_free = [tok_qs_mm] * 8
    qctr = 0
    tok_qtl = None
    for et in range(8):
        for cg in range(4):
            q = qctr % 8
            qctr += 1
            mm = em.push("pe", [tok_wqb, tok_sfence, ps_free[q]],
                         lambda t, et=et, cg=cg, q=q: t.matmul(
                             psq[q], b.wqb[:, et * 128:(et + 1) * 128],
                             b.qsT[0:64, cg * 512:(cg + 1) * 512],
                             start=True, stop=True), inc="pe")
            tok_qtl = em.push("act", [mm],
                              lambda s, et=et, cg=cg, q=q: s.copy(
                                  b.qTl[:, et * 2048 + cg * 512:
                                        et * 2048 + cg * 512 + 512], psq[q]),
                              inc="act")
            ps_free[q] = tok_qtl
    tok_ktl = None
    for kt in range(2):
        for cg in range(4):
            q = qctr % 8
            qctr += 1
            mm = em.push("pe", [tok_wkvbk, tok_sfence, ps_free[q]],
                         lambda t, kt=kt, cg=cg, q=q: t.matmul(
                             psq[q], b.wkvbk[:, kt * 128:(kt + 1) * 128],
                             b.kvsT[0:32, cg * 512:(cg + 1) * 512],
                             start=True, stop=True), inc="pe")
            tok_ktl = em.push("act", [mm],
                              lambda s, kt=kt, cg=cg, q=q: s.copy(
                                  b.kTloc[:, kt * 2048 + cg * 512:
                                          kt * 2048 + cg * 512 + 512], psq[q]),
                              inc="act")
            ps_free[q] = tok_ktl
    tok_vl = None
    for tt in range(16):
        q = qctr % 8
        qctr += 1
        mm = em.push("pe", [tok_wkvbv, tok_sfence, ps_free[q]],
                     lambda t, tt=tt, q=q: t.matmul(
                         psq[q][:, 0:256], b.kvsT[0:32, tt * 128:(tt + 1) * 128],
                         b.wkvbv[:, :], start=True, stop=True), inc="pe")
        vsl = b.vaugl[:, tt * 260:(tt + 1) * 260].rearrange(
            "p (kh x) -> p kh x", kh=4)[:, :, 0:64]
        tok_vl = em.push("act", [mm],
                         lambda s, q=q, vsl=vsl: s.copy(
                             vsl, psq[q][:, 0:256].rearrange(
                                 "p (kh x) -> p kh x", kh=4)),
                         inc="act")
        ps_free[q] = tok_vl

    tok_rope = None
    tiles = [("q", et) for et in range(8)] + [("k", kt) for kt in range(2)]
    for kind, ti in tiles:
        srcb = b.qTl if kind == "q" else b.kTloc
        c0 = ti * 2048
        dep = tok_qtl if kind == "q" else tok_ktl
        sw_toks = []
        for (d0, s0) in ((0, 32), (32, 0), (64, 96), (96, 64)):
            sw_toks.append(em.dma(
                "pool", [dep, tok_rope],
                b.swb[d0:d0 + 32, :], srcb[s0:s0 + 32, c0:c0 + 2048]))
        for bb in range(2):
            sl = slice(c0 + bb * LOCP, c0 + bb * LOCP + LOCP)
            swsl = slice(bb * LOCP, bb * LOCP + LOCP)
            em.push("dve", [dep, *sw_toks, tok_cos, tok_sin],
                    lambda v, srcb=srcb, sl=sl: v.tensor_tensor(
                        b.scratch[:, 0:1024], srcb[:, sl], b.costab[:, :],
                        ALU.mult))
            em.push("dve", [],
                    lambda v, swsl=swsl: v.tensor_tensor(
                        b.scratch[:, 1024:2048], b.swb[:, swsl],
                        b.sintab[:, :], ALU.mult))
            tok_rope = em.push("dve", [],
                               lambda v, srcb=srcb, sl=sl: v.tensor_tensor(
                                   srcb[:, sl], b.scratch[:, 0:1024],
                                   b.scratch[:, 1024:2048], ALU.add),
                               inc="dve")

    ctoks = []
    for j in range(8):
        ctoks.append(em.dma("sync", [tok_rope],
                            ccq_in[j * 128:(j + 1) * 128, :],
                            b.qTl[:, j * 2048:(j + 1) * 2048]))
    for j in range(8):
        kh = j // 2
        kt, half = kh // 2, kh % 2
        ctoks.append(em.dma("sync", [tok_rope],
                            cck_in[j * 64:(j + 1) * 64, :],
                            b.kTloc[half * 64:(half + 1) * 64,
                                    kt * 2048:(kt + 1) * 2048]))
    for j in range(8):
        kh = j // 2
        vsl = b.vaugl[:, :].rearrange("p (tt r) -> p tt r", tt=16)[
            :, :, kh * 65:kh * 65 + 64]
        dsl = ccv_in[j * 2048:(j + 1) * 2048, 0:64].rearrange(
            "(tt p) r -> p tt r", p=128)
        ctoks.append(em.dma("sync", [tok_vl, tok_vt], dsl, vsl))

    def a2a(g, cin, cout):
        with allow_ici_all_to_all():
            return g.collective_compute(
                "AllToAll", ALU.bypass, replica_groups=[list(range(8))],
                ins=[cin.opt()], outs=[cout.opt()])
    tok_ccq = em.push("pool", ctoks, lambda g: a2a(g, ccq_in, ccq_out),
                      inc=em.next_cc())
    tok_cck = em.push("pool", [], lambda g: a2a(g, cck_in, cck_out),
                      inc=em.next_cc())
    tok_ccv = em.push("pool", [tok_ccq], lambda g: a2a(g, ccv_in, ccv_out),
                      inc=em.next_cc())

    tok_qth = None
    for bb in range(2):
        for s in range(8):
            tok_qth = em.dma("sync", [tok_ccq],
                             b.qTh[:, bb * 8192 + s * 1024:
                                   bb * 8192 + s * 1024 + 1024],
                             ccq_out[s * 128:(s + 1) * 128,
                                     bb * 1024:(bb + 1) * 1024])
    tok_out = None
    pvps = [b.ps_b[:, 0:512], b.ps_b[:, 512:1024]]
    dbps = [b.ps_b[:, 1024:1536], b.ps_b[:, 1024:1536]]
    junkps = b.ps_b[:, 1536:2048]
    scpair = [b.ps_a[:, 0:1024], b.ps_a[:, 1024:2048]]

    def _later(a, b_):
        if a is None:
            return b_
        if b_ is None:
            return a
        assert a[0] == b_[0]
        return a if a[1] >= b_[1] else b_
    sc_free = [_later(ps_free[0], ps_free[1]), _later(ps_free[2], ps_free[3])]
    pv_free = [ps_free[4], ps_free[5]]
    db_free = [ps_free[6], ps_free[7]]
    exp_free = [None] * 4
    tok_vaugb = None
    pvctr = 0
    otoks = []
    ostore = [None, None]
    ca_war = [None, None]
    tok_kth = None
    tok_kth_war = None
    for bb in range(2):
        for half in range(2):
            for s in range(8):
                tok_kth = em.dma("sync", [tok_cck, tok_kth_war],
                                 b.kTh[half * 64:(half + 1) * 64,
                                       s * 1024:(s + 1) * 1024],
                                 cck_out[s * 64:(s + 1) * 64,
                                         bb * 1024:(bb + 1) * 1024])
        em.push("dve", [tok_qth, tok_kth],
                lambda v, bb=bb: v.tensor_reduce(
                    b.ccol[:, 2:3], b.qTh[:, bb * 8192:(bb + 1) * 8192],
                    AX.X, ALU.max, apply_absolute_value=True))
        em.push("dve", [],
                lambda v: v.tensor_tensor(b.ccol[:, 2:3], b.ccol[:, 2:3],
                                          b.ccol[:, 2:3], ALU.mult))
        em.push("dve", [],
                lambda v: v.tensor_reduce(
                    b.ccol[0:64, 3:4], b.kTh[0:64, :],
                    AX.X, ALU.max, apply_absolute_value=True))
        tok_bnd = em.push("dve", [],
                          lambda v: v.tensor_tensor(
                              b.ccol[0:64, 3:4], b.ccol[0:64, 3:4],
                              b.ccol[0:64, 3:4], ALU.mult), inc="dve")
        mm1 = em.push("pe", [tok_bnd, ps_free[2], sc_free[1]],
                      lambda t: t.matmul(b.ps_a[0:1, 1024:1025],
                                         b.ones_col, b.ccol[:, 2:3],
                                         start=True, stop=True), inc="pe")
        mm2 = em.push("pe", [],
                      lambda t: t.matmul(b.ps_a[0:1, 1026:1027],
                                         b.ones_col[0:64, :], b.ccol[0:64, 3:4],
                                         start=True, stop=True), inc="pe")
        em.push("act", [mm1, mm2],
                lambda s: s.activation(b.crow[0:1, 0:1], b.ps_a[0:1, 1024:1025],
                                       AF.Sqrt))
        tok_c1 = em.push("act", [],
                         lambda s: s.activation(b.crow[0:1, 1:2],
                                                b.ps_a[0:1, 1026:1027],
                                                AF.Sqrt), inc="act")
        em.push("dve", [tok_c1],
                lambda v: v.tensor_tensor(b.crow[0:1, 0:1], b.crow[0:1, 0:1],
                                          b.crow[0:1, 1:2], ALU.mult))
        tok_c2 = em.push("dve", [],
                         lambda v: v.tensor_scalar(b.crow[0:1, 0:1],
                                                   b.crow[0:1, 0:1],
                                                   -SCL, None, ALU.mult),
                         inc="dve")
        mm3 = em.push("pe", [tok_c2, tok_ones],
                      lambda t: t.matmul(b.ps_a[0:128, 1028:1029],
                                         b.ones_row[0:1, :], b.crow[0:1, 0:1],
                                         start=True, stop=True), inc="pe")
        tok_negc = em.push("dve", [mm3],
                           lambda v, bb=bb: v.tensor_copy(
                               b.ccol[:, bb:bb + 1], b.ps_a[:, 1028:1029]),
                           inc="dve")
        va_toks = []
        for s in range(8):
            va_toks.append(em.dma(
                "sync", [tok_ccv, tok_vaugb],
                b.vaugb.rearrange("p (jt r) -> p jt r", jt=64)[:, s * 8:
                                                              (s + 1) * 8, :],
                ccv_out[s * 2048 + bb * 1024:s * 2048 + bb * 1024 + 1024,
                        :].rearrange("(jt p) r -> p jt r", p=128)))
        tok_va = va_toks[-1]
        for iblk in range(16):
            for hl in range(2):
                pvp = pvps[pvctr % 2]
                njt = 4 * (iblk + 1)
                npair = njt // 2
                ptoks = {}
                pvtok = None
                for jp in range(npair + 2):
                    if jp < npair:
                        pr = jp % 2
                        scp = scpair[pr]
                        mm = None
                        for half in range(2):
                            jt = 2 * jp + half
                            mm = em.push(
                                "pe", [tok_qth, tok_kth, sc_free[pr],
                                       tok_negc],
                                lambda t, jt=jt, bb=bb, hl=hl, iblk=iblk,
                                scp=scp, half=half: t.matmul(
                                    scp[:, half * 512:(half + 1) * 512],
                                    b.kTh[hl * 64:(hl + 1) * 64,
                                          jt * 128:(jt + 1) * 128],
                                    b.qTh[hl * 64:(hl + 1) * 64,
                                          bb * 8192 + iblk * 512:
                                          bb * 8192 + iblk * 512 + 512],
                                    start=True, stop=True), inc="pe")
                        eb = jp % 4
                        etok = em.push(
                            "act", [mm, tok_negc, exp_free[eb]],
                            lambda s, eb=eb, scp=scp, bb=bb: s.activation(
                                b.expbuf[:, eb * 1024:(eb + 1) * 1024], scp,
                                AF.Exp, bias=b.ccol[:, bb:bb + 1], scale=SCL),
                            inc="act")
                        sc_free[pr] = etok
                        mtoks = [etok, etok]
                        for half in range(2):
                            jt = 2 * jp + half
                            if jt >= 4 * iblk:
                                m0 = 384 - 128 * (jt - 4 * iblk)
                                col = eb * 1024 + half * 512
                                mtoks[half] = em.push(
                                    "dve", [etok, tok_mask],
                                    lambda v, col=col, m0=m0: v.tensor_tensor(
                                        b.expbuf[:, col:col + 512],
                                        b.expbuf[:, col:col + 512],
                                        b.maskm[:, m0:m0 + 512], ALU.mult),
                                    inc="dve")
                        ptoks[jp] = mtoks
                    jq = jp - 2
                    if jq >= 0:
                        ebq = jq % 4
                        mtoks_q = ptoks.pop(jq)
                        for half in range(2):
                            jt = 2 * jq + half
                            col = ebq * 1024 + half * 512
                            pvtok = em.push(
                                "pe", [mtoks_q[half], tok_va,
                                       pv_free[pvctr % 2]],
                                lambda t, jt=jt, col=col, pvp=pvp, njt=njt:
                                t.matmul(
                                    pvp[0:65, :],
                                    b.vaugb[:, jt * 65:jt * 65 + 65],
                                    b.expbuf[:, col:col + 512],
                                    start=(jt == 0), stop=(jt == njt - 1)),
                                inc="pe")
                        exp_free[ebq] = pvtok
                        em.push("pe", [],
                                lambda t: t.matmul(
                                    junkps[:, 0:512], b.maskm[:, 0:128],
                                    b.maskm[:, 0:512], start=True, stop=True))
                slot = pvctr % 2
                asl = slice(slot * 512, slot * 512 + 512)
                dtok = em.push("dve", [pvtok, db_free[slot]],
                               lambda v, pvp=pvp: v.reciprocal(
                                   b.dinv[0:1, :], pvp[64:65, :]), inc="dve")
                catok = em.push("dve", [pvtok, ca_war[slot]],
                                lambda v, pvp=pvp, asl=asl: v.tensor_copy(
                                    b.Asb[:, asl], pvp[0:64, :]),
                                inc="dve")
                pv_free[slot] = catok
                dbp = dbps[slot]
                mmD = em.push("pe", [dtok, db_free[slot]],
                              lambda t, dbp=dbp: t.matmul(
                                  dbp[0:64, :], b.ones_row[0:1, 0:64],
                                  b.dinv[0:1, :], start=True, stop=True),
                              inc="pe")
                divtok = em.push(
                    "dve", [mmD, catok],
                    lambda v, slot=slot, asl=asl, dbp=dbp: v.tensor_tensor(
                        b.odiv[slot][:, :], b.Asb[:, asl], dbp[0:64, :], ALU.mult),
                    inc="dve")
                db_free[slot] = divtok
                ca_war[slot] = divtok
                jdst = iblk // 2
                tok_out = em.dma(
                    "sync", [divtok, ostore[slot]],
                    cco_in[jdst * 128 + hl * 64:jdst * 128 + hl * 64 + 64,
                           bb * 1024 + (iblk % 2) * 512:
                           bb * 1024 + (iblk % 2) * 512 + 512],
                    b.odiv[slot][:, :])
                ostore[slot] = tok_out
                otoks.append(tok_out)
                if dbg and bb == 0 and hl == 0 and iblk == 0:
                    d1 = em.dma("sync", [divtok], dbg["Asb"], b.Asb[:, :])
                    d2 = em.dma("sync", [divtok], dbg["dinv"],
                                b.dinv[0:1, :])
                    d3 = em.dma("sync", [divtok], dbg["exp0"],
                                b.expbuf[:, :])
                    d4 = em.dma("sync", [divtok], dbg["qTh"], b.qTh[:, :])
                    d5 = em.dma("sync", [divtok], dbg["kTh0"],
                                b.kTh[:, :])
                    d6 = em.dma("sync", [divtok], dbg["vaug0"],
                                b.vaugb[:, :])
                    d7 = em.dma("sync", [divtok], dbg["ccol"], b.ccol[:, :])
                    ostore[slot] = d7
                pvctr += 1
        tok_vaugb = tok_out
        tok_kth_war = em.cnt_tok("pe")

    tok_cco = em.push("pool", otoks, lambda g: a2a(g, cco_in, cco_out),
                      inc=em.next_cc())

    tok_wout = em.dma("pool", [tok_cco],
                      b.woutsb.rearrange("p (ec dd) -> p ec dd", ec=8),
                      wout_d.rearrange("(ec p) dd -> p ec dd", p=128))
    at_toks = {}
    store_toks = []
    tok_st = [None, None]
    ow_free = [None, None]
    for bb in range(2):
        tok_at = None
        for ec in range(8):
            tok_at = em.dma("sync", [tok_cco, at_toks.get("war")],
                            b.aT[:, ec * 1024:(ec + 1) * 1024],
                            cco_out[ec * 128:(ec + 1) * 128,
                                    bb * 1024:(bb + 1) * 1024])
        at_toks[bb] = tok_at
        for ic in range(4):
            for dt in range(8):
                slot = dt % 2
                psow = psq[4 + slot]
                rsl = (slice(dt * 128, (dt + 1) * 128),
                       slice(bb * 1024 + ic * 256, bb * 1024 + ic * 256 + 256))
                xr = em.dma("sync", [tok_st[slot]], b.xres[slot],
                            xt_d[rsl[0], rsl[1]])

                def fo(t, dt=dt, ic=ic, psow=psow):
                    for ec in range(8):
                        ins = t.matmul(psow[:, 0:256],
                                       b.woutsb[:, ec * 1024 + dt * 128:
                                                ec * 1024 + dt * 128 + 128],
                                       b.aT[:, ec * 1024 + ic * 256:
                                            ec * 1024 + ic * 256 + 256],
                                       start=(ec == 0), stop=(ec == 7))
                    return ins
                mmo = em.push("pe", [tok_at, tok_wout, ow_free[slot]], fo,
                              inc="pe")
                sttok = em.push(
                    "dve", [mmo, xr, tok_bout,
                            store_toks[-2] if len(store_toks) >= 2 else None],
                    lambda v, slot=slot, psow=psow, dt=dt: v.scalar_tensor_tensor(
                        b.ores[slot], psow[:, 0:256], b.boutc[:, dt:dt + 1],
                        b.xres[slot], ALU.add, ALU.add),
                    inc="dve")
                tok_st[slot] = sttok
                ow_free[slot] = sttok
                store_toks.append(em.dma("sync", [sttok],
                                         out_d[rsl[0], rsl[1]],
                                         b.ores[slot]))
        at_toks["war"] = em.cnt_tok("pe")

    em.push("sync", [store_toks[-1]], lambda sy: sy.nop())



class Pool:
    def __init__(self, ap_f32):
        self.ap = ap_f32
        self.off = 0

    def reset(self):
        self.off = 0

    def f32(self, cols):
        s = self.ap[:, self.off:self.off + cols]
        self.off += cols
        return s

    def bf16(self, cols):
        need = (cols + 1) // 2
        return self.f32(need).bitcast(BF16)[:, 0:cols]


def mk_ffn_bufs(p, shared):
    b = Bufs()
    b.__dict__.update(shared)
    b.c32 = p.f32(8 * LOCP)
    b.c_bf = p.bf16(8 * LOCP)
    b.xn = p.bf16(8 * TOKS)
    b.gate = [p.bf16(8 * 512) for _ in range(3)]
    b.wbuf = [p.bf16(8 * D) for _ in range(4)]
    b.xchunk = [b.c32[:, 0:TOKS], b.c32[:, TOKS:2 * TOKS]]
    b.scratch = p.f32(2560)
    b.xsq = [b.scratch[:, 0:512], b.scratch[:, 512:1024]]
    b.ln_tmp = b.scratch[:, 1024:1536]
    b.t1 = b.scratch[:, 0:512]
    b.t2 = b.scratch[:, 512:1024]
    b.hbuf = [b.scratch[:, 1024:1280].bitcast(BF16),
              b.scratch[:, 1280:1536].bitcast(BF16)]
    b.xres = [b.scratch[:, 1536:2048], b.scratch[:, 2048:2560]]
    b.ores = [b.scratch[:, 0:512], b.scratch[:, 512:1024]]
    b.biases = p.f32(40)
    b.m_row = p.f32(2048)[0:1, :]
    b.rs_row = p.f32(2048)[0:1, :]
    return b


def mk_attn0_bufs(p, shared):
    b = Bufs()
    b.__dict__.update(shared)
    b.bigbuf = p.bf16(16384)
    b.y = p.bf16(16384)
    b.wpool = p.bf16(16384)
    b.kT = p.bf16(8192)
    b.qT = p.bf16(2048)
    b.kTloc = p.bf16(2048)
    b.vsb = [p.bf16(2048) for _ in range(2)]
    b.usb = [p.bf16(2048) for _ in range(2)]
    b.tileAB = p.bf16(4096)
    b.utile = b.tileAB[:, 0:2048]
    b.vtile = b.tileAB[:, 2048:4096]
    b.Bsb = b.tileAB
    b.BTsb = p.bf16(4096)
    b.basebuf = p.f32(2048)
    b.vsb += [b.basebuf.bitcast(BF16)[:, 0:2048],
              b.basebuf.bitcast(BF16)[:, 2048:4096]]
    b.uvwb = p.bf16(8 * 128)
    b.ident = p.bf16(128)
    b.diag = p.bf16(8 * 128)
    b.corner = [p.f32(256) for _ in range(2)]
    b.costab = p.f32(LOCP)
    b.sintab = p.f32(LOCP)
    b.gb = p.f32(4)
    b.rsc = p.f32(8)
    b.smalls = p.f32(4)
    b.gsb = p.f32(2)[0:1, :]
    b.xchunk = [p.f32(TOKS) for _ in range(2)]
    b.scratch = p.f32(2560)
    b.xsq = [b.scratch[:, 0:512], b.scratch[:, 512:1024]]
    b.ln_tmp = b.scratch[:, 1024:1536]
    b.ropet = [b.scratch[:, 0:1024], b.scratch[:, 1024:2048]]
    b.rtile = b.scratch[:, 0:512]
    b.xres = [b.scratch[:, 512:768], b.scratch[:, 768:1024]]
    b.ores = [b.scratch[:, 1024:1280], b.scratch[:, 1280:1536]]
    ktf = b.kT.bitcast(F32)
    b.m_row = ktf[0:1, 0:2048]
    b.rs_row = ktf[0:1, 2048:4096]
    b.v_row = b.BTsb.bitcast(F32)[0:1, 0:2048]
    b.t_row = b.v_row
    b.rtile4 = b.xchunk[0]
    return b


def mk_attn1_bufs(p, shared):
    b = Bufs()
    b.__dict__.update(shared)
    b.big1 = p.bf16(16384)
    b.big2 = p.bf16(16384)
    b.big3 = p.bf16(16384)
    b.xn = b.big1
    b.qTh = b.big1
    b.qTl = b.big3
    b.kTh = b.big3[:, 0:8192]
    b.vaugb = b.big3[:, 8192:12352]
    b.outT = b.big2
    b.aT = b.big2[:, 0:8192]
    b.woutsb = b.big2[:, 8192:16384]
    b.expbuf = b.big2[:, 0:4096]
    big2f = b.big2.bitcast(F32)
    b.m_row = big2f[0:1, 0:2048]
    b.rs_row = big2f[0:1, 2048:4096]
    b.kTloc = p.bf16(4096)
    b.vaugl = p.bf16(16 * 260)
    b.qsT = p.bf16(2048)[0:64, :]
    b.kvsT = p.bf16(2048)[0:32, :]
    b.swb = p.bf16(2048)
    b.costab = p.f32(LOCP)
    b.sintab = p.f32(LOCP)
    b.maskm = p.bf16(896)
    b.wqa = p.bf16(8 * 64)
    b.wqb = p.bf16(D)[0:64, :]
    b.wkva = p.bf16(8 * 32)
    b.wkvbk = p.bf16(256)[0:32, :]
    b.wkvbv = p.bf16(256)[0:32, :]
    b.boutc = p.f32(8)
    b.Asb = p.f32(1024)[0:64, :]
    b.odiv = [p.bf16(512)[0:64, :] for _ in range(2)]
    b.dinv = p.f32(512)[0:1, :]
    b.ccol = p.f32(4)
    b.crow = p.f32(4)[0:1, :]
    b.xchunk = [p.f32(TOKS) for _ in range(2)]
    b.scratch = p.f32(2560)
    b.xsq = [b.scratch[:, 0:512], b.scratch[:, 512:1024]]
    b.ln_tmp = b.scratch[:, 1024:1536]
    b.xres = [b.scratch[:, 512:768], b.scratch[:, 768:1024]]
    b.ores = [b.scratch[:, 1024:1280], b.scratch[:, 1280:1536]]
    return b


def build_fused():
    nc = bass.Bass("TRN2", target_bir_lowering=False)
    xt_d = nc.dram_tensor("xt", [D, TOKS], F32, kind="ExternalInput").ap()
    out_d = nc.dram_tensor("xo", [D, TOKS], F32, kind="ExternalOutput").ap()
    xs = [nc.dram_tensor(f"xs{i}", [D, TOKS], F32).ap() for i in range(5)]

    wd = {}
    wdb = {}
    for L in (0, 1):
        for n in ("wf", "uf", "wi", "ui", "wc", "uc", "wo", "uo"):
            wd[(L, n)] = nc.dram_tensor(f"l{L}_{n}T", [D, D], F32,
                                        kind="ExternalInput").ap()
            wdb[(L, n)] = nc.dram_tensor(f"l{L}_{n}b", [D, D], BF16).ap()
    bias_d = {L: nc.dram_tensor(f"l{L}_biases", [128, 40], F32,
                                kind="ExternalInput").ap() for L in (0, 1)}
    uvw_d = nc.dram_tensor("uvwT", [D, 4224], F32, kind="ExternalInput").ap()
    ow_d = nc.dram_tensor("owT", [2048, D], F32, kind="ExternalInput").ap()
    gb_d = nc.dram_tensor("gbcols", [128, 4], F32, kind="ExternalInput").ap()
    g_d = nc.dram_tensor("gval", [1, 1], F32, kind="ExternalInput").ap()
    w2046_d = nc.dram_tensor("w2046", [1, 1], F32, kind="ExternalInput").ap()
    corner_d = nc.dram_tensor("corner", [64 * 128, 256], F32,
                              kind="ExternalInput").ap()
    rsc_d = nc.dram_tensor("rsc", [128, 8], F32, kind="ExternalInput").ap()
    cos0_d = nc.dram_tensor("cosT", [128, LOCP], F32, kind="ExternalInput").ap()
    sin0_d = nc.dram_tensor("sinT", [128, LOCP], F32, kind="ExternalInput").ap()
    ident_d = nc.dram_tensor("identb", [128, 128], F32, kind="ExternalInput").ap()
    wqa_d = nc.dram_tensor("wqa", [D, 64], F32, kind="ExternalInput").ap()
    wqb_d = nc.dram_tensor("wqbp", [64, D], F32, kind="ExternalInput").ap()
    wkva_d = nc.dram_tensor("wkva", [D, 32], F32, kind="ExternalInput").ap()
    wkvbk_d = nc.dram_tensor("wkvbk", [32, 256], F32, kind="ExternalInput").ap()
    wkvbv_d = nc.dram_tensor("wkvbv", [32, 256], F32, kind="ExternalInput").ap()
    wout_d = nc.dram_tensor("wout", [D, D], F32, kind="ExternalInput").ap()
    bout_d = nc.dram_tensor("boutc", [128, 8], F32, kind="ExternalInput").ap()
    cos1_d = nc.dram_tensor("cosF", [128, LOCP], F32, kind="ExternalInput").ap()
    sin1_d = nc.dram_tensor("sinF", [128, LOCP], F32, kind="ExternalInput").ap()
    mask_d = nc.dram_tensor("maskM", [128, 896], F32, kind="ExternalInput").ap()
    vtempl_d = nc.dram_tensor("vtempl", [8 * TOKS, 65], F32,
                              kind="ExternalInput").ap()
    consts_d = nc.dram_tensor("consts", [128, 2], F32, kind="ExternalInput").ap()
    onesrow_d = nc.dram_tensor("onesrow", [1, 128], F32,
                               kind="ExternalInput").ap()
    cc_in = nc.dram_tensor("cc0_in", [2176, 2048], BF16).ap()
    cc_out = nc.dram_tensor("cc0_out", [8 * 2176, 2048], BF16).ap()
    cck0_in = nc.dram_tensor("cck0_in", [128, 2048], BF16).ap()
    cck0_out = nc.dram_tensor("cck0_out", [8 * 128, 2048], BF16).ap()
    ccv0_in = nc.dram_tensor("ccv0_in", [1024, 2048], BF16).ap()
    ccv1_in = nc.dram_tensor("ccv1_in", [1024, 2048], BF16).ap()
    ccv0_out = nc.dram_tensor("ccv0_out", [8 * 1024, 2048], BF16).ap()
    ccv1_out = nc.dram_tensor("ccv1_out", [8 * 1024, 2048], BF16).ap()
    u_spill = nc.dram_tensor("u_spill", [2048, 2048], BF16).ap()
    ccq_in = nc.dram_tensor("cc1q_in", [1024, TOKS], BF16).ap()
    ccq_out = nc.dram_tensor("cc1q_out", [1024, TOKS], BF16).ap()
    cck_in = nc.dram_tensor("cc1k_in", [512, TOKS], BF16).ap()
    cck_out = nc.dram_tensor("cc1k_out", [512, TOKS], BF16).ap()
    ccv_in = nc.dram_tensor("cc1v_in", [8 * TOKS, 65], BF16).ap()
    ccv_out = nc.dram_tensor("cc1v_out", [8 * TOKS, 65], BF16).ap()
    cco_in = nc.dram_tensor("cc1o_in", [1024, TOKS], BF16).ap()
    cco_out = nc.dram_tensor("cc1o_out", [1024, TOKS], BF16).ap()

    with contextlib.ExitStack() as st:
        E = st.enter_context
        em = Em(nc, st)
        shared = {}
        consts_sb = E(nc.sbuf_tensor("consts_sb", [128, 2], F32))
        onesrow_sb = E(nc.sbuf_tensor("ones_row", [1, 128], F32))
        tiny = E(nc.sbuf_tensor("tiny", [1, 4], F32))
        shared["consts"] = consts_sb[:, :]
        shared["ones_col"] = consts_sb[:, 0:1]
        shared["eps_col"] = consts_sb[:, 1:2]
        shared["ones_row"] = onesrow_sb[:, :]
        pool_t = E(nc.sbuf_tensor("pool", [128, 51456], F32))
        p = Pool(pool_t[:, :])
        shared["ps_a"] = E(nc.psum_tensor("ps_a", [128, 2048], F32))[:, :]
        shared["ps_b"] = E(nc.psum_tensor("ps_b", [128, 2048], F32))[:, :]
        block = E(nc.Block())

        em._tok_consts = em.dma("pool", [], shared["consts"], consts_d)
        em.dma("pool", [], shared["ones_row"], onesrow_d)
        wready = {}
        for L in (0, 1):
            for n in ("wf", "uf", "wi", "ui", "wc", "uc", "wo", "uo"):
                wready[L] = em.dma("pool", [], wdb[(L, n)], wd[(L, n)])

        def bar():
            em.barrier(tiny[0:1, 0:1], shared["consts"][0:1, 0:1])

        stages = [
            ("ffn", 0, xt_d, xs[0]),
            ("attn0", None, xs[0], xs[1]),
            ("ffn", 0, xs[1], xs[2]),
            ("ffn", 1, xs[2], xs[3]),
            ("attn1", None, xs[3], xs[4]),
            ("ffn", 1, xs[4], out_d),
        ]
        for kind, L, xin, xout in stages:
            p.reset()
            if kind == "ffn":
                b = mk_ffn_bufs(p, shared)
                w_d = {n: wdb[(L, n)] for n in
                       ("wf", "uf", "wi", "ui", "wc", "uc", "wo", "uo")}
                em._w_ready = wready[L]
                emit_ffn_body(em, b, xin, w_d, bias_d[L], xout,
                              consts_d=consts_d, onesrow_d=onesrow_d)
                em._w_ready = None
            elif kind == "attn0":
                b = mk_attn0_bufs(p, shared)
                emit_attn0(em, b, xin, uvw_d, ow_d, gb_d, g_d, w2046_d,
                           corner_d, rsc_d, cos0_d, sin0_d, ident_d,
                           consts_d, onesrow_d, xout, cc_in, cc_out, u_spill,
                           cck0=(cck0_in, cck0_out), ccv0_out=ccv0_out,
                           ccv1_out=ccv1_out, ccv0_in=ccv0_in,
                           ccv1_in=ccv1_in)
            else:
                b = mk_attn1_bufs(p, shared)
                emit_attn1(em, b, xin, wqa_d, wqb_d, wkva_d, wkvbk_d,
                           wkvbv_d, wout_d, bout_d, cos1_d, sin1_d, mask_d,
                           vtempl_d, consts_d, onesrow_d, xout,
                           ccq_in, ccq_out, cck_in, cck_out,
                           ccv_in, ccv_out, cco_in, cco_out)
            bar()
        em.run(block)
    return nc


def run_fused(inputs):
    nc = _CACHE.get("nc_fused")
    if nc is None:
        nc = _CACHE["nc_fused"] = build_fused()
    x = np.ascontiguousarray(inputs["L0_attn_x"], dtype=np.float32)
    shards = shard_xT(x)
    consts = np.stack([np.ones(128, np.float32),
                       np.full(128, EPS, np.float32)], axis=1)
    onesrow = np.ones((1, 128), np.float32)
    w0, bias0 = _ffn_inputs(inputs, 0)
    w1, bias1 = _ffn_inputs(inputs, 1)
    a0 = _attn0_inputs(inputs, 0)
    a1 = _attn1_inputs(inputs)
    in_maps = []
    for c in range(NCORE):
        m = dict(xt=shards[c], consts=consts, onesrow=onesrow,
                 l0_biases=bias0, l1_biases=bias1,
                 uvwT=a0["uvwT"], owT=a0["owT"], gbcols=a0["gbcols"],
                 gval=a0["gval"], w2046=a0["w2046"], corner=a0["corners"][c],
                 rsc=a0["rsc"], cosT=a0["cosT"][c], sinT=a0["sinT"][c],
                 identb=a0["identb"],
                 wqa=a1["wqa"], wqbp=a1["wqbp"], wkva=a1["wkva"],
                 wkvbk=a1["wkvbk"], wkvbv=a1["wkvbv"], wout=a1["wout"],
                 boutc=a1["boutc"], cosF=a1["cosF"][c], sinF=a1["sinF"][c],
                 maskM=a1["maskM"], vtempl=a1["vtempl"])
        for n in ("wf", "uf", "wi", "ui", "wc", "uc", "wo", "uo"):
            m[f"l0_{n}T"] = w0[n + "T"]
            m[f"l1_{n}T"] = w1[n + "T"]
        in_maps.append(m)
    res = run_bass_kernel_spmd(nc, in_maps, core_ids=list(range(NCORE)))
    return unshard_xT([r["xo"] for r in res.results])


_CACHE = {}


def _ffn_inputs(inputs, L):
    key = "ffn_w_" + str(L)
    if key not in _CACHE:
        p = f"L{L}_ffn_"
        w = {n + "T": transpose_w(inputs[p + n])
             for n in ("wf", "uf", "wi", "ui", "wc", "uc", "wo", "uo")}
        bias = np.concatenate([colpack(inputs[p + n])
                               for n in ("bf", "bi", "bc", "bo", "in_c")], axis=1)
        _CACHE[key] = (w, bias)
    return _CACHE[key]


def run_ffn(x, inputs, L):
    nc_ffn = _CACHE.get("nc_ffn")
    if nc_ffn is None:
        nc_ffn = _CACHE["nc_ffn"] = build_ffn()
    shards = shard_xT(x)
    w, bias = _ffn_inputs(inputs, L)
    consts = np.stack([np.ones(128, np.float32),
                       np.full(128, EPS, np.float32)], axis=1)
    onesrow = np.ones((1, 128), np.float32)
    in_maps = [dict(xt=shards[c], biases=bias, consts=consts,
                    onesrow=onesrow, **w) for c in range(NCORE)]
    res = run_bass_kernel_spmd(nc_ffn, in_maps, core_ids=list(range(NCORE)))
    return unshard_xT([r["xo"] for r in res.results])


def _attn0_inputs(inputs, core):
    key = "attn0_w"
    if key not in _CACHE:
        uvwT = transpose_w(inputs["L0_attn_uv_w"])
        owT = transpose_w(inputs["L0_attn_o_w"])
        gamma = np.asarray(inputs["L0_attn_gamma"], dtype=np.float32)
        beta = np.asarray(inputs["L0_attn_beta"], dtype=np.float32)
        gb = np.ascontiguousarray(
            np.stack([gamma[0], beta[0], gamma[1], beta[1]], axis=1))
        gval = np.asarray(inputs["L0_attn_g"], dtype=np.float32).reshape(1, 1)
        wrel = np.asarray(inputs["L0_attn_w_rel"], dtype=np.float32)
        w2046 = np.ascontiguousarray(wrel[2046:2047].reshape(1, 1))
        rsc = colpack(inputs["L0_attn_res_scale"])
        tabs = rope_tables()
        corners = []
        for c in range(NCORE):
            jg = np.arange(16 * 128)[:, None]
            tiles = []
            for icc in range(4):
                ig = LOCP * c + icc * 256 + np.arange(256)[None, :]
                idx = np.minimum(jg - ig + 8191, 2046)
                tiles.append(wrel[idx].reshape(16, 128, 256))
            corners.append(np.ascontiguousarray(
                np.stack(tiles, axis=0).reshape(64 * 128, 256)))
        cosT, sinT = [], []
        for c in range(NCORE):
            sl = slice(LOCP * c, LOCP * (c + 1))
            ct = tabs["l0_cos"][sl].T
            stt_ = tabs["l0_sin"][sl].T
            cosT.append(np.ascontiguousarray(np.vstack([ct, ct])))
            sinT.append(np.ascontiguousarray(np.vstack([stt_, stt_])))
        ident = np.eye(128, dtype=np.float32)
        _CACHE[key] = dict(uvwT=uvwT, owT=owT, gbcols=gb, gval=gval,
                           w2046=w2046, rsc=rsc, corners=corners,
                           cosT=cosT, sinT=sinT, identb=ident)
    return _CACHE[key]


def run_attn0(x, inputs):
    nc = _CACHE.get("nc_attn0")
    if nc is None:
        nc = _CACHE["nc_attn0"] = build_attn0()
    shards = shard_xT(x)
    w = _attn0_inputs(inputs, 0)
    consts = np.stack([np.ones(128, np.float32),
                       np.full(128, EPS, np.float32)], axis=1)
    onesrow = np.ones((1, 128), np.float32)
    in_maps = [dict(xt=shards[c], uvwT=w["uvwT"], owT=w["owT"],
                    gbcols=w["gbcols"], gval=w["gval"], w2046=w["w2046"],
                    corner=w["corners"][c], rsc=w["rsc"], cosT=w["cosT"][c],
                    sinT=w["sinT"][c], identb=w["identb"], consts=consts,
                    onesrow=onesrow) for c in range(NCORE)]
    res = run_bass_kernel_spmd(nc, in_maps, core_ids=list(range(NCORE)))
    return unshard_xT([r["xo"] for r in res.results])


def _attn1_inputs(inputs):
    key = "attn1_w"
    if key not in _CACHE:
        wqb = np.asarray(inputs["L1_attn_Wqb"], dtype=np.float32)
        perm = np.empty(1024, np.int64)
        for h in range(16):
            for j in range(32):
                perm[h * 64 + j] = h * 64 + 2 * j
                perm[h * 64 + 32 + j] = h * 64 + 2 * j + 1
        wqbp = np.ascontiguousarray(wqb[:, perm])
        wkvb = np.asarray(inputs["L1_attn_Wkvb"], dtype=np.float32)
        permk = np.empty(256, np.int64)
        permv = np.empty(256, np.int64)
        for kh in range(4):
            for j in range(32):
                permk[kh * 64 + j] = kh * 128 + 4 * j
                permk[kh * 64 + 32 + j] = kh * 128 + 4 * j + 2
            for dd in range(64):
                permv[kh * 64 + dd] = kh * 128 + 2 * dd + 1
        wkvbk = np.ascontiguousarray(wkvb[:, permk])
        wkvbv = np.ascontiguousarray(wkvb[:, permv])
        tabs = rope_tables()
        cosF, sinF = [], []
        for c in range(NCORE):
            sl = slice(LOCP * c, LOCP * (c + 1))
            c32 = tabs["l1_cos"][sl].T
            s32 = tabs["l1_sin"][sl].T
            cosF.append(np.ascontiguousarray(np.vstack([c32] * 4)))
            sinF.append(np.ascontiguousarray(
                np.vstack([-s32, s32, -s32, s32])))
        p_ = np.arange(128)[:, None]
        m_ = np.arange(896)[None, :]
        maskM = (p_ <= m_ - 384).astype(np.float32)
        vtempl = np.zeros((8 * TOKS, 65), np.float32)
        vtempl[:, 64] = 1.0
        _CACHE[key] = dict(
            wqa=np.ascontiguousarray(np.asarray(inputs["L1_attn_Wqa"],
                                                dtype=np.float32)),
            wqbp=wqbp,
            wkva=np.ascontiguousarray(np.asarray(inputs["L1_attn_Wkva"],
                                                 dtype=np.float32)),
            wkvbk=wkvbk, wkvbv=wkvbv,
            wout=np.ascontiguousarray(np.asarray(inputs["L1_attn_Wout"],
                                                 dtype=np.float32)),
            boutc=colpack(inputs["L1_attn_bout"]),
            cosF=cosF, sinF=sinF, maskM=maskM, vtempl=vtempl)
    return _CACHE[key]


def run_attn1(x, inputs):
    nc = _CACHE.get("nc_attn1")
    if nc is None:
        nc = _CACHE["nc_attn1"] = build_attn1()
    shards = shard_xT(x)
    w = _attn1_inputs(inputs)
    consts = np.stack([np.ones(128, np.float32),
                       np.full(128, EPS, np.float32)], axis=1)
    onesrow = np.ones((1, 128), np.float32)
    in_maps = [dict(xt=shards[c], wqa=w["wqa"], wqbp=w["wqbp"],
                    wkva=w["wkva"], wkvbk=w["wkvbk"], wkvbv=w["wkvbv"],
                    wout=w["wout"], boutc=w["boutc"], cosF=w["cosF"][c],
                    sinF=w["sinF"][c], maskM=w["maskM"], vtempl=w["vtempl"],
                    consts=consts, onesrow=onesrow) for c in range(NCORE)]
    res = run_bass_kernel_spmd(nc, in_maps, core_ids=list(range(NCORE)))
    return unshard_xT([r["xo"] for r in res.results])


def kernel(**inputs):
    return np.ascontiguousarray(run_fused(inputs), dtype=np.float32)
